# version 02
# baseline (speedup 1.0000x reference)
import math
import numpy as np
import ml_dtypes
import concourse.bass as bass
import concourse.mybir as mybir
from concourse.bass_utils import run_bass_kernel_spmd

F32 = mybir.dt.float32
BF16 = mybir.dt.bfloat16
I32 = mybir.dt.int32
AF = mybir.ActivationFunctionType
ALU = mybir.AluOpType
AX = mybir.AxisListType

NCORES = 8
B = 2
S = 4096
D = 4096
TPC = 1024
H = 32
KVH = 8
DH = 128
G = 4
PAIRS = 2
CB = 16
SB = 16
NSEL = 4
WIN = 64
NC_CHUNKS = S // CB
CDIM = CB * DH
SCALE = DH ** -0.5
L1_CHUNK = 16
L1_TOPK = 8
NEGBIG = -30000.0
MSK = 80.0
EPS = 1e-5

DT_NP = {F32: np.float32, BF16: ml_dtypes.bfloat16, I32: np.int32}


ENGINES = ("sync", "vector", "scalar", "tensor", "gpsimd")
DMA_ENGINES = ("sync", "vector", "scalar", "gpsimd")


class _Op:
    __slots__ = ("eng", "fn", "reads", "writes", "kind", "sem", "val", "idx",
                 "stream", "floor")

    def __init__(self, eng, fn, reads, writes, kind, stream=None):
        self.eng = eng
        self.fn = fn
        self.reads = reads
        self.writes = writes
        self.kind = kind
        self.stream = stream
        self.sem = None
        self.val = 0
        self.floor = None


class Prog:

    def __init__(self, nc, same_engine_edges=False):
        self.nc = nc
        self.same_engine_edges = same_engine_edges
        self.ops = []
        self.last_writer = {}
        self.readers = {}
        self.counts = {}
        self.ccn = 0
        self.floor = {}

    def _sem_for(self, op):
        if op.kind == "c":
            return "e_" + op.eng, 1
        if op.kind == "d":
            assert op.stream is not None, "dma ops need an explicit stream"
            return "d:" + op.stream, 16
        self.ccn += 1
        return f"cc_{self.ccn}", 1

    def add(self, eng, fn, reads=(), writes=(), kind="c", stream=None):
        op = _Op(eng, fn, tuple(reads), tuple(writes), kind, stream)
        sem, inc = self._sem_for(op)
        self.counts[sem] = self.counts.get(sem, 0) + inc
        op.sem, op.val = sem, self.counts[sem]
        op.floor = self.floor
        op.idx = len(self.ops)
        self.ops.append(op)
        for k in op.reads:
            self.readers.setdefault(k, []).append(op)
        for k in op.writes:
            self.last_writer[k] = op
            self.readers[k] = []
            self.readers[k].append(op)
        return op

    def dma(self, eng, out, in_, reads=(), writes=(), stream=None, **kw):
        kind = kw.pop("kind", "d")
        return self.add(eng, lambda e: e.dma_start(out=out, in_=in_, **kw),
                        reads=reads, writes=writes, kind=kind, stream=stream)

    def barrier(self):
        self.floor = dict(self.counts)

    def _deps(self, op):
        need = dict(op.floor)
        def _add(src):
            if src is op:
                return
            if src.kind == "c" and src.eng == op.eng and not self.same_engine_edges:
                return
            if src.kind == "d":
                v = self.sem_prefix[src.sem][op.idx]
            else:
                v = src.val
            if v > need.get(src.sem, 0):
                need[src.sem] = v
        for k in op.reads:
            w = self.last_writer_at.get((k, op.idx))
            if w is not None:
                _add(w)
        for k in op.writes:
            for r in self.readers_at.get((k, op.idx), ()):
                _add(r)
        return need

    def emit(self, block, final_outputs_engine="gpsimd"):
        self.sem_prefix = {}
        dma_sems = {op.sem for op in self.ops if op.kind == "d"}
        for semname in dma_sems:
            pref = [0] * (len(self.ops) + 1)
            run = 0
            for i, op in enumerate(self.ops):
                pref[i] = run
                if op.sem == semname:
                    run = op.val
            pref[len(self.ops)] = run
            self.sem_prefix[semname] = pref
        self.last_writer_at = {}
        self.readers_at = {}
        lw = {}
        rd = {}
        for op in self.ops:
            for k in op.reads:
                if k in lw:
                    self.last_writer_at[(k, op.idx)] = lw[k]
            for k in op.writes:
                if k in lw:
                    self.last_writer_at.setdefault((k, op.idx), lw[k])
                self.readers_at[(k, op.idx)] = tuple(rd.get(k, ()))
            for k in op.reads:
                rd.setdefault(k, []).append(op)
            for k in op.writes:
                lw[k] = op
                rd[k] = [op]

        sems = {name: self.nc.alloc_semaphore(name) for name in self.counts}
        per_eng = {e: [] for e in ENGINES}
        for op in self.ops:
            per_eng[op.eng].append(op)
        finals = dict(self.counts)

        waited = {e: {} for e in ENGINES}

        def mk_section(eng):
            ops = per_eng[eng]

            def body(ee):
                hw = waited[eng]
                for op in ops:
                    need = self._deps(op)
                    for semname, v in sorted(need.items()):
                        if semname == "e_" + eng and not self.same_engine_edges:
                            continue
                        if hw.get(semname, 0) < v:
                            ee.wait_ge(sems[semname], v)
                            hw[semname] = v
                    inst = op.fn(ee)
                    inc = 1 if op.kind == "c" else 16
                    if op.kind == "cc":
                        inst.then_inc(sems[op.sem])
                    else:
                        inst.then_inc(sems[op.sem], inc)
                if eng == final_outputs_engine:
                    for semname, v in sorted(finals.items()):
                        if semname == "e_" + eng:
                            continue
                        if waited[eng].get(semname, 0) < v:
                            ee.wait_ge(sems[semname], v)
                            waited[eng][semname] = v
            return body

        engs_used = [e for e in ENGINES if per_eng[e] or e == final_outputs_engine]
        for e in engs_used:
            getattr(block, e)(mk_section(e))


def ceil_div(a, b):
    return -(-a) // b


DT_BYTES = {F32: 4, I32: 4, BF16: 2}


class Arena:
    def __init__(self, nc, base, limit):
        self.nc = nc
        self.base = base
        self.limit = limit
        self.off = base
        self.high = base

    def alloc(self, name, shape, dtype):
        nb = shape[-1] * DT_BYTES[dtype]
        off = (self.off + 31) & ~31
        assert off + nb <= self.limit, f"SBUF arena overflow at {name}: {off + nb} > {self.limit}"
        t = self.nc.alloc_sbuf_tensor_at(name, shape, dtype, offset=off)
        self.off = off + nb
        self.high = max(self.high, self.off)
        return t

    def reset(self):
        self.off = self.base


class _ArenaNc:

    def __init__(self, arena, prefix):
        self._arena = arena
        self._prefix = prefix

    def alloc_sbuf_tensor(self, name, shape, dtype):
        return self._arena.alloc(self._prefix + name, shape, dtype)


def _f32(x):
    return np.ascontiguousarray(x, dtype=np.float32)


def host_constants():
    C = {}
    C["ones_col"] = np.ones((128, 1), np.float32)
    C["ones_row"] = np.ones((1, 128), np.float32)
    C["ident_bf"] = np.eye(128).astype(ml_dtypes.bfloat16)
    C["bitmask"] = np.broadcast_to(
        (np.int64(1) << (np.arange(4096, dtype=np.int64) % 32)).astype(np.int64),
        (128, 4096)).astype(np.uint32).view(np.int32).copy()
    return C


def host_inputs_per_core(inp):
    cores = []
    x = np.asarray(inp["L0_attn_hidden_states"], np.float32)
    qw = np.asarray(inp["L0_ffn_qweight"])
    lut4 = _f32(inp["L0_ffn_lut4"])
    lut8 = _f32(inp["L0_ffn_lut8"])
    for c in range(NCORES):
        b, q4 = c // 4, c % 4
        d = {}
        d["x0"] = _f32(x[b, 1024 * q4:1024 * (q4 + 1), :].T)
        o0 = 512 * c
        d["qwT"] = np.ascontiguousarray(
            qw[:, o0:o0 + 512, :].transpose(0, 2, 1))
        d["lut4"] = _f32(lut4[o0:o0 + 512])
        d["lut8"] = _f32(lut8[o0:o0 + 512])
        cores.append(d)
    return cores


def emit_dequant(P, nc, ap, sb, ps):
    ot_range = range(4)
    for ot in ot_range:
        P.dma("sync", sb["lut4"][:, 16 * ot:16 * (ot + 1)], ap["lut4"][128 * ot:128 * (ot + 1), :],
              writes=[("lut4", ot)], stream="init")
        P.dma("sync", sb["lut8"][:, 256 * ot:256 * (ot + 1)], ap["lut8"][128 * ot:128 * (ot + 1), :],
              writes=[("lut8", ot)], stream="init")
    P.dma("sync", sb["bitmask"][:], ap["bitmask"], writes=["bitmask"], stream="init")
    P.dma("sync", sb["ident"][:], ap["ident_bf"], writes=["ident"], stream="init")

    for ot in ot_range:
        for p in range(8):
            P.dma("gpsimd", sb["qw"][:, 128 * p:128 * (p + 1)],
                  ap["qw"][p, 128 * ot:128 * (ot + 1), :],
                  writes=[("qw", p)], stream="qw")
        for p in range(8):
            qv = sb["qw"][:, 128 * p:128 * (p + 1)].rearrange(
                "o (w u) -> o w u", u=1).broadcast_to((128, 128, 32))
            bm = sb["bitmask"][:].rearrange("o (w v) -> o w v", v=32)
            P.add("vector", lambda e, qv=qv, bm=bm: e.tensor_tensor(
                out=sb["tmp32"][:].rearrange("o (w v) -> o w v", v=32),
                in0=qv, in1=bm, op=ALU.bitwise_and),
                reads=[("qw", p), "bitmask"], writes=["tmp32"])
            P.add("vector", lambda e: e.tensor_scalar(
                out=sb["bit"][:], in0=sb["tmp32"][:], scalar1=0.0, scalar2=None,
                op0=ALU.not_equal),
                reads=["tmp32"], writes=["bit"])
            tgt = "idx4" if p < 4 else "idx8"
            if p == 0:
                P.add("vector", lambda e: e.tensor_scalar(
                    out=sb["idx4"][:], in0=sb["bit"][:], scalar1=1.0, scalar2=None,
                    op0=ALU.mult), reads=["bit"], writes=["idx4"])
            elif p == 4:
                P.add("vector", lambda e: e.tensor_copy(out=sb["idx8"][:], in_=sb["idx4"][:]),
                      reads=["idx4"], writes=["idx8"])
                P.add("vector", lambda e, s=float(1 << p): e.scalar_tensor_tensor(
                    out=sb["idx8"][:], in0=sb["bit"][:], scalar=s, in1=sb["idx8"][:],
                    op0=ALU.mult, op1=ALU.add), reads=["bit", "idx8"], writes=["idx8"])
            else:
                P.add("vector", lambda e, s=float(1 << p), t=tgt: e.scalar_tensor_tensor(
                    out=sb[t][:], in0=sb["bit"][:], scalar=s, in1=sb[t][:],
                    op0=ALU.mult, op1=ALU.add), reads=["bit", tgt], writes=[tgt])

        for (wname, idxname, lutname, ncodes, lutw) in (
                ("wl", "idx4", "lut4", 16, 16), ("wh", "idx8", "lut8", 256, 256)):
            for ci in range(ncodes):
                lut_col = sb[lutname][:, lutw * ot + ci:lutw * ot + ci + 1]
                if ci == 0:
                    P.add("vector", lambda e, ci=ci, lc=lut_col, w=wname, ix=idxname:
                          e.tensor_scalar(out=sb[w][:], in0=sb[ix][:],
                                          scalar1=float(ci), scalar2=lc,
                                          op0=ALU.is_equal, op1=ALU.mult),
                          reads=[idxname, ("lutld", lutname, ot)] if False else [idxname, (lutname, ot)],
                          writes=[wname])
                else:
                    P.add("vector", lambda e, ci=ci, lc=lut_col, ix=idxname:
                          e.tensor_scalar(out=sb["m"][:], in0=sb[ix][:],
                                          scalar1=float(ci), scalar2=lc,
                                          op0=ALU.is_equal, op1=ALU.mult),
                          reads=[idxname, (lutname, ot)], writes=["m"])
                    P.add("vector", lambda e, w=wname: e.tensor_tensor(
                        out=sb[w][:], in0=sb["m"][:], in1=sb[w][:], op=ALU.add),
                        reads=["m", wname], writes=[wname])

        for qi, (wname, outname) in enumerate((("wl", "wl_sh"), ("wh", "wh_sh"))):
            q = "sync" if qi == 0 else "scalar"
            for it in range(32):
                P.add(q, lambda e, it=it, w=wname: e.dma_start_transpose(
                    sb["stag" + w][:, 128 * it:128 * (it + 1)],
                    sb[w][:, 128 * it:128 * (it + 1)]),
                    reads=[wname], writes=[("stag", wname, it)], kind="d",
                    stream="tp" + wname)
            P.dma(q, ap[outname][ot].rearrange("(it p) o -> p it o", p=128),
                  sb["stag" + wname][:].rearrange("p (it o) -> p it o", o=128),
                  reads=[("stag", wname, it2) for it2 in range(32)],
                  writes=[(outname, ot)], stream="st" + wname)
    return


def build_p0():
    nc = bass.Bass("TRN2", target_bir_lowering=False)
    ap = {}
    ap["qw"] = nc.dram_tensor("qw", [8, 512, 128], I32, kind="ExternalInput").ap()
    ap["lut4"] = nc.dram_tensor("lut4", [512, 16], F32, kind="ExternalInput").ap()
    ap["lut8"] = nc.dram_tensor("lut8", [512, 256], F32, kind="ExternalInput").ap()
    ap["bitmask"] = nc.dram_tensor("bitmask", [128, 4096], I32, kind="ExternalInput").ap()
    ap["ident_bf"] = nc.dram_tensor("ident_bf", [128, 128], BF16, kind="ExternalInput").ap()
    ap["wl_sh"] = nc.dram_tensor("wl_sh", [4, 4096, 128], BF16, kind="ExternalOutput").ap()
    ap["wh_sh"] = nc.dram_tensor("wh_sh", [4, 4096, 128], BF16, kind="ExternalOutput").ap()

    sb = {}
    sb["qw"] = nc.alloc_sbuf_tensor("s_qw", [128, 8 * 128], I32)
    sb["lut4"] = nc.alloc_sbuf_tensor("s_lut4", [128, 64], F32)
    sb["lut8"] = nc.alloc_sbuf_tensor("s_lut8", [128, 1024], F32)
    sb["bitmask"] = nc.alloc_sbuf_tensor("s_bitmask", [128, 4096], I32)
    sb["ident"] = nc.alloc_sbuf_tensor("s_ident", [128, 128], BF16)
    sb["tmp32"] = nc.alloc_sbuf_tensor("s_tmp32", [128, 4096], I32)
    sb["bit"] = nc.alloc_sbuf_tensor("s_bit", [128, 4096], BF16)
    sb["idx4"] = nc.alloc_sbuf_tensor("s_idx4", [128, 4096], BF16)
    sb["idx8"] = nc.alloc_sbuf_tensor("s_idx8", [128, 4096], BF16)
    sb["wl"] = nc.alloc_sbuf_tensor("s_wl", [128, 4096], BF16)
    sb["wh"] = nc.alloc_sbuf_tensor("s_wh", [128, 4096], BF16)
    sb["m"] = nc.alloc_sbuf_tensor("s_m", [128, 4096], BF16)
    sb["stagwl"] = nc.alloc_sbuf_tensor("s_stagwl", [128, 4096], BF16)
    sb["stagwh"] = nc.alloc_sbuf_tensor("s_stagwh", [128, 4096], BF16)
    ps = []

    P = Prog(nc)
    with nc.Block() as block:
        emit_dequant(P, nc, ap, sb, ps)
        P.emit(block)
    return nc


def emit_ln(P, nc, ap, sb, ps, compute_e=False, ln_out_hbm=None):
    x_in = ap["x_in"]
    for dt in range(32):
        xb = ("x", dt % 2)
        P.dma("sync", sb["xA"][dt % 2][:], x_in[dt], writes=[xb], stream=f"xA{dt % 2}")
        P.add("scalar", lambda e, dt=dt: e.square(out=sb["xsq"][dt % 2][:], in_=sb["xA"][dt % 2][:]),
              reads=[xb], writes=[("xsq", dt % 2)])
        for h in range(2):
            P.add("tensor", lambda e, dt=dt, h=h: e.matmul(
                ps[h].ap()[0:1, :], sb["ones_col"][:], sb["xA"][dt % 2][:, 512 * h:512 * (h + 1)],
                start=(dt == 0), stop=(dt == 31)),
                reads=[xb, "ones_col"], writes=[("ps", h)])
            P.add("tensor", lambda e, dt=dt, h=h: e.matmul(
                ps[2 + h].ap()[0:1, :], sb["ones_col"][:], sb["xsq"][dt % 2][:, 512 * h:512 * (h + 1)],
                start=(dt == 0), stop=(dt == 31)),
                reads=[("xsq", dt % 2), "ones_col"], writes=[("ps", 2 + h)])
    for h in range(2):
        sl = slice(512 * h, 512 * (h + 1))
        P.add("scalar", lambda e, h=h, sl=sl: e.mul(out=sb["mean"][0:1, sl], in_=ps[h].ap()[0:1, :], mul=1.0 / 4096),
              reads=[("ps", h)], writes=[("mean", h), ("ps", h)])
        P.add("scalar", lambda e, h=h, sl=sl: e.mul(out=sb["msq"][0:1, sl], in_=ps[2 + h].ap()[0:1, :], mul=1.0 / 4096),
              reads=[("ps", 2 + h)], writes=[("msq", h), ("ps", 2 + h)])
    P.add("scalar", lambda e: e.square(out=sb["m2"][0:1, :], in_=sb["mean"][0:1, :]),
          reads=[("mean", 0), ("mean", 1)], writes=["m2"])
    P.add("vector", lambda e: e.scalar_tensor_tensor(out=sb["var"][0:1, :], in0=sb["msq"][0:1, :],
                                                     scalar=EPS, in1=sb["m2"][0:1, :],
                                                     op0=ALU.add, op1=ALU.subtract),
          reads=["m2", ("msq", 0), ("msq", 1)], writes=["var"])
    P.add("scalar", lambda e: e.activation(out=sb["srv"][0:1, :], in_=sb["var"][0:1, :],
                                           func=AF.Sqrt, bias=0.0, scale=1.0),
          reads=["var"], writes=["srv"])
    P.add("vector", lambda e: e.reciprocal(out=sb["rstd"][0:1, :], in_=sb["srv"][0:1, :]),
          reads=["srv"], writes=["rstd"])
    P.add("vector", lambda e: e.tensor_mul(out=sb["nmr"][0:1, :], in0=sb["mean"][0:1, :], in1=sb["rstd"][0:1, :]),
          reads=[("mean", 0), ("mean", 1), "rstd"], writes=["nmr0"])
    P.add("vector", lambda e: e.tensor_scalar_mul(out=sb["nmr"][0:1, :], in0=sb["nmr"][0:1, :], scalar1=-1.0),
          reads=["nmr0"], writes=["nmr"])
    for h in range(2):
        sl = slice(512 * h, 512 * (h + 1))
        P.add("tensor", lambda e, h=h, sl=sl: e.matmul(
            ps[h].ap()[:, :], sb["ones_row"][0:1, :], sb["rstd"][0:1, sl], start=True, stop=True),
            reads=["rstd", "ones_row", ("ps", h)], writes=[("ps", h)])
        P.add("tensor", lambda e, h=h, sl=sl: e.matmul(
            ps[2 + h].ap()[:, :], sb["ones_row"][0:1, :], sb["nmr"][0:1, sl], start=True, stop=True),
            reads=["nmr", "ones_row", ("ps", 2 + h)], writes=[("ps", 2 + h)])
        P.add("scalar", lambda e, h=h, sl=sl: e.copy(out=sb["R"][:, sl], in_=ps[h].ap()[:, :]),
              reads=[("ps", h)], writes=[("R", h), ("ps", h)])
        P.add("scalar", lambda e, h=h, sl=sl: e.copy(out=sb["A"][:, sl], in_=ps[2 + h].ap()[:, :]),
              reads=[("ps", 2 + h)], writes=[("A", h), ("ps", 2 + h)])
    Rk = [("R", 0), ("R", 1)]
    Ak = [("A", 0), ("A", 1)]
    for dt in range(32):
        xb = ("x", dt % 2)
        P.dma("sync", sb["xA"][dt % 2][:], x_in[dt], writes=[xb], stream=f"xA{dt % 2}")
        P.add("vector", lambda e, dt=dt: e.tensor_mul(
            out=sb["t"][dt % 2][:], in0=sb["xA"][dt % 2][:], in1=sb["R"][:]),
            reads=[xb] + Rk, writes=[("t", dt % 2)])
        P.add("vector", lambda e, dt=dt: e.tensor_add(
            out=sb["t"][dt % 2][:], in0=sb["t"][dt % 2][:], in1=sb["A"][:]),
            reads=[("t", dt % 2)] + Ak, writes=[("t", dt % 2)])
        P.add("scalar", lambda e, dt=dt: e.copy(
            out=sb["ln"][:, 1024 * dt:1024 * (dt + 1)], in_=sb["t"][dt % 2][:]),
            reads=[("t", dt % 2)], writes=[("ln", dt)])
        if compute_e:
            P.add("scalar", lambda e, dt=dt: e.square(out=sb["xsq"][dt % 2][:], in_=sb["t"][dt % 2][:]),
                  reads=[("t", dt % 2)], writes=[("xsq", dt % 2)])
            for h in range(2):
                P.add("tensor", lambda e, dt=dt, h=h: e.matmul(
                    ps[4 + h].ap()[0:1, :], sb["ones_col"][:], sb["xsq"][dt % 2][:, 512 * h:512 * (h + 1)],
                    start=(dt == 0), stop=(dt == 31)),
                    reads=[("xsq", dt % 2), "ones_col"], writes=[("ps", 4 + h)])
        if ln_out_hbm is not None:
            P.dma("scalar", ln_out_hbm[dt], sb["ln"][:, 1024 * dt:1024 * (dt + 1)],
                  reads=[("ln", dt)], writes=[("lnout", dt)], stream="lnst")
    if compute_e:
        for h in range(2):
            sl = slice(512 * h, 512 * (h + 1))
            P.add("scalar", lambda e, h=h, sl=sl: e.activation(
                out=sb["srv"][0:1, sl], in_=ps[4 + h].ap()[0:1, :], func=AF.Sqrt, bias=0.0, scale=1.0),
                reads=[("ps", 4 + h)], writes=["srv", ("e", h), ("ps", 4 + h)])
        P.add("vector", lambda e: e.tensor_scalar(
            out=sb["m01"][0:1, :], in0=sb["srv"][0:1, :], scalar1=64.0, scalar2=None, op0=ALU.is_gt),
            reads=[("e", 0), ("e", 1)], writes=["m01"])
        for h in range(2):
            sl = slice(512 * h, 512 * (h + 1))
            P.add("tensor", lambda e, h=h, sl=sl: e.matmul(
                ps[4 + h].ap()[:, :], sb["ones_row"][0:1, :], sb["m01"][0:1, sl], start=True, stop=True),
                reads=["m01", "ones_row", ("ps", 4 + h)], writes=[("ps", 4 + h)])
            P.add("scalar", lambda e, h=h, sl=sl: e.copy(out=sb["Mb"][:, sl], in_=ps[4 + h].ap()[:, :]),
                  reads=[("ps", 4 + h)], writes=[("Mb", h), ("ps", 4 + h)])


def alloc_ln_bufs(nc, sb):
    sb["ones_col"] = nc.alloc_sbuf_tensor("s_ones_col", [128, 1], F32)
    sb["ones_row"] = nc.alloc_sbuf_tensor("s_ones_row", [1, 128], F32)
    sb["xA"] = [nc.alloc_sbuf_tensor(f"s_xA{i}", [128, 1024], F32) for i in range(2)]
    sb["xsq"] = [nc.alloc_sbuf_tensor(f"s_xsq{i}", [128, 1024], F32) for i in range(2)]
    sb["t"] = [nc.alloc_sbuf_tensor(f"s_t{i}", [128, 1024], F32) for i in range(2)]
    sb["ln"] = nc.alloc_sbuf_tensor("s_ln", [128, 32 * 1024], BF16)
    for nm in ("mean", "msq", "m2", "var", "srv", "rstd", "nmr", "m01"):
        sb[nm] = nc.alloc_sbuf_tensor("s_" + nm, [1, 1024], F32)
    sb["R"] = nc.alloc_sbuf_tensor("s_R", [128, 1024], F32)
    sb["A"] = nc.alloc_sbuf_tensor("s_A", [128, 1024], F32)
    sb["Mb"] = nc.alloc_sbuf_tensor("s_Mb", [128, 1024], F32)


def ln_load_consts(P, ap, sb):
    P.dma("sync", sb["ones_col"][:], ap["ones_col"], writes=["ones_col"], stream="init")
    P.dma("sync", sb["ones_row"][:], ap["ones_row"], writes=["ones_row"], stream="init")


def emit_ffn(P, nc, ap, sb, ps, kind):
    dual = (kind == "l0")
    P.dma("sync", sb["bias"][:], ap["bias"], writes=["bias"], stream="init")
    emit_ln(P, nc, ap, sb, ps, compute_e=dual)
    mats = ("wl", "wh") if dual else ("wf",)
    for ot in range(32):
        par = ot % 2
        for mi, m in enumerate(mats):
            P.dma("sync" if mi == 0 else "scalar",
                  sb[m][par][:].rearrange("p (dt o) -> p dt o", o=128),
                  ap[m + "_t"][ot].rearrange("(dt p) o -> p dt o", p=128),
                  writes=[(m, par)], stream=f"{m}{par}")
        P.dma("gpsimd", sb["xo"][par][:], ap["x_in"][ot], writes=[("xo", par)], stream=f"xo{par}")
        for mi, m in enumerate(mats):
            for h in range(2):
                bi = par * 4 + mi * 2 + h if dual else par * 4 + h
                bank = ps[bi]
                wbuf = sb[m][par]
                for dt in range(32):
                    P.add("tensor", lambda e, wbuf=wbuf, dt=dt, h=h, bank=bank: e.matmul(
                        bank.ap()[:, 0:512],
                        wbuf[:, 128 * dt:128 * (dt + 1)],
                        sb["ln"][:, 1024 * dt + 512 * h:1024 * dt + 512 * h + 512],
                        start=(dt == 0), stop=(dt == 31)),
                        reads=[(m, par), ("ln", dt)], writes=[("ps", bi)])
        for h in range(2):
            sl = slice(512 * h, 512 * (h + 1))
            bias_col = sb["bias"][:, ot:ot + 1]
            if dual:
                bil, bih = par * 4 + h, par * 4 + 2 + h
                bl, bh = ps[bil], ps[bih]
                P.add("scalar", lambda e, bl=bl, sl=sl, bc=bias_col: e.activation(
                    out=sb["a"][:, sl], in_=bl.ap()[:, 0:512], func=AF.Identity, bias=bc, scale=1.0),
                    reads=[("ps", bil), "bias"], writes=[("a", h), ("ps", bil)])
                P.add("scalar", lambda e, bh=bh, sl=sl, bc=bias_col: e.activation(
                    out=sb["hh"][:, sl], in_=bh.ap()[:, 0:512], func=AF.Identity, bias=bc, scale=1.0),
                    reads=[("ps", bih), "bias"], writes=[("hh", h), ("ps", bih)])
                P.add("vector", lambda e, sl=sl: e.tensor_sub(
                    out=sb["hh"][:, sl], in0=sb["hh"][:, sl], in1=sb["a"][:, sl]),
                    reads=[("a", h), ("hh", h)], writes=[("hh", h)])
                P.add("vector", lambda e, sl=sl: e.tensor_mul(
                    out=sb["hh"][:, sl], in0=sb["hh"][:, sl], in1=sb["Mb"][:, sl]),
                    reads=[("hh", h), ("Mb", 0), ("Mb", 1)], writes=[("hh", h)])
                P.add("vector", lambda e, sl=sl: e.tensor_add(
                    out=sb["a"][:, sl], in0=sb["a"][:, sl], in1=sb["hh"][:, sl]),
                    reads=[("a", h), ("hh", h)], writes=[("a", h)])
            else:
                bi = par * 4 + h
                bank = ps[bi]
                P.add("scalar", lambda e, bank=bank, sl=sl, bc=bias_col: e.activation(
                    out=sb["a"][:, sl], in_=bank.ap()[:, 0:512], func=AF.Identity, bias=bc, scale=1.0),
                    reads=[("ps", bi), "bias"], writes=[("a", h), ("ps", bi)])
            P.add("vector", lambda e, sl=sl, par=par: e.scalar_tensor_tensor(
                out=sb["xo"][par][:, sl], in0=sb["a"][:, sl], scalar=0.5, in1=sb["xo"][par][:, sl],
                op0=ALU.mult, op1=ALU.add),
                reads=[("a", h), ("xo", par)], writes=[("xo", par)])
        P.dma("gpsimd", ap["x_out"][ot], sb["xo"][par][:],
              reads=[("xo", par)], writes=[("xout", ot)], stream=f"xst{par}")


def alloc_ffn_bufs(nc, sb, dual):
    mats = ("wl", "wh") if dual else ("wf",)
    for m in mats:
        sb[m] = [nc.alloc_sbuf_tensor(f"s_{m}{i}", [128, 4096], BF16) for i in range(2)]
    sb["bias"] = nc.alloc_sbuf_tensor("s_bias", [128, 32], F32)
    sb["a"] = nc.alloc_sbuf_tensor("s_a", [128, 1024], F32)
    sb["hh"] = nc.alloc_sbuf_tensor("s_hh", [128, 1024], F32)
    sb["xo"] = [nc.alloc_sbuf_tensor(f"s_xo{i}", [128, 1024], F32) for i in range(2)]


def build_ffn(kind):
    nc = bass.Bass("TRN2", target_bir_lowering=False)
    dual = (kind == "l0")
    ap = {}
    ap["x_in"] = nc.dram_tensor("x_in", [32, 128, 1024], F32, kind="ExternalInput").ap()
    ap["ones_col"] = nc.dram_tensor("ones_col", [128, 1], F32, kind="ExternalInput").ap()
    ap["ones_row"] = nc.dram_tensor("ones_row", [1, 128], F32, kind="ExternalInput").ap()
    ap["bias"] = nc.dram_tensor("bias", [128, 32], F32, kind="ExternalInput").ap()
    if dual:
        ap["wl_t"] = nc.dram_tensor("wl_t", [32, 4096, 128], BF16, kind="ExternalInput").ap()
        ap["wh_t"] = nc.dram_tensor("wh_t", [32, 4096, 128], BF16, kind="ExternalInput").ap()
    else:
        ap["wf_t"] = nc.dram_tensor("wf_t", [32, 4096, 128], BF16, kind="ExternalInput").ap()
    ap["x_out"] = nc.dram_tensor("x_out", [32, 128, 1024], F32, kind="ExternalOutput").ap()

    sb = {}
    alloc_ln_bufs(nc, sb)
    alloc_ffn_bufs(nc, sb, dual)
    ps = [nc.alloc_psum_tensor(f"ps{i}", [128, 512], F32) for i in range(8)]
    P = Prog(nc)
    with nc.Block() as block:
        ln_load_consts(P, ap, sb)
        emit_ffn(P, nc, ap, sb, ps, kind)
        P.emit(block)
    return nc


CAST_SPECS = [
    ("woT0_sl", 4096, 512, "wo0_sh"),
    ("woT1_sl", 4096, 512, "wo1_sh"),
    ("kw1T_sl", 2048, 256, "kw1_sh"),
    ("vw1T_sl", 2048, 256, "vw1_sh"),
    ("kw2T", 2048, 128, "kw2_bf"),
    ("vw2T", 2048, 128, "vw2_bf"),
    ("wqkvT_sl", 4096, 1536, "wqkv_bf"),
    ("wqT_sl", 4096, 1024, "wq_bf"),
    ("wkT_sl", 4096, 256, "wk_bf"),
    ("wvT_sl", 4096, 256, "wv_bf"),
]


def _emit_cast(P, sb, in_ap, out_ap, rows, cols, oname, base_otl=0):
    nr = rows // 128
    for r in range(nr):
        par = r % 2
        P.dma("sync", sb["cf"][par][:, 0:cols], in_ap[128 * r:128 * (r + 1), :],
              writes=[("cf", par)], stream=f"cf{par}")
        P.add("vector", lambda e, par=par, cols=cols: e.tensor_copy(
            out=sb["cb"][par][:, 0:cols], in_=sb["cf"][par][:, 0:cols]),
            reads=[("cf", par)], writes=[("cb", par)])
        for otl in range(cols // 128):
            P.dma("gpsimd", out_ap[base_otl + otl, 128 * r:128 * (r + 1), :],
                  sb["cb"][par][:, 128 * otl:128 * (otl + 1)],
                  reads=[("cb", par)], writes=[(oname, r, base_otl + otl)],
                  stream=f"cst{par}")


def emit_weight_prep(P, nc, ap, sb, ps, cast_specs=CAST_SPECS, extra_casts=()):
    for (iname, rows, cols, oname) in cast_specs:
        _emit_cast(P, sb, ap[iname], ap[oname], rows, cols, oname)
    for (in_ap, out_ap, rows, cols, base_otl, oname) in extra_casts:
        _emit_cast(P, sb, in_ap, out_ap, rows, cols, oname, base_otl=base_otl)
    for r in range(32):
        par = r % 2
        P.dma("sync", sb["cf"][par][:, 0:24], ap["wgateT_sl"][128 * r:128 * (r + 1), :],
              writes=[("cf", par)], stream=f"cf{par}")
        P.add("vector", lambda e, par=par: e.tensor_copy(
            out=sb["cb"][par][:, 0:24], in_=sb["cf"][par][:, 0:24]),
            reads=[("cf", par)], writes=[("cb", par)])
        P.dma("gpsimd", ap["wgate_bf"][128 * r:128 * (r + 1), :], sb["cb"][par][:, 0:24],
              reads=[("cb", par)], writes=[("wgate_bf", r)], stream=f"cst{par}")
    P.dma("sync", sb["aer"][:].rearrange("p (t d) -> p t d", d=4096),
          ap["A_er"].rearrange("(t p) d -> p t d", p=128), writes=["aer_f"], stream="init2")
    P.dma("sync", sb["bt"][:].rearrange("p (t o) -> p t o", o=512),
          ap["Bt_sl"].rearrange("(t p) o -> p t o", p=128), writes=["bt_f"], stream="init2")
    for t in range(2):
        P.dma("sync", sb["probs"][t][:], ap["probs_rep"][t],
              writes=[("probs", t)], stream="init2")
    P.add("vector", lambda e: e.tensor_copy(out=sb["aer_bf"][:], in_=sb["aer"][:]),
          reads=["aer_f"], writes=["aer_bf"])
    for t in range(2):
        P.add("vector", lambda e, t=t: e.tensor_scalar(
            out=sb["bt_bf"][:, 512 * t:512 * (t + 1)], in0=sb["bt"][:, 512 * t:512 * (t + 1)],
            scalar1=sb["probs"][t][:, 0:1], scalar2=None, op0=ALU.mult),
            reads=["bt_f", ("probs", t)], writes=["bt_bf"])
    for dt in range(32):
        par = dt % 2
        P.dma("scalar", sb["wt"][par][:], ap["WT_sl"][128 * dt:128 * (dt + 1), :],
              writes=[("wt", par)], stream=f"wt{par}")
        for otl in range(4):
            bi = 4 + 2 * par + (otl % 2)
            P.add("tensor", lambda e, dt=dt, otl=otl, bi=bi: e.matmul(
                ps[bi].ap()[:, 0:128], sb["aer_bf"][:, 4096 * 0 + 128 * dt:4096 * 0 + 128 * dt + 128],
                sb["bt_bf"][:, 512 * 0 + 128 * otl:512 * 0 + 128 * otl + 128],
                start=True, stop=False),
                reads=["aer_bf", "bt_bf"], writes=[("ps", bi)])
            P.add("tensor", lambda e, dt=dt, otl=otl, bi=bi: e.matmul(
                ps[bi].ap()[:, 0:128], sb["aer_bf"][:, 4096 * 1 + 128 * dt:4096 * 1 + 128 * dt + 128],
                sb["bt_bf"][:, 512 * 1 + 128 * otl:512 * 1 + 128 * otl + 128],
                start=False, stop=True),
                reads=["aer_bf", "bt_bf"], writes=[("ps", bi)])
            P.add("vector", lambda e, par=par, otl=otl, bi=bi: e.tensor_add(
                out=sb["weff"][par][:, 128 * otl:128 * (otl + 1)],
                in0=ps[bi].ap()[:, 0:128],
                in1=sb["wt"][par][:, 128 * otl:128 * (otl + 1)]),
                reads=[("ps", bi), ("wt", par)], writes=[("weff", par, otl), ("ps", bi)])
            P.dma("gpsimd", ap["weff_sh"][otl, 128 * dt:128 * (dt + 1), :],
                  sb["weff"][par][:, 128 * otl:128 * (otl + 1)],
                  reads=[("weff", par, otl)], writes=[("weffo", dt, otl)], stream=f"wfs{par}")


def alloc_prep_bufs(nc, sb):
    sb["qw"] = nc.alloc_sbuf_tensor("s_qw", [128, 8 * 128], I32)
    sb["lut4"] = nc.alloc_sbuf_tensor("s_lut4", [128, 64], F32)
    sb["lut8"] = nc.alloc_sbuf_tensor("s_lut8", [128, 1024], F32)
    sb["bitmask"] = nc.alloc_sbuf_tensor("s_bitmask", [128, 4096], I32)
    sb["ident"] = nc.alloc_sbuf_tensor("s_ident", [128, 128], BF16)
    sb["tmp32"] = nc.alloc_sbuf_tensor("s_tmp32", [128, 4096], I32)
    sb["bit"] = nc.alloc_sbuf_tensor("s_bit", [128, 4096], BF16)
    sb["idx4"] = nc.alloc_sbuf_tensor("s_idx4", [128, 4096], BF16)
    sb["idx8"] = nc.alloc_sbuf_tensor("s_idx8", [128, 4096], BF16)
    sb["wl"] = nc.alloc_sbuf_tensor("s_wl", [128, 4096], BF16)
    sb["wh"] = nc.alloc_sbuf_tensor("s_wh", [128, 4096], BF16)
    sb["m"] = nc.alloc_sbuf_tensor("s_m", [128, 4096], BF16)
    sb["stagwl"] = nc.alloc_sbuf_tensor("s_stagwl", [128, 4096], BF16)
    sb["stagwh"] = nc.alloc_sbuf_tensor("s_stagwh", [128, 4096], BF16)
    sb["cf"] = [nc.alloc_sbuf_tensor(f"s_cf{i}", [128, 1536], F32) for i in range(2)]
    sb["cb"] = [nc.alloc_sbuf_tensor(f"s_cb{i}", [128, 1536], BF16) for i in range(2)]
    sb["aer"] = nc.alloc_sbuf_tensor("s_aer", [128, 2 * 4096], F32)
    sb["aer_bf"] = nc.alloc_sbuf_tensor("s_aer_bf", [128, 2 * 4096], BF16)
    sb["bt"] = nc.alloc_sbuf_tensor("s_bt", [128, 2 * 512], F32)
    sb["bt_bf"] = nc.alloc_sbuf_tensor("s_bt_bf", [128, 2 * 512], BF16)
    sb["probs"] = [nc.alloc_sbuf_tensor(f"s_probs{t}", [128, 1], F32) for t in range(2)]
    sb["wt"] = [nc.alloc_sbuf_tensor(f"s_wt{i}", [128, 512], F32) for i in range(2)]
    sb["weff"] = [nc.alloc_sbuf_tensor(f"s_weff{i}", [128, 512], BF16) for i in range(2)]


def build_p0full():
    nc = bass.Bass("TRN2", target_bir_lowering=False)
    ap = {}
    ap["qw"] = nc.dram_tensor("qw", [8, 512, 128], I32, kind="ExternalInput").ap()
    ap["lut4"] = nc.dram_tensor("lut4", [512, 16], F32, kind="ExternalInput").ap()
    ap["lut8"] = nc.dram_tensor("lut8", [512, 256], F32, kind="ExternalInput").ap()
    ap["bitmask"] = nc.dram_tensor("bitmask", [128, 4096], I32, kind="ExternalInput").ap()
    ap["ident_bf"] = nc.dram_tensor("ident_bf", [128, 128], BF16, kind="ExternalInput").ap()
    ap["wl_sh"] = nc.dram_tensor("wl_sh", [4, 4096, 128], BF16, kind="ExternalOutput").ap()
    ap["wh_sh"] = nc.dram_tensor("wh_sh", [4, 4096, 128], BF16, kind="ExternalOutput").ap()
    for (iname, rows, cols, oname) in CAST_SPECS:
        ap[iname] = nc.dram_tensor(iname, [rows, cols], F32, kind="ExternalInput").ap()
        ap[oname] = nc.dram_tensor(oname, [cols // 128, rows, 128], BF16, kind="ExternalOutput").ap()
    ap["wgateT_sl"] = nc.dram_tensor("wgateT_sl", [4096, 24], F32, kind="ExternalInput").ap()
    ap["wgate_bf"] = nc.dram_tensor("wgate_bf", [4096, 24], BF16, kind="ExternalOutput").ap()
    ap["A_er"] = nc.dram_tensor("A_er", [256, 4096], F32, kind="ExternalInput").ap()
    ap["Bt_sl"] = nc.dram_tensor("Bt_sl", [256, 512], F32, kind="ExternalInput").ap()
    ap["probs_rep"] = nc.dram_tensor("probs_rep", [2, 128, 1], F32, kind="ExternalInput").ap()
    ap["WT_sl"] = nc.dram_tensor("WT_sl", [4096, 512], F32, kind="ExternalInput").ap()
    ap["weff_sh"] = nc.dram_tensor("weff_sh", [4, 4096, 128], BF16, kind="ExternalOutput").ap()

    sb = {}
    alloc_prep_bufs(nc, sb)
    ps = [nc.alloc_psum_tensor(f"ps{i}", [128, 512], F32) for i in range(8)]
    P = Prog(nc)
    with nc.Block() as block:
        emit_dequant(P, nc, ap, sb, ps)
        emit_weight_prep(P, nc, ap, sb, ps)
        P.emit(block)
    return nc


def emit_wo(P, nc, ap, sb, ps):
    for dt in range(32):
        P.dma("sync", sb["ln"][:, 1024 * dt:1024 * (dt + 1)], ap["attn_in"][dt],
              writes=[("ln", dt)], stream=f"at{dt % 2}")
    for ot in range(32):
        par = ot % 2
        P.dma("scalar", sb["wf"][par][:].rearrange("p (dt o) -> p dt o", o=128),
              ap["wo_t"][ot].rearrange("(dt p) o -> p dt o", p=128),
              writes=[("wf", par)], stream=f"wf{par}")
        P.dma("gpsimd", sb["xo"][par][:], ap["x_in"][ot], writes=[("xo", par)], stream=f"xo{par}")
        for h in range(2):
            bi = par * 4 + h
            bank = ps[bi]
            wbuf = sb["wf"][par]
            for dt in range(32):
                P.add("tensor", lambda e, wbuf=wbuf, dt=dt, h=h, bank=bank: e.matmul(
                    bank.ap()[:, 0:512],
                    wbuf[:, 128 * dt:128 * (dt + 1)],
                    sb["ln"][:, 1024 * dt + 512 * h:1024 * dt + 512 * h + 512],
                    start=(dt == 0), stop=(dt == 31)),
                    reads=[("wf", par), ("ln", dt)], writes=[("ps", bi)])
            sl = slice(512 * h, 512 * (h + 1))
            P.add("vector", lambda e, sl=sl, par=par, bi=bi: e.tensor_add(
                out=sb["xo"][par][:, sl], in0=ps[bi].ap()[:, 0:512], in1=sb["xo"][par][:, sl]),
                reads=[("ps", bi), ("xo", par)], writes=[("xo", par), ("ps", bi)])
        P.dma("gpsimd", ap["x_out"][ot], sb["xo"][par][:],
              reads=[("xo", par)], writes=[("xout", ot)], stream=f"xst{par}")


def build_wo():
    nc = bass.Bass("TRN2", target_bir_lowering=False)
    ap = {}
    ap["x_in"] = nc.dram_tensor("x_in", [32, 128, 1024], F32, kind="ExternalInput").ap()
    ap["attn_in"] = nc.dram_tensor("attn_in", [32, 128, 1024], BF16, kind="ExternalInput").ap()
    ap["wo_t"] = nc.dram_tensor("wo_t", [32, 4096, 128], BF16, kind="ExternalInput").ap()
    ap["x_out"] = nc.dram_tensor("x_out", [32, 128, 1024], F32, kind="ExternalOutput").ap()
    sb = {}
    sb["ln"] = nc.alloc_sbuf_tensor("s_ln", [128, 32 * 1024], BF16)
    sb["wf"] = [nc.alloc_sbuf_tensor(f"s_wf{i}", [128, 4096], BF16) for i in range(2)]
    sb["xo"] = [nc.alloc_sbuf_tensor(f"s_xo{i}", [128, 1024], F32) for i in range(2)]
    ps = [nc.alloc_psum_tensor(f"ps{i}", [128, 512], F32) for i in range(8)]
    P = Prog(nc)
    with nc.Block() as block:
        emit_wo(P, nc, ap, sb, ps)
        P.emit(block)
    return nc


def build_lnout():
    nc = bass.Bass("TRN2", target_bir_lowering=False)
    ap = {}
    ap["x_in"] = nc.dram_tensor("x_in", [32, 128, 1024], F32, kind="ExternalInput").ap()
    ap["ones_col"] = nc.dram_tensor("ones_col", [128, 1], F32, kind="ExternalInput").ap()
    ap["ones_row"] = nc.dram_tensor("ones_row", [1, 128], F32, kind="ExternalInput").ap()
    ap["ln_out"] = nc.dram_tensor("ln_out", [32, 128, 1024], BF16, kind="ExternalOutput").ap()
    sb = {}
    alloc_ln_bufs(nc, sb)
    ps = [nc.alloc_psum_tensor(f"ps{i}", [128, 512], F32) for i in range(6)]
    P = Prog(nc)
    with nc.Block() as block:
        ln_load_consts(P, ap, sb)
        emit_ln(P, nc, ap, sb, ps, compute_e=False, ln_out_hbm=ap["ln_out"])
        P.emit(block)
    return nc


RSCALE = 1.0 / SCALE
PEN = 904.0
MSKB = PEN * SCALE
CAUS = -30000.0 * RSCALE


def emit_attn1(P, nc, ap, sb, ps):
    for nm in ("C", "S", "protT", "identF", "Ddiag4", "Dprev4"):
        P.dma("sync", sb[nm][:], ap[nm], writes=[nm], stream="init")
    P.dma("sync", sb["DcausQK"][:].rearrange("p (v f) -> p v f", f=512),
          ap["DcausQK"].rearrange("v p f -> p v f"), writes=["DcausQK"], stream="init")
    P.dma("sync", sb["recent4"][:], ap["recent4"], writes=["recent4"], stream="init")
    P.dma("sync", sb["E80"][:].rearrange("p (v f) -> p v f", f=128),
          ap["E80"].rearrange("v p f -> p v f"), writes=["E80"], stream="init")
    P.dma("sync", sb["E01"][:].rearrange("p (v f) -> p v f", f=128),
          ap["E01"].rearrange("v p f -> p v f"), writes=["E01"], stream="init")
    P.dma("sync", sb["init01"][:], ap["init01"], writes=["init01"], stream="init")

    RQK = [("rq", oi, tt) for oi in range(4) for tt in range(8)] + [("rk", tt) for tt in range(8)]
    for pair in range(PAIRS):
        for j in range(4):
            P.dma("sync", sb["wq"][:, 4096 * j:4096 * (j + 1)].rearrange("p (dt o) -> p dt o", o=128),
                  ap["wq_bf"][4 * pair + j].rearrange("(dt p) o -> p dt o", p=128),
                  writes=[("wq", j)], stream="wload")
        P.dma("sync", sb["wk"][:].rearrange("p (dt o) -> p dt o", o=128),
              ap["wk_bf"][pair].rearrange("(dt p) o -> p dt o", p=128),
              writes=["wk"], stream="wload")
        P.dma("sync", sb["wv"][:].rearrange("p (dt o) -> p dt o", o=128),
              ap["wv_bf"][pair].rearrange("(dt p) o -> p dt o", p=128),
              writes=["wv"], stream="wload")
        for tt in range(8):
            tpar = 0
            for dt in range(32):
                P.dma("gpsimd", sb["lnt"][tpar][:, 512 * dt:512 * (dt + 1)],
                      ap["lnf"](dt, 512 * tt, 512 * (tt + 1)),
                      writes=[("lnt", tpar, dt)], stream=f"lnt{tpar}")
            for oi in range(6):
                bi = (tt % 2) * 4 + (oi % 4) if False else oi
                bank = ps[oi]
                if oi < 4:
                    wsl = sb["wq"][:, 4096 * oi:4096 * oi + 4096]
                    wkey = ("wq", oi)
                elif oi == 4:
                    wsl, wkey = sb["wk"][:], "wk"
                else:
                    wsl, wkey = sb["wv"][:], "wv"
                for dt in range(32):
                    P.add("tensor", lambda e, wsl=wsl, dt=dt, bank=bank, tpar=tpar: e.matmul(
                        bank.ap()[:, 0:512],
                        wsl[:, 128 * dt:128 * (dt + 1)],
                        sb["lnt"][tpar][:, 512 * dt:512 * (dt + 1)],
                        start=(dt == 0), stop=(dt == 31)),
                        reads=[wkey, ("lnt", tpar, dt)], writes=[("ps", oi)])
                if oi == 5:
                    P.add("scalar", lambda e, bank=bank, tt=tt: e.copy(
                        out=sb["v"][:, 512 * tt:512 * (tt + 1)], in_=bank.ap()[:, 0:512]),
                        reads=[("ps", oi)], writes=[("v", tt), ("ps", oi)])
                else:
                    P.add("scalar", lambda e, bank=bank: e.copy(
                        out=sb["qf"][:], in_=bank.ap()[:, 0:512]),
                        reads=[("ps", oi)], writes=["qf", ("ps", oi)])
                    P.add("tensor", lambda e: e.matmul(
                        ps[6].ap()[:, 0:512], sb["protT"][:], sb["qf"][:],
                        start=True, stop=True),
                        reads=["qf", "protT", ("ps", 6)], writes=[("ps", 6)])
                    P.add("vector", lambda e, tt=tt: e.tensor_mul(
                        out=sb["t1"][:], in0=ps[6].ap()[:, 0:512],
                        in1=sb["S"][:, 512 * tt:512 * (tt + 1)]),
                        reads=[("ps", 6), "S"], writes=["t1", ("ps", 6)])
                    P.add("vector", lambda e, tt=tt: e.tensor_mul(
                        out=sb["qf"][:], in0=sb["qf"][:],
                        in1=sb["C"][:, 512 * tt:512 * (tt + 1)]),
                        reads=["qf", "C"], writes=["qf"])
                    dstkey = ("rq", oi, tt) if oi < 4 else ("rk", tt)
                    dst = (sb["q_all"][:, 4096 * oi + 512 * tt:4096 * oi + 512 * (tt + 1)]
                           if oi < 4 else sb["rk"][:, 512 * tt:512 * (tt + 1)])
                    P.add("vector", lambda e, dst=dst: e.tensor_add(
                        out=dst, in0=sb["qf"][:], in1=sb["t1"][:]),
                        reads=["qf", "t1"], writes=[dstkey])
        for kt in range(32):
            P.add("vector", lambda e, kt=kt: e.memset(sb["vt"][:, 256 * kt + 128:256 * kt + 129], 1.0),
                  reads=[], writes=[("vt1", kt)])
            P.add("sync", lambda e, kt=kt: e.dma_start_transpose(
                sb["vt"][:, 256 * kt:256 * kt + 128], sb["v"][:, 128 * kt:128 * (kt + 1)]),
                reads=[("v", kt // 4)], writes=[("vt", kt)], kind="d", stream="vtp")
        for i in range(32):
            ng = i // 4 + 1
            nkt = i + 1
            P.add("vector", lambda e: e.memset(sb["cm"][:], -3.0e38), reads=["cm"], writes=["cm"])
            for hh in range(4):
                for g in range(ng):
                    bi = g % 2
                    bank = ps[bi]
                    P.add("tensor", lambda e, hh=hh, g=g, i=i, bank=bank: e.matmul(
                        bank.ap()[:, 0:512],
                        sb["q_all"][:, 4096 * hh + 128 * i:4096 * hh + 128 * (i + 1)],
                        sb["rk"][:, 512 * g:512 * (g + 1)],
                        start=True, stop=(g != i // 4)),
                        reads=RQK + [("ps", bi)], writes=[("ps", bi)])
                    if g == i // 4:
                        P.add("tensor", lambda e, i=i, bank=bank: e.matmul(
                            bank.ap()[:, 0:512], sb["identF"][:],
                            sb["DcausQK"][:, 512 * (i % 4):512 * (i % 4 + 1)],
                            start=False, stop=True),
                            reads=["identF", "DcausQK", ("ps", bi)], writes=[("ps", bi)])
                    P.add("vector", lambda e, g=g, hh=hh, bank=bank: e.reduce_max(
                        out=sb["cm"][:, 32 * g:32 * (g + 1)],
                        in_=bank.ap()[:, 0:512].rearrange("p (s c) -> p s c", c=16),
                        axis=AX.X),
                        reads=[("ps", bi)], writes=[("cm",), ("ps", bi)])
                P.add("vector", lambda e: e.max(out=sb["m8"][:], in_=sb["cm"][:]),
                      reads=[("cm",)], writes=["m8"])
                P.dma("gpsimd", sb["t8b"][:], sb["m8"][:, 7:8],
                      reads=["m8"], writes=["t8b"], stream="t8cp")
                P.add("vector", lambda e: e.tensor_scalar(
                    out=sb["cok"][:], in0=sb["cm"][:], scalar1=sb["t8b"][:, 0:1], scalar2=None,
                    op0=ALU.is_ge),
                    reads=[("cm",), "t8b"], writes=["cok"])
                P.add("vector", lambda e: e.tensor_max(
                    out=sb["cok"][:], in0=sb["cok"][:], in1=sb["init01"][:]),
                    reads=["cok", "init01"], writes=["cok"])
                for gI in range(2):
                    P.add("scalar", lambda e, gI=gI, hh=hh: e.dma_start_transpose(
                        sb["cokT"][:, 512 * gI + 128 * hh:512 * gI + 128 * (hh + 1)],
                        sb["cok"][:, 128 * gI:128 * (gI + 1)]),
                        reads=["cok"], writes=[("cokT", gI, hh)], kind="d", stream="ctp")
                if hh < 3:
                    P.add("vector", lambda e: e.memset(sb["cm"][:], -3.0e38),
                          reads=[("cm",)], writes=[("cm",)])
            for kt in range(nkt):
                bi = kt % 2
                bank = ps[bi]
                gI = kt // 16
                cokT_keys = [("cokT", gI, hh) for hh in range(4)]
                P.add("tensor", lambda e, kt=kt, i=i, bank=bank: e.matmul(
                    bank.ap()[:, 0:512],
                    sb["rk"][:, 128 * kt:128 * (kt + 1)],
                    sb["q_all"][:].rearrange("p (h n) -> p h n", n=4096)[:, :, 128 * i:128 * (i + 1)],
                    start=True, stop=False),
                    reads=RQK + [("ps", bi)],
                    writes=[("ps", bi)])
                is_diag = (kt == i)
                is_prev = (kt == i - 1)
                if not is_diag:
                    P.add("tensor", lambda e, kt=kt, gI=gI, bank=bank, st=(not is_prev): e.matmul(
                        bank.ap()[:, 0:512],
                        sb["E80"][:, 128 * (kt % 16):128 * (kt % 16 + 1)],
                        sb["cokT"][:, 512 * gI:512 * (gI + 1)],
                        start=False, stop=st),
                        reads=["E80", ("ps", bi)] + cokT_keys, writes=[("ps", bi)])
                if is_prev:
                    P.add("tensor", lambda e, bank=bank: e.matmul(
                        bank.ap()[:, 0:512], sb["identF"][:], sb["Dprev4"][:],
                        start=False, stop=True),
                        reads=["identF", "Dprev4", ("ps", bi)], writes=[("ps", bi)])
                    P.add("tensor", lambda e, kt=kt, gI=gI: e.matmul(
                        ps[2].ap()[:, 0:512],
                        sb["E01"][:, 128 * (kt % 16):128 * (kt % 16 + 1)],
                        sb["cokT"][:, 512 * gI:512 * (gI + 1)],
                        start=True, stop=True),
                        reads=["E01", ("ps", 2)] + cokT_keys, writes=[("ps", 2)])
                    P.add("scalar", lambda e: e.copy(out=sb["a01"][:], in_=ps[2].ap()[:, 0:512]),
                          reads=[("ps", 2)], writes=["a01", ("ps", 2)])
                    P.add("vector", lambda e: e.tensor_mul(
                        out=sb["a01"][:], in0=sb["a01"][:], in1=sb["recent4"][:]),
                        reads=["a01", "recent4"], writes=["a01"])
                if is_diag:
                    P.add("tensor", lambda e, bank=bank: e.matmul(
                        bank.ap()[:, 0:512], sb["identF"][:], sb["Ddiag4"][:],
                        start=False, stop=True),
                        reads=["identF", "Ddiag4", ("ps", bi)], writes=[("ps", bi)])
                ppar = kt % 2
                if is_prev:
                    P.add("vector", lambda e, bank=bank: e.scalar_tensor_tensor(
                        out=sb["sfix"][:], in0=sb["a01"][:], scalar=-PEN, in1=bank.ap()[:, 0:512],
                        op0=ALU.mult, op1=ALU.add),
                        reads=["a01", ("ps", bi)], writes=["sfix"])
                    P.add("scalar", lambda e, ppar=ppar: e.activation(
                        out=sb["pp"][ppar][:], in_=sb["sfix"][:], func=AF.Exp,
                        bias=sb["neg80"][:, 0:1], scale=SCALE),
                        reads=["sfix", "neg80"], writes=[("pp", ppar)])

                else:
                    bcol = sb["zero1"][:, 0:1] if is_diag else sb["neg80"][:, 0:1]
                    bkey = "zero1" if is_diag else "neg80"
                    P.add("scalar", lambda e, bank=bank, ppar=ppar, bcol=bcol: e.activation(
                        out=sb["pp"][ppar][:], in_=bank.ap()[:, 0:512], func=AF.Exp,
                        bias=bcol, scale=SCALE),
                        reads=[("ps", bi), bkey], writes=[("pp", ppar), ("ps", bi)])
                for hh in range(4):
                    pb = ps[3 + hh]
                    P.add("tensor", lambda e, kt=kt, hh=hh, pb=pb, ppar=ppar, i=i: e.matmul(
                        pb.ap()[:, 0:129],
                        sb["pp"][ppar][:, 128 * hh:128 * (hh + 1)],
                        sb["vt"][:, 256 * kt:256 * kt + 129],
                        start=(kt == 0), stop=(kt == i)),
                        reads=[("pp", ppar), ("vt", kt), ("vt1", kt), ("ps", 3 + hh)], writes=[("ps", 3 + hh)])
            for hh in range(4):
                pb = ps[3 + hh]
                P.add("scalar", lambda e, pb=pb: e.copy(
                    out=sb["pv"][:], in_=pb.ap()[:, 0:129]),
                    reads=[("ps", 3 + hh)], writes=["pv", ("ps", 3 + hh)])
                P.dma("gpsimd", sb["rs"][:], sb["pv"][:, 128:129],
                      reads=["pv"], writes=["rs"], stream="rscp")
                P.add("vector", lambda e: e.reciprocal(
                    out=sb["rs2"][:], in_=sb["rs"][:]),
                    reads=["rs"], writes=["rs2"])
                P.add("scalar", lambda e: e.activation(
                    out=sb["att"][:], in_=sb["pv"][:, 0:128], func=AF.Identity,
                    bias=0.0, scale=sb["rs2"][:, 0:1]),
                    reads=["pv", "rs2"], writes=["att"])
                P.add("sync", lambda e: e.dma_start_transpose(sb["attT"][:], sb["att"][:]),
                      reads=["att"], writes=["attT"], kind="d", stream="atp")
                P.dma("gpsimd", ap["attn_out"][4 * pair + hh, :, 128 * i:128 * (i + 1)],
                      sb["attT"][:], reads=["attT"], writes=[("ao", pair, hh, i)],
                      stream="aost")
    return


def alloc_attn1_bufs(nc, sb):
    sb["wq"] = nc.alloc_sbuf_tensor("s_wq", [128, 4 * 4096], BF16)
    sb["wk"] = nc.alloc_sbuf_tensor("s_wk", [128, 4096], BF16)
    sb["wv"] = nc.alloc_sbuf_tensor("s_wv", [128, 4096], BF16)
    sb["lnt"] = [nc.alloc_sbuf_tensor("s_lnt0", [128, 512 * 32], BF16)]
    sb["q_all"] = nc.alloc_sbuf_tensor("s_q_all", [128, 4 * 4096], BF16)
    sb["rk"] = nc.alloc_sbuf_tensor("s_rk", [128, 4096], BF16)
    sb["v"] = nc.alloc_sbuf_tensor("s_v", [128, 4096], BF16)
    sb["vt"] = nc.alloc_sbuf_tensor("s_vt", [128, 32 * 256], BF16)
    sb["qf"] = nc.alloc_sbuf_tensor("s_qf", [128, 512], F32)
    sb["qcb"] = nc.alloc_sbuf_tensor("s_qcb", [128, 512], BF16)
    sb["t1"] = nc.alloc_sbuf_tensor("s_t1", [128, 512], F32)
    sb["C"] = nc.alloc_sbuf_tensor("s_C", [128, 4096], BF16)
    sb["S"] = nc.alloc_sbuf_tensor("s_S", [128, 4096], BF16)
    sb["protT"] = nc.alloc_sbuf_tensor("s_protT", [128, 128], F32)
    sb["identF"] = nc.alloc_sbuf_tensor("s_identF", [128, 128], F32)
    sb["Ddiag4"] = nc.alloc_sbuf_tensor("s_Ddiag4", [128, 512], F32)
    sb["Dprev4"] = nc.alloc_sbuf_tensor("s_Dprev4", [128, 512], F32)
    sb["DcausQK"] = nc.alloc_sbuf_tensor("s_DcausQK", [128, 4 * 512], F32)
    sb["recent4"] = nc.alloc_sbuf_tensor("s_recent4", [128, 512], BF16)
    sb["E80"] = nc.alloc_sbuf_tensor("s_E80", [128, 16 * 128], BF16)
    sb["E01"] = nc.alloc_sbuf_tensor("s_E01", [128, 16 * 128], BF16)
    sb["init01"] = nc.alloc_sbuf_tensor("s_init01", [128, 256], BF16)
    sb["zero512"] = nc.alloc_sbuf_tensor("s_zero512", [1, 512], BF16)
    sb["cm"] = nc.alloc_sbuf_tensor("s_cm", [128, 256], F32)
    sb["m8"] = nc.alloc_sbuf_tensor("s_m8", [128, 8], F32)
    sb["t8b"] = nc.alloc_sbuf_tensor("s_t8b", [128, 1], F32)
    sb["cok"] = nc.alloc_sbuf_tensor("s_cok", [128, 256], BF16)
    sb["cokT"] = nc.alloc_sbuf_tensor("s_cokT", [128, 2 * 512], BF16)
    sb["a01"] = nc.alloc_sbuf_tensor("s_a01", [128, 512], BF16)
    sb["sfix"] = nc.alloc_sbuf_tensor("s_sfix", [128, 512], F32)
    sb["pp"] = [nc.alloc_sbuf_tensor(f"s_pp{i}", [128, 512], BF16) for i in range(2)]
    sb["rs"] = nc.alloc_sbuf_tensor("s_rs", [128, 1], F32)
    sb["rs2"] = nc.alloc_sbuf_tensor("s_rs2", [128, 1], F32)
    sb["pv"] = nc.alloc_sbuf_tensor("s_pv", [128, 129], F32)
    sb["att"] = nc.alloc_sbuf_tensor("s_att", [128, 128], BF16)
    sb["attT"] = nc.alloc_sbuf_tensor("s_attT", [128, 128], BF16)
    sb["neg80"] = nc.alloc_sbuf_tensor("s_neg80", [128, 1], F32)
    sb["zero1"] = nc.alloc_sbuf_tensor("s_zero1", [128, 1], F32)


def build_attn1():
    nc = bass.Bass("TRN2", target_bir_lowering=False)
    ap = {}
    ap["ln_full"] = nc.dram_tensor("ln_full", [32, 128, 4096], BF16, kind="ExternalInput").ap()
    ap["lnf"] = lambda dt, c0, c1: ap["ln_full"][dt, :, c0:c1]
    ap["wq_bf"] = nc.dram_tensor("wq_bf", [8, 4096, 128], BF16, kind="ExternalInput").ap()
    ap["wk_bf"] = nc.dram_tensor("wk_bf", [2, 4096, 128], BF16, kind="ExternalInput").ap()
    ap["wv_bf"] = nc.dram_tensor("wv_bf", [2, 4096, 128], BF16, kind="ExternalInput").ap()
    ap["C"] = nc.dram_tensor("C", [128, 4096], BF16, kind="ExternalInput").ap()
    ap["S"] = nc.dram_tensor("S", [128, 4096], BF16, kind="ExternalInput").ap()
    ap["protT"] = nc.dram_tensor("protT", [128, 128], F32, kind="ExternalInput").ap()
    ap["identF"] = nc.dram_tensor("identF", [128, 128], F32, kind="ExternalInput").ap()
    ap["Ddiag4"] = nc.dram_tensor("Ddiag4", [128, 512], F32, kind="ExternalInput").ap()
    ap["Dprev4"] = nc.dram_tensor("Dprev4", [128, 512], F32, kind="ExternalInput").ap()
    ap["DcausQK"] = nc.dram_tensor("DcausQK", [4, 128, 512], F32, kind="ExternalInput").ap()
    ap["recent4"] = nc.dram_tensor("recent4", [128, 512], BF16, kind="ExternalInput").ap()
    ap["E80"] = nc.dram_tensor("E80", [16, 128, 128], BF16, kind="ExternalInput").ap()
    ap["E01"] = nc.dram_tensor("E01", [16, 128, 128], BF16, kind="ExternalInput").ap()
    ap["init01"] = nc.dram_tensor("init01", [128, 256], BF16, kind="ExternalInput").ap()
    ap["zero512"] = nc.dram_tensor("zero512", [1, 512], BF16, kind="ExternalInput").ap()
    ap["attn_out"] = nc.dram_tensor("attn_out", [8, 128, 4096], BF16, kind="ExternalOutput").ap()

    sb = {}
    alloc_attn1_bufs(nc, sb)
    ps = [nc.alloc_psum_tensor(f"ps{i}", [128, 512], F32) for i in range(7)]
    P = Prog(nc)
    with nc.Block() as block:
        P.dma("sync", sb["zero512"][0:1, :], ap["zero512"], writes=["zero512"], stream="init")
        P.add("vector", lambda e: e.memset(sb["neg80"][:], -MSKB), writes=["neg80"])
        P.add("vector", lambda e: e.memset(sb["zero1"][:], 0.0), writes=["zero1"])
        emit_attn1(P, nc, ap, sb, ps)
        P.emit(block)
    return nc


def host_attn1_consts():
    C = {}
    inv = 1.0 / 10000.0 ** (np.arange(0, 128, 2, dtype=np.float64) / 128)
    pos = np.arange(4096, dtype=np.float64)
    fr = pos[:, None] * inv[None, :]
    emb = np.concatenate([fr, fr], axis=1)
    C["C"] = np.ascontiguousarray(np.cos(emb).T.astype(ml_dtypes.bfloat16))
    C["S"] = np.ascontiguousarray(np.sin(emb).T.astype(ml_dtypes.bfloat16))
    prot = np.zeros((128, 128), np.float32)
    for d in range(64):
        prot[d, d + 64] = -1.0
        prot[d + 64, d] = 1.0
    C["protT"] = np.ascontiguousarray(prot.T)
    C["identF"] = np.eye(128, dtype=np.float32)
    kk = np.arange(128)[:, None]
    rr = np.arange(128)[None, :]
    diag = np.where(kk <= rr, 0.0, CAUS).astype(np.float32)
    C["Ddiag4"] = np.ascontiguousarray(np.tile(diag, (1, 4)))
    prev = np.where(kk > rr, PEN, 0.0).astype(np.float32)
    C["Dprev4"] = np.ascontiguousarray(np.tile(prev, (1, 4)))
    C["recent4"] = np.ascontiguousarray(np.tile((kk > rr), (1, 4)).astype(ml_dtypes.bfloat16))
    dc = np.zeros((4, 128, 512), np.float32)
    for v in range(4):
        for p2 in range(4):
            blk = dc[v][:, 128 * p2:128 * (p2 + 1)]
            if p2 < v:
                blk[:] = 0.0
            elif p2 == v:
                blk[:] = np.where(kk.T >= rr.T, 0.0, CAUS).T
            else:
                blk[:] = CAUS
    for v in range(4):
        dc[v][:, 128 * v:128 * (v + 1)] = np.where(rr <= kk, 0.0, CAUS)
    C["DcausQK"] = np.ascontiguousarray(dc)
    e80 = np.zeros((16, 128, 128), np.float32)
    for v in range(16):
        for j in range(8):
            e80[v, 8 * v + j, 16 * j:16 * (j + 1)] = PEN
    C["E80"] = e80.astype(ml_dtypes.bfloat16)
    C["E01"] = (e80 / PEN).astype(ml_dtypes.bfloat16)
    ini = np.zeros((128, 256), np.float32)
    ini[:, 0:8] = 1.0
    C["init01"] = ini.astype(ml_dtypes.bfloat16)
    C["zero512"] = np.zeros((1, 512), ml_dtypes.bfloat16)
    return C


def host_attn1_weights(inp, c):
    b, q4 = c // 4, c % 4
    kvhs = [2 * q4, 2 * q4 + 1]
    heads = [4 * kvh + g for kvh in kvhs for g in range(4)]
    wq = np.asarray(inp["L1_attn_wq"], np.float32)
    wk = np.asarray(inp["L1_attn_wk"], np.float32)
    wv = np.asarray(inp["L1_attn_wv"], np.float32)
    d = {}
    d["wqT_sl"] = np.ascontiguousarray(
        np.concatenate([wq[128 * h:128 * (h + 1)].T for h in heads], axis=1))
    d["wkT_sl"] = np.ascontiguousarray(
        np.concatenate([wk[128 * kvh:128 * (kvh + 1)].T for kvh in kvhs], axis=1))
    d["wvT_sl"] = np.ascontiguousarray(
        np.concatenate([wv[128 * kvh:128 * (kvh + 1)].T for kvh in kvhs], axis=1))
    return d


def emit_attn0(P, nc, ap, sb, ps):
    for nm in ("C0", "S0", "pswapT", "identF", "Dfdiag4", "Dsdiag4", "Dsprev4"):
        P.dma("sync", sb[nm][:], ap[nm], writes=[nm], stream="init")
    P.dma("sync", sb["E80"][:].rearrange("p (v f) -> p v f", f=128),
          ap["E80"].rearrange("v p f -> p v f"), writes=["E80"], stream="init")
    P.dma("sync", sb["bgate"][0:1, :], ap["bgate_row"], writes=["bgate"], stream="init")
    P.dma("sync", sb["onesr"][0:1, :], ap["ones_row"], writes=["onesr"], stream="init")
    P.dma("sync", sb["kb1"][:], ap["kb1_cols"], writes=["kb1"], stream="init")
    P.dma("sync", sb["vb1"][:], ap["vb1_cols"], writes=["vb1"], stream="init")
    P.dma("sync", sb["kb2"][:], ap["kb2_col"], writes=["kb2"], stream="init")
    P.dma("sync", sb["vb2"][:], ap["vb2_col"], writes=["vb2"], stream="init")
    P.dma("sync", sb["kpos"][:], ap["kpos_cols"], writes=["kpos"], stream="init")
    P.dma("sync", sb["vpos"][:], ap["vpos_cols"], writes=["vpos"], stream="init")
    P.dma("sync", sb["memk"][:], ap["memk_cols"], writes=["memk"], stream="init")
    P.dma("sync", sb["memvr"][0:1, :], ap["memv_rows"].rearrange("a p b -> p (a b)"),
          writes=["memvr"], stream="init")
    P.dma("sync", sb["wgate"][:].rearrange("p (dt o) -> p dt o", o=24),
          ap["wgate_bf"].rearrange("(dt p) o -> p dt o", p=128), writes=["wgate"], stream="init")

    for i in range(32):
        for dt in range(32):
            P.dma("gpsimd", sb["lng"][dt % 2][:], ap["lnf"](dt, 128 * i, 128 * (i + 1)),
                  writes=[("lng", dt % 2)], stream=f"lng{dt % 2}")
            P.add("tensor", lambda e, dt=dt: e.matmul(
                ps[7].ap()[:, 0:24], sb["lng"][dt % 2][:], sb["wgate"][:, 24 * dt:24 * (dt + 1)],
                start=(dt == 0), stop=False),
                reads=[("lng", dt % 2), "wgate"], writes=[("ps", 7)])
        P.add("tensor", lambda e: e.matmul(
            ps[7].ap()[:, 0:24], sb["onesr"][0:1, :], sb["bgate"][0:1, :],
            start=False, stop=True),
            reads=["onesr", "bgate", ("ps", 7)], writes=[("ps", 7)])
        P.add("scalar", lambda e, i=i: e.activation(
            out=sb["gates"][:, 24 * i:24 * (i + 1)], in_=ps[7].ap()[:, 0:24],
            func=AF.Sigmoid, bias=0.0, scale=1.0),
            reads=[("ps", 7)], writes=[("gates", i), ("ps", 7)])

    RQK = [("rq", oi, tt) for oi in range(4) for tt in range(8)] + [("rk", tt) for tt in range(8)]
    for pair in range(PAIRS):
        for tt in range(8):
            for dt in range(32):
                P.dma("gpsimd", sb["lnt"][0][:, 512 * dt:512 * (dt + 1)],
                      ap["lnf"](dt, 512 * tt, 512 * (tt + 1)),
                      writes=[("lnt", 0, dt)], stream="lnt0")
            for oi in range(6):
                bank = ps[oi]
                wtile = 6 * pair + oi
                P.dma("sync", sb["wje"][0][:].rearrange("p (dt o) -> p dt o", o=128),
                      ap["wqkv_bf"][wtile].rearrange("(dt p) o -> p dt o", p=128),
                      writes=[("wje", 0)], stream="wje0")
                for dt in range(32):
                    P.add("tensor", lambda e, oi=oi, dt=dt, bank=bank: e.matmul(
                        bank.ap()[:, 0:512],
                        sb["wje"][0][:, 128 * dt:128 * (dt + 1)],
                        sb["lnt"][0][:, 512 * dt:512 * (dt + 1)],
                        start=(dt == 0), stop=(dt == 31)),
                        reads=[("wje", 0), ("lnt", 0, dt)], writes=[("ps", oi)])
                if oi == 5:
                    P.add("scalar", lambda e, bank=bank, tt=tt: e.copy(
                        out=sb["v"][:, 512 * tt:512 * (tt + 1)], in_=bank.ap()[:, 0:512]),
                        reads=[("ps", oi)], writes=[("v", tt), ("ps", oi)])
                elif oi == 4:
                    P.add("scalar", lambda e, bank=bank: e.copy(
                        out=sb["kt"][:], in_=bank.ap()[:, 0:512]),
                        reads=[("ps", oi)], writes=[("kt",), ("ps", oi)])
                    for s2 in range(16):
                        P.dma("gpsimd", sb["posc"][:], sb["kpos"][:, 16 * pair + s2:16 * pair + s2 + 1],
                              reads=["kpos"], writes=["posc"], stream="posc")
                        P.add("vector", lambda e, s2=s2, tt=tt: e.tensor_scalar(
                            out=sb["cin"][:, 256 * s2 + 32 * tt:256 * s2 + 32 * (tt + 1)],
                            in0=sb["kt"][:].rearrange("p (c s3) -> p c s3", s3=16)[:, :, s2],
                            scalar1=sb["posc"][:, 0:1], scalar2=None, op0=ALU.add),
                            reads=[("kt",), "posc"], writes=[("cin", s2)])
                    P.add("tensor", lambda e: e.matmul(
                        ps[6].ap()[:, 0:512], sb["pswapT"][:], sb["kt"][:],
                        start=True, stop=True),
                        reads=[("kt",), "pswapT", ("ps", 6)], writes=[("ps", 6)])
                    P.add("vector", lambda e, tt=tt: e.tensor_mul(
                        out=sb["t1"][:], in0=ps[6].ap()[:, 0:512],
                        in1=sb["S0"][:, 512 * tt:512 * (tt + 1)]),
                        reads=[("ps", 6), "S0"], writes=["t1", ("ps", 6)])
                    P.add("vector", lambda e, tt=tt: e.tensor_mul(
                        out=sb["qf"][:], in0=sb["kt"][:],
                        in1=sb["C0"][:, 512 * tt:512 * (tt + 1)]),
                        reads=[("kt",), "C0"], writes=["qf"])
                    P.add("vector", lambda e, tt=tt: e.tensor_add(
                        out=sb["rk"][:, 512 * tt:512 * (tt + 1)], in0=sb["qf"][:], in1=sb["t1"][:]),
                        reads=["qf", "t1"], writes=[("rk", tt)])
                else:
                    P.add("scalar", lambda e, bank=bank: e.copy(
                        out=sb["qf"][:], in_=bank.ap()[:, 0:512]),
                        reads=[("ps", oi)], writes=["qf", ("ps", oi)])
                    P.add("vector", lambda e: e.tensor_copy(
                        out=sb["qcb"][:], in_=sb["qf"][:]),
                        reads=["qf"], writes=["qcb"])
                    P.dma("gpsimd", ap["qc_hbm"][pair, oi, :, 512 * tt:512 * (tt + 1)],
                          sb["qcb"][:], reads=["qcb"], writes=[("qc", pair, oi, tt)],
                          stream="qcst")
                    P.add("tensor", lambda e: e.matmul(
                        ps[6].ap()[:, 0:512], sb["pswapT"][:], sb["qf"][:],
                        start=True, stop=True),
                        reads=["qf", "pswapT", ("ps", 6)], writes=[("ps", 6)])
                    P.add("vector", lambda e, tt=tt: e.tensor_mul(
                        out=sb["t1"][:], in0=ps[6].ap()[:, 0:512],
                        in1=sb["S0"][:, 512 * tt:512 * (tt + 1)]),
                        reads=[("ps", 6), "S0"], writes=["t1", ("ps", 6)])
                    P.add("vector", lambda e, tt=tt: e.tensor_mul(
                        out=sb["qf"][:], in0=sb["qf"][:],
                        in1=sb["C0"][:, 512 * tt:512 * (tt + 1)]),
                        reads=["qf", "C0"], writes=["qf"])
                    P.add("vector", lambda e, oi=oi, tt=tt: e.tensor_add(
                        out=sb["q_all"][:, 4096 * oi + 512 * tt:4096 * oi + 512 * (tt + 1)],
                        in0=sb["qf"][:], in1=sb["t1"][:]),
                        reads=["qf", "t1"], writes=[("rq", oi, tt)])
        for kt in range(32):
            P.add("vector", lambda e, kt=kt: e.memset(sb["vt"][:, 256 * kt + 128:256 * kt + 129], 1.0),
                  reads=[], writes=[("vt1", kt)])
            P.add("sync", lambda e, kt=kt: e.dma_start_transpose(
                sb["vt"][:, 256 * kt:256 * kt + 128], sb["v"][:, 128 * kt:128 * (kt + 1)]),
                reads=[("v", kt // 4)], writes=[("vt", kt)], kind="d", stream="vtp")
        VALL = [("v", tt) for tt in range(8)]
        _emit_comp_mlp(P, nc, ap, sb, ps, pair, "k")
        for s2 in range(16):
            P.dma("gpsimd", sb["posc"][:], sb["vpos"][:, 16 * pair + s2:16 * pair + s2 + 1],
                  reads=["vpos"], writes=["posc"], stream="posc")
            P.add("vector", lambda e, s2=s2: e.tensor_scalar(
                out=sb["cin"][:, 256 * s2:256 * (s2 + 1)],
                in0=sb["v"][:].rearrange("p (c s3) -> p c s3", s3=16)[:, :, s2],
                scalar1=sb["posc"][:, 0:1],
                scalar2=None, op0=ALU.add),
                reads=VALL + ["posc", ("ckdone",)], writes=[("cin", s2)])
        _emit_comp_mlp(P, nc, ap, sb, ps, pair, "v")
        for c2 in range(2):
            P.add("vector", lambda e, c2=c2: e.memset(
                sb["cvt"][:, 256 * c2 + 128:256 * c2 + 129], 1.0), reads=[], writes=[("cvt1", c2)])
            P.add("sync", lambda e, c2=c2: e.dma_start_transpose(
                sb["cvt"][:, 256 * c2:256 * c2 + 128], sb["cv"][:, 128 * c2:128 * (c2 + 1)]),
                reads=[("cv",)], writes=[("cvt", c2)], kind="d", stream="vtp")
        P.dma("gpsimd", sb["posc"][:], sb["memk"][:, pair:pair + 1],
              reads=["memk"], writes=["posc"], stream="posc")
        P.add("vector", lambda e: e.tensor_scalar_mul(
            out=sb["memk_bf"][:, 0:1], in0=sb["posc"][:], scalar1=SCALE),
            reads=["posc"], writes=["memk_bf"])
        P.add("vector", lambda e, pair=pair: e.tensor_copy(
            out=sb["memv_bf"][0:1, :], in_=sb["memvr"][0:1, 129 * pair:129 * (pair + 1)]),
            reads=["memvr"], writes=["memv_bf"])
        for i in range(32):
            nkt = i + 1
            gslot = ("gates", i)
            for hh in range(4):
                for br in range(3):
                    kk = 3 * hh + br
                    gb = 24 * i + 3 * (4 * pair + hh) + br
                    P.dma("gpsimd", sb["g1b"][kk][:], sb["gates"][:, gb:gb + 1],
                          reads=[gslot], writes=[("g1b", kk)], stream="g1cp")
            for hh in range(4):
                P.dma("sync", sb["qcs"][:, 128 * hh:128 * (hh + 1)],
                      ap["qc_hbm"][pair, hh, :, 128 * i:128 * (i + 1)],
                      reads=[("qc", pair, hh, i // 4)], writes=[("qcs", hh)],
                      stream="qcld")
            for jt in range(2):
                P.dma("gpsimd", sb["Dc"][:], ap["Dc"][2 * i + jt],
                      writes=["Dc"], stream="dcld")
                P.add("tensor", lambda e, jt=jt: e.matmul(
                    ps[jt].ap()[:, 0:512],
                    sb["ck"][:, 128 * jt:128 * (jt + 1)],
                    sb["qcs"][:],
                    start=True, stop=True),
                    reads=[("ck",)] + [("qcs", hh) for hh in range(4)] + [("ps", jt)],
                    writes=[("ps", jt)])
                P.add("vector", lambda e, jt=jt: e.scalar_tensor_tensor(
                    out=sb["csf"][:], in0=sb["Dc"][:], scalar=1.0, in1=ps[jt].ap()[:, 0:512],
                    op0=ALU.mult, op1=ALU.add),
                    reads=["Dc", ("ps", jt)], writes=["csf", ("ps", jt)])
                P.add("scalar", lambda e, jt=jt: e.copy(
                    out=sb["csraw"][:, 512 * jt:512 * (jt + 1)], in_=sb["csf"][:]),
                    reads=["csf"], writes=[("csraw", jt)])
                P.add("scalar", lambda e, jt=jt: e.activation(
                    out=sb["pc"][:, 512 * jt:512 * (jt + 1)], in_=sb["csf"][:],
                    func=AF.Exp, bias=0.0, scale=1.0),
                    reads=["csf"], writes=[("pc", jt)])
            P.add("tensor", lambda e: e.matmul(
                ps[2].ap()[0:1, 0:512], sb["memk_bf"][:, 0:1],
                sb["qcs"][:],
                start=True, stop=True),
                reads=["memk_bf"] + [("qcs", hh) for hh in range(4)] + [("ps", 2)],
                writes=[("ps", 2)])
            P.add("scalar", lambda e: e.activation(
                out=sb["pmem"][0:1, 0:512], in_=ps[2].ap()[0:1, 0:512], func=AF.Exp,
                bias=0.0, scale=1.0),
                reads=[("ps", 2)], writes=["pmem", ("ps", 2)])
            for hh in range(4):
                pb = ps[3 + hh]
                for jt in range(2):
                    P.add("tensor", lambda e, jt=jt, hh=hh, pb=pb: e.matmul(
                        pb.ap()[:, 0:129],
                        sb["pc"][:, 512 * jt + 128 * hh:512 * jt + 128 * (hh + 1)],
                        sb["cvt"][:, 256 * jt:256 * jt + 129],
                        start=(jt == 0), stop=False),
                        reads=[("pc", jt), ("cvt", jt), ("cvt1", jt), ("ps", 3 + hh)],
                        writes=[("ps", 3 + hh)])
                P.add("tensor", lambda e, hh=hh, pb=pb: e.matmul(
                    pb.ap()[:, 0:129], sb["pmem"][0:1, 128 * hh:128 * (hh + 1)],
                    sb["memv_bf"][0:1, 0:129], start=False, stop=True),
                    reads=["pmem", "memv_bf", ("ps", 3 + hh)], writes=[("ps", 3 + hh)])
                P.add("scalar", lambda e, hh=hh, pb=pb: e.copy(
                    out=sb["cpv"][hh][:, 0:129], in_=pb.ap()[:, 0:129]),
                    reads=[("ps", 3 + hh)], writes=[("cpv", hh), ("ps", 3 + hh)])
                P.dma("gpsimd", sb["rsb"][3 * hh][:], sb["cpv"][hh][:, 128:129],
                      reads=[("cpv", hh)], writes=[("rsb", 3 * hh)], stream="rscp")
            for jt in range(2):
                base = 512 * jt
                P.add("vector", lambda e, base=base: e.tensor_add(
                    out=sb["impa"][:], in0=sb["csraw"][:, base:base + 128],
                    in1=sb["csraw"][:, base + 128:base + 256]),
                    reads=[("csraw", jt)], writes=["impa"])
                P.add("vector", lambda e, base=base: e.tensor_add(
                    out=sb["impb"][:], in0=sb["csraw"][:, base + 256:base + 384],
                    in1=sb["csraw"][:, base + 384:base + 512]),
                    reads=[("csraw", jt)], writes=["impb"])
                P.add("vector", lambda e: e.tensor_add(
                    out=sb["impa"][:], in0=sb["impa"][:], in1=sb["impb"][:]),
                    reads=["impa", "impb"], writes=["impa"])
                P.add("vector", lambda e, jt=jt: e.tensor_scalar_mul(
                    out=sb["impj"][:, 128 * jt:128 * (jt + 1)], in0=sb["impa"][:], scalar1=0.25),
                    reads=["impa"], writes=[("impj", jt)])
                P.add("scalar", lambda e, jt=jt: e.dma_start_transpose(
                    sb["impq"][:, 128 * jt:128 * (jt + 1)], sb["impj"][:, 128 * jt:128 * (jt + 1)]),
                    reads=[("impj", jt)], writes=[("impq", jt)], kind="d", stream="itp")
            P.add("vector", lambda e: e.reduce_max(out=sb["im"][:], in_=sb["impq"][:], axis=AX.X),
                  reads=[("impq", 0), ("impq", 1)], writes=["im"])
            P.add("vector", lambda e: e.tensor_scalar_max(out=sb["im"][:], in0=sb["im"][:], scalar1=-1000.0),
                  reads=["im"], writes=["im"])
            P.add("vector", lambda e: e.tensor_scalar_mul(out=sb["imn"][:], in0=sb["im"][:], scalar1=-1.0),
                  reads=["im"], writes=["imn"])
            P.add("scalar", lambda e: e.activation(
                out=sb["uu"][:], in_=sb["impq"][:], func=AF.Exp, bias=sb["imn"][:, 0:1], scale=1.0),
                reads=[("impq", 0), ("impq", 1), "imn"], writes=["uu"])
            P.add("vector", lambda e: e.reduce_sum(out=sb["zz"][:], in_=sb["uu"][:], axis=AX.X),
                  reads=["uu"], writes=["zz"])
            P.add("scalar", lambda e: e.activation(
                out=sb["sen"][:], in_=sb["imn"][:], func=AF.Exp, bias=-1000.0 if False else sb["negk"][:, 0:1],
                scale=1.0), reads=["imn", "negk"], writes=["sen"])
            P.add("vector", lambda e: e.tensor_add(out=sb["zz"][:], in0=sb["zz"][:], in1=sb["sen"][:]),
                  reads=["zz", "sen"], writes=["zz"])
            P.add("vector", lambda e: e.tensor_scalar_mul(out=sb["zz"][:], in0=sb["zz"][:], scalar1=1e-10),
                  reads=["zz"], writes=["zz"])
            P.add("vector", lambda e: e.max(out=sb["m8"][:], in_=sb["uu"][:]), reads=["uu"], writes=["m8"])
            P.dma("gpsimd", sb["t4b"][:], sb["m8"][:, 3:4], reads=["m8"], writes=["t4b"], stream="t8cp")
            P.add("vector", lambda e: e.tensor_scalar(
                out=sb["selq"][:], in0=sb["uu"][:], scalar1=sb["t4b"][:, 0:1], scalar2=None, op0=ALU.is_ge),
                reads=["uu", "t4b"], writes=["selq"])
            P.add("vector", lambda e: e.tensor_scalar(
                out=sb["selq2"][:], in0=sb["uu"][:], scalar1=sb["zz"][:, 0:1], scalar2=None, op0=ALU.is_gt),
                reads=["uu", "zz"], writes=["selq2"])
            P.add("vector", lambda e: e.tensor_mul(
                out=sb["selq"][:], in0=sb["selq"][:], in1=sb["selq2"][:]),
                reads=["selq", "selq2"], writes=["selq"])
            for gI in range(2):
                for rep in range(4):
                    P.add("scalar", lambda e, gI=gI, rep=rep: e.dma_start_transpose(
                        sb["selT"][:, 512 * gI + 128 * rep:512 * gI + 128 * (rep + 1)],
                        sb["selq"][:, 128 * gI:128 * (gI + 1)]),
                        reads=["selq"], writes=[("selT", gI, rep)], kind="d", stream="stp")
            for kt in range(nkt):
                bi = kt % 2
                bank = ps[bi]
                gI = kt // 16
                selk = [("selT", gI, rep) for rep in range(4)]
                P.add("tensor", lambda e, kt=kt, i=i, bank=bank: e.matmul(
                    bank.ap()[:, 0:512],
                    sb["rk"][:, 128 * kt:128 * (kt + 1)],
                    sb["q_all"][:].rearrange("p (h n) -> p h n", n=4096)[:, :, 128 * i:128 * (i + 1)],
                    start=True, stop=False),
                    reads=RQK + [("ps", bi)], writes=[("ps", bi)])
                P.add("tensor", lambda e, kt=kt, gI=gI, bank=bank: e.matmul(
                    bank.ap()[:, 0:512],
                    sb["E80"][:, 128 * (kt % 16):128 * (kt % 16 + 1)],
                    sb["selT"][:, 512 * gI:512 * (gI + 1)],
                    start=False, stop=True),
                    reads=["E80", ("ps", bi)] + selk, writes=[("ps", bi)])
                ppar = kt % 2
                if kt == i:
                    P.add("vector", lambda e, bank=bank: e.scalar_tensor_tensor(
                        out=sb["csf"][:], in0=sb["Dfdiag4"][:], scalar=1.0,
                        in1=bank.ap()[:, 0:512], op0=ALU.mult, op1=ALU.add),
                        reads=["Dfdiag4", ("ps", bi)], writes=["csf", ("ps", bi)])
                    P.add("scalar", lambda e, ppar=ppar: e.activation(
                        out=sb["pp"][ppar][:], in_=sb["csf"][:], func=AF.Exp,
                        bias=sb["neg80"][:, 0:1], scale=SCALE),
                        reads=["csf", "neg80"], writes=[("pp", ppar)])
                else:
                    P.add("scalar", lambda e, bank=bank, ppar=ppar: e.activation(
                        out=sb["pp"][ppar][:], in_=bank.ap()[:, 0:512], func=AF.Exp,
                        bias=sb["neg80"][:, 0:1], scale=SCALE),
                        reads=[("ps", bi), "neg80"], writes=[("pp", ppar), ("ps", bi)])
                for hh in range(4):
                    pb = ps[3 + hh]
                    P.add("tensor", lambda e, kt=kt, hh=hh, pb=pb, ppar=ppar, i=i: e.matmul(
                        pb.ap()[:, 0:129],
                        sb["pp"][ppar][:, 128 * hh:128 * (hh + 1)],
                        sb["vt"][:, 256 * kt:256 * kt + 129],
                        start=(kt == 0), stop=(kt == i)),
                        reads=[("pp", ppar), ("vt", kt), ("vt1", kt), ("ps", 3 + hh)],
                        writes=[("ps", 3 + hh)])
            for hh in range(4):
                P.add("scalar", lambda e, hh=hh: e.copy(
                    out=sb["fpv"][hh][:, 0:129], in_=ps[3 + hh].ap()[:, 0:129]),
                    reads=[("ps", 3 + hh)], writes=[("fpv", hh), ("ps", 3 + hh)])
                P.dma("gpsimd", sb["rsb"][3 * hh + 1][:], sb["fpv"][hh][:, 128:129],
                      reads=[("fpv", hh)], writes=[("rsb", 3 * hh + 1)], stream="rscp")
            skts = [kt for kt in (i - 1, i) if kt >= 0]
            for sj, kt in enumerate(skts):
                bi = sj % 2
                bank = ps[bi]
                dname = "Dsprev4" if kt == i - 1 else "Dsdiag4"
                P.add("tensor", lambda e, kt=kt, i=i, bank=bank: e.matmul(
                    bank.ap()[:, 0:512],
                    sb["rk"][:, 128 * kt:128 * (kt + 1)],
                    sb["q_all"][:].rearrange("p (h n) -> p h n", n=4096)[:, :, 128 * i:128 * (i + 1)],
                    start=True, stop=True),
                    reads=RQK + [("ps", bi)], writes=[("ps", bi)])
                P.add("vector", lambda e, bank=bank, dname=dname: e.scalar_tensor_tensor(
                    out=sb["csf"][:], in0=sb[dname][:], scalar=1.0,
                    in1=bank.ap()[:, 0:512], op0=ALU.mult, op1=ALU.add),
                    reads=[dname, ("ps", bi)], writes=["csf", ("ps", bi)])
                P.add("scalar", lambda e, sj=sj: e.activation(
                    out=sb["pp"][sj][:], in_=sb["csf"][:], func=AF.Exp,
                    bias=sb["neg80"][:, 0:1], scale=SCALE),
                    reads=["csf", "neg80"], writes=[("pp", sj)])
                for hh in range(4):
                    pb = ps[3 + hh]
                    P.add("tensor", lambda e, kt=kt, hh=hh, pb=pb, sj=sj, last=(kt == i): e.matmul(
                        pb.ap()[:, 0:129],
                        sb["pp"][sj][:, 128 * hh:128 * (hh + 1)],
                        sb["vt"][:, 256 * kt:256 * kt + 129],
                        start=(sj == 0), stop=last),
                        reads=[("pp", sj), ("vt", kt), ("vt1", kt), ("ps", 3 + hh)],
                        writes=[("ps", 3 + hh)])
            for hh in range(4):
                pb = ps[3 + hh]
                P.add("scalar", lambda e, hh=hh, pb=pb: e.copy(
                    out=sb["spvh"][hh][:, 0:129], in_=pb.ap()[:, 0:129]),
                    reads=[("ps", 3 + hh)], writes=[("spv", hh), ("ps", 3 + hh)])
                P.dma("gpsimd", sb["rsb"][3 * hh + 2][:], sb["spvh"][hh][:, 128:129],
                      reads=[("spv", hh)], writes=[("rsb", 3 * hh + 2)], stream="rscp")
            for kk in range(12):
                P.add("vector", lambda e, kk=kk: e.reciprocal(
                    out=sb["rs2b"][kk][:], in_=sb["rsb"][kk][:]),
                    reads=[("rsb", kk)], writes=[("rs2b", kk)])
            for kk in range(12):
                P.add("vector", lambda e, kk=kk: e.tensor_mul(
                    out=sb["coefb"][kk][:], in0=sb["rs2b"][kk][:], in1=sb["g1b"][kk][:]),
                    reads=[("rs2b", kk), ("g1b", kk)], writes=[("coefb", kk)])
            for hh in range(4):
                for br, buf in ((0, "cpv"), (2, "spv"), (1, "fpv")):
                    kk = 3 * hh + br
                    src2 = {"cpv": sb["cpv"][hh], "fpv": sb["fpv"][hh],
                            "spv": sb["spvh"][hh]}[buf]
                    if br == 0:
                        P.add("vector", lambda e, src2=src2, kk=kk: e.tensor_scalar_mul(
                            out=sb["att"][:], in0=src2[:, 0:128],
                            scalar1=sb["coefb"][kk][:, 0:1]),
                            reads=[(buf, hh), ("coefb", kk)], writes=["att"])
                    else:
                        P.add("vector", lambda e, src2=src2, kk=kk: e.scalar_tensor_tensor(
                            out=sb["att"][:], in0=src2[:, 0:128],
                            scalar=sb["coefb"][kk][:, 0:1], in1=sb["att"][:],
                            op0=ALU.mult, op1=ALU.add),
                            reads=[(buf, hh), ("coefb", kk), "att"], writes=["att"])
                P.add("scalar", lambda e: e.copy(out=sb["attb"][:], in_=sb["att"][:]),
                      reads=["att"], writes=["attb"])
                P.add("sync", lambda e: e.dma_start_transpose(sb["attT"][:], sb["attb"][:]),
                      reads=["attb"], writes=["attT"], kind="d", stream="atp")
                P.dma("gpsimd", ap["attn_out"][4 * pair + hh, :, 128 * i:128 * (i + 1)],
                      sb["attT"][:], reads=["attT"], writes=[("ao", pair, hh, i)], stream="aost")
    return


def _emit_comp_mlp(P, nc, ap, sb, ps, pair, side):
    wsh = ap[("kw1_sh" if side == "k" else "vw1_sh")]
    w2 = ap[("kw2_bf" if side == "k" else "vw2_bf")]
    bcols = "kb1" if side == "k" else "vb1"
    b2col = "kb2" if side == "k" else "vb2"
    outk = "ck" if side == "k" else "cv"
    for o2 in range(16):
        par = 0
        P.dma("scalar", sb["w1b"][par][:].rearrange("p (ct o) -> p ct o", o=128),
              wsh[o2].rearrange("(ct p) o -> p ct o", p=128),
              writes=[("w1b", par)], stream=f"w1b{par}")
        bank = ps[o2 % 2]
        cin_keys = [("cin", s2) for s2 in range(16)]
        for ct in range(16):
            P.add("tensor", lambda e, o2=o2, ct=ct, bank=bank, par=par: e.matmul(
                bank.ap()[:, 0:256],
                sb["w1b"][par][:, 128 * ct:128 * (ct + 1)],
                sb["cin"][:, 256 * ct:256 * (ct + 1)],
                start=(ct == 0), stop=(ct == 15)),
                reads=[("w1b", par)] + cin_keys, writes=[("ps", o2 % 2)])
        P.add("scalar", lambda e, o2=o2, bank=bank, bcols=bcols: e.activation(
            out=sb["h1"][:, 256 * o2:256 * (o2 + 1)], in_=bank.ap()[:, 0:256],
            func=AF.Relu, bias=sb[bcols][:, o2:o2 + 1], scale=1.0),
            reads=[("ps", o2 % 2), bcols], writes=[("h1", o2), ("ps", o2 % 2)])
    P.dma("scalar", sb["w2b"][:].rearrange("p (ct o) -> p ct o", o=128),
          w2[0].rearrange("(ct p) o -> p ct o", p=128), writes=["w2b"], stream="w2b")
    for ct in range(16):
        P.add("tensor", lambda e, ct=ct: e.matmul(
            ps[2].ap()[:, 0:256], sb["w2b"][:, 128 * ct:128 * (ct + 1)],
            sb["h1"][:, 256 * ct:256 * (ct + 1)],
            start=(ct == 0), stop=(ct == 15)),
            reads=["w2b", ("h1", ct)], writes=[("ps", 2)])
    if side == "k":
        P.add("vector", lambda e: e.tensor_scalar(
            out=sb["ck"][:], in0=ps[2].ap()[:, 0:256], scalar1=sb["kb2"][:, 0:1],
            scalar2=SCALE, op0=ALU.add, op1=ALU.mult),
            reads=[("ps", 2), "kb2"], writes=[("ck",), ("ps", 2), ("ckdone",)])
    else:
        P.add("vector", lambda e: e.tensor_scalar(
            out=sb["cv"][:], in0=ps[2].ap()[:, 0:256], scalar1=sb["vb2"][:, 0:1],
            scalar2=None, op0=ALU.add),
            reads=[("ps", 2), "vb2"], writes=[("cv",), ("ps", 2)])


def build_attn0():
    nc = bass.Bass("TRN2", target_bir_lowering=False)
    ap = {}
    ap["ln_full"] = nc.dram_tensor("ln_full", [32, 128, 4096], BF16, kind="ExternalInput").ap()
    ap["lnf"] = lambda dt, c0, c1: ap["ln_full"][dt, :, c0:c1]
    ap["wqkv_bf"] = nc.dram_tensor("wqkv_bf", [12, 4096, 128], BF16, kind="ExternalInput").ap()
    ap["wgate_bf"] = nc.dram_tensor("wgate_bf", [4096, 24], BF16, kind="ExternalInput").ap()
    ap["kw1_sh"] = nc.dram_tensor("kw1_sh", [16, 2048, 128], BF16, kind="ExternalInput").ap()
    ap["vw1_sh"] = nc.dram_tensor("vw1_sh", [16, 2048, 128], BF16, kind="ExternalInput").ap()
    ap["kw2_bf"] = nc.dram_tensor("kw2_bf", [1, 2048, 128], BF16, kind="ExternalInput").ap()
    ap["vw2_bf"] = nc.dram_tensor("vw2_bf", [1, 2048, 128], BF16, kind="ExternalInput").ap()
    ap["C0"] = nc.dram_tensor("C0", [128, 4096], BF16, kind="ExternalInput").ap()
    ap["S0"] = nc.dram_tensor("S0", [128, 4096], BF16, kind="ExternalInput").ap()
    ap["pswapT"] = nc.dram_tensor("pswapT", [128, 128], F32, kind="ExternalInput").ap()
    ap["identF"] = nc.dram_tensor("identF", [128, 128], F32, kind="ExternalInput").ap()
    ap["Dfdiag4"] = nc.dram_tensor("Dfdiag4", [128, 512], BF16, kind="ExternalInput").ap()
    ap["Dsdiag4"] = nc.dram_tensor("Dsdiag4", [128, 512], BF16, kind="ExternalInput").ap()
    ap["Dsprev4"] = nc.dram_tensor("Dsprev4", [128, 512], BF16, kind="ExternalInput").ap()
    ap["Dc"] = nc.dram_tensor("Dc", [64, 128, 512], BF16, kind="ExternalInput").ap()
    ap["E80"] = nc.dram_tensor("E80", [16, 128, 128], BF16, kind="ExternalInput").ap()
    ap["ones_row"] = nc.dram_tensor("ones_row", [1, 128], F32, kind="ExternalInput").ap()
    ap["bgate_row"] = nc.dram_tensor("bgate_row", [1, 24], F32, kind="ExternalInput").ap()
    ap["kb1_cols"] = nc.dram_tensor("kb1_cols", [128, 16], F32, kind="ExternalInput").ap()
    ap["vb1_cols"] = nc.dram_tensor("vb1_cols", [128, 16], F32, kind="ExternalInput").ap()
    ap["kb2_col"] = nc.dram_tensor("kb2_col", [128, 1], F32, kind="ExternalInput").ap()
    ap["vb2_col"] = nc.dram_tensor("vb2_col", [128, 1], F32, kind="ExternalInput").ap()
    ap["kpos_cols"] = nc.dram_tensor("kpos_cols", [128, 32], F32, kind="ExternalInput").ap()
    ap["vpos_cols"] = nc.dram_tensor("vpos_cols", [128, 32], F32, kind="ExternalInput").ap()
    ap["memk_cols"] = nc.dram_tensor("memk_cols", [128, 2], F32, kind="ExternalInput").ap()
    ap["memv_rows"] = nc.dram_tensor("memv_rows", [2, 1, 129], F32, kind="ExternalInput").ap()
    ap["attn_out"] = nc.dram_tensor("attn_out", [8, 128, 4096], BF16, kind="ExternalOutput").ap()
    ap["qc_hbm"] = nc.dram_tensor("qc_hbm", [2, 4, 128, 4096], BF16, kind="Internal").ap()

    sb = {}
    alloc_attn0_bufs(nc, sb)
    ps = [nc.alloc_psum_tensor(f"ps{i}", [128, 512], F32) for i in range(8)]
    P = Prog(nc)
    with nc.Block() as block:
        P.add("vector", lambda e: e.memset(sb["neg80"][:], -MSKB), writes=["neg80"])
        P.add("vector", lambda e: e.memset(sb["negk"][:], -1000.0), writes=["negk"])
        emit_attn0(P, nc, ap, sb, ps)
        P.emit(block)
    return nc


def alloc_attn0_bufs(nc, sb):
    sb["C0"] = nc.alloc_sbuf_tensor("s_C0", [128, 4096], BF16)
    sb["S0"] = nc.alloc_sbuf_tensor("s_S0", [128, 4096], BF16)
    sb["pswapT"] = nc.alloc_sbuf_tensor("s_pswapT", [128, 128], F32)
    sb["identF"] = nc.alloc_sbuf_tensor("s_identF", [128, 128], F32)
    sb["Dfdiag4"] = nc.alloc_sbuf_tensor("s_Dfdiag4", [128, 512], BF16)
    sb["Dsdiag4"] = nc.alloc_sbuf_tensor("s_Dsdiag4", [128, 512], BF16)
    sb["Dsprev4"] = nc.alloc_sbuf_tensor("s_Dsprev4", [128, 512], BF16)
    sb["Dc"] = nc.alloc_sbuf_tensor("s_Dc", [128, 512], BF16)
    sb["E80"] = nc.alloc_sbuf_tensor("s_E80", [128, 16 * 128], BF16)
    sb["onesr"] = nc.alloc_sbuf_tensor("s_onesr", [1, 128], F32)
    sb["bgate"] = nc.alloc_sbuf_tensor("s_bgate", [1, 24], F32)
    sb["wgate"] = nc.alloc_sbuf_tensor("s_wgate", [128, 32 * 24], BF16)
    sb["gates"] = nc.alloc_sbuf_tensor("s_gates", [128, 32 * 24], BF16)
    sb["lng"] = [nc.alloc_sbuf_tensor(f"s_lng{i}", [128, 128], BF16) for i in range(2)]
    sb["lnt"] = [nc.alloc_sbuf_tensor("s_lnt0", [128, 512 * 32], BF16)]
    sb["wje"] = [nc.alloc_sbuf_tensor("s_wje0", [128, 4096], BF16)]
    sb["q_all"] = nc.alloc_sbuf_tensor("s_q_all", [128, 4 * 4096], BF16)
    sb["qcs"] = nc.alloc_sbuf_tensor("s_qcs", [128, 512], BF16)
    sb["rk"] = nc.alloc_sbuf_tensor("s_rk", [128, 4096], BF16)
    sb["kt"] = nc.alloc_sbuf_tensor("s_kt", [128, 512], F32)
    sb["v"] = nc.alloc_sbuf_tensor("s_v", [128, 4096], BF16)
    sb["vt"] = nc.alloc_sbuf_tensor("s_vt", [128, 32 * 256], BF16)
    sb["qf"] = nc.alloc_sbuf_tensor("s_qf", [128, 512], F32)
    sb["qcb"] = nc.alloc_sbuf_tensor("s_qcb", [128, 512], BF16)
    sb["t1"] = nc.alloc_sbuf_tensor("s_t1", [128, 512], F32)
    sb["csf"] = nc.alloc_sbuf_tensor("s_csf", [128, 512], F32)
    sb["kpos"] = nc.alloc_sbuf_tensor("s_kpos", [128, 32], F32)
    sb["vpos"] = nc.alloc_sbuf_tensor("s_vpos", [128, 32], F32)
    sb["posc"] = nc.alloc_sbuf_tensor("s_posc", [128, 1], F32)
    sb["kb1"] = nc.alloc_sbuf_tensor("s_kb1", [128, 16], F32)
    sb["vb1"] = nc.alloc_sbuf_tensor("s_vb1", [128, 16], F32)
    sb["kb2"] = nc.alloc_sbuf_tensor("s_kb2", [128, 1], F32)
    sb["vb2"] = nc.alloc_sbuf_tensor("s_vb2", [128, 1], F32)
    sb["memk"] = nc.alloc_sbuf_tensor("s_memk", [128, 2], F32)
    sb["memvr"] = nc.alloc_sbuf_tensor("s_memvr", [1, 258], F32)
    sb["memk_bf"] = nc.alloc_sbuf_tensor("s_memk_bf", [128, 1], BF16)
    sb["memv_bf"] = nc.alloc_sbuf_tensor("s_memv_bf", [1, 129], BF16)
    sb["cin"] = nc.alloc_sbuf_tensor("s_cin", [128, 16 * 256], BF16)
    sb["w1b"] = [nc.alloc_sbuf_tensor("s_w1b0", [128, 2048], BF16)]
    sb["w2b"] = nc.alloc_sbuf_tensor("s_w2b", [128, 2048], BF16)
    sb["h1"] = nc.alloc_sbuf_tensor("s_h1", [128, 16 * 256], BF16)
    sb["ck"] = nc.alloc_sbuf_tensor("s_ck", [128, 256], BF16)
    sb["cv"] = nc.alloc_sbuf_tensor("s_cv", [128, 256], BF16)
    sb["cvt"] = nc.alloc_sbuf_tensor("s_cvt", [128, 2 * 256], BF16)
    sb["csraw"] = nc.alloc_sbuf_tensor("s_csraw", [128, 1024], BF16)
    sb["pc"] = nc.alloc_sbuf_tensor("s_pc", [128, 1024], BF16)
    sb["pmem"] = nc.alloc_sbuf_tensor("s_pmem", [1, 512], BF16)
    sb["cpv"] = [nc.alloc_sbuf_tensor(f"s_cpv{h}", [128, 132], F32) for h in range(4)]
    sb["fpv"] = [nc.alloc_sbuf_tensor(f"s_fpv{h}", [128, 132], F32) for h in range(4)]
    sb["spvh"] = [nc.alloc_sbuf_tensor(f"s_spvh{h}", [128, 132], F32) for h in range(4)]
    sb["rsb"] = [nc.alloc_sbuf_tensor(f"s_rsb{k}", [128, 1], F32) for k in range(12)]
    sb["rs2b"] = [nc.alloc_sbuf_tensor(f"s_rs2b{k}", [128, 1], F32) for k in range(12)]
    sb["coefb"] = [nc.alloc_sbuf_tensor(f"s_coefb{k}", [128, 1], F32) for k in range(12)]
    sb["g1b"] = [nc.alloc_sbuf_tensor(f"s_g1b{k}", [128, 1], F32) for k in range(12)]
    sb["impa"] = nc.alloc_sbuf_tensor("s_impa", [128, 128], F32)
    sb["impb"] = nc.alloc_sbuf_tensor("s_impb", [128, 128], F32)
    sb["impj"] = nc.alloc_sbuf_tensor("s_impj", [128, 256], BF16)
    sb["impq"] = nc.alloc_sbuf_tensor("s_impq", [128, 256], BF16)
    sb["im"] = nc.alloc_sbuf_tensor("s_im", [128, 1], F32)
    sb["imn"] = nc.alloc_sbuf_tensor("s_imn", [128, 1], F32)
    sb["uu"] = nc.alloc_sbuf_tensor("s_uu", [128, 256], F32)
    sb["zz"] = nc.alloc_sbuf_tensor("s_zz", [128, 1], F32)
    sb["sen"] = nc.alloc_sbuf_tensor("s_sen", [128, 1], F32)
    sb["m8"] = nc.alloc_sbuf_tensor("s_m8", [128, 8], F32)
    sb["t4b"] = nc.alloc_sbuf_tensor("s_t4b", [128, 1], F32)
    sb["selq"] = nc.alloc_sbuf_tensor("s_selq", [128, 256], BF16)
    sb["selq2"] = nc.alloc_sbuf_tensor("s_selq2", [128, 256], BF16)
    sb["selT"] = nc.alloc_sbuf_tensor("s_selT", [128, 2 * 512], BF16)
    sb["pp"] = [nc.alloc_sbuf_tensor(f"s_pp{i}", [128, 512], BF16) for i in range(2)]
    sb["rs"] = nc.alloc_sbuf_tensor("s_rs", [128, 1], F32)
    sb["rs2"] = nc.alloc_sbuf_tensor("s_rs2", [128, 1], F32)
    sb["coef"] = nc.alloc_sbuf_tensor("s_coef", [128, 1], F32)
    sb["att"] = nc.alloc_sbuf_tensor("s_att", [128, 128], F32)
    sb["attb"] = nc.alloc_sbuf_tensor("s_attb", [128, 128], BF16)
    sb["attT"] = nc.alloc_sbuf_tensor("s_attT", [128, 128], BF16)
    sb["neg80"] = nc.alloc_sbuf_tensor("s_neg80", [128, 1], F32)
    sb["negk"] = nc.alloc_sbuf_tensor("s_negk", [128, 1], F32)


def host_attn0_consts():
    C = {}
    inv = 1.0 / 10000.0 ** (np.arange(0, 128, 2, dtype=np.float64) / 128)
    pos = np.arange(4096, dtype=np.float64)
    ang = pos[:, None] * inv[None, :]
    cosv = np.cos(ang); sinv = np.sin(ang)
    C0 = np.zeros((128, 4096), np.float64)
    S0 = np.zeros((128, 4096), np.float64)
    C0[0::2, :] = cosv.T; C0[1::2, :] = cosv.T
    S0[0::2, :] = -sinv.T; S0[1::2, :] = sinv.T
    C["C0"] = C0.astype(ml_dtypes.bfloat16)
    C["S0"] = S0.astype(ml_dtypes.bfloat16)
    pswap = np.zeros((128, 128), np.float32)
    for d in range(0, 128, 2):
        pswap[d, d + 1] = 1.0
        pswap[d + 1, d] = 1.0
    C["pswapT"] = np.ascontiguousarray(pswap.T)
    C["identF"] = np.eye(128, dtype=np.float32)
    kk = np.arange(128)[:, None]
    rr = np.arange(128)[None, :]
    fd = np.where((kk // 16 == rr // 16) & (kk % 16 <= rr % 16), PEN, 0.0).astype(np.float32)
    C["Dfdiag4"] = np.ascontiguousarray(np.tile(fd, (1, 4)).astype(ml_dtypes.bfloat16))
    dq = rr - kk
    sd = np.where((dq >= 0) & (dq <= 64), PEN, 0.0).astype(np.float32)
    C["Dsdiag4"] = np.ascontiguousarray(np.tile(sd, (1, 4)).astype(ml_dtypes.bfloat16))
    dqp = (rr + 128) - kk
    sp = np.where((dqp >= 0) & (dqp <= 64), PEN, 0.0).astype(np.float32)
    C["Dsprev4"] = np.ascontiguousarray(np.tile(sp, (1, 4)).astype(ml_dtypes.bfloat16))
    dc = np.zeros((128, 64 * 512), np.float32)
    for i in range(32):
        for jt in range(2):
            j = 128 * jt + np.arange(128)
            q = 128 * i + np.arange(128)
            allowed = (16 * (j[:, None] + 1) - 1) < q[None, :]
            blk = np.where(allowed, 0.0, -30000.0).astype(np.float32)
            dc[:, 512 * (2 * i + jt):512 * (2 * i + jt) + 512] = np.tile(blk, (1, 4))
    C["Dc"] = np.ascontiguousarray(
        dc.reshape(128, 64, 512).transpose(1, 0, 2).astype(ml_dtypes.bfloat16))
    e80 = np.zeros((16, 128, 128), np.float32)
    for v in range(16):
        for j in range(8):
            e80[v, 8 * v + j, 16 * j:16 * (j + 1)] = PEN
    C["E80"] = e80.astype(ml_dtypes.bfloat16)
    C["ones_row"] = np.ones((1, 128), np.float32)
    return C


def host_attn0_weights(inp, c):
    b, q4 = c // 4, c % 4
    kvhs = [2 * q4, 2 * q4 + 1]
    wqkv = np.asarray(inp["L0_attn_w_qkv"], np.float32)
    cols = []
    for kvh in kvhs:
        for g in range(4):
            h = 4 * kvh + g
            cols.append(wqkv[128 * h:128 * (h + 1)])
    d = {}
    blocks = []
    for pi, kvh in enumerate(kvhs):
        for g in range(4):
            h = 4 * kvh + g
            blocks.append(wqkv[128 * h:128 * (h + 1)])
        blocks.append(wqkv[4096 + 128 * kvh:4096 + 128 * (kvh + 1)])
        blocks.append(wqkv[5120 + 128 * kvh:5120 + 128 * (kvh + 1)])
    d["wqkvT_sl"] = np.ascontiguousarray(np.concatenate([blk.T for blk in blocks], axis=1))
    wg = np.asarray(inp["L0_attn_w_gate"], np.float32)
    gcols = []
    for kvh in kvhs:
        for g in range(4):
            h = 4 * kvh + g
            gcols.append(wg[3 * h:3 * (h + 1)])
    d["wgateT_sl"] = np.ascontiguousarray(np.concatenate([gc.T for gc in gcols], axis=1))
    bg = np.asarray(inp["L0_attn_b_gate"], np.float32)
    d["bgate_row"] = np.ascontiguousarray(
        np.concatenate([bg[3 * (4 * kvh + g):3 * (4 * kvh + g + 1)]
                        for kvh in kvhs for g in range(4)])[None, :])
    kpos = np.asarray(inp["L0_attn_k_pos"], np.float32)
    vpos = np.asarray(inp["L0_attn_v_pos"], np.float32)
    d["kpos_cols"] = np.ascontiguousarray(
        np.concatenate([kpos[kvh].T for kvh in kvhs], axis=1))
    d["vpos_cols"] = np.ascontiguousarray(
        np.concatenate([vpos[kvh].T for kvh in kvhs], axis=1))
    mem = np.asarray(inp["L0_attn_mem_kv"], np.float32)
    d["memk_cols"] = np.ascontiguousarray(
        np.stack([mem[0, kvh, 0] for kvh in kvhs], axis=1))
    rows = np.zeros((2, 1, 129), np.float32)
    for pi, kvh in enumerate(kvhs):
        rows[pi, 0, :128] = mem[1, kvh, 0]
        rows[pi, 0, 128] = 1.0
    d["memv_rows"] = rows
    d["kb1_cols"] = np.ascontiguousarray(np.asarray(inp["L0_attn_k_b1"], np.float32).reshape(16, 128).T)
    d["vb1_cols"] = np.ascontiguousarray(np.asarray(inp["L0_attn_v_b1"], np.float32).reshape(16, 128).T)
    d["kb2_col"] = np.asarray(inp["L0_attn_k_b2"], np.float32).reshape(128, 1).copy()
    d["vb2_col"] = np.asarray(inp["L0_attn_v_b2"], np.float32).reshape(128, 1).copy()
    return d


MEGA_CAST_SPECS = [s for s in CAST_SPECS if s[0] not in ("woT0_sl", "woT1_sl")]


def emit_wo_partial(P, nc, ap, sb, ps):
    for k in range(8):
        P.dma("sync", sb["ash"][:, 4096 * k:4096 * (k + 1)], ap["attn_sh"][k],
              writes=[("ash", k)], stream=f"at{k % 2}")
    for ot in range(32):
        par = ot % 2
        P.dma("scalar", sb["wop"][par][:].rearrange("p (k o) -> p k o", o=128),
              ap["wo_t"][ot].rearrange("(k p) o -> p k o", p=128),
              writes=[("wop", par)], stream=f"wf{par}")
        for qt in range(4):
            for h in range(2):
                bi = par * 4 + 2 * (qt % 2) + h
                bank = ps[bi]
                for k in range(8):
                    P.add("tensor", lambda e, k=k, qt=qt, h=h, bank=bank, par=par: e.matmul(
                        bank.ap()[:, 0:512],
                        sb["wop"][par][:, 128 * k:128 * (k + 1)],
                        sb["ash"][:, 4096 * k + 1024 * qt + 512 * h:
                                  4096 * k + 1024 * qt + 512 * h + 512],
                        start=(k == 0), stop=(k == 7)),
                        reads=[("wop", par), ("ash", k), ("ps", bi)], writes=[("ps", bi)])
                pwi = 2 * (qt % 2) + h
                P.add("scalar", lambda e, bank=bank, pwi=pwi: e.copy(
                    out=sb["pw"][pwi][:], in_=bank.ap()[:, 0:512]),
                    reads=[("ps", bi)], writes=[("pw", pwi), ("ps", bi)])
                P.dma("gpsimd", ap["prs"][qt, ot, :, 512 * h:512 * (h + 1)], sb["pw"][pwi][:],
                      reads=[("pw", pwi)], writes=[("prs", qt, ot, h)], stream=f"pst{par}")


def alloc_wo_bufs(nc, sb):
    sb["ash"] = nc.alloc_sbuf_tensor("s_ash", [128, 8 * 4096], BF16)
    sb["wop"] = [nc.alloc_sbuf_tensor(f"s_wop{i}", [128, 1024], BF16) for i in range(2)]
    sb["pw"] = [nc.alloc_sbuf_tensor(f"s_pw{i}", [128, 512], F32) for i in range(4)]


def emit_addres(P, nc, ap, sb, ps):
    for ot in range(32):
        par = ot % 2
        P.dma("sync", sb["xo"][par][:], ap["x_in"][ot], writes=[("xo", par)], stream=f"xo{par}")
        P.dma("scalar", sb["dl"][par][:], ap["delta"][ot], writes=[("dl", par)], stream=f"dl{par}")
        P.add("vector", lambda e, par=par: e.tensor_add(
            out=sb["xo"][par][:], in0=sb["xo"][par][:], in1=sb["dl"][par][:]),
            reads=[("xo", par), ("dl", par)], writes=[("xo", par)])
        P.dma("gpsimd", ap["x_out"][ot], sb["xo"][par][:],
              reads=[("xo", par)], writes=[("xout", ot)], stream=f"xst{par}")


def alloc_addres_bufs(nc, sb):
    sb["xo"] = [nc.alloc_sbuf_tensor(f"s_xo{i}", [128, 1024], F32) for i in range(2)]
    sb["dl"] = [nc.alloc_sbuf_tensor(f"s_dl{i}", [128, 1024], F32) for i in range(2)]


def _mk_lnf(v4):
    def f(dt, c0, c1):
        j = c0 // 1024
        return v4[j, dt, :, c0 - 1024 * j:c1 - 1024 * j]
    return f


def build_mega():
    nc = bass.Bass("TRN2", target_bir_lowering=False)

    def ein(n, s, d):
        return nc.dram_tensor(n, s, d, kind="ExternalInput").ap()

    def eout(n, s, d):
        return nc.dram_tensor(n, s, d, kind="ExternalOutput").ap()

    def idram(n, s, d):
        return nc.dram_tensor(n, s, d, kind="Internal")

    ap = {}
    ap["qw"] = ein("qw", [8, 512, 128], I32)
    ap["lut4"] = ein("lut4", [512, 16], F32)
    ap["lut8"] = ein("lut8", [512, 256], F32)
    ap["bitmask"] = ein("bitmask", [128, 4096], I32)
    ap["ident_bf"] = ein("ident_bf", [128, 128], BF16)
    for (iname, rows, cols, oname) in MEGA_CAST_SPECS:
        ap[iname] = ein(iname, [rows, cols], F32)
    ap["wgateT_sl"] = ein("wgateT_sl", [4096, 24], F32)
    ap["A_er"] = ein("A_er", [256, 4096], F32)
    ap["Bt_sl"] = ein("Bt_sl", [256, 512], F32)
    ap["probs_rep"] = ein("probs_rep", [2, 128, 1], F32)
    ap["WT_sl"] = ein("WT_sl", [4096, 512], F32)
    ap["woP0"] = ein("woP0", [4, 1024, 1024], F32)
    ap["woP1"] = ein("woP1", [4, 1024, 1024], F32)
    ap["x_in0"] = ein("x_in0", [32, 128, 1024], F32)
    ap["bias0"] = ein("bias0", [128, 32], F32)
    ap["bias1"] = ein("bias1", [128, 32], F32)
    ap["ones_col"] = ein("ones_col", [128, 1], F32)
    ap["ones_row"] = ein("ones_row", [1, 128], F32)
    ap["C0"] = ein("C0", [128, 4096], BF16)
    ap["S0"] = ein("S0", [128, 4096], BF16)
    ap["pswapT"] = ein("pswapT", [128, 128], F32)
    ap["identF"] = ein("identF", [128, 128], F32)
    ap["Dfdiag4"] = ein("Dfdiag4", [128, 512], BF16)
    ap["Dsdiag4"] = ein("Dsdiag4", [128, 512], BF16)
    ap["Dsprev4"] = ein("Dsprev4", [128, 512], BF16)
    ap["Dc"] = ein("Dc", [64, 128, 512], BF16)
    ap["E80"] = ein("E80", [16, 128, 128], BF16)
    ap["bgate_row"] = ein("bgate_row", [1, 24], F32)
    ap["kb1_cols"] = ein("kb1_cols", [128, 16], F32)
    ap["vb1_cols"] = ein("vb1_cols", [128, 16], F32)
    ap["kb2_col"] = ein("kb2_col", [128, 1], F32)
    ap["vb2_col"] = ein("vb2_col", [128, 1], F32)
    ap["kpos_cols"] = ein("kpos_cols", [128, 32], F32)
    ap["vpos_cols"] = ein("vpos_cols", [128, 32], F32)
    ap["memk_cols"] = ein("memk_cols", [128, 2], F32)
    ap["memv_rows"] = ein("memv_rows", [2, 1, 129], F32)
    ap["C"] = ein("C", [128, 4096], BF16)
    ap["S"] = ein("S", [128, 4096], BF16)
    ap["protT"] = ein("protT", [128, 128], F32)
    ap["Ddiag4"] = ein("Ddiag4", [128, 512], F32)
    ap["Dprev4"] = ein("Dprev4", [128, 512], F32)
    ap["DcausQK"] = ein("DcausQK", [4, 128, 512], F32)
    ap["recent4"] = ein("recent4", [128, 512], BF16)
    ap["E01"] = ein("E01", [16, 128, 128], BF16)
    ap["init01"] = ein("init01", [128, 256], BF16)
    ap["zero512"] = ein("zero512", [1, 512], BF16)
    ap["x_out"] = eout("x_out", [32, 128, 1024], F32)

    cc_pairs = {}
    shard3 = {}
    full3 = {}
    for (nm, a, b) in (("wl", 4, 4096), ("wh", 4, 4096), ("weff", 4, 4096),
                       ("kw1", 2, 2048), ("vw1", 2, 2048)):
        sh = idram(nm + "_shd", [a * b, 128], BF16)
        fu = idram(nm + "_ful", [8 * a * b, 128], BF16)
        cc_pairs[nm] = (sh, fu)
        shard3[nm] = sh.ap().rearrange("(a b) c -> a b c", b=b)
        full3[nm] = fu.ap().rearrange("(a b) c -> a b c", b=b)
    kw2_bf = idram("kw2_bfT", [1, 2048, 128], BF16).ap()
    vw2_bf = idram("vw2_bfT", [1, 2048, 128], BF16).ap()
    wqkv_bf = idram("wqkv_bfT", [12, 4096, 128], BF16).ap()
    wgate_bf = idram("wgate_bfT", [4096, 24], BF16).ap()
    wq_bf = idram("wq_bfT", [8, 4096, 128], BF16).ap()
    wk_bf = idram("wk_bfT", [2, 4096, 128], BF16).ap()
    wv_bf = idram("wv_bfT", [2, 4096, 128], BF16).ap()
    wo_t = [idram(f"wo_t{l}", [32, 1024, 128], BF16).ap() for l in range(2)]
    qc_hbm = idram("qc_hbm", [2, 4, 128, 4096], BF16).ap()
    xa = idram("xa", [32, 128, 1024], F32).ap()
    xb = idram("xb", [32, 128, 1024], F32).ap()
    lnsh = [idram(f"lnsh{l}", [4096, 1024], BF16) for l in range(2)]
    lng = [idram(f"lng{l}", [16384, 1024], BF16) for l in range(2)]
    lnsh3 = [t.ap().rearrange("(a p) t -> a p t", p=128) for t in lnsh]
    lng4 = [t.ap().rearrange("(r a p) t -> r a p t", a=32, p=128) for t in lng]
    atsh = [idram(f"atsh{l}", [8, 128, 4096], BF16).ap() for l in range(2)]
    prs = [idram(f"prs{l}", [16384, 1024], F32) for l in range(2)]
    prs4 = [t.ap().rearrange("(q a p) t -> q a p t", a=32, p=128) for t in prs]
    dl = [idram(f"dl{l}", [4096, 1024], F32) for l in range(2)]
    dl3 = [t.ap().rearrange("(a p) t -> a p t", p=128) for t in dl]

    ps = [nc.alloc_psum_tensor(f"ps{i}", [128, 512], F32) for i in range(8)]
    arena = Arena(nc, nc.sbuf_base, nc.sbuf_top)
    P = Prog(nc)
    groups8 = [list(range(8))]
    groups4 = [[0, 1, 2, 3], [4, 5, 6, 7]]

    def cc_op(kind, alu, groups, in_t, out_t, wkey):
        P.add("gpsimd", lambda e, in_t=in_t, out_t=out_t: e.collective_compute(
            kind, alu, replica_groups=groups,
            ins=[in_t.ap().opt()], outs=[out_t.ap().opt()]),
            reads=(), writes=[wkey], kind="cc")

    def base_ap(**kw):
        d = dict(ap)
        d.update(kw)
        return d

    def ffn_phase(prefix, kind, x_in, x_out):
        arena.reset()
        an = _ArenaNc(arena, prefix)
        sb = {}
        alloc_ln_bufs(an, sb)
        alloc_ffn_bufs(an, sb, kind == "l0")
        if kind == "l0":
            apd = base_ap(x_in=x_in, x_out=x_out, bias=ap["bias0"],
                          wl_t=full3["wl"], wh_t=full3["wh"])
        else:
            apd = base_ap(x_in=x_in, x_out=x_out, bias=ap["bias1"],
                          wf_t=full3["weff"])
        ln_load_consts(P, apd, sb)
        emit_ffn(P, nc, apd, sb, ps, kind)
        P.barrier()

    with nc.Block() as block:
        arena.reset()
        an = _ArenaNc(arena, "p0_")
        sb = {}
        alloc_prep_bufs(an, sb)
        ap_prep = base_ap(wl_sh=shard3["wl"], wh_sh=shard3["wh"],
                          weff_sh=shard3["weff"], kw1_sh=shard3["kw1"],
                          vw1_sh=shard3["vw1"], kw2_bf=kw2_bf, vw2_bf=vw2_bf,
                          wqkv_bf=wqkv_bf, wgate_bf=wgate_bf, wq_bf=wq_bf,
                          wk_bf=wk_bf, wv_bf=wv_bf)
        wo_casts = []
        for l in range(2):
            for cb in range(4):
                wo_casts.append((ap[f"woP{l}"][cb], wo_t[l], 1024, 1024,
                                 8 * cb, f"wo_t{l}"))
        emit_dequant(P, nc, ap_prep, sb, ps)
        emit_weight_prep(P, nc, ap_prep, sb, ps, cast_specs=MEGA_CAST_SPECS,
                         extra_casts=wo_casts)
        P.barrier()
        for nm in ("wl", "wh", "weff", "kw1", "vw1"):
            cc_op("AllGather", ALU.bypass, groups8, cc_pairs[nm][0],
                  cc_pairs[nm][1], nm + "_ful")
        P.barrier()

        for l, (x_pre, x_mid) in enumerate(((ap["x_in0"], xa), (xa, xb))):
            kind = "l0" if l == 0 else "l1"
            ffn_phase(f"f{l}a_", kind, x_pre, x_mid)
            arena.reset()
            an = _ArenaNc(arena, f"ln{l}_")
            sb = {}
            alloc_ln_bufs(an, sb)
            apd = base_ap(x_in=x_mid)
            ln_load_consts(P, apd, sb)
            emit_ln(P, nc, apd, sb, ps, compute_e=False, ln_out_hbm=lnsh3[l])
            P.barrier()
            cc_op("AllGather", ALU.bypass, groups4, lnsh[l], lng[l], ("lng", l))
            P.barrier()
            arena.reset()
            an = _ArenaNc(arena, f"a{l}_")
            sb = {}
            if l == 0:
                alloc_attn0_bufs(an, sb)
                apd = base_ap(lnf=_mk_lnf(lng4[l]), attn_out=atsh[l],
                              qc_hbm=qc_hbm, kw1_sh=full3["kw1"],
                              vw1_sh=full3["vw1"], kw2_bf=kw2_bf,
                              vw2_bf=vw2_bf, wqkv_bf=wqkv_bf,
                              wgate_bf=wgate_bf)
                P.add("vector", lambda e, sb=sb: e.memset(sb["neg80"][:], -MSKB),
                      writes=["neg80"])
                P.add("vector", lambda e, sb=sb: e.memset(sb["negk"][:], -1000.0),
                      writes=["negk"])
                emit_attn0(P, nc, apd, sb, ps)
            else:
                alloc_attn1_bufs(an, sb)
                apd = base_ap(lnf=_mk_lnf(lng4[l]), attn_out=atsh[l],
                              wq_bf=wq_bf, wk_bf=wk_bf, wv_bf=wv_bf)
                P.dma("sync", sb["zero512"][0:1, :], apd["zero512"],
                      writes=["zero512"], stream="init")
                P.add("vector", lambda e, sb=sb: e.memset(sb["neg80"][:], -MSKB),
                      writes=["neg80"])
                P.add("vector", lambda e, sb=sb: e.memset(sb["zero1"][:], 0.0),
                      writes=["zero1"])
                emit_attn1(P, nc, apd, sb, ps)
            P.barrier()
            arena.reset()
            an = _ArenaNc(arena, f"w{l}_")
            sb = {}
            alloc_wo_bufs(an, sb)
            apd = base_ap(attn_sh=atsh[l], wo_t=wo_t[l], prs=prs4[l])
            emit_wo_partial(P, nc, apd, sb, ps)
            P.barrier()
            cc_op("ReduceScatter", ALU.add, groups4, prs[l], dl[l], ("dl", l))
            P.barrier()
            arena.reset()
            an = _ArenaNc(arena, f"r{l}_")
            sb = {}
            alloc_addres_bufs(an, sb)
            x_post = xb if l == 0 else xa
            apd = base_ap(x_in=x_mid, delta=dl3[l], x_out=x_post)
            emit_addres(P, nc, apd, sb, ps)
            P.barrier()
            x_fin = xa if l == 0 else ap["x_out"]
            ffn_phase(f"f{l}b_", kind, x_post, x_fin)
        P.emit(block)
    return nc


def host_mega_inputs(inputs, consts, ac0, ac1, c):
    b, q4 = c // 4, c % 4
    m = {}
    qw = np.asarray(inputs["L0_ffn_qweight"])[:, 512 * c:512 * (c + 1), :]
    m["qw"] = np.ascontiguousarray(qw)
    m["lut4"] = _f32(np.asarray(inputs["L0_ffn_lut4"], np.float32)[512 * c:512 * (c + 1)])
    m["lut8"] = _f32(np.asarray(inputs["L0_ffn_lut8"], np.float32)[512 * c:512 * (c + 1)])
    m["bitmask"] = consts["bitmask"]
    m["ident_bf"] = consts["ident_bf"]
    w0 = host_attn0_weights(inputs, c)
    w1 = host_attn1_weights(inputs, c)
    m["wqkvT_sl"] = w0["wqkvT_sl"]
    m["wgateT_sl"] = w0["wgateT_sl"]
    m["wqT_sl"] = w1["wqT_sl"]
    m["wkT_sl"] = w1["wkT_sl"]
    m["wvT_sl"] = w1["wvT_sl"]
    m["kw1T_sl"] = _f32(np.asarray(inputs["L0_attn_k_w1"], np.float32).T[:, 256 * c:256 * (c + 1)])
    m["vw1T_sl"] = _f32(np.asarray(inputs["L0_attn_v_w1"], np.float32).T[:, 256 * c:256 * (c + 1)])
    m["kw2T"] = _f32(np.asarray(inputs["L0_attn_k_w2"], np.float32).T)
    m["vw2T"] = _f32(np.asarray(inputs["L0_attn_v_w2"], np.float32).T)
    m["A_er"] = _f32(np.asarray(inputs["L1_ffn_A"], np.float32).reshape(256, 4096))
    m["Bt_sl"] = _f32(np.asarray(inputs["L1_ffn_B"], np.float32)
                      .transpose(0, 2, 1).reshape(256, 4096)[:, 512 * c:512 * (c + 1)])
    pr = np.asarray(inputs["L1_ffn_probs"], np.float32)
    prep = np.zeros((2, 128, 1), np.float32)
    for t in range(2):
        prep[t, :, 0] = pr[(t * 128 + np.arange(128)) // 16]
    m["probs_rep"] = prep
    m["WT_sl"] = _f32(np.asarray(inputs["L1_ffn_W"], np.float32).T[:, 512 * c:512 * (c + 1)])
    for l, wname in ((0, "L0_attn_w_o"), (1, "L1_attn_wo")):
        woT = np.asarray(inputs[wname], np.float32).T[1024 * q4:1024 * (q4 + 1), :]
        m[f"woP{l}"] = np.ascontiguousarray(
            woT.reshape(1024, 4, 1024).transpose(1, 0, 2))
    x0 = np.asarray(inputs["L0_attn_hidden_states"], np.float32)
    m["x_in0"] = _fm(x0[b], q4)
    m["bias0"] = np.ascontiguousarray(np.asarray(inputs["L0_ffn_bias"], np.float32).reshape(32, 128).T)
    m["bias1"] = np.ascontiguousarray(np.asarray(inputs["L1_ffn_b"], np.float32).reshape(32, 128).T)
    m["ones_col"] = consts["ones_col"]
    m["ones_row"] = consts["ones_row"]
    for k in ("C0", "S0", "pswapT", "identF", "Dfdiag4", "Dsdiag4", "Dsprev4",
              "Dc", "E80"):
        m[k] = ac0[k]
    for k in ("bgate_row", "kb1_cols", "vb1_cols", "kb2_col", "vb2_col",
              "kpos_cols", "vpos_cols", "memk_cols", "memv_rows"):
        m[k] = w0[k]
    for k in ("C", "S", "protT", "Ddiag4", "Dprev4", "DcausQK", "recent4",
              "E01", "init01", "zero512"):
        m[k] = ac1[k]
    return m


def _fm(xb, q4):
    return np.ascontiguousarray(
        xb[1024 * q4:1024 * (q4 + 1), :].T.reshape(32, 128, 1024))


def host_p0full_inputs(inp, consts, c):
    m = {}
    qw = np.asarray(inp["L0_ffn_qweight"])[:, 512 * c:512 * (c + 1), :]
    m["qw"] = np.ascontiguousarray(qw)
    m["lut4"] = _f32(np.asarray(inp["L0_ffn_lut4"], np.float32)[512 * c:512 * (c + 1)])
    m["lut8"] = _f32(np.asarray(inp["L0_ffn_lut8"], np.float32)[512 * c:512 * (c + 1)])
    m["bitmask"] = consts["bitmask"]
    m["ident_bf"] = consts["ident_bf"]
    w0 = host_attn0_weights(inp, c)
    w1 = host_attn1_weights(inp, c)
    m["wqkvT_sl"] = w0["wqkvT_sl"]
    m["wgateT_sl"] = w0["wgateT_sl"]
    m["wqT_sl"] = w1["wqT_sl"]
    m["wkT_sl"] = w1["wkT_sl"]
    m["wvT_sl"] = w1["wvT_sl"]
    m["woT0_sl"] = _f32(np.asarray(inp["L0_attn_w_o"], np.float32).T[:, 512 * c:512 * (c + 1)])
    m["woT1_sl"] = _f32(np.asarray(inp["L1_attn_wo"], np.float32).T[:, 512 * c:512 * (c + 1)])
    m["kw1T_sl"] = _f32(np.asarray(inp["L0_attn_k_w1"], np.float32).T[:, 256 * c:256 * (c + 1)])
    m["vw1T_sl"] = _f32(np.asarray(inp["L0_attn_v_w1"], np.float32).T[:, 256 * c:256 * (c + 1)])
    m["kw2T"] = _f32(np.asarray(inp["L0_attn_k_w2"], np.float32).T)
    m["vw2T"] = _f32(np.asarray(inp["L0_attn_v_w2"], np.float32).T)
    m["A_er"] = _f32(np.asarray(inp["L1_ffn_A"], np.float32).reshape(256, 4096))
    m["Bt_sl"] = _f32(np.asarray(inp["L1_ffn_B"], np.float32)
                      .transpose(0, 2, 1).reshape(256, 4096)[:, 512 * c:512 * (c + 1)])
    pr = np.asarray(inp["L1_ffn_probs"], np.float32)
    prep = np.zeros((2, 128, 1), np.float32)
    for t in range(2):
        prep[t, :, 0] = pr[(t * 128 + np.arange(128)) // 16]
    m["probs_rep"] = prep
    m["WT_sl"] = _f32(np.asarray(inp["L1_ffn_W"], np.float32).T[:, 512 * c:512 * (c + 1)])
    return m


def kernel(**inputs):
    consts = host_constants()
    ac0 = host_attn0_consts()
    ac1 = host_attn1_consts()
    nc = build_mega()
    maps = [host_mega_inputs(inputs, consts, ac0, ac1, c) for c in range(8)]
    res = run_bass_kernel_spmd(nc, maps, core_ids=list(range(8)))
    out = np.empty((2, 4096, 4096), np.float32)
    for c in range(8):
        b, q4 = c // 4, c % 4
        xs = np.asarray(res.results[c]["x_out"])
        out[b, 1024 * q4:1024 * (q4 + 1), :] = xs.reshape(4096, 1024).T
    return out


def kernel_unfused(**inputs):
    consts = host_constants()
    nc = build_p0full()
    res = run_bass_kernel_spmd(
        nc, [host_p0full_inputs(inputs, consts, c) for c in range(8)],
        core_ids=list(range(8)))
    W = {}
    for nm in ("wl_sh", "wh_sh", "wo0_sh", "wo1_sh", "weff_sh", "kw1_sh", "vw1_sh"):
        W[nm] = np.ascontiguousarray(
            np.concatenate([np.asarray(res.results[c][nm]) for c in range(8)], axis=0))
    for nm in ("kw2_bf", "vw2_bf"):
        W[nm] = np.ascontiguousarray(np.asarray(res.results[0][nm]))
    for nm in ("wqkv_bf", "wgate_bf", "wq_bf", "wk_bf", "wv_bf"):
        W[nm] = [np.ascontiguousarray(np.asarray(res.results[c][nm])) for c in range(8)]

    bias0 = np.ascontiguousarray(np.asarray(inputs["L0_ffn_bias"], np.float32).reshape(32, 128).T)
    bias1 = np.ascontiguousarray(np.asarray(inputs["L1_ffn_b"], np.float32).reshape(32, 128).T)

    def run_ffn(kind, xs):
        nc = build_ffn(kind)
        maps = []
        for c in range(8):
            m = {"x_in": xs[c], "ones_col": consts["ones_col"],
                 "ones_row": consts["ones_row"]}
            if kind == "l0":
                m["bias"] = bias0
                m["wl_t"] = W["wl_sh"]
                m["wh_t"] = W["wh_sh"]
            else:
                m["bias"] = bias1
                m["wf_t"] = W["weff_sh"]
            maps.append(m)
        r = run_bass_kernel_spmd(nc, maps, core_ids=list(range(8)))
        return [np.ascontiguousarray(np.asarray(r.results[c]["x_out"])) for c in range(8)]

    def run_lnout(xs):
        nc = build_lnout()
        maps = [{"x_in": xs[c], "ones_col": consts["ones_col"],
                 "ones_row": consts["ones_row"]} for c in range(8)]
        r = run_bass_kernel_spmd(nc, maps, core_ids=list(range(8)))
        ln = [np.asarray(r.results[c]["ln_out"]) for c in range(8)]
        return [np.ascontiguousarray(np.concatenate([ln[4 * b + j] for j in range(4)], axis=2))
                for b in range(2)]

    def run_wo(xs, att_by_batch, wo_t):
        nc = build_wo()
        maps = []
        for c in range(8):
            b, q4 = c // 4, c % 4
            sl = slice(1024 * q4, 1024 * (q4 + 1))
            maps.append({"x_in": xs[c], "wo_t": wo_t,
                         "attn_in": np.ascontiguousarray(att_by_batch[b][:, :, sl])})
        r = run_bass_kernel_spmd(nc, maps, core_ids=list(range(8)))
        return [np.ascontiguousarray(np.asarray(r.results[c]["x_out"])) for c in range(8)]

    x0 = np.asarray(inputs["L0_attn_hidden_states"], np.float32)
    xs = [_fm(x0[c // 4], c % 4) for c in range(8)]

    xs = run_ffn("l0", xs)
    ln_b = run_lnout(xs)
    ac0 = host_attn0_consts()
    nc = build_attn0()
    maps = []
    for c in range(8):
        b = c // 4
        w0 = host_attn0_weights(inputs, c)
        m = {"ln_full": ln_b[b], "wqkv_bf": W["wqkv_bf"][c], "wgate_bf": W["wgate_bf"][c],
             "kw1_sh": W["kw1_sh"], "vw1_sh": W["vw1_sh"],
             "kw2_bf": W["kw2_bf"], "vw2_bf": W["vw2_bf"]}
        for kk in ("bgate_row", "kb1_cols", "vb1_cols", "kb2_col", "vb2_col",
                   "kpos_cols", "vpos_cols", "memk_cols", "memv_rows"):
            m[kk] = w0[kk]
        m.update(ac0)
        maps.append(m)
    r = run_bass_kernel_spmd(nc, maps, core_ids=list(range(8)))
    att_b = [np.concatenate([np.asarray(r.results[4 * b + j]["attn_out"]) for j in range(4)],
                            axis=0) for b in range(2)]
    xs = run_wo(xs, att_b, W["wo0_sh"])
    xs = run_ffn("l0", xs)

    xs = run_ffn("l1", xs)
    ln_b = run_lnout(xs)
    ac1 = host_attn1_consts()
    nc = build_attn1()
    maps = []
    for c in range(8):
        b = c // 4
        m = {"ln_full": ln_b[b], "wq_bf": W["wq_bf"][c], "wk_bf": W["wk_bf"][c],
             "wv_bf": W["wv_bf"][c]}
        m.update(ac1)
        maps.append(m)
    r = run_bass_kernel_spmd(nc, maps, core_ids=list(range(8)))
    att_b = [np.concatenate([np.asarray(r.results[4 * b + j]["attn_out"]) for j in range(4)],
                            axis=0) for b in range(2)]
    xs = run_wo(xs, att_b, W["wo1_sh"])
    xs = run_ffn("l1", xs)

    out = np.empty((2, 4096, 4096), np.float32)
    for c in range(8):
        b, q4 = c // 4, c % 4
        out[b, 1024 * q4:1024 * (q4 + 1), :] = xs[c].reshape(4096, 1024).T
    return out


# version 03
# speedup vs baseline: 1.0457x; 1.0457x over previous
import math
import numpy as np
import ml_dtypes
import concourse.bass as bass
import concourse.mybir as mybir
from concourse.bass_utils import run_bass_kernel_spmd

F32 = mybir.dt.float32
BF16 = mybir.dt.bfloat16
I32 = mybir.dt.int32
AF = mybir.ActivationFunctionType
ALU = mybir.AluOpType
AX = mybir.AxisListType

NCORES = 8
B = 2
S = 4096
D = 4096
TPC = 1024
H = 32
KVH = 8
DH = 128
G = 4
PAIRS = 2
CB = 16
SB = 16
NSEL = 4
WIN = 64
NC_CHUNKS = S // CB
CDIM = CB * DH
SCALE = DH ** -0.5
L1_CHUNK = 16
L1_TOPK = 8
NEGBIG = -30000.0
MSK = 80.0
EPS = 1e-5

DT_NP = {F32: np.float32, BF16: ml_dtypes.bfloat16, I32: np.int32}


ENGINES = ("sync", "vector", "scalar", "tensor", "gpsimd")
DMA_ENGINES = ("sync", "vector", "scalar", "gpsimd")


class _Op:
    __slots__ = ("eng", "fn", "reads", "writes", "kind", "sem", "val", "idx",
                 "stream", "floor")

    def __init__(self, eng, fn, reads, writes, kind, stream=None):
        self.eng = eng
        self.fn = fn
        self.reads = reads
        self.writes = writes
        self.kind = kind
        self.stream = stream
        self.sem = None
        self.val = 0
        self.floor = None


class Prog:

    def __init__(self, nc, same_engine_edges=False):
        self.nc = nc
        self.same_engine_edges = same_engine_edges
        self.ops = []
        self.last_writer = {}
        self.readers = {}
        self.counts = {}
        self.ccn = 0
        self.floor = {}

    def _sem_for(self, op):
        if op.kind == "c":
            return "e_" + op.eng, 1
        if op.kind == "d":
            assert op.stream is not None, "dma ops need an explicit stream"
            return "d:" + op.stream, 16
        self.ccn += 1
        return f"cc_{self.ccn}", 1

    def add(self, eng, fn, reads=(), writes=(), kind="c", stream=None):
        op = _Op(eng, fn, tuple(reads), tuple(writes), kind, stream)
        sem, inc = self._sem_for(op)
        self.counts[sem] = self.counts.get(sem, 0) + inc
        op.sem, op.val = sem, self.counts[sem]
        op.floor = self.floor
        op.idx = len(self.ops)
        self.ops.append(op)
        for k in op.reads:
            self.readers.setdefault(k, []).append(op)
        for k in op.writes:
            self.last_writer[k] = op
            self.readers[k] = []
            self.readers[k].append(op)
        return op

    def dma(self, eng, out, in_, reads=(), writes=(), stream=None, **kw):
        kind = kw.pop("kind", "d")
        return self.add(eng, lambda e: e.dma_start(out=out, in_=in_, **kw),
                        reads=reads, writes=writes, kind=kind, stream=stream)

    def barrier(self):
        self.floor = dict(self.counts)

    def _deps(self, op):
        need = dict(op.floor)
        def _add(src):
            if src is op:
                return
            if src.kind == "c" and src.eng == op.eng and not self.same_engine_edges:
                return
            if src.kind == "d":
                v = self.sem_prefix[src.sem][op.idx]
            else:
                v = src.val
            if v > need.get(src.sem, 0):
                need[src.sem] = v
        for k in op.reads:
            w = self.last_writer_at.get((k, op.idx))
            if w is not None:
                _add(w)
        for k in op.writes:
            for r in self.readers_at.get((k, op.idx), ()):
                _add(r)
        return need

    def emit(self, block, final_outputs_engine="gpsimd"):
        self.sem_prefix = {}
        dma_sems = {op.sem for op in self.ops if op.kind == "d"}
        for semname in dma_sems:
            pref = [0] * (len(self.ops) + 1)
            run = 0
            for i, op in enumerate(self.ops):
                pref[i] = run
                if op.sem == semname:
                    run = op.val
            pref[len(self.ops)] = run
            self.sem_prefix[semname] = pref
        self.last_writer_at = {}
        self.readers_at = {}
        lw = {}
        rd = {}
        for op in self.ops:
            for k in op.reads:
                if k in lw:
                    self.last_writer_at[(k, op.idx)] = lw[k]
            for k in op.writes:
                if k in lw:
                    self.last_writer_at.setdefault((k, op.idx), lw[k])
                self.readers_at[(k, op.idx)] = tuple(rd.get(k, ()))
            for k in op.reads:
                rd.setdefault(k, []).append(op)
            for k in op.writes:
                lw[k] = op
                rd[k] = [op]

        sems = {name: self.nc.alloc_semaphore(name) for name in self.counts}
        per_eng = {e: [] for e in ENGINES}
        for op in self.ops:
            per_eng[op.eng].append(op)
        finals = dict(self.counts)

        waited = {e: {} for e in ENGINES}

        def mk_section(eng):
            ops = per_eng[eng]

            def body(ee):
                hw = waited[eng]
                for op in ops:
                    need = self._deps(op)
                    for semname, v in sorted(need.items()):
                        if semname == "e_" + eng and not self.same_engine_edges:
                            continue
                        if hw.get(semname, 0) < v:
                            ee.wait_ge(sems[semname], v)
                            hw[semname] = v
                    inst = op.fn(ee)
                    inc = 1 if op.kind == "c" else 16
                    if op.kind == "cc":
                        inst.then_inc(sems[op.sem])
                    else:
                        inst.then_inc(sems[op.sem], inc)
                if eng == final_outputs_engine:
                    for semname, v in sorted(finals.items()):
                        if semname == "e_" + eng:
                            continue
                        if waited[eng].get(semname, 0) < v:
                            ee.wait_ge(sems[semname], v)
                            waited[eng][semname] = v
            return body

        engs_used = [e for e in ENGINES if per_eng[e] or e == final_outputs_engine]
        for e in engs_used:
            getattr(block, e)(mk_section(e))


def ceil_div(a, b):
    return -(-a) // b


DT_BYTES = {F32: 4, I32: 4, BF16: 2}


class Arena:
    def __init__(self, nc, base, limit):
        self.nc = nc
        self.base = base
        self.limit = limit
        self.off = base
        self.high = base

    def alloc(self, name, shape, dtype):
        nb = shape[-1] * DT_BYTES[dtype]
        off = (self.off + 31) & ~31
        assert off + nb <= self.limit, f"SBUF arena overflow at {name}: {off + nb} > {self.limit}"
        t = self.nc.alloc_sbuf_tensor_at(name, shape, dtype, offset=off)
        self.off = off + nb
        self.high = max(self.high, self.off)
        return t

    def reset(self):
        self.off = self.base


class _ArenaNc:

    def __init__(self, arena, prefix):
        self._arena = arena
        self._prefix = prefix

    def alloc_sbuf_tensor(self, name, shape, dtype):
        return self._arena.alloc(self._prefix + name, shape, dtype)


def _f32(x):
    return np.ascontiguousarray(x, dtype=np.float32)


def host_constants():
    C = {}
    C["ones_col"] = np.ones((128, 1), np.float32)
    C["ones_row"] = np.ones((1, 128), np.float32)
    C["ident_bf"] = np.eye(128).astype(ml_dtypes.bfloat16)
    C["bitmask"] = np.broadcast_to(
        (np.int64(1) << (np.arange(4096, dtype=np.int64) % 32)).astype(np.int64),
        (128, 4096)).astype(np.uint32).view(np.int32).copy()
    return C


def host_inputs_per_core(inp):
    cores = []
    x = np.asarray(inp["L0_attn_hidden_states"], np.float32)
    qw = np.asarray(inp["L0_ffn_qweight"])
    lut4 = _f32(inp["L0_ffn_lut4"])
    lut8 = _f32(inp["L0_ffn_lut8"])
    for c in range(NCORES):
        b, q4 = c // 4, c % 4
        d = {}
        d["x0"] = _f32(x[b, 1024 * q4:1024 * (q4 + 1), :].T)
        o0 = 512 * c
        d["qwT"] = np.ascontiguousarray(
            qw[:, o0:o0 + 512, :].transpose(0, 2, 1))
        d["lut4"] = _f32(lut4[o0:o0 + 512])
        d["lut8"] = _f32(lut8[o0:o0 + 512])
        cores.append(d)
    return cores


def emit_dequant(P, nc, ap, sb, ps):
    ot_range = range(4)
    for ot in ot_range:
        P.dma("sync", sb["lut4"][:, 16 * ot:16 * (ot + 1)], ap["lut4"][128 * ot:128 * (ot + 1), :],
              writes=[("lut4", ot)], stream="init")
        P.dma("sync", sb["lut8"][:, 256 * ot:256 * (ot + 1)], ap["lut8"][128 * ot:128 * (ot + 1), :],
              writes=[("lut8", ot)], stream="init")
    P.dma("sync", sb["bitmask"][:], ap["bitmask"], writes=["bitmask"], stream="init")
    P.dma("sync", sb["ident"][:], ap["ident_bf"], writes=["ident"], stream="init")

    for ot in ot_range:
        for p in range(8):
            P.dma("gpsimd", sb["qw"][:, 128 * p:128 * (p + 1)],
                  ap["qw"][p, 128 * ot:128 * (ot + 1), :],
                  writes=[("qw", p)], stream="qw")
        for p in range(8):
            qv = sb["qw"][:, 128 * p:128 * (p + 1)].rearrange(
                "o (w u) -> o w u", u=1).broadcast_to((128, 128, 32))
            bm = sb["bitmask"][:].rearrange("o (w v) -> o w v", v=32)
            P.add("vector", lambda e, qv=qv, bm=bm: e.tensor_tensor(
                out=sb["tmp32"][:].rearrange("o (w v) -> o w v", v=32),
                in0=qv, in1=bm, op=ALU.bitwise_and),
                reads=[("qw", p), "bitmask"], writes=["tmp32"])
            P.add("vector", lambda e: e.tensor_scalar(
                out=sb["bit"][:], in0=sb["tmp32"][:], scalar1=0.0, scalar2=None,
                op0=ALU.not_equal),
                reads=["tmp32"], writes=["bit"])
            tgt = "idx4" if p < 4 else "idx8"
            if p == 0:
                P.add("vector", lambda e: e.tensor_scalar(
                    out=sb["idx4"][:], in0=sb["bit"][:], scalar1=1.0, scalar2=None,
                    op0=ALU.mult), reads=["bit"], writes=["idx4"])
            elif p == 4:
                P.add("vector", lambda e: e.tensor_copy(out=sb["idx8"][:], in_=sb["idx4"][:]),
                      reads=["idx4"], writes=["idx8"])
                P.add("vector", lambda e, s=float(1 << p): e.scalar_tensor_tensor(
                    out=sb["idx8"][:], in0=sb["bit"][:], scalar=s, in1=sb["idx8"][:],
                    op0=ALU.mult, op1=ALU.add), reads=["bit", "idx8"], writes=["idx8"])
            else:
                P.add("vector", lambda e, s=float(1 << p), t=tgt: e.scalar_tensor_tensor(
                    out=sb[t][:], in0=sb["bit"][:], scalar=s, in1=sb[t][:],
                    op0=ALU.mult, op1=ALU.add), reads=["bit", tgt], writes=[tgt])

        for (wname, idxname, lutname, ncodes, lutw) in (
                ("wl", "idx4", "lut4", 16, 16), ("wh", "idx8", "lut8", 256, 256)):
            for ci in range(ncodes):
                lut_col = sb[lutname][:, lutw * ot + ci:lutw * ot + ci + 1]
                if ci == 0:
                    P.add("vector", lambda e, ci=ci, lc=lut_col, w=wname, ix=idxname:
                          e.tensor_scalar(out=sb[w][:], in0=sb[ix][:],
                                          scalar1=float(ci), scalar2=lc,
                                          op0=ALU.is_equal, op1=ALU.mult),
                          reads=[idxname, ("lutld", lutname, ot)] if False else [idxname, (lutname, ot)],
                          writes=[wname])
                else:
                    P.add("vector", lambda e, ci=ci, lc=lut_col, ix=idxname:
                          e.tensor_scalar(out=sb["m"][:], in0=sb[ix][:],
                                          scalar1=float(ci), scalar2=lc,
                                          op0=ALU.is_equal, op1=ALU.mult),
                          reads=[idxname, (lutname, ot)], writes=["m"])
                    P.add("vector", lambda e, w=wname: e.tensor_tensor(
                        out=sb[w][:], in0=sb["m"][:], in1=sb[w][:], op=ALU.add),
                        reads=["m", wname], writes=[wname])

        for qi, (wname, outname) in enumerate((("wl", "wl_sh"), ("wh", "wh_sh"))):
            q = "sync" if qi == 0 else "scalar"
            for it in range(32):
                P.add(q, lambda e, it=it, w=wname: e.dma_start_transpose(
                    sb["stag" + w][:, 128 * it:128 * (it + 1)],
                    sb[w][:, 128 * it:128 * (it + 1)]),
                    reads=[wname], writes=[("stag", wname, it)], kind="d",
                    stream="tp" + wname)
            P.dma(q, ap[outname][ot].rearrange("(it p) o -> p it o", p=128),
                  sb["stag" + wname][:].rearrange("p (it o) -> p it o", o=128),
                  reads=[("stag", wname, it2) for it2 in range(32)],
                  writes=[(outname, ot)], stream="st" + wname)
    return


def build_p0():
    nc = bass.Bass("TRN2", target_bir_lowering=False)
    ap = {}
    ap["qw"] = nc.dram_tensor("qw", [8, 512, 128], I32, kind="ExternalInput").ap()
    ap["lut4"] = nc.dram_tensor("lut4", [512, 16], F32, kind="ExternalInput").ap()
    ap["lut8"] = nc.dram_tensor("lut8", [512, 256], F32, kind="ExternalInput").ap()
    ap["bitmask"] = nc.dram_tensor("bitmask", [128, 4096], I32, kind="ExternalInput").ap()
    ap["ident_bf"] = nc.dram_tensor("ident_bf", [128, 128], BF16, kind="ExternalInput").ap()
    ap["wl_sh"] = nc.dram_tensor("wl_sh", [4, 4096, 128], BF16, kind="ExternalOutput").ap()
    ap["wh_sh"] = nc.dram_tensor("wh_sh", [4, 4096, 128], BF16, kind="ExternalOutput").ap()

    sb = {}
    sb["qw"] = nc.alloc_sbuf_tensor("s_qw", [128, 8 * 128], I32)
    sb["lut4"] = nc.alloc_sbuf_tensor("s_lut4", [128, 64], F32)
    sb["lut8"] = nc.alloc_sbuf_tensor("s_lut8", [128, 1024], F32)
    sb["bitmask"] = nc.alloc_sbuf_tensor("s_bitmask", [128, 4096], I32)
    sb["ident"] = nc.alloc_sbuf_tensor("s_ident", [128, 128], BF16)
    sb["tmp32"] = nc.alloc_sbuf_tensor("s_tmp32", [128, 4096], I32)
    sb["bit"] = nc.alloc_sbuf_tensor("s_bit", [128, 4096], BF16)
    sb["idx4"] = nc.alloc_sbuf_tensor("s_idx4", [128, 4096], BF16)
    sb["idx8"] = nc.alloc_sbuf_tensor("s_idx8", [128, 4096], BF16)
    sb["wl"] = nc.alloc_sbuf_tensor("s_wl", [128, 4096], BF16)
    sb["wh"] = nc.alloc_sbuf_tensor("s_wh", [128, 4096], BF16)
    sb["m"] = nc.alloc_sbuf_tensor("s_m", [128, 4096], BF16)
    sb["stagwl"] = nc.alloc_sbuf_tensor("s_stagwl", [128, 4096], BF16)
    sb["stagwh"] = nc.alloc_sbuf_tensor("s_stagwh", [128, 4096], BF16)
    ps = []

    P = Prog(nc)
    with nc.Block() as block:
        emit_dequant(P, nc, ap, sb, ps)
        P.emit(block)
    return nc


def emit_ln(P, nc, ap, sb, ps, compute_e=False, ln_out_hbm=None):
    x_in = ap["x_in"]
    for dt in range(32):
        xb = ("x", dt % 2)
        P.dma("sync", sb["xA"][dt % 2][:], x_in[dt], writes=[xb], stream=f"xA{dt % 2}")
        P.add("scalar", lambda e, dt=dt: e.square(out=sb["xsq"][dt % 2][:], in_=sb["xA"][dt % 2][:]),
              reads=[xb], writes=[("xsq", dt % 2)])
        for h in range(2):
            P.add("tensor", lambda e, dt=dt, h=h: e.matmul(
                ps[h].ap()[0:1, :], sb["ones_col"][:], sb["xA"][dt % 2][:, 512 * h:512 * (h + 1)],
                start=(dt == 0), stop=(dt == 31)),
                reads=[xb, "ones_col"], writes=[("ps", h)])
            P.add("tensor", lambda e, dt=dt, h=h: e.matmul(
                ps[2 + h].ap()[0:1, :], sb["ones_col"][:], sb["xsq"][dt % 2][:, 512 * h:512 * (h + 1)],
                start=(dt == 0), stop=(dt == 31)),
                reads=[("xsq", dt % 2), "ones_col"], writes=[("ps", 2 + h)])
    for h in range(2):
        sl = slice(512 * h, 512 * (h + 1))
        P.add("scalar", lambda e, h=h, sl=sl: e.mul(out=sb["mean"][0:1, sl], in_=ps[h].ap()[0:1, :], mul=1.0 / 4096),
              reads=[("ps", h)], writes=[("mean", h), ("ps", h)])
        P.add("scalar", lambda e, h=h, sl=sl: e.mul(out=sb["msq"][0:1, sl], in_=ps[2 + h].ap()[0:1, :], mul=1.0 / 4096),
              reads=[("ps", 2 + h)], writes=[("msq", h), ("ps", 2 + h)])
    P.add("scalar", lambda e: e.square(out=sb["m2"][0:1, :], in_=sb["mean"][0:1, :]),
          reads=[("mean", 0), ("mean", 1)], writes=["m2"])
    P.add("vector", lambda e: e.scalar_tensor_tensor(out=sb["var"][0:1, :], in0=sb["msq"][0:1, :],
                                                     scalar=EPS, in1=sb["m2"][0:1, :],
                                                     op0=ALU.add, op1=ALU.subtract),
          reads=["m2", ("msq", 0), ("msq", 1)], writes=["var"])
    P.add("scalar", lambda e: e.activation(out=sb["srv"][0:1, :], in_=sb["var"][0:1, :],
                                           func=AF.Sqrt, bias=0.0, scale=1.0),
          reads=["var"], writes=["srv"])
    P.add("vector", lambda e: e.reciprocal(out=sb["rstd"][0:1, :], in_=sb["srv"][0:1, :]),
          reads=["srv"], writes=["rstd"])
    P.add("vector", lambda e: e.tensor_mul(out=sb["nmr"][0:1, :], in0=sb["mean"][0:1, :], in1=sb["rstd"][0:1, :]),
          reads=[("mean", 0), ("mean", 1), "rstd"], writes=["nmr0"])
    P.add("vector", lambda e: e.tensor_scalar_mul(out=sb["nmr"][0:1, :], in0=sb["nmr"][0:1, :], scalar1=-1.0),
          reads=["nmr0"], writes=["nmr"])
    for h in range(2):
        sl = slice(512 * h, 512 * (h + 1))
        P.add("tensor", lambda e, h=h, sl=sl: e.matmul(
            ps[h].ap()[:, :], sb["ones_row"][0:1, :], sb["rstd"][0:1, sl], start=True, stop=True),
            reads=["rstd", "ones_row", ("ps", h)], writes=[("ps", h)])
        P.add("tensor", lambda e, h=h, sl=sl: e.matmul(
            ps[2 + h].ap()[:, :], sb["ones_row"][0:1, :], sb["nmr"][0:1, sl], start=True, stop=True),
            reads=["nmr", "ones_row", ("ps", 2 + h)], writes=[("ps", 2 + h)])
        P.add("scalar", lambda e, h=h, sl=sl: e.copy(out=sb["R"][:, sl], in_=ps[h].ap()[:, :]),
              reads=[("ps", h)], writes=[("R", h), ("ps", h)])
        P.add("scalar", lambda e, h=h, sl=sl: e.copy(out=sb["A"][:, sl], in_=ps[2 + h].ap()[:, :]),
              reads=[("ps", 2 + h)], writes=[("A", h), ("ps", 2 + h)])
    Rk = [("R", 0), ("R", 1)]
    Ak = [("A", 0), ("A", 1)]
    for dt in range(32):
        xb = ("x", dt % 2)
        P.dma("sync", sb["xA"][dt % 2][:], x_in[dt], writes=[xb], stream=f"xA{dt % 2}")
        P.add("vector", lambda e, dt=dt: e.tensor_mul(
            out=sb["t"][dt % 2][:], in0=sb["xA"][dt % 2][:], in1=sb["R"][:]),
            reads=[xb] + Rk, writes=[("t", dt % 2)])
        P.add("vector", lambda e, dt=dt: e.tensor_add(
            out=sb["t"][dt % 2][:], in0=sb["t"][dt % 2][:], in1=sb["A"][:]),
            reads=[("t", dt % 2)] + Ak, writes=[("t", dt % 2)])
        P.add("scalar", lambda e, dt=dt: e.copy(
            out=sb["ln"][:, 1024 * dt:1024 * (dt + 1)], in_=sb["t"][dt % 2][:]),
            reads=[("t", dt % 2)], writes=[("ln", dt)])
        if compute_e:
            P.add("scalar", lambda e, dt=dt: e.square(out=sb["xsq"][dt % 2][:], in_=sb["t"][dt % 2][:]),
                  reads=[("t", dt % 2)], writes=[("xsq", dt % 2)])
            for h in range(2):
                P.add("tensor", lambda e, dt=dt, h=h: e.matmul(
                    ps[4 + h].ap()[0:1, :], sb["ones_col"][:], sb["xsq"][dt % 2][:, 512 * h:512 * (h + 1)],
                    start=(dt == 0), stop=(dt == 31)),
                    reads=[("xsq", dt % 2), "ones_col"], writes=[("ps", 4 + h)])
        if ln_out_hbm is not None:
            P.dma("scalar", ln_out_hbm[dt], sb["ln"][:, 1024 * dt:1024 * (dt + 1)],
                  reads=[("ln", dt)], writes=[("lnout", dt)], stream="lnst")
    if compute_e:
        for h in range(2):
            sl = slice(512 * h, 512 * (h + 1))
            P.add("scalar", lambda e, h=h, sl=sl: e.activation(
                out=sb["srv"][0:1, sl], in_=ps[4 + h].ap()[0:1, :], func=AF.Sqrt, bias=0.0, scale=1.0),
                reads=[("ps", 4 + h)], writes=["srv", ("e", h), ("ps", 4 + h)])
        P.add("vector", lambda e: e.tensor_scalar(
            out=sb["m01"][0:1, :], in0=sb["srv"][0:1, :], scalar1=64.0, scalar2=None, op0=ALU.is_gt),
            reads=[("e", 0), ("e", 1)], writes=["m01"])
        for h in range(2):
            sl = slice(512 * h, 512 * (h + 1))
            P.add("tensor", lambda e, h=h, sl=sl: e.matmul(
                ps[4 + h].ap()[:, :], sb["ones_row"][0:1, :], sb["m01"][0:1, sl], start=True, stop=True),
                reads=["m01", "ones_row", ("ps", 4 + h)], writes=[("ps", 4 + h)])
            P.add("scalar", lambda e, h=h, sl=sl: e.copy(out=sb["Mb"][:, sl], in_=ps[4 + h].ap()[:, :]),
                  reads=[("ps", 4 + h)], writes=[("Mb", h), ("ps", 4 + h)])


def alloc_ln_bufs(nc, sb):
    sb["ones_col"] = nc.alloc_sbuf_tensor("s_ones_col", [128, 1], F32)
    sb["ones_row"] = nc.alloc_sbuf_tensor("s_ones_row", [1, 128], F32)
    sb["xA"] = [nc.alloc_sbuf_tensor(f"s_xA{i}", [128, 1024], F32) for i in range(2)]
    sb["xsq"] = [nc.alloc_sbuf_tensor(f"s_xsq{i}", [128, 1024], F32) for i in range(2)]
    sb["t"] = [nc.alloc_sbuf_tensor(f"s_t{i}", [128, 1024], F32) for i in range(2)]
    sb["ln"] = nc.alloc_sbuf_tensor("s_ln", [128, 32 * 1024], BF16)
    for nm in ("mean", "msq", "m2", "var", "srv", "rstd", "nmr", "m01"):
        sb[nm] = nc.alloc_sbuf_tensor("s_" + nm, [1, 1024], F32)
    sb["R"] = nc.alloc_sbuf_tensor("s_R", [128, 1024], F32)
    sb["A"] = nc.alloc_sbuf_tensor("s_A", [128, 1024], F32)
    sb["Mb"] = nc.alloc_sbuf_tensor("s_Mb", [128, 1024], F32)


def ln_load_consts(P, ap, sb):
    P.dma("sync", sb["ones_col"][:], ap["ones_col"], writes=["ones_col"], stream="init")
    P.dma("sync", sb["ones_row"][:], ap["ones_row"], writes=["ones_row"], stream="init")


def emit_ffn(P, nc, ap, sb, ps, kind):
    dual = (kind == "l0")
    P.dma("sync", sb["bias"][:], ap["bias"], writes=["bias"], stream="init")
    emit_ln(P, nc, ap, sb, ps, compute_e=dual)
    mats = ("wl", "wh") if dual else ("wf",)
    for ot in range(32):
        par = ot % 2
        for mi, m in enumerate(mats):
            P.dma("sync" if mi == 0 else "scalar",
                  sb[m][par][:].rearrange("p (dt o) -> p dt o", o=128),
                  ap[m + "_t"][ot].rearrange("(dt p) o -> p dt o", p=128),
                  reads=tuple(ap.get("wdep", ())),
                  writes=[(m, par)], stream=f"{m}{par}")
        P.dma("gpsimd", sb["xo"][par][:], ap["x_in"][ot], writes=[("xo", par)], stream=f"xo{par}")
        for mi, m in enumerate(mats):
            for h in range(2):
                bi = par * 4 + mi * 2 + h if dual else par * 4 + h
                bank = ps[bi]
                wbuf = sb[m][par]
                for dt in range(32):
                    P.add("tensor", lambda e, wbuf=wbuf, dt=dt, h=h, bank=bank: e.matmul(
                        bank.ap()[:, 0:512],
                        wbuf[:, 128 * dt:128 * (dt + 1)],
                        sb["ln"][:, 1024 * dt + 512 * h:1024 * dt + 512 * h + 512],
                        start=(dt == 0), stop=(dt == 31)),
                        reads=[(m, par), ("ln", dt)], writes=[("ps", bi)])
        for h in range(2):
            sl = slice(512 * h, 512 * (h + 1))
            bias_col = sb["bias"][:, ot:ot + 1]
            if dual:
                bil, bih = par * 4 + h, par * 4 + 2 + h
                bl, bh = ps[bil], ps[bih]
                P.add("scalar", lambda e, bl=bl, sl=sl, bc=bias_col: e.activation(
                    out=sb["a"][:, sl], in_=bl.ap()[:, 0:512], func=AF.Identity, bias=bc, scale=1.0),
                    reads=[("ps", bil), "bias"], writes=[("a", h), ("ps", bil)])
                P.add("scalar", lambda e, bh=bh, sl=sl, bc=bias_col: e.activation(
                    out=sb["hh"][:, sl], in_=bh.ap()[:, 0:512], func=AF.Identity, bias=bc, scale=1.0),
                    reads=[("ps", bih), "bias"], writes=[("hh", h), ("ps", bih)])
                P.add("vector", lambda e, sl=sl: e.tensor_sub(
                    out=sb["hh"][:, sl], in0=sb["hh"][:, sl], in1=sb["a"][:, sl]),
                    reads=[("a", h), ("hh", h)], writes=[("hh", h)])
                P.add("vector", lambda e, sl=sl: e.tensor_mul(
                    out=sb["hh"][:, sl], in0=sb["hh"][:, sl], in1=sb["Mb"][:, sl]),
                    reads=[("hh", h), ("Mb", 0), ("Mb", 1)], writes=[("hh", h)])
                P.add("vector", lambda e, sl=sl: e.tensor_add(
                    out=sb["a"][:, sl], in0=sb["a"][:, sl], in1=sb["hh"][:, sl]),
                    reads=[("a", h), ("hh", h)], writes=[("a", h)])
            else:
                bi = par * 4 + h
                bank = ps[bi]
                P.add("scalar", lambda e, bank=bank, sl=sl, bc=bias_col: e.activation(
                    out=sb["a"][:, sl], in_=bank.ap()[:, 0:512], func=AF.Identity, bias=bc, scale=1.0),
                    reads=[("ps", bi), "bias"], writes=[("a", h), ("ps", bi)])
            P.add("vector", lambda e, sl=sl, par=par: e.scalar_tensor_tensor(
                out=sb["xo"][par][:, sl], in0=sb["a"][:, sl], scalar=0.5, in1=sb["xo"][par][:, sl],
                op0=ALU.mult, op1=ALU.add),
                reads=[("a", h), ("xo", par)], writes=[("xo", par)])
        P.dma("gpsimd", ap["x_out"][ot], sb["xo"][par][:],
              reads=[("xo", par)], writes=[("xout", ot)], stream=f"xst{par}")


def alloc_ffn_bufs(nc, sb, dual):
    mats = ("wl", "wh") if dual else ("wf",)
    for m in mats:
        sb[m] = [nc.alloc_sbuf_tensor(f"s_{m}{i}", [128, 4096], BF16) for i in range(2)]
    sb["bias"] = nc.alloc_sbuf_tensor("s_bias", [128, 32], F32)
    sb["a"] = nc.alloc_sbuf_tensor("s_a", [128, 1024], F32)
    sb["hh"] = nc.alloc_sbuf_tensor("s_hh", [128, 1024], F32)
    sb["xo"] = [nc.alloc_sbuf_tensor(f"s_xo{i}", [128, 1024], F32) for i in range(2)]


def build_ffn(kind):
    nc = bass.Bass("TRN2", target_bir_lowering=False)
    dual = (kind == "l0")
    ap = {}
    ap["x_in"] = nc.dram_tensor("x_in", [32, 128, 1024], F32, kind="ExternalInput").ap()
    ap["ones_col"] = nc.dram_tensor("ones_col", [128, 1], F32, kind="ExternalInput").ap()
    ap["ones_row"] = nc.dram_tensor("ones_row", [1, 128], F32, kind="ExternalInput").ap()
    ap["bias"] = nc.dram_tensor("bias", [128, 32], F32, kind="ExternalInput").ap()
    if dual:
        ap["wl_t"] = nc.dram_tensor("wl_t", [32, 4096, 128], BF16, kind="ExternalInput").ap()
        ap["wh_t"] = nc.dram_tensor("wh_t", [32, 4096, 128], BF16, kind="ExternalInput").ap()
    else:
        ap["wf_t"] = nc.dram_tensor("wf_t", [32, 4096, 128], BF16, kind="ExternalInput").ap()
    ap["x_out"] = nc.dram_tensor("x_out", [32, 128, 1024], F32, kind="ExternalOutput").ap()

    sb = {}
    alloc_ln_bufs(nc, sb)
    alloc_ffn_bufs(nc, sb, dual)
    ps = [nc.alloc_psum_tensor(f"ps{i}", [128, 512], F32) for i in range(8)]
    P = Prog(nc)
    with nc.Block() as block:
        ln_load_consts(P, ap, sb)
        emit_ffn(P, nc, ap, sb, ps, kind)
        P.emit(block)
    return nc


CAST_SPECS = [
    ("woT0_sl", 4096, 512, "wo0_sh"),
    ("woT1_sl", 4096, 512, "wo1_sh"),
    ("kw1T_sl", 2048, 256, "kw1_sh"),
    ("vw1T_sl", 2048, 256, "vw1_sh"),
    ("kw2T", 2048, 128, "kw2_bf"),
    ("vw2T", 2048, 128, "vw2_bf"),
    ("wqkvT_sl", 4096, 1536, "wqkv_bf"),
    ("wqT_sl", 4096, 1024, "wq_bf"),
    ("wkT_sl", 4096, 256, "wk_bf"),
    ("wvT_sl", 4096, 256, "wv_bf"),
]


def _emit_cast(P, sb, in_ap, out_ap, rows, cols, oname, base_otl=0):
    nr = rows // 128
    for r in range(nr):
        par = r % 2
        P.dma("sync", sb["cf"][par][:, 0:cols], in_ap[128 * r:128 * (r + 1), :],
              writes=[("cf", par)], stream=f"cf{par}")
        P.add("vector", lambda e, par=par, cols=cols: e.tensor_copy(
            out=sb["cb"][par][:, 0:cols], in_=sb["cf"][par][:, 0:cols]),
            reads=[("cf", par)], writes=[("cb", par)])
        for otl in range(cols // 128):
            P.dma("gpsimd", out_ap[base_otl + otl, 128 * r:128 * (r + 1), :],
                  sb["cb"][par][:, 128 * otl:128 * (otl + 1)],
                  reads=[("cb", par)], writes=[(oname, r, base_otl + otl)],
                  stream=f"cst{par}")


def emit_weight_prep(P, nc, ap, sb, ps, cast_specs=CAST_SPECS, extra_casts=()):
    for (iname, rows, cols, oname) in cast_specs:
        _emit_cast(P, sb, ap[iname], ap[oname], rows, cols, oname)
    for (in_ap, out_ap, rows, cols, base_otl, oname) in extra_casts:
        _emit_cast(P, sb, in_ap, out_ap, rows, cols, oname, base_otl=base_otl)
    for r in range(32):
        par = r % 2
        P.dma("sync", sb["cf"][par][:, 0:24], ap["wgateT_sl"][128 * r:128 * (r + 1), :],
              writes=[("cf", par)], stream=f"cf{par}")
        P.add("vector", lambda e, par=par: e.tensor_copy(
            out=sb["cb"][par][:, 0:24], in_=sb["cf"][par][:, 0:24]),
            reads=[("cf", par)], writes=[("cb", par)])
        P.dma("gpsimd", ap["wgate_bf"][128 * r:128 * (r + 1), :], sb["cb"][par][:, 0:24],
              reads=[("cb", par)], writes=[("wgate_bf", r)], stream=f"cst{par}")
    P.dma("sync", sb["aer"][:].rearrange("p (t d) -> p t d", d=4096),
          ap["A_er"].rearrange("(t p) d -> p t d", p=128), writes=["aer_f"], stream="init2")
    P.dma("sync", sb["bt"][:].rearrange("p (t o) -> p t o", o=512),
          ap["Bt_sl"].rearrange("(t p) o -> p t o", p=128), writes=["bt_f"], stream="init2")
    for t in range(2):
        P.dma("sync", sb["probs"][t][:], ap["probs_rep"][t],
              writes=[("probs", t)], stream="init2")
    P.add("vector", lambda e: e.tensor_copy(out=sb["aer_bf"][:], in_=sb["aer"][:]),
          reads=["aer_f"], writes=["aer_bf"])
    for t in range(2):
        P.add("vector", lambda e, t=t: e.tensor_scalar(
            out=sb["bt_bf"][:, 512 * t:512 * (t + 1)], in0=sb["bt"][:, 512 * t:512 * (t + 1)],
            scalar1=sb["probs"][t][:, 0:1], scalar2=None, op0=ALU.mult),
            reads=["bt_f", ("probs", t)], writes=["bt_bf"])
    for dt in range(32):
        par = dt % 2
        P.dma("scalar", sb["wt"][par][:], ap["WT_sl"][128 * dt:128 * (dt + 1), :],
              writes=[("wt", par)], stream=f"wt{par}")
        for otl in range(4):
            bi = 4 + 2 * par + (otl % 2)
            P.add("tensor", lambda e, dt=dt, otl=otl, bi=bi: e.matmul(
                ps[bi].ap()[:, 0:128], sb["aer_bf"][:, 4096 * 0 + 128 * dt:4096 * 0 + 128 * dt + 128],
                sb["bt_bf"][:, 512 * 0 + 128 * otl:512 * 0 + 128 * otl + 128],
                start=True, stop=False),
                reads=["aer_bf", "bt_bf"], writes=[("ps", bi)])
            P.add("tensor", lambda e, dt=dt, otl=otl, bi=bi: e.matmul(
                ps[bi].ap()[:, 0:128], sb["aer_bf"][:, 4096 * 1 + 128 * dt:4096 * 1 + 128 * dt + 128],
                sb["bt_bf"][:, 512 * 1 + 128 * otl:512 * 1 + 128 * otl + 128],
                start=False, stop=True),
                reads=["aer_bf", "bt_bf"], writes=[("ps", bi)])
            P.add("vector", lambda e, par=par, otl=otl, bi=bi: e.tensor_add(
                out=sb["weff"][par][:, 128 * otl:128 * (otl + 1)],
                in0=ps[bi].ap()[:, 0:128],
                in1=sb["wt"][par][:, 128 * otl:128 * (otl + 1)]),
                reads=[("ps", bi), ("wt", par)], writes=[("weff", par, otl), ("ps", bi)])
            P.dma("gpsimd", ap["weff_sh"][otl, 128 * dt:128 * (dt + 1), :],
                  sb["weff"][par][:, 128 * otl:128 * (otl + 1)],
                  reads=[("weff", par, otl)], writes=[("weffo", dt, otl)], stream=f"wfs{par}")


def alloc_prep_bufs(nc, sb):
    sb["qw"] = nc.alloc_sbuf_tensor("s_qw", [128, 8 * 128], I32)
    sb["lut4"] = nc.alloc_sbuf_tensor("s_lut4", [128, 64], F32)
    sb["lut8"] = nc.alloc_sbuf_tensor("s_lut8", [128, 1024], F32)
    sb["bitmask"] = nc.alloc_sbuf_tensor("s_bitmask", [128, 4096], I32)
    sb["ident"] = nc.alloc_sbuf_tensor("s_ident", [128, 128], BF16)
    sb["tmp32"] = nc.alloc_sbuf_tensor("s_tmp32", [128, 4096], I32)
    sb["bit"] = nc.alloc_sbuf_tensor("s_bit", [128, 4096], BF16)
    sb["idx4"] = nc.alloc_sbuf_tensor("s_idx4", [128, 4096], BF16)
    sb["idx8"] = nc.alloc_sbuf_tensor("s_idx8", [128, 4096], BF16)
    sb["wl"] = nc.alloc_sbuf_tensor("s_wl", [128, 4096], BF16)
    sb["wh"] = nc.alloc_sbuf_tensor("s_wh", [128, 4096], BF16)
    sb["m"] = nc.alloc_sbuf_tensor("s_m", [128, 4096], BF16)
    sb["stagwl"] = nc.alloc_sbuf_tensor("s_stagwl", [128, 4096], BF16)
    sb["stagwh"] = nc.alloc_sbuf_tensor("s_stagwh", [128, 4096], BF16)
    sb["cf"] = [nc.alloc_sbuf_tensor(f"s_cf{i}", [128, 1536], F32) for i in range(2)]
    sb["cb"] = [nc.alloc_sbuf_tensor(f"s_cb{i}", [128, 1536], BF16) for i in range(2)]
    sb["aer"] = nc.alloc_sbuf_tensor("s_aer", [128, 2 * 4096], F32)
    sb["aer_bf"] = nc.alloc_sbuf_tensor("s_aer_bf", [128, 2 * 4096], BF16)
    sb["bt"] = nc.alloc_sbuf_tensor("s_bt", [128, 2 * 512], F32)
    sb["bt_bf"] = nc.alloc_sbuf_tensor("s_bt_bf", [128, 2 * 512], BF16)
    sb["probs"] = [nc.alloc_sbuf_tensor(f"s_probs{t}", [128, 1], F32) for t in range(2)]
    sb["wt"] = [nc.alloc_sbuf_tensor(f"s_wt{i}", [128, 512], F32) for i in range(2)]
    sb["weff"] = [nc.alloc_sbuf_tensor(f"s_weff{i}", [128, 512], BF16) for i in range(2)]


def build_p0full():
    nc = bass.Bass("TRN2", target_bir_lowering=False)
    ap = {}
    ap["qw"] = nc.dram_tensor("qw", [8, 512, 128], I32, kind="ExternalInput").ap()
    ap["lut4"] = nc.dram_tensor("lut4", [512, 16], F32, kind="ExternalInput").ap()
    ap["lut8"] = nc.dram_tensor("lut8", [512, 256], F32, kind="ExternalInput").ap()
    ap["bitmask"] = nc.dram_tensor("bitmask", [128, 4096], I32, kind="ExternalInput").ap()
    ap["ident_bf"] = nc.dram_tensor("ident_bf", [128, 128], BF16, kind="ExternalInput").ap()
    ap["wl_sh"] = nc.dram_tensor("wl_sh", [4, 4096, 128], BF16, kind="ExternalOutput").ap()
    ap["wh_sh"] = nc.dram_tensor("wh_sh", [4, 4096, 128], BF16, kind="ExternalOutput").ap()
    for (iname, rows, cols, oname) in CAST_SPECS:
        ap[iname] = nc.dram_tensor(iname, [rows, cols], F32, kind="ExternalInput").ap()
        ap[oname] = nc.dram_tensor(oname, [cols // 128, rows, 128], BF16, kind="ExternalOutput").ap()
    ap["wgateT_sl"] = nc.dram_tensor("wgateT_sl", [4096, 24], F32, kind="ExternalInput").ap()
    ap["wgate_bf"] = nc.dram_tensor("wgate_bf", [4096, 24], BF16, kind="ExternalOutput").ap()
    ap["A_er"] = nc.dram_tensor("A_er", [256, 4096], F32, kind="ExternalInput").ap()
    ap["Bt_sl"] = nc.dram_tensor("Bt_sl", [256, 512], F32, kind="ExternalInput").ap()
    ap["probs_rep"] = nc.dram_tensor("probs_rep", [2, 128, 1], F32, kind="ExternalInput").ap()
    ap["WT_sl"] = nc.dram_tensor("WT_sl", [4096, 512], F32, kind="ExternalInput").ap()
    ap["weff_sh"] = nc.dram_tensor("weff_sh", [4, 4096, 128], BF16, kind="ExternalOutput").ap()

    sb = {}
    alloc_prep_bufs(nc, sb)
    ps = [nc.alloc_psum_tensor(f"ps{i}", [128, 512], F32) for i in range(8)]
    P = Prog(nc)
    with nc.Block() as block:
        emit_dequant(P, nc, ap, sb, ps)
        emit_weight_prep(P, nc, ap, sb, ps)
        P.emit(block)
    return nc


def emit_wo(P, nc, ap, sb, ps):
    for dt in range(32):
        P.dma("sync", sb["ln"][:, 1024 * dt:1024 * (dt + 1)], ap["attn_in"][dt],
              writes=[("ln", dt)], stream=f"at{dt % 2}")
    for ot in range(32):
        par = ot % 2
        P.dma("scalar", sb["wf"][par][:].rearrange("p (dt o) -> p dt o", o=128),
              ap["wo_t"][ot].rearrange("(dt p) o -> p dt o", p=128),
              writes=[("wf", par)], stream=f"wf{par}")
        P.dma("gpsimd", sb["xo"][par][:], ap["x_in"][ot], writes=[("xo", par)], stream=f"xo{par}")
        for h in range(2):
            bi = par * 4 + h
            bank = ps[bi]
            wbuf = sb["wf"][par]
            for dt in range(32):
                P.add("tensor", lambda e, wbuf=wbuf, dt=dt, h=h, bank=bank: e.matmul(
                    bank.ap()[:, 0:512],
                    wbuf[:, 128 * dt:128 * (dt + 1)],
                    sb["ln"][:, 1024 * dt + 512 * h:1024 * dt + 512 * h + 512],
                    start=(dt == 0), stop=(dt == 31)),
                    reads=[("wf", par), ("ln", dt)], writes=[("ps", bi)])
            sl = slice(512 * h, 512 * (h + 1))
            P.add("vector", lambda e, sl=sl, par=par, bi=bi: e.tensor_add(
                out=sb["xo"][par][:, sl], in0=ps[bi].ap()[:, 0:512], in1=sb["xo"][par][:, sl]),
                reads=[("ps", bi), ("xo", par)], writes=[("xo", par), ("ps", bi)])
        P.dma("gpsimd", ap["x_out"][ot], sb["xo"][par][:],
              reads=[("xo", par)], writes=[("xout", ot)], stream=f"xst{par}")


def build_wo():
    nc = bass.Bass("TRN2", target_bir_lowering=False)
    ap = {}
    ap["x_in"] = nc.dram_tensor("x_in", [32, 128, 1024], F32, kind="ExternalInput").ap()
    ap["attn_in"] = nc.dram_tensor("attn_in", [32, 128, 1024], BF16, kind="ExternalInput").ap()
    ap["wo_t"] = nc.dram_tensor("wo_t", [32, 4096, 128], BF16, kind="ExternalInput").ap()
    ap["x_out"] = nc.dram_tensor("x_out", [32, 128, 1024], F32, kind="ExternalOutput").ap()
    sb = {}
    sb["ln"] = nc.alloc_sbuf_tensor("s_ln", [128, 32 * 1024], BF16)
    sb["wf"] = [nc.alloc_sbuf_tensor(f"s_wf{i}", [128, 4096], BF16) for i in range(2)]
    sb["xo"] = [nc.alloc_sbuf_tensor(f"s_xo{i}", [128, 1024], F32) for i in range(2)]
    ps = [nc.alloc_psum_tensor(f"ps{i}", [128, 512], F32) for i in range(8)]
    P = Prog(nc)
    with nc.Block() as block:
        emit_wo(P, nc, ap, sb, ps)
        P.emit(block)
    return nc


def build_lnout():
    nc = bass.Bass("TRN2", target_bir_lowering=False)
    ap = {}
    ap["x_in"] = nc.dram_tensor("x_in", [32, 128, 1024], F32, kind="ExternalInput").ap()
    ap["ones_col"] = nc.dram_tensor("ones_col", [128, 1], F32, kind="ExternalInput").ap()
    ap["ones_row"] = nc.dram_tensor("ones_row", [1, 128], F32, kind="ExternalInput").ap()
    ap["ln_out"] = nc.dram_tensor("ln_out", [32, 128, 1024], BF16, kind="ExternalOutput").ap()
    sb = {}
    alloc_ln_bufs(nc, sb)
    ps = [nc.alloc_psum_tensor(f"ps{i}", [128, 512], F32) for i in range(6)]
    P = Prog(nc)
    with nc.Block() as block:
        ln_load_consts(P, ap, sb)
        emit_ln(P, nc, ap, sb, ps, compute_e=False, ln_out_hbm=ap["ln_out"])
        P.emit(block)
    return nc


RSCALE = 1.0 / SCALE
PEN = 904.0
MSKB = PEN * SCALE
CAUS = -30000.0 * RSCALE


def emit_attn1(P, nc, ap, sb, ps):
    for nm in ("C", "S", "protT", "identF", "Ddiag4", "Dprev4"):
        P.dma("sync", sb[nm][:], ap[nm], writes=[nm], stream="init")
    P.dma("sync", sb["DcausQK"][:].rearrange("p (v f) -> p v f", f=512),
          ap["DcausQK"].rearrange("v p f -> p v f"), writes=["DcausQK"], stream="init")
    P.dma("sync", sb["recent4"][:], ap["recent4"], writes=["recent4"], stream="init")
    P.dma("sync", sb["E80"][:].rearrange("p (v f) -> p v f", f=128),
          ap["E80"].rearrange("v p f -> p v f"), writes=["E80"], stream="init")
    P.dma("sync", sb["E01"][:].rearrange("p (v f) -> p v f", f=128),
          ap["E01"].rearrange("v p f -> p v f"), writes=["E01"], stream="init")
    P.dma("sync", sb["init01"][:], ap["init01"], writes=["init01"], stream="init")

    RQK = [("rq", oi, tt) for oi in range(4) for tt in range(8)] + [("rk", tt) for tt in range(8)]
    for pair in range(PAIRS):
        for j in range(4):
            P.dma("sync", sb["wq"][:, 4096 * j:4096 * (j + 1)].rearrange("p (dt o) -> p dt o", o=128),
                  ap["wq_bf"][4 * pair + j].rearrange("(dt p) o -> p dt o", p=128),
                  writes=[("wq", j)], stream="wload")
        P.dma("sync", sb["wk"][:].rearrange("p (dt o) -> p dt o", o=128),
              ap["wk_bf"][pair].rearrange("(dt p) o -> p dt o", p=128),
              writes=["wk"], stream="wload")
        P.dma("sync", sb["wv"][:].rearrange("p (dt o) -> p dt o", o=128),
              ap["wv_bf"][pair].rearrange("(dt p) o -> p dt o", p=128),
              writes=["wv"], stream="wload")
        for tt in range(8):
            tpar = 0
            for dt in range(32):
                P.dma("gpsimd", sb["lnt"][tpar][:, 512 * dt:512 * (dt + 1)],
                      ap["lnf"](dt, 512 * tt, 512 * (tt + 1)),
                      reads=tuple(ap.get("lnf_dep", ())),
                      writes=[("lnt", tpar, dt)], stream=f"lnt{tpar}")
            for oi in range(6):
                bi = (tt % 2) * 4 + (oi % 4) if False else oi
                bank = ps[oi]
                if oi < 4:
                    wsl = sb["wq"][:, 4096 * oi:4096 * oi + 4096]
                    wkey = ("wq", oi)
                elif oi == 4:
                    wsl, wkey = sb["wk"][:], "wk"
                else:
                    wsl, wkey = sb["wv"][:], "wv"
                for dt in range(32):
                    P.add("tensor", lambda e, wsl=wsl, dt=dt, bank=bank, tpar=tpar: e.matmul(
                        bank.ap()[:, 0:512],
                        wsl[:, 128 * dt:128 * (dt + 1)],
                        sb["lnt"][tpar][:, 512 * dt:512 * (dt + 1)],
                        start=(dt == 0), stop=(dt == 31)),
                        reads=[wkey, ("lnt", tpar, dt)], writes=[("ps", oi)])
                if oi == 5:
                    P.add("scalar", lambda e, bank=bank, tt=tt: e.copy(
                        out=sb["v"][:, 512 * tt:512 * (tt + 1)], in_=bank.ap()[:, 0:512]),
                        reads=[("ps", oi)], writes=[("v", tt), ("ps", oi)])
                else:
                    P.add("scalar", lambda e, bank=bank: e.copy(
                        out=sb["qf"][:], in_=bank.ap()[:, 0:512]),
                        reads=[("ps", oi)], writes=["qf", ("ps", oi)])
                    P.add("tensor", lambda e: e.matmul(
                        ps[6].ap()[:, 0:512], sb["protT"][:], sb["qf"][:],
                        start=True, stop=True),
                        reads=["qf", "protT", ("ps", 6)], writes=[("ps", 6)])
                    P.add("vector", lambda e, tt=tt: e.tensor_mul(
                        out=sb["t1"][:], in0=ps[6].ap()[:, 0:512],
                        in1=sb["S"][:, 512 * tt:512 * (tt + 1)]),
                        reads=[("ps", 6), "S"], writes=["t1", ("ps", 6)])
                    P.add("vector", lambda e, tt=tt: e.tensor_mul(
                        out=sb["qf"][:], in0=sb["qf"][:],
                        in1=sb["C"][:, 512 * tt:512 * (tt + 1)]),
                        reads=["qf", "C"], writes=["qf"])
                    dstkey = ("rq", oi, tt) if oi < 4 else ("rk", tt)
                    dst = (sb["q_all"][:, 4096 * oi + 512 * tt:4096 * oi + 512 * (tt + 1)]
                           if oi < 4 else sb["rk"][:, 512 * tt:512 * (tt + 1)])
                    P.add("vector", lambda e, dst=dst: e.tensor_add(
                        out=dst, in0=sb["qf"][:], in1=sb["t1"][:]),
                        reads=["qf", "t1"], writes=[dstkey])
        for kt in range(32):
            P.add("vector", lambda e, kt=kt: e.memset(sb["vt"][:, 256 * kt + 128:256 * kt + 129], 1.0),
                  reads=[], writes=[("vt1", kt)])
            P.add("sync", lambda e, kt=kt: e.dma_start_transpose(
                sb["vt"][:, 256 * kt:256 * kt + 128], sb["v"][:, 128 * kt:128 * (kt + 1)]),
                reads=[("v", kt // 4)], writes=[("vt", kt)], kind="d", stream="vtp")
        for i in range(32):
            ng = i // 4 + 1
            nkt = i + 1
            P.add("vector", lambda e: e.memset(sb["cm"][:], -3.0e38), reads=["cm"], writes=["cm"])
            for hh in range(4):
                for g in range(ng):
                    bi = g % 2
                    bank = ps[bi]
                    P.add("tensor", lambda e, hh=hh, g=g, i=i, bank=bank: e.matmul(
                        bank.ap()[:, 0:512],
                        sb["q_all"][:, 4096 * hh + 128 * i:4096 * hh + 128 * (i + 1)],
                        sb["rk"][:, 512 * g:512 * (g + 1)],
                        start=True, stop=(g != i // 4)),
                        reads=RQK + [("ps", bi)], writes=[("ps", bi)])
                    if g == i // 4:
                        P.add("tensor", lambda e, i=i, bank=bank: e.matmul(
                            bank.ap()[:, 0:512], sb["identF"][:],
                            sb["DcausQK"][:, 512 * (i % 4):512 * (i % 4 + 1)],
                            start=False, stop=True),
                            reads=["identF", "DcausQK", ("ps", bi)], writes=[("ps", bi)])
                    P.add("vector", lambda e, g=g, hh=hh, bank=bank: e.reduce_max(
                        out=sb["cm"][:, 32 * g:32 * (g + 1)],
                        in_=bank.ap()[:, 0:512].rearrange("p (s c) -> p s c", c=16),
                        axis=AX.X),
                        reads=[("ps", bi)], writes=[("cm",), ("ps", bi)])
                P.add("vector", lambda e: e.max(out=sb["m8"][:], in_=sb["cm"][:]),
                      reads=[("cm",)], writes=["m8"])
                P.dma("gpsimd", sb["t8b"][:], sb["m8"][:, 7:8],
                      reads=["m8"], writes=["t8b"], stream="t8cp")
                P.add("vector", lambda e: e.tensor_scalar(
                    out=sb["cok"][:], in0=sb["cm"][:], scalar1=sb["t8b"][:, 0:1], scalar2=None,
                    op0=ALU.is_ge),
                    reads=[("cm",), "t8b"], writes=["cok"])
                P.add("vector", lambda e: e.tensor_max(
                    out=sb["cok"][:], in0=sb["cok"][:], in1=sb["init01"][:]),
                    reads=["cok", "init01"], writes=["cok"])
                for gI in range(2):
                    P.add("scalar", lambda e, gI=gI, hh=hh: e.dma_start_transpose(
                        sb["cokT"][:, 512 * gI + 128 * hh:512 * gI + 128 * (hh + 1)],
                        sb["cok"][:, 128 * gI:128 * (gI + 1)]),
                        reads=["cok"], writes=[("cokT", gI, hh)], kind="d", stream="ctp")
                if hh < 3:
                    P.add("vector", lambda e: e.memset(sb["cm"][:], -3.0e38),
                          reads=[("cm",)], writes=[("cm",)])
            for kt in range(nkt):
                bi = kt % 2
                bank = ps[bi]
                gI = kt // 16
                cokT_keys = [("cokT", gI, hh) for hh in range(4)]
                P.add("tensor", lambda e, kt=kt, i=i, bank=bank: e.matmul(
                    bank.ap()[:, 0:512],
                    sb["rk"][:, 128 * kt:128 * (kt + 1)],
                    sb["q_all"][:].rearrange("p (h n) -> p h n", n=4096)[:, :, 128 * i:128 * (i + 1)],
                    start=True, stop=False),
                    reads=RQK + [("ps", bi)],
                    writes=[("ps", bi)])
                is_diag = (kt == i)
                is_prev = (kt == i - 1)
                if not is_diag:
                    P.add("tensor", lambda e, kt=kt, gI=gI, bank=bank, st=(not is_prev): e.matmul(
                        bank.ap()[:, 0:512],
                        sb["E80"][:, 128 * (kt % 16):128 * (kt % 16 + 1)],
                        sb["cokT"][:, 512 * gI:512 * (gI + 1)],
                        start=False, stop=st),
                        reads=["E80", ("ps", bi)] + cokT_keys, writes=[("ps", bi)])
                if is_prev:
                    P.add("tensor", lambda e, bank=bank: e.matmul(
                        bank.ap()[:, 0:512], sb["identF"][:], sb["Dprev4"][:],
                        start=False, stop=True),
                        reads=["identF", "Dprev4", ("ps", bi)], writes=[("ps", bi)])
                    P.add("tensor", lambda e, kt=kt, gI=gI: e.matmul(
                        ps[2].ap()[:, 0:512],
                        sb["E01"][:, 128 * (kt % 16):128 * (kt % 16 + 1)],
                        sb["cokT"][:, 512 * gI:512 * (gI + 1)],
                        start=True, stop=True),
                        reads=["E01", ("ps", 2)] + cokT_keys, writes=[("ps", 2)])
                    P.add("scalar", lambda e: e.copy(out=sb["a01"][:], in_=ps[2].ap()[:, 0:512]),
                          reads=[("ps", 2)], writes=["a01", ("ps", 2)])
                    P.add("vector", lambda e: e.tensor_mul(
                        out=sb["a01"][:], in0=sb["a01"][:], in1=sb["recent4"][:]),
                        reads=["a01", "recent4"], writes=["a01"])
                if is_diag:
                    P.add("tensor", lambda e, bank=bank: e.matmul(
                        bank.ap()[:, 0:512], sb["identF"][:], sb["Ddiag4"][:],
                        start=False, stop=True),
                        reads=["identF", "Ddiag4", ("ps", bi)], writes=[("ps", bi)])
                ppar = kt % 2
                if is_prev:
                    P.add("vector", lambda e, bank=bank: e.scalar_tensor_tensor(
                        out=sb["sfix"][:], in0=sb["a01"][:], scalar=-PEN, in1=bank.ap()[:, 0:512],
                        op0=ALU.mult, op1=ALU.add),
                        reads=["a01", ("ps", bi)], writes=["sfix"])
                    P.add("scalar", lambda e, ppar=ppar: e.activation(
                        out=sb["pp"][ppar][:], in_=sb["sfix"][:], func=AF.Exp,
                        bias=sb["neg80"][:, 0:1], scale=SCALE),
                        reads=["sfix", "neg80"], writes=[("pp", ppar)])

                else:
                    bcol = sb["zero1"][:, 0:1] if is_diag else sb["neg80"][:, 0:1]
                    bkey = "zero1" if is_diag else "neg80"
                    P.add("scalar", lambda e, bank=bank, ppar=ppar, bcol=bcol: e.activation(
                        out=sb["pp"][ppar][:], in_=bank.ap()[:, 0:512], func=AF.Exp,
                        bias=bcol, scale=SCALE),
                        reads=[("ps", bi), bkey], writes=[("pp", ppar), ("ps", bi)])
                for hh in range(4):
                    pb = ps[3 + hh]
                    P.add("tensor", lambda e, kt=kt, hh=hh, pb=pb, ppar=ppar, i=i: e.matmul(
                        pb.ap()[:, 0:129],
                        sb["pp"][ppar][:, 128 * hh:128 * (hh + 1)],
                        sb["vt"][:, 256 * kt:256 * kt + 129],
                        start=(kt == 0), stop=(kt == i)),
                        reads=[("pp", ppar), ("vt", kt), ("vt1", kt), ("ps", 3 + hh)], writes=[("ps", 3 + hh)])
            for hh in range(4):
                pb = ps[3 + hh]
                P.add("scalar", lambda e, pb=pb: e.copy(
                    out=sb["pv"][:], in_=pb.ap()[:, 0:129]),
                    reads=[("ps", 3 + hh)], writes=["pv", ("ps", 3 + hh)])
                P.dma("gpsimd", sb["rs"][:], sb["pv"][:, 128:129],
                      reads=["pv"], writes=["rs"], stream="rscp")
                P.add("vector", lambda e: e.reciprocal(
                    out=sb["rs2"][:], in_=sb["rs"][:]),
                    reads=["rs"], writes=["rs2"])
                P.add("scalar", lambda e: e.activation(
                    out=sb["att"][:], in_=sb["pv"][:, 0:128], func=AF.Identity,
                    bias=0.0, scale=sb["rs2"][:, 0:1]),
                    reads=["pv", "rs2"], writes=["att"])
                P.add("sync", lambda e: e.dma_start_transpose(sb["attT"][:], sb["att"][:]),
                      reads=["att"], writes=["attT"], kind="d", stream="atp")
                P.dma("gpsimd", ap["attn_out"][4 * pair + hh, :, 128 * i:128 * (i + 1)],
                      sb["attT"][:], reads=["attT"], writes=[("ao", pair, hh, i)],
                      stream="aost")
    return


def alloc_attn1_bufs(nc, sb):
    sb["wq"] = nc.alloc_sbuf_tensor("s_wq", [128, 4 * 4096], BF16)
    sb["wk"] = nc.alloc_sbuf_tensor("s_wk", [128, 4096], BF16)
    sb["wv"] = nc.alloc_sbuf_tensor("s_wv", [128, 4096], BF16)
    sb["lnt"] = [nc.alloc_sbuf_tensor("s_lnt0", [128, 512 * 32], BF16)]
    sb["q_all"] = nc.alloc_sbuf_tensor("s_q_all", [128, 4 * 4096], BF16)
    sb["rk"] = nc.alloc_sbuf_tensor("s_rk", [128, 4096], BF16)
    sb["v"] = nc.alloc_sbuf_tensor("s_v", [128, 4096], BF16)
    sb["vt"] = nc.alloc_sbuf_tensor("s_vt", [128, 32 * 256], BF16)
    sb["qf"] = nc.alloc_sbuf_tensor("s_qf", [128, 512], F32)
    sb["qcb"] = nc.alloc_sbuf_tensor("s_qcb", [128, 512], BF16)
    sb["t1"] = nc.alloc_sbuf_tensor("s_t1", [128, 512], F32)
    sb["C"] = nc.alloc_sbuf_tensor("s_C", [128, 4096], BF16)
    sb["S"] = nc.alloc_sbuf_tensor("s_S", [128, 4096], BF16)
    sb["protT"] = nc.alloc_sbuf_tensor("s_protT", [128, 128], F32)
    sb["identF"] = nc.alloc_sbuf_tensor("s_identF", [128, 128], F32)
    sb["Ddiag4"] = nc.alloc_sbuf_tensor("s_Ddiag4", [128, 512], F32)
    sb["Dprev4"] = nc.alloc_sbuf_tensor("s_Dprev4", [128, 512], F32)
    sb["DcausQK"] = nc.alloc_sbuf_tensor("s_DcausQK", [128, 4 * 512], F32)
    sb["recent4"] = nc.alloc_sbuf_tensor("s_recent4", [128, 512], BF16)
    sb["E80"] = nc.alloc_sbuf_tensor("s_E80", [128, 16 * 128], BF16)
    sb["E01"] = nc.alloc_sbuf_tensor("s_E01", [128, 16 * 128], BF16)
    sb["init01"] = nc.alloc_sbuf_tensor("s_init01", [128, 256], BF16)
    sb["zero512"] = nc.alloc_sbuf_tensor("s_zero512", [1, 512], BF16)
    sb["cm"] = nc.alloc_sbuf_tensor("s_cm", [128, 256], F32)
    sb["m8"] = nc.alloc_sbuf_tensor("s_m8", [128, 8], F32)
    sb["t8b"] = nc.alloc_sbuf_tensor("s_t8b", [128, 1], F32)
    sb["cok"] = nc.alloc_sbuf_tensor("s_cok", [128, 256], BF16)
    sb["cokT"] = nc.alloc_sbuf_tensor("s_cokT", [128, 2 * 512], BF16)
    sb["a01"] = nc.alloc_sbuf_tensor("s_a01", [128, 512], BF16)
    sb["sfix"] = nc.alloc_sbuf_tensor("s_sfix", [128, 512], F32)
    sb["pp"] = [nc.alloc_sbuf_tensor(f"s_pp{i}", [128, 512], BF16) for i in range(2)]
    sb["rs"] = nc.alloc_sbuf_tensor("s_rs", [128, 1], F32)
    sb["rs2"] = nc.alloc_sbuf_tensor("s_rs2", [128, 1], F32)
    sb["pv"] = nc.alloc_sbuf_tensor("s_pv", [128, 129], F32)
    sb["att"] = nc.alloc_sbuf_tensor("s_att", [128, 128], BF16)
    sb["attT"] = nc.alloc_sbuf_tensor("s_attT", [128, 128], BF16)
    sb["neg80"] = nc.alloc_sbuf_tensor("s_neg80", [128, 1], F32)
    sb["zero1"] = nc.alloc_sbuf_tensor("s_zero1", [128, 1], F32)


def build_attn1():
    nc = bass.Bass("TRN2", target_bir_lowering=False)
    ap = {}
    ap["ln_full"] = nc.dram_tensor("ln_full", [32, 128, 4096], BF16, kind="ExternalInput").ap()
    ap["lnf"] = lambda dt, c0, c1: ap["ln_full"][dt, :, c0:c1]
    ap["wq_bf"] = nc.dram_tensor("wq_bf", [8, 4096, 128], BF16, kind="ExternalInput").ap()
    ap["wk_bf"] = nc.dram_tensor("wk_bf", [2, 4096, 128], BF16, kind="ExternalInput").ap()
    ap["wv_bf"] = nc.dram_tensor("wv_bf", [2, 4096, 128], BF16, kind="ExternalInput").ap()
    ap["C"] = nc.dram_tensor("C", [128, 4096], BF16, kind="ExternalInput").ap()
    ap["S"] = nc.dram_tensor("S", [128, 4096], BF16, kind="ExternalInput").ap()
    ap["protT"] = nc.dram_tensor("protT", [128, 128], F32, kind="ExternalInput").ap()
    ap["identF"] = nc.dram_tensor("identF", [128, 128], F32, kind="ExternalInput").ap()
    ap["Ddiag4"] = nc.dram_tensor("Ddiag4", [128, 512], F32, kind="ExternalInput").ap()
    ap["Dprev4"] = nc.dram_tensor("Dprev4", [128, 512], F32, kind="ExternalInput").ap()
    ap["DcausQK"] = nc.dram_tensor("DcausQK", [4, 128, 512], F32, kind="ExternalInput").ap()
    ap["recent4"] = nc.dram_tensor("recent4", [128, 512], BF16, kind="ExternalInput").ap()
    ap["E80"] = nc.dram_tensor("E80", [16, 128, 128], BF16, kind="ExternalInput").ap()
    ap["E01"] = nc.dram_tensor("E01", [16, 128, 128], BF16, kind="ExternalInput").ap()
    ap["init01"] = nc.dram_tensor("init01", [128, 256], BF16, kind="ExternalInput").ap()
    ap["zero512"] = nc.dram_tensor("zero512", [1, 512], BF16, kind="ExternalInput").ap()
    ap["attn_out"] = nc.dram_tensor("attn_out", [8, 128, 4096], BF16, kind="ExternalOutput").ap()

    sb = {}
    alloc_attn1_bufs(nc, sb)
    ps = [nc.alloc_psum_tensor(f"ps{i}", [128, 512], F32) for i in range(7)]
    P = Prog(nc)
    with nc.Block() as block:
        P.dma("sync", sb["zero512"][0:1, :], ap["zero512"], writes=["zero512"], stream="init")
        P.add("vector", lambda e: e.memset(sb["neg80"][:], -MSKB), writes=["neg80"])
        P.add("vector", lambda e: e.memset(sb["zero1"][:], 0.0), writes=["zero1"])
        emit_attn1(P, nc, ap, sb, ps)
        P.emit(block)
    return nc


def host_attn1_consts():
    C = {}
    inv = 1.0 / 10000.0 ** (np.arange(0, 128, 2, dtype=np.float64) / 128)
    pos = np.arange(4096, dtype=np.float64)
    fr = pos[:, None] * inv[None, :]
    emb = np.concatenate([fr, fr], axis=1)
    C["C"] = np.ascontiguousarray(np.cos(emb).T.astype(ml_dtypes.bfloat16))
    C["S"] = np.ascontiguousarray(np.sin(emb).T.astype(ml_dtypes.bfloat16))
    prot = np.zeros((128, 128), np.float32)
    for d in range(64):
        prot[d, d + 64] = -1.0
        prot[d + 64, d] = 1.0
    C["protT"] = np.ascontiguousarray(prot.T)
    C["identF"] = np.eye(128, dtype=np.float32)
    kk = np.arange(128)[:, None]
    rr = np.arange(128)[None, :]
    diag = np.where(kk <= rr, 0.0, CAUS).astype(np.float32)
    C["Ddiag4"] = np.ascontiguousarray(np.tile(diag, (1, 4)))
    prev = np.where(kk > rr, PEN, 0.0).astype(np.float32)
    C["Dprev4"] = np.ascontiguousarray(np.tile(prev, (1, 4)))
    C["recent4"] = np.ascontiguousarray(np.tile((kk > rr), (1, 4)).astype(ml_dtypes.bfloat16))
    dc = np.zeros((4, 128, 512), np.float32)
    for v in range(4):
        for p2 in range(4):
            blk = dc[v][:, 128 * p2:128 * (p2 + 1)]
            if p2 < v:
                blk[:] = 0.0
            elif p2 == v:
                blk[:] = np.where(kk.T >= rr.T, 0.0, CAUS).T
            else:
                blk[:] = CAUS
    for v in range(4):
        dc[v][:, 128 * v:128 * (v + 1)] = np.where(rr <= kk, 0.0, CAUS)
    C["DcausQK"] = np.ascontiguousarray(dc)
    e80 = np.zeros((16, 128, 128), np.float32)
    for v in range(16):
        for j in range(8):
            e80[v, 8 * v + j, 16 * j:16 * (j + 1)] = PEN
    C["E80"] = e80.astype(ml_dtypes.bfloat16)
    C["E01"] = (e80 / PEN).astype(ml_dtypes.bfloat16)
    ini = np.zeros((128, 256), np.float32)
    ini[:, 0:8] = 1.0
    C["init01"] = ini.astype(ml_dtypes.bfloat16)
    C["zero512"] = np.zeros((1, 512), ml_dtypes.bfloat16)
    return C


def host_attn1_weights(inp, c):
    b, q4 = c // 4, c % 4
    kvhs = [2 * q4, 2 * q4 + 1]
    heads = [4 * kvh + g for kvh in kvhs for g in range(4)]
    wq = np.asarray(inp["L1_attn_wq"], np.float32)
    wk = np.asarray(inp["L1_attn_wk"], np.float32)
    wv = np.asarray(inp["L1_attn_wv"], np.float32)
    d = {}
    d["wqT_sl"] = np.ascontiguousarray(
        np.concatenate([wq[128 * h:128 * (h + 1)].T for h in heads], axis=1))
    d["wkT_sl"] = np.ascontiguousarray(
        np.concatenate([wk[128 * kvh:128 * (kvh + 1)].T for kvh in kvhs], axis=1))
    d["wvT_sl"] = np.ascontiguousarray(
        np.concatenate([wv[128 * kvh:128 * (kvh + 1)].T for kvh in kvhs], axis=1))
    return d


def emit_attn0(P, nc, ap, sb, ps):
    for nm in ("C0", "S0", "pswapT", "identF", "Dfdiag4", "Dsdiag4", "Dsprev4"):
        P.dma("sync", sb[nm][:], ap[nm], writes=[nm], stream="init")
    P.dma("sync", sb["E80"][:].rearrange("p (v f) -> p v f", f=128),
          ap["E80"].rearrange("v p f -> p v f"), writes=["E80"], stream="init")
    P.dma("sync", sb["bgate"][0:1, :], ap["bgate_row"], writes=["bgate"], stream="init")
    P.dma("sync", sb["onesr"][0:1, :], ap["ones_row"], writes=["onesr"], stream="init")
    P.dma("sync", sb["kb1"][:], ap["kb1_cols"], writes=["kb1"], stream="init")
    P.dma("sync", sb["vb1"][:], ap["vb1_cols"], writes=["vb1"], stream="init")
    P.dma("sync", sb["kb2"][:], ap["kb2_col"], writes=["kb2"], stream="init")
    P.dma("sync", sb["vb2"][:], ap["vb2_col"], writes=["vb2"], stream="init")
    P.dma("sync", sb["kpos"][:], ap["kpos_cols"], writes=["kpos"], stream="init")
    P.dma("sync", sb["vpos"][:], ap["vpos_cols"], writes=["vpos"], stream="init")
    P.dma("sync", sb["memk"][:], ap["memk_cols"], writes=["memk"], stream="init")
    P.dma("sync", sb["memvr"][0:1, :], ap["memv_rows"].rearrange("a p b -> p (a b)"),
          writes=["memvr"], stream="init")
    P.dma("sync", sb["wgate"][:].rearrange("p (dt o) -> p dt o", o=24),
          ap["wgate_bf"].rearrange("(dt p) o -> p dt o", p=128), writes=["wgate"], stream="init")

    RQK = [("rq", oi, tt) for oi in range(4) for tt in range(8)] + [("rk", tt) for tt in range(8)]
    for pair in range(PAIRS):
        for tt in range(8):
            for dt in range(32):
                P.dma("gpsimd", sb["lnt"][0][:, 512 * dt:512 * (dt + 1)],
                      ap["lnf"](dt, 512 * tt, 512 * (tt + 1)),
                      reads=tuple(ap.get("lnf_dep", ())),
                      writes=[("lnt", 0, dt)], stream="lnt0")
            if pair == 0:
                for sub in range(4):
                    i2 = 4 * tt + sub
                    for dt in range(32):
                        P.add("tensor", lambda e, dt=dt, sub=sub: e.matmul(
                            ps[7].ap()[:, 0:24],
                            sb["lnt"][0][:, 512 * dt + 128 * sub:512 * dt + 128 * (sub + 1)],
                            sb["wgate"][:, 24 * dt:24 * (dt + 1)],
                            start=(dt == 0), stop=False),
                            reads=[("lnt", 0, dt), "wgate", ("ps", 7)], writes=[("ps", 7)])
                    P.add("tensor", lambda e: e.matmul(
                        ps[7].ap()[:, 0:24], sb["onesr"][0:1, :], sb["bgate"][0:1, :],
                        start=False, stop=True),
                        reads=["onesr", "bgate", ("ps", 7)], writes=[("ps", 7)])
                    P.add("scalar", lambda e, i2=i2: e.activation(
                        out=sb["gates"][:, 24 * i2:24 * (i2 + 1)], in_=ps[7].ap()[:, 0:24],
                        func=AF.Sigmoid, bias=0.0, scale=1.0),
                        reads=[("ps", 7)], writes=[("gates", i2), ("ps", 7)])
            for oi in range(6):
                bank = ps[oi]
                wtile = 6 * pair + oi
                P.dma("sync", sb["wje"][0][:].rearrange("p (dt o) -> p dt o", o=128),
                      ap["wqkv_bf"][wtile].rearrange("(dt p) o -> p dt o", p=128),
                      writes=[("wje", 0)], stream="wje0")
                for dt in range(32):
                    P.add("tensor", lambda e, oi=oi, dt=dt, bank=bank: e.matmul(
                        bank.ap()[:, 0:512],
                        sb["wje"][0][:, 128 * dt:128 * (dt + 1)],
                        sb["lnt"][0][:, 512 * dt:512 * (dt + 1)],
                        start=(dt == 0), stop=(dt == 31)),
                        reads=[("wje", 0), ("lnt", 0, dt)], writes=[("ps", oi)])
                if oi == 5:
                    P.add("scalar", lambda e, bank=bank, tt=tt: e.copy(
                        out=sb["v"][:, 512 * tt:512 * (tt + 1)], in_=bank.ap()[:, 0:512]),
                        reads=[("ps", oi)], writes=[("v", tt), ("ps", oi)])
                elif oi == 4:
                    P.add("scalar", lambda e, bank=bank: e.copy(
                        out=sb["kt"][:], in_=bank.ap()[:, 0:512]),
                        reads=[("ps", oi)], writes=[("kt",), ("ps", oi)])
                    for s2 in range(16):
                        P.dma("gpsimd", sb["posc"][:], sb["kpos"][:, 16 * pair + s2:16 * pair + s2 + 1],
                              reads=["kpos"], writes=["posc"], stream="posc")
                        P.add("vector", lambda e, s2=s2, tt=tt: e.tensor_scalar(
                            out=sb["cin"][:, 256 * s2 + 32 * tt:256 * s2 + 32 * (tt + 1)],
                            in0=sb["kt"][:].rearrange("p (c s3) -> p c s3", s3=16)[:, :, s2],
                            scalar1=sb["posc"][:, 0:1], scalar2=None, op0=ALU.add),
                            reads=[("kt",), "posc"], writes=[("cin", s2)])
                    P.add("tensor", lambda e: e.matmul(
                        ps[6].ap()[:, 0:512], sb["pswapT"][:], sb["kt"][:],
                        start=True, stop=True),
                        reads=[("kt",), "pswapT", ("ps", 6)], writes=[("ps", 6)])
                    P.add("vector", lambda e, tt=tt: e.tensor_mul(
                        out=sb["t1"][:], in0=ps[6].ap()[:, 0:512],
                        in1=sb["S0"][:, 512 * tt:512 * (tt + 1)]),
                        reads=[("ps", 6), "S0"], writes=["t1", ("ps", 6)])
                    P.add("vector", lambda e, tt=tt: e.tensor_mul(
                        out=sb["qf"][:], in0=sb["kt"][:],
                        in1=sb["C0"][:, 512 * tt:512 * (tt + 1)]),
                        reads=[("kt",), "C0"], writes=["qf"])
                    P.add("vector", lambda e, tt=tt: e.tensor_add(
                        out=sb["rk"][:, 512 * tt:512 * (tt + 1)], in0=sb["qf"][:], in1=sb["t1"][:]),
                        reads=["qf", "t1"], writes=[("rk", tt)])
                else:
                    P.add("scalar", lambda e, bank=bank: e.copy(
                        out=sb["qf"][:], in_=bank.ap()[:, 0:512]),
                        reads=[("ps", oi)], writes=["qf", ("ps", oi)])
                    P.add("vector", lambda e: e.tensor_copy(
                        out=sb["qcb"][:], in_=sb["qf"][:]),
                        reads=["qf"], writes=["qcb"])
                    P.dma("gpsimd", ap["qc_hbm"][pair, oi, :, 512 * tt:512 * (tt + 1)],
                          sb["qcb"][:], reads=["qcb"], writes=[("qc", pair, oi, tt)],
                          stream="qcst")
                    P.add("tensor", lambda e: e.matmul(
                        ps[6].ap()[:, 0:512], sb["pswapT"][:], sb["qf"][:],
                        start=True, stop=True),
                        reads=["qf", "pswapT", ("ps", 6)], writes=[("ps", 6)])
                    P.add("vector", lambda e, tt=tt: e.tensor_mul(
                        out=sb["t1"][:], in0=ps[6].ap()[:, 0:512],
                        in1=sb["S0"][:, 512 * tt:512 * (tt + 1)]),
                        reads=[("ps", 6), "S0"], writes=["t1", ("ps", 6)])
                    P.add("vector", lambda e, tt=tt: e.tensor_mul(
                        out=sb["qf"][:], in0=sb["qf"][:],
                        in1=sb["C0"][:, 512 * tt:512 * (tt + 1)]),
                        reads=["qf", "C0"], writes=["qf"])
                    P.add("vector", lambda e, oi=oi, tt=tt: e.tensor_add(
                        out=sb["q_all"][:, 4096 * oi + 512 * tt:4096 * oi + 512 * (tt + 1)],
                        in0=sb["qf"][:], in1=sb["t1"][:]),
                        reads=["qf", "t1"], writes=[("rq", oi, tt)])
        for kt in range(32):
            P.add("vector", lambda e, kt=kt: e.memset(sb["vt"][:, 256 * kt + 128:256 * kt + 129], 1.0),
                  reads=[], writes=[("vt1", kt)])
            P.add("sync", lambda e, kt=kt: e.dma_start_transpose(
                sb["vt"][:, 256 * kt:256 * kt + 128], sb["v"][:, 128 * kt:128 * (kt + 1)]),
                reads=[("v", kt // 4)], writes=[("vt", kt)], kind="d", stream="vtp")
        VALL = [("v", tt) for tt in range(8)]
        _emit_comp_mlp(P, nc, ap, sb, ps, pair, "k")
        for s2 in range(16):
            P.dma("gpsimd", sb["posc"][:], sb["vpos"][:, 16 * pair + s2:16 * pair + s2 + 1],
                  reads=["vpos"], writes=["posc"], stream="posc")
            P.add("vector", lambda e, s2=s2: e.tensor_scalar(
                out=sb["cin"][:, 256 * s2:256 * (s2 + 1)],
                in0=sb["v"][:].rearrange("p (c s3) -> p c s3", s3=16)[:, :, s2],
                scalar1=sb["posc"][:, 0:1],
                scalar2=None, op0=ALU.add),
                reads=VALL + ["posc", ("ckdone",)], writes=[("cin", s2)])
        _emit_comp_mlp(P, nc, ap, sb, ps, pair, "v")
        for c2 in range(2):
            P.add("vector", lambda e, c2=c2: e.memset(
                sb["cvt"][:, 256 * c2 + 128:256 * c2 + 129], 1.0), reads=[], writes=[("cvt1", c2)])
            P.add("sync", lambda e, c2=c2: e.dma_start_transpose(
                sb["cvt"][:, 256 * c2:256 * c2 + 128], sb["cv"][:, 128 * c2:128 * (c2 + 1)]),
                reads=[("cv",)], writes=[("cvt", c2)], kind="d", stream="vtp")
        P.dma("gpsimd", sb["posc"][:], sb["memk"][:, pair:pair + 1],
              reads=["memk"], writes=["posc"], stream="posc")
        P.add("vector", lambda e: e.tensor_scalar_mul(
            out=sb["memk_bf"][:, 0:1], in0=sb["posc"][:], scalar1=SCALE),
            reads=["posc"], writes=["memk_bf"])
        P.add("vector", lambda e, pair=pair: e.tensor_copy(
            out=sb["memv_bf"][0:1, :], in_=sb["memvr"][0:1, 129 * pair:129 * (pair + 1)]),
            reads=["memvr"], writes=["memv_bf"])
        for i in range(32):
            nkt = i + 1
            gslot = ("gates", i)
            for hh in range(4):
                for br in range(3):
                    kk = 3 * hh + br
                    gb = 24 * i + 3 * (4 * pair + hh) + br
                    P.dma("gpsimd", sb["g1b"][kk][:], sb["gates"][:, gb:gb + 1],
                          reads=[gslot], writes=[("g1b", kk)], stream="g1cp")
            for hh in range(4):
                P.dma("sync", sb["qcs"][:, 128 * hh:128 * (hh + 1)],
                      ap["qc_hbm"][pair, hh, :, 128 * i:128 * (i + 1)],
                      reads=[("qc", pair, hh, i // 4)], writes=[("qcs", hh)],
                      stream="qcld")
            for jt in range(2):
                P.dma("gpsimd", sb["Dc"][:], ap["Dc"][2 * i + jt],
                      writes=["Dc"], stream="dcld")
                P.add("tensor", lambda e, jt=jt: e.matmul(
                    ps[jt].ap()[:, 0:512],
                    sb["ck"][:, 128 * jt:128 * (jt + 1)],
                    sb["qcs"][:],
                    start=True, stop=True),
                    reads=[("ck",)] + [("qcs", hh) for hh in range(4)] + [("ps", jt)],
                    writes=[("ps", jt)])
                P.add("vector", lambda e, jt=jt: e.scalar_tensor_tensor(
                    out=sb["csf"][:], in0=sb["Dc"][:], scalar=1.0, in1=ps[jt].ap()[:, 0:512],
                    op0=ALU.mult, op1=ALU.add),
                    reads=["Dc", ("ps", jt)], writes=["csf", ("ps", jt)])
                P.add("scalar", lambda e, jt=jt: e.copy(
                    out=sb["csraw"][:, 512 * jt:512 * (jt + 1)], in_=sb["csf"][:]),
                    reads=["csf"], writes=[("csraw", jt)])
                P.add("scalar", lambda e, jt=jt: e.activation(
                    out=sb["pc"][:, 512 * jt:512 * (jt + 1)], in_=sb["csf"][:],
                    func=AF.Exp, bias=0.0, scale=1.0),
                    reads=["csf"], writes=[("pc", jt)])
            P.add("tensor", lambda e: e.matmul(
                ps[2].ap()[0:1, 0:512], sb["memk_bf"][:, 0:1],
                sb["qcs"][:],
                start=True, stop=True),
                reads=["memk_bf"] + [("qcs", hh) for hh in range(4)] + [("ps", 2)],
                writes=[("ps", 2)])
            P.add("scalar", lambda e: e.activation(
                out=sb["pmem"][0:1, 0:512], in_=ps[2].ap()[0:1, 0:512], func=AF.Exp,
                bias=0.0, scale=1.0),
                reads=[("ps", 2)], writes=["pmem", ("ps", 2)])
            for hh in range(4):
                pb = ps[3 + hh]
                for jt in range(2):
                    P.add("tensor", lambda e, jt=jt, hh=hh, pb=pb: e.matmul(
                        pb.ap()[:, 0:129],
                        sb["pc"][:, 512 * jt + 128 * hh:512 * jt + 128 * (hh + 1)],
                        sb["cvt"][:, 256 * jt:256 * jt + 129],
                        start=(jt == 0), stop=False),
                        reads=[("pc", jt), ("cvt", jt), ("cvt1", jt), ("ps", 3 + hh)],
                        writes=[("ps", 3 + hh)])
                P.add("tensor", lambda e, hh=hh, pb=pb: e.matmul(
                    pb.ap()[:, 0:129], sb["pmem"][0:1, 128 * hh:128 * (hh + 1)],
                    sb["memv_bf"][0:1, 0:129], start=False, stop=True),
                    reads=["pmem", "memv_bf", ("ps", 3 + hh)], writes=[("ps", 3 + hh)])
                P.add("scalar", lambda e, hh=hh, pb=pb: e.copy(
                    out=sb["cpv"][hh][:, 0:129], in_=pb.ap()[:, 0:129]),
                    reads=[("ps", 3 + hh)], writes=[("cpv", hh), ("ps", 3 + hh)])
                P.dma("gpsimd", sb["rsb"][3 * hh][:], sb["cpv"][hh][:, 128:129],
                      reads=[("cpv", hh)], writes=[("rsb", 3 * hh)], stream="rscp")
            for jt in range(2):
                base = 512 * jt
                P.add("vector", lambda e, base=base: e.tensor_add(
                    out=sb["impa"][:], in0=sb["csraw"][:, base:base + 128],
                    in1=sb["csraw"][:, base + 128:base + 256]),
                    reads=[("csraw", jt)], writes=["impa"])
                P.add("vector", lambda e, base=base: e.tensor_add(
                    out=sb["impb"][:], in0=sb["csraw"][:, base + 256:base + 384],
                    in1=sb["csraw"][:, base + 384:base + 512]),
                    reads=[("csraw", jt)], writes=["impb"])
                P.add("vector", lambda e: e.tensor_add(
                    out=sb["impa"][:], in0=sb["impa"][:], in1=sb["impb"][:]),
                    reads=["impa", "impb"], writes=["impa"])
                P.add("vector", lambda e, jt=jt: e.tensor_scalar_mul(
                    out=sb["impj"][:, 128 * jt:128 * (jt + 1)], in0=sb["impa"][:], scalar1=0.25),
                    reads=["impa"], writes=[("impj", jt)])
                P.add("scalar", lambda e, jt=jt: e.dma_start_transpose(
                    sb["impq"][:, 128 * jt:128 * (jt + 1)], sb["impj"][:, 128 * jt:128 * (jt + 1)]),
                    reads=[("impj", jt)], writes=[("impq", jt)], kind="d", stream="itp")
            P.add("vector", lambda e: e.reduce_max(out=sb["im"][:], in_=sb["impq"][:], axis=AX.X),
                  reads=[("impq", 0), ("impq", 1)], writes=["im"])
            P.add("vector", lambda e: e.tensor_scalar_max(out=sb["im"][:], in0=sb["im"][:], scalar1=-1000.0),
                  reads=["im"], writes=["im"])
            P.add("vector", lambda e: e.tensor_scalar_mul(out=sb["imn"][:], in0=sb["im"][:], scalar1=-1.0),
                  reads=["im"], writes=["imn"])
            P.add("scalar", lambda e: e.activation(
                out=sb["uu"][:], in_=sb["impq"][:], func=AF.Exp, bias=sb["imn"][:, 0:1], scale=1.0),
                reads=[("impq", 0), ("impq", 1), "imn"], writes=["uu"])
            P.add("vector", lambda e: e.reduce_sum(out=sb["zz"][:], in_=sb["uu"][:], axis=AX.X),
                  reads=["uu"], writes=["zz"])
            P.add("scalar", lambda e: e.activation(
                out=sb["sen"][:], in_=sb["imn"][:], func=AF.Exp, bias=-1000.0 if False else sb["negk"][:, 0:1],
                scale=1.0), reads=["imn", "negk"], writes=["sen"])
            P.add("vector", lambda e: e.tensor_add(out=sb["zz"][:], in0=sb["zz"][:], in1=sb["sen"][:]),
                  reads=["zz", "sen"], writes=["zz"])
            P.add("vector", lambda e: e.tensor_scalar_mul(out=sb["zz"][:], in0=sb["zz"][:], scalar1=1e-10),
                  reads=["zz"], writes=["zz"])
            P.add("vector", lambda e: e.max(out=sb["m8"][:], in_=sb["uu"][:]), reads=["uu"], writes=["m8"])
            P.dma("gpsimd", sb["t4b"][:], sb["m8"][:, 3:4], reads=["m8"], writes=["t4b"], stream="t8cp")
            P.add("vector", lambda e: e.tensor_scalar(
                out=sb["selq"][:], in0=sb["uu"][:], scalar1=sb["t4b"][:, 0:1], scalar2=None, op0=ALU.is_ge),
                reads=["uu", "t4b"], writes=["selq"])
            P.add("vector", lambda e: e.tensor_scalar(
                out=sb["selq2"][:], in0=sb["uu"][:], scalar1=sb["zz"][:, 0:1], scalar2=None, op0=ALU.is_gt),
                reads=["uu", "zz"], writes=["selq2"])
            P.add("vector", lambda e: e.tensor_mul(
                out=sb["selq"][:], in0=sb["selq"][:], in1=sb["selq2"][:]),
                reads=["selq", "selq2"], writes=["selq"])
            for gI in range(2):
                for rep in range(4):
                    P.add("scalar", lambda e, gI=gI, rep=rep: e.dma_start_transpose(
                        sb["selT"][:, 512 * gI + 128 * rep:512 * gI + 128 * (rep + 1)],
                        sb["selq"][:, 128 * gI:128 * (gI + 1)]),
                        reads=["selq"], writes=[("selT", gI, rep)], kind="d", stream="stp")
            for kt in range(nkt):
                bi = kt % 2
                bank = ps[bi]
                gI = kt // 16
                selk = [("selT", gI, rep) for rep in range(4)]
                P.add("tensor", lambda e, kt=kt, i=i, bank=bank: e.matmul(
                    bank.ap()[:, 0:512],
                    sb["rk"][:, 128 * kt:128 * (kt + 1)],
                    sb["q_all"][:].rearrange("p (h n) -> p h n", n=4096)[:, :, 128 * i:128 * (i + 1)],
                    start=True, stop=False),
                    reads=RQK + [("ps", bi)], writes=[("ps", bi)])
                P.add("tensor", lambda e, kt=kt, gI=gI, bank=bank: e.matmul(
                    bank.ap()[:, 0:512],
                    sb["E80"][:, 128 * (kt % 16):128 * (kt % 16 + 1)],
                    sb["selT"][:, 512 * gI:512 * (gI + 1)],
                    start=False, stop=True),
                    reads=["E80", ("ps", bi)] + selk, writes=[("ps", bi)])
                ppar = kt % 2
                if kt == i:
                    P.add("vector", lambda e, bank=bank: e.scalar_tensor_tensor(
                        out=sb["csf"][:], in0=sb["Dfdiag4"][:], scalar=1.0,
                        in1=bank.ap()[:, 0:512], op0=ALU.mult, op1=ALU.add),
                        reads=["Dfdiag4", ("ps", bi)], writes=["csf", ("ps", bi)])
                    P.add("scalar", lambda e, ppar=ppar: e.activation(
                        out=sb["pp"][ppar][:], in_=sb["csf"][:], func=AF.Exp,
                        bias=sb["neg80"][:, 0:1], scale=SCALE),
                        reads=["csf", "neg80"], writes=[("pp", ppar)])
                else:
                    P.add("scalar", lambda e, bank=bank, ppar=ppar: e.activation(
                        out=sb["pp"][ppar][:], in_=bank.ap()[:, 0:512], func=AF.Exp,
                        bias=sb["neg80"][:, 0:1], scale=SCALE),
                        reads=[("ps", bi), "neg80"], writes=[("pp", ppar), ("ps", bi)])
                for hh in range(4):
                    pb = ps[3 + hh]
                    P.add("tensor", lambda e, kt=kt, hh=hh, pb=pb, ppar=ppar, i=i: e.matmul(
                        pb.ap()[:, 0:129],
                        sb["pp"][ppar][:, 128 * hh:128 * (hh + 1)],
                        sb["vt"][:, 256 * kt:256 * kt + 129],
                        start=(kt == 0), stop=(kt == i)),
                        reads=[("pp", ppar), ("vt", kt), ("vt1", kt), ("ps", 3 + hh)],
                        writes=[("ps", 3 + hh)])
            for hh in range(4):
                P.add("scalar", lambda e, hh=hh: e.copy(
                    out=sb["fpv"][hh][:, 0:129], in_=ps[3 + hh].ap()[:, 0:129]),
                    reads=[("ps", 3 + hh)], writes=[("fpv", hh), ("ps", 3 + hh)])
                P.dma("gpsimd", sb["rsb"][3 * hh + 1][:], sb["fpv"][hh][:, 128:129],
                      reads=[("fpv", hh)], writes=[("rsb", 3 * hh + 1)], stream="rscp")
            skts = [kt for kt in (i - 1, i) if kt >= 0]
            for sj, kt in enumerate(skts):
                bi = sj % 2
                bank = ps[bi]
                dname = "Dsprev4" if kt == i - 1 else "Dsdiag4"
                P.add("tensor", lambda e, kt=kt, i=i, bank=bank: e.matmul(
                    bank.ap()[:, 0:512],
                    sb["rk"][:, 128 * kt:128 * (kt + 1)],
                    sb["q_all"][:].rearrange("p (h n) -> p h n", n=4096)[:, :, 128 * i:128 * (i + 1)],
                    start=True, stop=True),
                    reads=RQK + [("ps", bi)], writes=[("ps", bi)])
                P.add("vector", lambda e, bank=bank, dname=dname: e.scalar_tensor_tensor(
                    out=sb["csf"][:], in0=sb[dname][:], scalar=1.0,
                    in1=bank.ap()[:, 0:512], op0=ALU.mult, op1=ALU.add),
                    reads=[dname, ("ps", bi)], writes=["csf", ("ps", bi)])
                P.add("scalar", lambda e, sj=sj: e.activation(
                    out=sb["pp"][sj][:], in_=sb["csf"][:], func=AF.Exp,
                    bias=sb["neg80"][:, 0:1], scale=SCALE),
                    reads=["csf", "neg80"], writes=[("pp", sj)])
                for hh in range(4):
                    pb = ps[3 + hh]
                    P.add("tensor", lambda e, kt=kt, hh=hh, pb=pb, sj=sj, last=(kt == i): e.matmul(
                        pb.ap()[:, 0:129],
                        sb["pp"][sj][:, 128 * hh:128 * (hh + 1)],
                        sb["vt"][:, 256 * kt:256 * kt + 129],
                        start=(sj == 0), stop=last),
                        reads=[("pp", sj), ("vt", kt), ("vt1", kt), ("ps", 3 + hh)],
                        writes=[("ps", 3 + hh)])
            for hh in range(4):
                pb = ps[3 + hh]
                P.add("scalar", lambda e, hh=hh, pb=pb: e.copy(
                    out=sb["spvh"][hh][:, 0:129], in_=pb.ap()[:, 0:129]),
                    reads=[("ps", 3 + hh)], writes=[("spv", hh), ("ps", 3 + hh)])
                P.dma("gpsimd", sb["rsb"][3 * hh + 2][:], sb["spvh"][hh][:, 128:129],
                      reads=[("spv", hh)], writes=[("rsb", 3 * hh + 2)], stream="rscp")
            for kk in range(12):
                P.add("vector", lambda e, kk=kk: e.reciprocal(
                    out=sb["rs2b"][kk][:], in_=sb["rsb"][kk][:]),
                    reads=[("rsb", kk)], writes=[("rs2b", kk)])
            for kk in range(12):
                P.add("vector", lambda e, kk=kk: e.tensor_mul(
                    out=sb["coefb"][kk][:], in0=sb["rs2b"][kk][:], in1=sb["g1b"][kk][:]),
                    reads=[("rs2b", kk), ("g1b", kk)], writes=[("coefb", kk)])
            for hh in range(4):
                for br, buf in ((0, "cpv"), (2, "spv"), (1, "fpv")):
                    kk = 3 * hh + br
                    src2 = {"cpv": sb["cpv"][hh], "fpv": sb["fpv"][hh],
                            "spv": sb["spvh"][hh]}[buf]
                    if br == 0:
                        P.add("vector", lambda e, src2=src2, kk=kk: e.tensor_scalar_mul(
                            out=sb["att"][:], in0=src2[:, 0:128],
                            scalar1=sb["coefb"][kk][:, 0:1]),
                            reads=[(buf, hh), ("coefb", kk)], writes=["att"])
                    else:
                        P.add("vector", lambda e, src2=src2, kk=kk: e.scalar_tensor_tensor(
                            out=sb["att"][:], in0=src2[:, 0:128],
                            scalar=sb["coefb"][kk][:, 0:1], in1=sb["att"][:],
                            op0=ALU.mult, op1=ALU.add),
                            reads=[(buf, hh), ("coefb", kk), "att"], writes=["att"])
                P.add("scalar", lambda e: e.copy(out=sb["attb"][:], in_=sb["att"][:]),
                      reads=["att"], writes=["attb"])
                P.add("sync", lambda e: e.dma_start_transpose(sb["attT"][:], sb["attb"][:]),
                      reads=["attb"], writes=["attT"], kind="d", stream="atp")
                P.dma("gpsimd", ap["attn_out"][4 * pair + hh, :, 128 * i:128 * (i + 1)],
                      sb["attT"][:], reads=["attT"], writes=[("ao", pair, hh, i)], stream="aost")
    return


def _emit_comp_mlp(P, nc, ap, sb, ps, pair, side):
    wsh = ap[("kw1_sh" if side == "k" else "vw1_sh")]
    w2 = ap[("kw2_bf" if side == "k" else "vw2_bf")]
    bcols = "kb1" if side == "k" else "vb1"
    b2col = "kb2" if side == "k" else "vb2"
    outk = "ck" if side == "k" else "cv"
    for o2 in range(16):
        par = 0
        P.dma("scalar", sb["w1b"][par][:].rearrange("p (ct o) -> p ct o", o=128),
              wsh[o2].rearrange("(ct p) o -> p ct o", p=128),
              writes=[("w1b", par)], stream=f"w1b{par}")
        bank = ps[o2 % 2]
        cin_keys = [("cin", s2) for s2 in range(16)]
        for ct in range(16):
            P.add("tensor", lambda e, o2=o2, ct=ct, bank=bank, par=par: e.matmul(
                bank.ap()[:, 0:256],
                sb["w1b"][par][:, 128 * ct:128 * (ct + 1)],
                sb["cin"][:, 256 * ct:256 * (ct + 1)],
                start=(ct == 0), stop=(ct == 15)),
                reads=[("w1b", par)] + cin_keys, writes=[("ps", o2 % 2)])
        P.add("scalar", lambda e, o2=o2, bank=bank, bcols=bcols: e.activation(
            out=sb["h1"][:, 256 * o2:256 * (o2 + 1)], in_=bank.ap()[:, 0:256],
            func=AF.Relu, bias=sb[bcols][:, o2:o2 + 1], scale=1.0),
            reads=[("ps", o2 % 2), bcols], writes=[("h1", o2), ("ps", o2 % 2)])
    P.dma("scalar", sb["w2b"][:].rearrange("p (ct o) -> p ct o", o=128),
          w2[0].rearrange("(ct p) o -> p ct o", p=128), writes=["w2b"], stream="w2b")
    for ct in range(16):
        P.add("tensor", lambda e, ct=ct: e.matmul(
            ps[2].ap()[:, 0:256], sb["w2b"][:, 128 * ct:128 * (ct + 1)],
            sb["h1"][:, 256 * ct:256 * (ct + 1)],
            start=(ct == 0), stop=(ct == 15)),
            reads=["w2b", ("h1", ct)], writes=[("ps", 2)])
    if side == "k":
        P.add("vector", lambda e: e.tensor_scalar(
            out=sb["ck"][:], in0=ps[2].ap()[:, 0:256], scalar1=sb["kb2"][:, 0:1],
            scalar2=SCALE, op0=ALU.add, op1=ALU.mult),
            reads=[("ps", 2), "kb2"], writes=[("ck",), ("ps", 2), ("ckdone",)])
    else:
        P.add("vector", lambda e: e.tensor_scalar(
            out=sb["cv"][:], in0=ps[2].ap()[:, 0:256], scalar1=sb["vb2"][:, 0:1],
            scalar2=None, op0=ALU.add),
            reads=[("ps", 2), "vb2"], writes=[("cv",), ("ps", 2)])


def build_attn0():
    nc = bass.Bass("TRN2", target_bir_lowering=False)
    ap = {}
    ap["ln_full"] = nc.dram_tensor("ln_full", [32, 128, 4096], BF16, kind="ExternalInput").ap()
    ap["lnf"] = lambda dt, c0, c1: ap["ln_full"][dt, :, c0:c1]
    ap["wqkv_bf"] = nc.dram_tensor("wqkv_bf", [12, 4096, 128], BF16, kind="ExternalInput").ap()
    ap["wgate_bf"] = nc.dram_tensor("wgate_bf", [4096, 24], BF16, kind="ExternalInput").ap()
    ap["kw1_sh"] = nc.dram_tensor("kw1_sh", [16, 2048, 128], BF16, kind="ExternalInput").ap()
    ap["vw1_sh"] = nc.dram_tensor("vw1_sh", [16, 2048, 128], BF16, kind="ExternalInput").ap()
    ap["kw2_bf"] = nc.dram_tensor("kw2_bf", [1, 2048, 128], BF16, kind="ExternalInput").ap()
    ap["vw2_bf"] = nc.dram_tensor("vw2_bf", [1, 2048, 128], BF16, kind="ExternalInput").ap()
    ap["C0"] = nc.dram_tensor("C0", [128, 4096], BF16, kind="ExternalInput").ap()
    ap["S0"] = nc.dram_tensor("S0", [128, 4096], BF16, kind="ExternalInput").ap()
    ap["pswapT"] = nc.dram_tensor("pswapT", [128, 128], F32, kind="ExternalInput").ap()
    ap["identF"] = nc.dram_tensor("identF", [128, 128], F32, kind="ExternalInput").ap()
    ap["Dfdiag4"] = nc.dram_tensor("Dfdiag4", [128, 512], BF16, kind="ExternalInput").ap()
    ap["Dsdiag4"] = nc.dram_tensor("Dsdiag4", [128, 512], BF16, kind="ExternalInput").ap()
    ap["Dsprev4"] = nc.dram_tensor("Dsprev4", [128, 512], BF16, kind="ExternalInput").ap()
    ap["Dc"] = nc.dram_tensor("Dc", [64, 128, 512], BF16, kind="ExternalInput").ap()
    ap["E80"] = nc.dram_tensor("E80", [16, 128, 128], BF16, kind="ExternalInput").ap()
    ap["ones_row"] = nc.dram_tensor("ones_row", [1, 128], F32, kind="ExternalInput").ap()
    ap["bgate_row"] = nc.dram_tensor("bgate_row", [1, 24], F32, kind="ExternalInput").ap()
    ap["kb1_cols"] = nc.dram_tensor("kb1_cols", [128, 16], F32, kind="ExternalInput").ap()
    ap["vb1_cols"] = nc.dram_tensor("vb1_cols", [128, 16], F32, kind="ExternalInput").ap()
    ap["kb2_col"] = nc.dram_tensor("kb2_col", [128, 1], F32, kind="ExternalInput").ap()
    ap["vb2_col"] = nc.dram_tensor("vb2_col", [128, 1], F32, kind="ExternalInput").ap()
    ap["kpos_cols"] = nc.dram_tensor("kpos_cols", [128, 32], F32, kind="ExternalInput").ap()
    ap["vpos_cols"] = nc.dram_tensor("vpos_cols", [128, 32], F32, kind="ExternalInput").ap()
    ap["memk_cols"] = nc.dram_tensor("memk_cols", [128, 2], F32, kind="ExternalInput").ap()
    ap["memv_rows"] = nc.dram_tensor("memv_rows", [2, 1, 129], F32, kind="ExternalInput").ap()
    ap["attn_out"] = nc.dram_tensor("attn_out", [8, 128, 4096], BF16, kind="ExternalOutput").ap()
    ap["qc_hbm"] = nc.dram_tensor("qc_hbm", [2, 4, 128, 4096], BF16, kind="Internal").ap()

    sb = {}
    alloc_attn0_bufs(nc, sb)
    ps = [nc.alloc_psum_tensor(f"ps{i}", [128, 512], F32) for i in range(8)]
    P = Prog(nc)
    with nc.Block() as block:
        P.add("vector", lambda e: e.memset(sb["neg80"][:], -MSKB), writes=["neg80"])
        P.add("vector", lambda e: e.memset(sb["negk"][:], -1000.0), writes=["negk"])
        emit_attn0(P, nc, ap, sb, ps)
        P.emit(block)
    return nc


def alloc_attn0_bufs(nc, sb):
    sb["C0"] = nc.alloc_sbuf_tensor("s_C0", [128, 4096], BF16)
    sb["S0"] = nc.alloc_sbuf_tensor("s_S0", [128, 4096], BF16)
    sb["pswapT"] = nc.alloc_sbuf_tensor("s_pswapT", [128, 128], F32)
    sb["identF"] = nc.alloc_sbuf_tensor("s_identF", [128, 128], F32)
    sb["Dfdiag4"] = nc.alloc_sbuf_tensor("s_Dfdiag4", [128, 512], BF16)
    sb["Dsdiag4"] = nc.alloc_sbuf_tensor("s_Dsdiag4", [128, 512], BF16)
    sb["Dsprev4"] = nc.alloc_sbuf_tensor("s_Dsprev4", [128, 512], BF16)
    sb["Dc"] = nc.alloc_sbuf_tensor("s_Dc", [128, 512], BF16)
    sb["E80"] = nc.alloc_sbuf_tensor("s_E80", [128, 16 * 128], BF16)
    sb["onesr"] = nc.alloc_sbuf_tensor("s_onesr", [1, 128], F32)
    sb["bgate"] = nc.alloc_sbuf_tensor("s_bgate", [1, 24], F32)
    sb["wgate"] = nc.alloc_sbuf_tensor("s_wgate", [128, 32 * 24], BF16)
    sb["gates"] = nc.alloc_sbuf_tensor("s_gates", [128, 32 * 24], BF16)
    sb["lng"] = [nc.alloc_sbuf_tensor(f"s_lng{i}", [128, 128], BF16) for i in range(2)]
    sb["lnt"] = [nc.alloc_sbuf_tensor("s_lnt0", [128, 512 * 32], BF16)]
    sb["wje"] = [nc.alloc_sbuf_tensor("s_wje0", [128, 4096], BF16)]
    sb["q_all"] = nc.alloc_sbuf_tensor("s_q_all", [128, 4 * 4096], BF16)
    sb["qcs"] = nc.alloc_sbuf_tensor("s_qcs", [128, 512], BF16)
    sb["rk"] = nc.alloc_sbuf_tensor("s_rk", [128, 4096], BF16)
    sb["kt"] = nc.alloc_sbuf_tensor("s_kt", [128, 512], F32)
    sb["v"] = nc.alloc_sbuf_tensor("s_v", [128, 4096], BF16)
    sb["vt"] = nc.alloc_sbuf_tensor("s_vt", [128, 32 * 256], BF16)
    sb["qf"] = nc.alloc_sbuf_tensor("s_qf", [128, 512], F32)
    sb["qcb"] = nc.alloc_sbuf_tensor("s_qcb", [128, 512], BF16)
    sb["t1"] = nc.alloc_sbuf_tensor("s_t1", [128, 512], F32)
    sb["csf"] = nc.alloc_sbuf_tensor("s_csf", [128, 512], F32)
    sb["kpos"] = nc.alloc_sbuf_tensor("s_kpos", [128, 32], F32)
    sb["vpos"] = nc.alloc_sbuf_tensor("s_vpos", [128, 32], F32)
    sb["posc"] = nc.alloc_sbuf_tensor("s_posc", [128, 1], F32)
    sb["kb1"] = nc.alloc_sbuf_tensor("s_kb1", [128, 16], F32)
    sb["vb1"] = nc.alloc_sbuf_tensor("s_vb1", [128, 16], F32)
    sb["kb2"] = nc.alloc_sbuf_tensor("s_kb2", [128, 1], F32)
    sb["vb2"] = nc.alloc_sbuf_tensor("s_vb2", [128, 1], F32)
    sb["memk"] = nc.alloc_sbuf_tensor("s_memk", [128, 2], F32)
    sb["memvr"] = nc.alloc_sbuf_tensor("s_memvr", [1, 258], F32)
    sb["memk_bf"] = nc.alloc_sbuf_tensor("s_memk_bf", [128, 1], BF16)
    sb["memv_bf"] = nc.alloc_sbuf_tensor("s_memv_bf", [1, 129], BF16)
    sb["cin"] = nc.alloc_sbuf_tensor("s_cin", [128, 16 * 256], BF16)
    sb["w1b"] = [nc.alloc_sbuf_tensor("s_w1b0", [128, 2048], BF16)]
    sb["w2b"] = nc.alloc_sbuf_tensor("s_w2b", [128, 2048], BF16)
    sb["h1"] = nc.alloc_sbuf_tensor("s_h1", [128, 16 * 256], BF16)
    sb["ck"] = nc.alloc_sbuf_tensor("s_ck", [128, 256], BF16)
    sb["cv"] = nc.alloc_sbuf_tensor("s_cv", [128, 256], BF16)
    sb["cvt"] = nc.alloc_sbuf_tensor("s_cvt", [128, 2 * 256], BF16)
    sb["csraw"] = nc.alloc_sbuf_tensor("s_csraw", [128, 1024], BF16)
    sb["pc"] = nc.alloc_sbuf_tensor("s_pc", [128, 1024], BF16)
    sb["pmem"] = nc.alloc_sbuf_tensor("s_pmem", [1, 512], BF16)
    sb["cpv"] = [nc.alloc_sbuf_tensor(f"s_cpv{h}", [128, 132], F32) for h in range(4)]
    sb["fpv"] = [nc.alloc_sbuf_tensor(f"s_fpv{h}", [128, 132], F32) for h in range(4)]
    sb["spvh"] = [nc.alloc_sbuf_tensor(f"s_spvh{h}", [128, 132], F32) for h in range(4)]
    sb["rsb"] = [nc.alloc_sbuf_tensor(f"s_rsb{k}", [128, 1], F32) for k in range(12)]
    sb["rs2b"] = [nc.alloc_sbuf_tensor(f"s_rs2b{k}", [128, 1], F32) for k in range(12)]
    sb["coefb"] = [nc.alloc_sbuf_tensor(f"s_coefb{k}", [128, 1], F32) for k in range(12)]
    sb["g1b"] = [nc.alloc_sbuf_tensor(f"s_g1b{k}", [128, 1], F32) for k in range(12)]
    sb["impa"] = nc.alloc_sbuf_tensor("s_impa", [128, 128], F32)
    sb["impb"] = nc.alloc_sbuf_tensor("s_impb", [128, 128], F32)
    sb["impj"] = nc.alloc_sbuf_tensor("s_impj", [128, 256], BF16)
    sb["impq"] = nc.alloc_sbuf_tensor("s_impq", [128, 256], BF16)
    sb["im"] = nc.alloc_sbuf_tensor("s_im", [128, 1], F32)
    sb["imn"] = nc.alloc_sbuf_tensor("s_imn", [128, 1], F32)
    sb["uu"] = nc.alloc_sbuf_tensor("s_uu", [128, 256], F32)
    sb["zz"] = nc.alloc_sbuf_tensor("s_zz", [128, 1], F32)
    sb["sen"] = nc.alloc_sbuf_tensor("s_sen", [128, 1], F32)
    sb["m8"] = nc.alloc_sbuf_tensor("s_m8", [128, 8], F32)
    sb["t4b"] = nc.alloc_sbuf_tensor("s_t4b", [128, 1], F32)
    sb["selq"] = nc.alloc_sbuf_tensor("s_selq", [128, 256], BF16)
    sb["selq2"] = nc.alloc_sbuf_tensor("s_selq2", [128, 256], BF16)
    sb["selT"] = nc.alloc_sbuf_tensor("s_selT", [128, 2 * 512], BF16)
    sb["pp"] = [nc.alloc_sbuf_tensor(f"s_pp{i}", [128, 512], BF16) for i in range(2)]
    sb["rs"] = nc.alloc_sbuf_tensor("s_rs", [128, 1], F32)
    sb["rs2"] = nc.alloc_sbuf_tensor("s_rs2", [128, 1], F32)
    sb["coef"] = nc.alloc_sbuf_tensor("s_coef", [128, 1], F32)
    sb["att"] = nc.alloc_sbuf_tensor("s_att", [128, 128], F32)
    sb["attb"] = nc.alloc_sbuf_tensor("s_attb", [128, 128], BF16)
    sb["attT"] = nc.alloc_sbuf_tensor("s_attT", [128, 128], BF16)
    sb["neg80"] = nc.alloc_sbuf_tensor("s_neg80", [128, 1], F32)
    sb["negk"] = nc.alloc_sbuf_tensor("s_negk", [128, 1], F32)


def host_attn0_consts():
    C = {}
    inv = 1.0 / 10000.0 ** (np.arange(0, 128, 2, dtype=np.float64) / 128)
    pos = np.arange(4096, dtype=np.float64)
    ang = pos[:, None] * inv[None, :]
    cosv = np.cos(ang); sinv = np.sin(ang)
    C0 = np.zeros((128, 4096), np.float64)
    S0 = np.zeros((128, 4096), np.float64)
    C0[0::2, :] = cosv.T; C0[1::2, :] = cosv.T
    S0[0::2, :] = -sinv.T; S0[1::2, :] = sinv.T
    C["C0"] = C0.astype(ml_dtypes.bfloat16)
    C["S0"] = S0.astype(ml_dtypes.bfloat16)
    pswap = np.zeros((128, 128), np.float32)
    for d in range(0, 128, 2):
        pswap[d, d + 1] = 1.0
        pswap[d + 1, d] = 1.0
    C["pswapT"] = np.ascontiguousarray(pswap.T)
    C["identF"] = np.eye(128, dtype=np.float32)
    kk = np.arange(128)[:, None]
    rr = np.arange(128)[None, :]
    fd = np.where((kk // 16 == rr // 16) & (kk % 16 <= rr % 16), PEN, 0.0).astype(np.float32)
    C["Dfdiag4"] = np.ascontiguousarray(np.tile(fd, (1, 4)).astype(ml_dtypes.bfloat16))
    dq = rr - kk
    sd = np.where((dq >= 0) & (dq <= 64), PEN, 0.0).astype(np.float32)
    C["Dsdiag4"] = np.ascontiguousarray(np.tile(sd, (1, 4)).astype(ml_dtypes.bfloat16))
    dqp = (rr + 128) - kk
    sp = np.where((dqp >= 0) & (dqp <= 64), PEN, 0.0).astype(np.float32)
    C["Dsprev4"] = np.ascontiguousarray(np.tile(sp, (1, 4)).astype(ml_dtypes.bfloat16))
    dc = np.zeros((128, 64 * 512), np.float32)
    for i in range(32):
        for jt in range(2):
            j = 128 * jt + np.arange(128)
            q = 128 * i + np.arange(128)
            allowed = (16 * (j[:, None] + 1) - 1) < q[None, :]
            blk = np.where(allowed, 0.0, -30000.0).astype(np.float32)
            dc[:, 512 * (2 * i + jt):512 * (2 * i + jt) + 512] = np.tile(blk, (1, 4))
    C["Dc"] = np.ascontiguousarray(
        dc.reshape(128, 64, 512).transpose(1, 0, 2).astype(ml_dtypes.bfloat16))
    e80 = np.zeros((16, 128, 128), np.float32)
    for v in range(16):
        for j in range(8):
            e80[v, 8 * v + j, 16 * j:16 * (j + 1)] = PEN
    C["E80"] = e80.astype(ml_dtypes.bfloat16)
    C["ones_row"] = np.ones((1, 128), np.float32)
    return C


def host_attn0_weights(inp, c):
    b, q4 = c // 4, c % 4
    kvhs = [2 * q4, 2 * q4 + 1]
    wqkv = np.asarray(inp["L0_attn_w_qkv"], np.float32)
    cols = []
    for kvh in kvhs:
        for g in range(4):
            h = 4 * kvh + g
            cols.append(wqkv[128 * h:128 * (h + 1)])
    d = {}
    blocks = []
    for pi, kvh in enumerate(kvhs):
        for g in range(4):
            h = 4 * kvh + g
            blocks.append(wqkv[128 * h:128 * (h + 1)])
        blocks.append(wqkv[4096 + 128 * kvh:4096 + 128 * (kvh + 1)])
        blocks.append(wqkv[5120 + 128 * kvh:5120 + 128 * (kvh + 1)])
    d["wqkvT_sl"] = np.ascontiguousarray(np.concatenate([blk.T for blk in blocks], axis=1))
    wg = np.asarray(inp["L0_attn_w_gate"], np.float32)
    gcols = []
    for kvh in kvhs:
        for g in range(4):
            h = 4 * kvh + g
            gcols.append(wg[3 * h:3 * (h + 1)])
    d["wgateT_sl"] = np.ascontiguousarray(np.concatenate([gc.T for gc in gcols], axis=1))
    bg = np.asarray(inp["L0_attn_b_gate"], np.float32)
    d["bgate_row"] = np.ascontiguousarray(
        np.concatenate([bg[3 * (4 * kvh + g):3 * (4 * kvh + g + 1)]
                        for kvh in kvhs for g in range(4)])[None, :])
    kpos = np.asarray(inp["L0_attn_k_pos"], np.float32)
    vpos = np.asarray(inp["L0_attn_v_pos"], np.float32)
    d["kpos_cols"] = np.ascontiguousarray(
        np.concatenate([kpos[kvh].T for kvh in kvhs], axis=1))
    d["vpos_cols"] = np.ascontiguousarray(
        np.concatenate([vpos[kvh].T for kvh in kvhs], axis=1))
    mem = np.asarray(inp["L0_attn_mem_kv"], np.float32)
    d["memk_cols"] = np.ascontiguousarray(
        np.stack([mem[0, kvh, 0] for kvh in kvhs], axis=1))
    rows = np.zeros((2, 1, 129), np.float32)
    for pi, kvh in enumerate(kvhs):
        rows[pi, 0, :128] = mem[1, kvh, 0]
        rows[pi, 0, 128] = 1.0
    d["memv_rows"] = rows
    d["kb1_cols"] = np.ascontiguousarray(np.asarray(inp["L0_attn_k_b1"], np.float32).reshape(16, 128).T)
    d["vb1_cols"] = np.ascontiguousarray(np.asarray(inp["L0_attn_v_b1"], np.float32).reshape(16, 128).T)
    d["kb2_col"] = np.asarray(inp["L0_attn_k_b2"], np.float32).reshape(128, 1).copy()
    d["vb2_col"] = np.asarray(inp["L0_attn_v_b2"], np.float32).reshape(128, 1).copy()
    return d


MEGA_CAST_SPECS = [s for s in CAST_SPECS if s[0] not in ("woT0_sl", "woT1_sl")]


def emit_wo_partial(P, nc, ap, sb, ps):
    for k in range(8):
        P.dma("sync", sb["ash"][:, 4096 * k:4096 * (k + 1)], ap["attn_sh"][k],
              writes=[("ash", k)], stream=f"at{k % 2}")
    for ot in range(32):
        par = ot % 2
        P.dma("scalar", sb["wop"][par][:].rearrange("p (k o) -> p k o", o=128),
              ap["wo_t"][ot].rearrange("(k p) o -> p k o", p=128),
              writes=[("wop", par)], stream=f"wf{par}")
        for qt in range(4):
            for h in range(2):
                bi = par * 4 + 2 * (qt % 2) + h
                bank = ps[bi]
                for k in range(8):
                    P.add("tensor", lambda e, k=k, qt=qt, h=h, bank=bank, par=par: e.matmul(
                        bank.ap()[:, 0:512],
                        sb["wop"][par][:, 128 * k:128 * (k + 1)],
                        sb["ash"][:, 4096 * k + 1024 * qt + 512 * h:
                                  4096 * k + 1024 * qt + 512 * h + 512],
                        start=(k == 0), stop=(k == 7)),
                        reads=[("wop", par), ("ash", k), ("ps", bi)], writes=[("ps", bi)])
                pwi = 2 * (qt % 2) + h
                P.add("scalar", lambda e, bank=bank, pwi=pwi: e.copy(
                    out=sb["pw"][pwi][:], in_=bank.ap()[:, 0:512]),
                    reads=[("ps", bi)], writes=[("pw", pwi), ("ps", bi)])
                P.dma("gpsimd", ap["prs"][qt, ot, :, 512 * h:512 * (h + 1)], sb["pw"][pwi][:],
                      reads=[("pw", pwi)], writes=[("prs", qt, ot, h)], stream=f"pst{par}")


def alloc_wo_bufs(nc, sb):
    sb["ash"] = nc.alloc_sbuf_tensor("s_ash", [128, 8 * 4096], BF16)
    sb["wop"] = [nc.alloc_sbuf_tensor(f"s_wop{i}", [128, 1024], BF16) for i in range(2)]
    sb["pw"] = [nc.alloc_sbuf_tensor(f"s_pw{i}", [128, 512], F32) for i in range(4)]


def emit_addres(P, nc, ap, sb, ps):
    for ot in range(32):
        par = ot % 2
        P.dma("sync", sb["xo"][par][:], ap["x_in"][ot], writes=[("xo", par)], stream=f"xo{par}")
        P.dma("scalar", sb["dl"][par][:], ap["delta"][ot],
              reads=tuple(ap.get("dl_dep", ())),
              writes=[("dl", par)], stream=f"dl{par}")
        P.add("vector", lambda e, par=par: e.tensor_add(
            out=sb["xo"][par][:], in0=sb["xo"][par][:], in1=sb["dl"][par][:]),
            reads=[("xo", par), ("dl", par)], writes=[("xo", par)])
        P.dma("gpsimd", ap["x_out"][ot], sb["xo"][par][:],
              reads=[("xo", par)], writes=[("xout", ot)], stream=f"xst{par}")


def alloc_addres_bufs(nc, sb):
    sb["xo"] = [nc.alloc_sbuf_tensor(f"s_xo{i}", [128, 1024], F32) for i in range(2)]
    sb["dl"] = [nc.alloc_sbuf_tensor(f"s_dl{i}", [128, 1024], F32) for i in range(2)]


def _mk_lnf(v4):
    def f(dt, c0, c1):
        j = c0 // 1024
        return v4[j, dt, :, c0 - 1024 * j:c1 - 1024 * j]
    return f


def build_mega():
    nc = bass.Bass("TRN2", target_bir_lowering=False)

    def ein(n, s, d):
        return nc.dram_tensor(n, s, d, kind="ExternalInput").ap()

    def eout(n, s, d):
        return nc.dram_tensor(n, s, d, kind="ExternalOutput").ap()

    def idram(n, s, d):
        return nc.dram_tensor(n, s, d, kind="Internal")

    ap = {}
    ap["qw"] = ein("qw", [8, 512, 128], I32)
    ap["lut4"] = ein("lut4", [512, 16], F32)
    ap["lut8"] = ein("lut8", [512, 256], F32)
    ap["bitmask"] = ein("bitmask", [128, 4096], I32)
    ap["ident_bf"] = ein("ident_bf", [128, 128], BF16)
    for (iname, rows, cols, oname) in MEGA_CAST_SPECS:
        ap[iname] = ein(iname, [rows, cols], F32)
    ap["wgateT_sl"] = ein("wgateT_sl", [4096, 24], F32)
    ap["A_er"] = ein("A_er", [256, 4096], F32)
    ap["Bt_sl"] = ein("Bt_sl", [256, 512], F32)
    ap["probs_rep"] = ein("probs_rep", [2, 128, 1], F32)
    ap["WT_sl"] = ein("WT_sl", [4096, 512], F32)
    ap["woP0"] = ein("woP0", [4, 1024, 1024], F32)
    ap["woP1"] = ein("woP1", [4, 1024, 1024], F32)
    ap["x_in0"] = ein("x_in0", [32, 128, 1024], F32)
    ap["bias0"] = ein("bias0", [128, 32], F32)
    ap["bias1"] = ein("bias1", [128, 32], F32)
    ap["ones_col"] = ein("ones_col", [128, 1], F32)
    ap["ones_row"] = ein("ones_row", [1, 128], F32)
    ap["C0"] = ein("C0", [128, 4096], BF16)
    ap["S0"] = ein("S0", [128, 4096], BF16)
    ap["pswapT"] = ein("pswapT", [128, 128], F32)
    ap["identF"] = ein("identF", [128, 128], F32)
    ap["Dfdiag4"] = ein("Dfdiag4", [128, 512], BF16)
    ap["Dsdiag4"] = ein("Dsdiag4", [128, 512], BF16)
    ap["Dsprev4"] = ein("Dsprev4", [128, 512], BF16)
    ap["Dc"] = ein("Dc", [64, 128, 512], BF16)
    ap["E80"] = ein("E80", [16, 128, 128], BF16)
    ap["bgate_row"] = ein("bgate_row", [1, 24], F32)
    ap["kb1_cols"] = ein("kb1_cols", [128, 16], F32)
    ap["vb1_cols"] = ein("vb1_cols", [128, 16], F32)
    ap["kb2_col"] = ein("kb2_col", [128, 1], F32)
    ap["vb2_col"] = ein("vb2_col", [128, 1], F32)
    ap["kpos_cols"] = ein("kpos_cols", [128, 32], F32)
    ap["vpos_cols"] = ein("vpos_cols", [128, 32], F32)
    ap["memk_cols"] = ein("memk_cols", [128, 2], F32)
    ap["memv_rows"] = ein("memv_rows", [2, 1, 129], F32)
    ap["C"] = ein("C", [128, 4096], BF16)
    ap["S"] = ein("S", [128, 4096], BF16)
    ap["protT"] = ein("protT", [128, 128], F32)
    ap["Ddiag4"] = ein("Ddiag4", [128, 512], F32)
    ap["Dprev4"] = ein("Dprev4", [128, 512], F32)
    ap["DcausQK"] = ein("DcausQK", [4, 128, 512], F32)
    ap["recent4"] = ein("recent4", [128, 512], BF16)
    ap["E01"] = ein("E01", [16, 128, 128], BF16)
    ap["init01"] = ein("init01", [128, 256], BF16)
    ap["zero512"] = ein("zero512", [1, 512], BF16)
    ap["x_out"] = eout("x_out", [32, 128, 1024], F32)

    cc_pairs = {}
    shard3 = {}
    full3 = {}
    for (nm, a, b) in (("wl", 4, 4096), ("wh", 4, 4096), ("weff", 4, 4096),
                       ("kw1", 2, 2048), ("vw1", 2, 2048)):
        sh = idram(nm + "_shd", [a * b, 128], BF16)
        fu = idram(nm + "_ful", [8 * a * b, 128], BF16)
        cc_pairs[nm] = (sh, fu)
        shard3[nm] = sh.ap().rearrange("(a b) c -> a b c", b=b)
        full3[nm] = fu.ap().rearrange("(a b) c -> a b c", b=b)
    kw2_bf = idram("kw2_bfT", [1, 2048, 128], BF16).ap()
    vw2_bf = idram("vw2_bfT", [1, 2048, 128], BF16).ap()
    wqkv_bf = idram("wqkv_bfT", [12, 4096, 128], BF16).ap()
    wgate_bf = idram("wgate_bfT", [4096, 24], BF16).ap()
    wq_bf = idram("wq_bfT", [8, 4096, 128], BF16).ap()
    wk_bf = idram("wk_bfT", [2, 4096, 128], BF16).ap()
    wv_bf = idram("wv_bfT", [2, 4096, 128], BF16).ap()
    wo_t = [idram(f"wo_t{l}", [32, 1024, 128], BF16).ap() for l in range(2)]
    qc_hbm = idram("qc_hbm", [2, 4, 128, 4096], BF16).ap()
    xa = idram("xa", [32, 128, 1024], F32).ap()
    xb = idram("xb", [32, 128, 1024], F32).ap()
    lnsh = [idram(f"lnsh{l}", [4096, 1024], BF16) for l in range(2)]
    lng = [idram(f"lng{l}", [16384, 1024], BF16) for l in range(2)]
    lnsh3 = [t.ap().rearrange("(a p) t -> a p t", p=128) for t in lnsh]
    lng4 = [t.ap().rearrange("(r a p) t -> r a p t", a=32, p=128) for t in lng]
    atsh = [idram(f"atsh{l}", [8, 128, 4096], BF16).ap() for l in range(2)]
    prs = [idram(f"prs{l}", [16384, 1024], F32) for l in range(2)]
    prs4 = [t.ap().rearrange("(q a p) t -> q a p t", a=32, p=128) for t in prs]
    dl = [idram(f"dl{l}", [4096, 1024], F32) for l in range(2)]
    dl3 = [t.ap().rearrange("(a p) t -> a p t", p=128) for t in dl]

    ps = [nc.alloc_psum_tensor(f"ps{i}", [128, 512], F32) for i in range(8)]
    arena = Arena(nc, nc.sbuf_base, nc.sbuf_top)
    P = Prog(nc)
    groups8 = [list(range(8))]
    groups4 = [[0, 1, 2, 3], [4, 5, 6, 7]]

    def cc_op(kind, alu, groups, in_t, out_t, wkey):
        P.add("gpsimd", lambda e, in_t=in_t, out_t=out_t: e.collective_compute(
            kind, alu, replica_groups=groups,
            ins=[in_t.ap().opt()], outs=[out_t.ap().opt()]),
            reads=(), writes=[wkey], kind="cc")

    def base_ap(**kw):
        d = dict(ap)
        d.update(kw)
        return d

    def ffn_phase(prefix, kind, x_in, x_out, wdep=()):
        arena.reset()
        an = _ArenaNc(arena, prefix)
        sb = {}
        alloc_ln_bufs(an, sb)
        alloc_ffn_bufs(an, sb, kind == "l0")
        if kind == "l0":
            apd = base_ap(x_in=x_in, x_out=x_out, bias=ap["bias0"],
                          wl_t=full3["wl"], wh_t=full3["wh"], wdep=wdep)
        else:
            apd = base_ap(x_in=x_in, x_out=x_out, bias=ap["bias1"],
                          wf_t=full3["weff"], wdep=wdep)
        ln_load_consts(P, apd, sb)
        emit_ffn(P, nc, apd, sb, ps, kind)
        P.barrier()

    with nc.Block() as block:
        arena.reset()
        an = _ArenaNc(arena, "p0_")
        sb = {}
        alloc_prep_bufs(an, sb)
        ap_prep = base_ap(wl_sh=shard3["wl"], wh_sh=shard3["wh"],
                          weff_sh=shard3["weff"], kw1_sh=shard3["kw1"],
                          vw1_sh=shard3["vw1"], kw2_bf=kw2_bf, vw2_bf=vw2_bf,
                          wqkv_bf=wqkv_bf, wgate_bf=wgate_bf, wq_bf=wq_bf,
                          wk_bf=wk_bf, wv_bf=wv_bf)
        wo_casts = []
        for l in range(2):
            for cb in range(4):
                wo_casts.append((ap[f"woP{l}"][cb], wo_t[l], 1024, 1024,
                                 8 * cb, f"wo_t{l}"))
        emit_dequant(P, nc, ap_prep, sb, ps)
        emit_weight_prep(P, nc, ap_prep, sb, ps, cast_specs=MEGA_CAST_SPECS,
                         extra_casts=wo_casts)
        P.barrier()
        for nm in ("wl", "wh", "weff", "kw1", "vw1"):
            cc_op("AllGather", ALU.bypass, groups8, cc_pairs[nm][0],
                  cc_pairs[nm][1], nm + "_ful")

        for l, (x_pre, x_mid) in enumerate(((ap["x_in0"], xa), (xa, xb))):
            kind = "l0" if l == 0 else "l1"
            ffn_phase(f"f{l}a_", kind, x_pre, x_mid,
                      wdep=("wl_ful", "wh_ful") if l == 0 else ("weff_ful",))
            arena.reset()
            an = _ArenaNc(arena, f"ln{l}_")
            sb = {}
            alloc_ln_bufs(an, sb)
            apd = base_ap(x_in=x_mid)
            ln_load_consts(P, apd, sb)
            emit_ln(P, nc, apd, sb, ps, compute_e=False, ln_out_hbm=lnsh3[l])
            P.barrier()
            cc_op("AllGather", ALU.bypass, groups4, lnsh[l], lng[l], ("lng", l))
            arena.reset()
            an = _ArenaNc(arena, f"a{l}_")
            sb = {}
            if l == 0:
                alloc_attn0_bufs(an, sb)
                apd = base_ap(lnf=_mk_lnf(lng4[l]), attn_out=atsh[l],
                              qc_hbm=qc_hbm, kw1_sh=full3["kw1"],
                              vw1_sh=full3["vw1"], kw2_bf=kw2_bf,
                              vw2_bf=vw2_bf, wqkv_bf=wqkv_bf,
                              wgate_bf=wgate_bf, lnf_dep=(("lng", l),))
                P.add("vector", lambda e, sb=sb: e.memset(sb["neg80"][:], -MSKB),
                      writes=["neg80"])
                P.add("vector", lambda e, sb=sb: e.memset(sb["negk"][:], -1000.0),
                      writes=["negk"])
                emit_attn0(P, nc, apd, sb, ps)
            else:
                alloc_attn1_bufs(an, sb)
                apd = base_ap(lnf=_mk_lnf(lng4[l]), attn_out=atsh[l],
                              wq_bf=wq_bf, wk_bf=wk_bf, wv_bf=wv_bf,
                              lnf_dep=(("lng", l),))
                P.dma("sync", sb["zero512"][0:1, :], apd["zero512"],
                      writes=["zero512"], stream="init")
                P.add("vector", lambda e, sb=sb: e.memset(sb["neg80"][:], -MSKB),
                      writes=["neg80"])
                P.add("vector", lambda e, sb=sb: e.memset(sb["zero1"][:], 0.0),
                      writes=["zero1"])
                emit_attn1(P, nc, apd, sb, ps)
            P.barrier()
            arena.reset()
            an = _ArenaNc(arena, f"w{l}_")
            sb = {}
            alloc_wo_bufs(an, sb)
            apd = base_ap(attn_sh=atsh[l], wo_t=wo_t[l], prs=prs4[l])
            emit_wo_partial(P, nc, apd, sb, ps)
            P.barrier()
            cc_op("ReduceScatter", ALU.add, groups4, prs[l], dl[l], ("dl", l))
            arena.reset()
            an = _ArenaNc(arena, f"r{l}_")
            sb = {}
            alloc_addres_bufs(an, sb)
            x_post = xb if l == 0 else xa
            apd = base_ap(x_in=x_mid, delta=dl3[l], x_out=x_post,
                          dl_dep=(("dl", l),))
            emit_addres(P, nc, apd, sb, ps)
            P.barrier()
            x_fin = xa if l == 0 else ap["x_out"]
            ffn_phase(f"f{l}b_", kind, x_post, x_fin)
        P.emit(block)
    return nc


def host_mega_inputs(inputs, consts, ac0, ac1, c):
    b, q4 = c // 4, c % 4
    m = {}
    qw = np.asarray(inputs["L0_ffn_qweight"])[:, 512 * c:512 * (c + 1), :]
    m["qw"] = np.ascontiguousarray(qw)
    m["lut4"] = _f32(np.asarray(inputs["L0_ffn_lut4"], np.float32)[512 * c:512 * (c + 1)])
    m["lut8"] = _f32(np.asarray(inputs["L0_ffn_lut8"], np.float32)[512 * c:512 * (c + 1)])
    m["bitmask"] = consts["bitmask"]
    m["ident_bf"] = consts["ident_bf"]
    w0 = host_attn0_weights(inputs, c)
    w1 = host_attn1_weights(inputs, c)
    m["wqkvT_sl"] = w0["wqkvT_sl"]
    m["wgateT_sl"] = w0["wgateT_sl"]
    m["wqT_sl"] = w1["wqT_sl"]
    m["wkT_sl"] = w1["wkT_sl"]
    m["wvT_sl"] = w1["wvT_sl"]
    m["kw1T_sl"] = _f32(np.asarray(inputs["L0_attn_k_w1"], np.float32).T[:, 256 * c:256 * (c + 1)])
    m["vw1T_sl"] = _f32(np.asarray(inputs["L0_attn_v_w1"], np.float32).T[:, 256 * c:256 * (c + 1)])
    m["kw2T"] = _f32(np.asarray(inputs["L0_attn_k_w2"], np.float32).T)
    m["vw2T"] = _f32(np.asarray(inputs["L0_attn_v_w2"], np.float32).T)
    m["A_er"] = _f32(np.asarray(inputs["L1_ffn_A"], np.float32).reshape(256, 4096))
    m["Bt_sl"] = _f32(np.asarray(inputs["L1_ffn_B"], np.float32)
                      .transpose(0, 2, 1).reshape(256, 4096)[:, 512 * c:512 * (c + 1)])
    pr = np.asarray(inputs["L1_ffn_probs"], np.float32)
    prep = np.zeros((2, 128, 1), np.float32)
    for t in range(2):
        prep[t, :, 0] = pr[(t * 128 + np.arange(128)) // 16]
    m["probs_rep"] = prep
    m["WT_sl"] = _f32(np.asarray(inputs["L1_ffn_W"], np.float32).T[:, 512 * c:512 * (c + 1)])
    for l, wname in ((0, "L0_attn_w_o"), (1, "L1_attn_wo")):
        woT = np.asarray(inputs[wname], np.float32).T[1024 * q4:1024 * (q4 + 1), :]
        m[f"woP{l}"] = np.ascontiguousarray(
            woT.reshape(1024, 4, 1024).transpose(1, 0, 2))
    x0 = np.asarray(inputs["L0_attn_hidden_states"], np.float32)
    m["x_in0"] = _fm(x0[b], q4)
    m["bias0"] = np.ascontiguousarray(np.asarray(inputs["L0_ffn_bias"], np.float32).reshape(32, 128).T)
    m["bias1"] = np.ascontiguousarray(np.asarray(inputs["L1_ffn_b"], np.float32).reshape(32, 128).T)
    m["ones_col"] = consts["ones_col"]
    m["ones_row"] = consts["ones_row"]
    for k in ("C0", "S0", "pswapT", "identF", "Dfdiag4", "Dsdiag4", "Dsprev4",
              "Dc", "E80"):
        m[k] = ac0[k]
    for k in ("bgate_row", "kb1_cols", "vb1_cols", "kb2_col", "vb2_col",
              "kpos_cols", "vpos_cols", "memk_cols", "memv_rows"):
        m[k] = w0[k]
    for k in ("C", "S", "protT", "Ddiag4", "Dprev4", "DcausQK", "recent4",
              "E01", "init01", "zero512"):
        m[k] = ac1[k]
    return m


def _fm(xb, q4):
    return np.ascontiguousarray(
        xb[1024 * q4:1024 * (q4 + 1), :].T.reshape(32, 128, 1024))


def host_p0full_inputs(inp, consts, c):
    m = {}
    qw = np.asarray(inp["L0_ffn_qweight"])[:, 512 * c:512 * (c + 1), :]
    m["qw"] = np.ascontiguousarray(qw)
    m["lut4"] = _f32(np.asarray(inp["L0_ffn_lut4"], np.float32)[512 * c:512 * (c + 1)])
    m["lut8"] = _f32(np.asarray(inp["L0_ffn_lut8"], np.float32)[512 * c:512 * (c + 1)])
    m["bitmask"] = consts["bitmask"]
    m["ident_bf"] = consts["ident_bf"]
    w0 = host_attn0_weights(inp, c)
    w1 = host_attn1_weights(inp, c)
    m["wqkvT_sl"] = w0["wqkvT_sl"]
    m["wgateT_sl"] = w0["wgateT_sl"]
    m["wqT_sl"] = w1["wqT_sl"]
    m["wkT_sl"] = w1["wkT_sl"]
    m["wvT_sl"] = w1["wvT_sl"]
    m["woT0_sl"] = _f32(np.asarray(inp["L0_attn_w_o"], np.float32).T[:, 512 * c:512 * (c + 1)])
    m["woT1_sl"] = _f32(np.asarray(inp["L1_attn_wo"], np.float32).T[:, 512 * c:512 * (c + 1)])
    m["kw1T_sl"] = _f32(np.asarray(inp["L0_attn_k_w1"], np.float32).T[:, 256 * c:256 * (c + 1)])
    m["vw1T_sl"] = _f32(np.asarray(inp["L0_attn_v_w1"], np.float32).T[:, 256 * c:256 * (c + 1)])
    m["kw2T"] = _f32(np.asarray(inp["L0_attn_k_w2"], np.float32).T)
    m["vw2T"] = _f32(np.asarray(inp["L0_attn_v_w2"], np.float32).T)
    m["A_er"] = _f32(np.asarray(inp["L1_ffn_A"], np.float32).reshape(256, 4096))
    m["Bt_sl"] = _f32(np.asarray(inp["L1_ffn_B"], np.float32)
                      .transpose(0, 2, 1).reshape(256, 4096)[:, 512 * c:512 * (c + 1)])
    pr = np.asarray(inp["L1_ffn_probs"], np.float32)
    prep = np.zeros((2, 128, 1), np.float32)
    for t in range(2):
        prep[t, :, 0] = pr[(t * 128 + np.arange(128)) // 16]
    m["probs_rep"] = prep
    m["WT_sl"] = _f32(np.asarray(inp["L1_ffn_W"], np.float32).T[:, 512 * c:512 * (c + 1)])
    return m


def kernel(**inputs):
    consts = host_constants()
    ac0 = host_attn0_consts()
    ac1 = host_attn1_consts()
    nc = build_mega()
    maps = [host_mega_inputs(inputs, consts, ac0, ac1, c) for c in range(8)]
    res = run_bass_kernel_spmd(nc, maps, core_ids=list(range(8)))
    out = np.empty((2, 4096, 4096), np.float32)
    for c in range(8):
        b, q4 = c // 4, c % 4
        xs = np.asarray(res.results[c]["x_out"])
        out[b, 1024 * q4:1024 * (q4 + 1), :] = xs.reshape(4096, 1024).T
    return out


def kernel_unfused(**inputs):
    consts = host_constants()
    nc = build_p0full()
    res = run_bass_kernel_spmd(
        nc, [host_p0full_inputs(inputs, consts, c) for c in range(8)],
        core_ids=list(range(8)))
    W = {}
    for nm in ("wl_sh", "wh_sh", "wo0_sh", "wo1_sh", "weff_sh", "kw1_sh", "vw1_sh"):
        W[nm] = np.ascontiguousarray(
            np.concatenate([np.asarray(res.results[c][nm]) for c in range(8)], axis=0))
    for nm in ("kw2_bf", "vw2_bf"):
        W[nm] = np.ascontiguousarray(np.asarray(res.results[0][nm]))
    for nm in ("wqkv_bf", "wgate_bf", "wq_bf", "wk_bf", "wv_bf"):
        W[nm] = [np.ascontiguousarray(np.asarray(res.results[c][nm])) for c in range(8)]

    bias0 = np.ascontiguousarray(np.asarray(inputs["L0_ffn_bias"], np.float32).reshape(32, 128).T)
    bias1 = np.ascontiguousarray(np.asarray(inputs["L1_ffn_b"], np.float32).reshape(32, 128).T)

    def run_ffn(kind, xs):
        nc = build_ffn(kind)
        maps = []
        for c in range(8):
            m = {"x_in": xs[c], "ones_col": consts["ones_col"],
                 "ones_row": consts["ones_row"]}
            if kind == "l0":
                m["bias"] = bias0
                m["wl_t"] = W["wl_sh"]
                m["wh_t"] = W["wh_sh"]
            else:
                m["bias"] = bias1
                m["wf_t"] = W["weff_sh"]
            maps.append(m)
        r = run_bass_kernel_spmd(nc, maps, core_ids=list(range(8)))
        return [np.ascontiguousarray(np.asarray(r.results[c]["x_out"])) for c in range(8)]

    def run_lnout(xs):
        nc = build_lnout()
        maps = [{"x_in": xs[c], "ones_col": consts["ones_col"],
                 "ones_row": consts["ones_row"]} for c in range(8)]
        r = run_bass_kernel_spmd(nc, maps, core_ids=list(range(8)))
        ln = [np.asarray(r.results[c]["ln_out"]) for c in range(8)]
        return [np.ascontiguousarray(np.concatenate([ln[4 * b + j] for j in range(4)], axis=2))
                for b in range(2)]

    def run_wo(xs, att_by_batch, wo_t):
        nc = build_wo()
        maps = []
        for c in range(8):
            b, q4 = c // 4, c % 4
            sl = slice(1024 * q4, 1024 * (q4 + 1))
            maps.append({"x_in": xs[c], "wo_t": wo_t,
                         "attn_in": np.ascontiguousarray(att_by_batch[b][:, :, sl])})
        r = run_bass_kernel_spmd(nc, maps, core_ids=list(range(8)))
        return [np.ascontiguousarray(np.asarray(r.results[c]["x_out"])) for c in range(8)]

    x0 = np.asarray(inputs["L0_attn_hidden_states"], np.float32)
    xs = [_fm(x0[c // 4], c % 4) for c in range(8)]

    xs = run_ffn("l0", xs)
    ln_b = run_lnout(xs)
    ac0 = host_attn0_consts()
    nc = build_attn0()
    maps = []
    for c in range(8):
        b = c // 4
        w0 = host_attn0_weights(inputs, c)
        m = {"ln_full": ln_b[b], "wqkv_bf": W["wqkv_bf"][c], "wgate_bf": W["wgate_bf"][c],
             "kw1_sh": W["kw1_sh"], "vw1_sh": W["vw1_sh"],
             "kw2_bf": W["kw2_bf"], "vw2_bf": W["vw2_bf"]}
        for kk in ("bgate_row", "kb1_cols", "vb1_cols", "kb2_col", "vb2_col",
                   "kpos_cols", "vpos_cols", "memk_cols", "memv_rows"):
            m[kk] = w0[kk]
        m.update(ac0)
        maps.append(m)
    r = run_bass_kernel_spmd(nc, maps, core_ids=list(range(8)))
    att_b = [np.concatenate([np.asarray(r.results[4 * b + j]["attn_out"]) for j in range(4)],
                            axis=0) for b in range(2)]
    xs = run_wo(xs, att_b, W["wo0_sh"])
    xs = run_ffn("l0", xs)

    xs = run_ffn("l1", xs)
    ln_b = run_lnout(xs)
    ac1 = host_attn1_consts()
    nc = build_attn1()
    maps = []
    for c in range(8):
        b = c // 4
        m = {"ln_full": ln_b[b], "wq_bf": W["wq_bf"][c], "wk_bf": W["wk_bf"][c],
             "wv_bf": W["wv_bf"][c]}
        m.update(ac1)
        maps.append(m)
    r = run_bass_kernel_spmd(nc, maps, core_ids=list(range(8)))
    att_b = [np.concatenate([np.asarray(r.results[4 * b + j]["attn_out"]) for j in range(4)],
                            axis=0) for b in range(2)]
    xs = run_wo(xs, att_b, W["wo1_sh"])
    xs = run_ffn("l1", xs)

    out = np.empty((2, 4096, 4096), np.float32)
    for c in range(8):
        b, q4 = c // 4, c % 4
        out[b, 1024 * q4:1024 * (q4 + 1), :] = xs[c].reshape(4096, 1024).T
    return out


# version 04 (final)
# speedup vs baseline: 1.1844x; 1.1327x over previous
import math
import numpy as np
import ml_dtypes
import concourse.bass as bass
import concourse.mybir as mybir
from concourse.bass_utils import run_bass_kernel_spmd

F32 = mybir.dt.float32
BF16 = mybir.dt.bfloat16
I32 = mybir.dt.int32
AF = mybir.ActivationFunctionType
ALU = mybir.AluOpType
AX = mybir.AxisListType

NCORES = 8
B = 2
S = 4096
D = 4096
TPC = 1024
H = 32
KVH = 8
DH = 128
G = 4
PAIRS = 2
CB = 16
SB = 16
NSEL = 4
WIN = 64
NC_CHUNKS = S // CB
CDIM = CB * DH
SCALE = DH ** -0.5
L1_CHUNK = 16
L1_TOPK = 8
NEGBIG = -30000.0
MSK = 80.0
EPS = 1e-5

DT_NP = {F32: np.float32, BF16: ml_dtypes.bfloat16, I32: np.int32}


ENGINES = ("sync", "vector", "scalar", "tensor", "gpsimd")
DMA_ENGINES = ("sync", "vector", "scalar", "gpsimd")


class _Op:
    __slots__ = ("eng", "fn", "reads", "writes", "kind", "sem", "val", "idx",
                 "stream", "floor")

    def __init__(self, eng, fn, reads, writes, kind, stream=None):
        self.eng = eng
        self.fn = fn
        self.reads = reads
        self.writes = writes
        self.kind = kind
        self.stream = stream
        self.sem = None
        self.val = 0
        self.floor = None


class Prog:

    def __init__(self, nc, same_engine_edges=False):
        self.nc = nc
        self.same_engine_edges = same_engine_edges
        self.ops = []
        self.last_writer = {}
        self.readers = {}
        self.counts = {}
        self.ccn = 0
        self.floor = {}

    def _sem_for(self, op):
        if op.kind == "c":
            return "e_" + op.eng, 1
        if op.kind == "d":
            assert op.stream is not None, "dma ops need an explicit stream"
            return "d:" + op.stream, 16
        self.ccn += 1
        return f"cc_{self.ccn}", 1

    def add(self, eng, fn, reads=(), writes=(), kind="c", stream=None):
        op = _Op(eng, fn, tuple(reads), tuple(writes), kind, stream)
        sem, inc = self._sem_for(op)
        self.counts[sem] = self.counts.get(sem, 0) + inc
        op.sem, op.val = sem, self.counts[sem]
        op.floor = self.floor
        op.idx = len(self.ops)
        self.ops.append(op)
        for k in op.reads:
            self.readers.setdefault(k, []).append(op)
        for k in op.writes:
            self.last_writer[k] = op
            self.readers[k] = []
            self.readers[k].append(op)
        return op

    def dma(self, eng, out, in_, reads=(), writes=(), stream=None, **kw):
        kind = kw.pop("kind", "d")
        return self.add(eng, lambda e: e.dma_start(out=out, in_=in_, **kw),
                        reads=reads, writes=writes, kind=kind, stream=stream)

    def barrier(self):
        self.floor = dict(self.counts)

    def _deps(self, op):
        need = dict(op.floor)
        def _add(src):
            if src is op:
                return
            if src.kind == "c" and src.eng == op.eng and not self.same_engine_edges:
                return
            if src.kind == "d":
                v = self.sem_prefix[src.sem][op.idx]
            else:
                v = src.val
            if v > need.get(src.sem, 0):
                need[src.sem] = v
        for k in op.reads:
            w = self.last_writer_at.get((k, op.idx))
            if w is not None:
                _add(w)
        for k in op.writes:
            for r in self.readers_at.get((k, op.idx), ()):
                _add(r)
        return need

    def emit(self, block, final_outputs_engine="gpsimd"):
        self.sem_prefix = {}
        dma_sems = {op.sem for op in self.ops if op.kind == "d"}
        for semname in dma_sems:
            pref = [0] * (len(self.ops) + 1)
            run = 0
            for i, op in enumerate(self.ops):
                pref[i] = run
                if op.sem == semname:
                    run = op.val
            pref[len(self.ops)] = run
            self.sem_prefix[semname] = pref
        self.last_writer_at = {}
        self.readers_at = {}
        lw = {}
        rd = {}
        for op in self.ops:
            for k in op.reads:
                if k in lw:
                    self.last_writer_at[(k, op.idx)] = lw[k]
            for k in op.writes:
                if k in lw:
                    self.last_writer_at.setdefault((k, op.idx), lw[k])
                self.readers_at[(k, op.idx)] = tuple(rd.get(k, ()))
            for k in op.reads:
                rd.setdefault(k, []).append(op)
            for k in op.writes:
                lw[k] = op
                rd[k] = [op]

        sems = {name: self.nc.alloc_semaphore(name) for name in self.counts}
        per_eng = {e: [] for e in ENGINES}
        for op in self.ops:
            per_eng[op.eng].append(op)
        finals = dict(self.counts)

        waited = {e: {} for e in ENGINES}

        def mk_section(eng):
            ops = per_eng[eng]

            def body(ee):
                hw = waited[eng]
                for op in ops:
                    need = self._deps(op)
                    for semname, v in sorted(need.items()):
                        if semname == "e_" + eng and not self.same_engine_edges:
                            continue
                        if hw.get(semname, 0) < v:
                            ee.wait_ge(sems[semname], v)
                            hw[semname] = v
                    inst = op.fn(ee)
                    inc = 1 if op.kind == "c" else 16
                    if op.kind == "cc":
                        inst.then_inc(sems[op.sem])
                    else:
                        inst.then_inc(sems[op.sem], inc)
                if eng == final_outputs_engine:
                    for semname, v in sorted(finals.items()):
                        if semname == "e_" + eng:
                            continue
                        if waited[eng].get(semname, 0) < v:
                            ee.wait_ge(sems[semname], v)
                            waited[eng][semname] = v
            return body

        engs_used = [e for e in ENGINES if per_eng[e] or e == final_outputs_engine]
        for e in engs_used:
            getattr(block, e)(mk_section(e))


def ceil_div(a, b):
    return -(-a) // b


DT_BYTES = {F32: 4, I32: 4, BF16: 2}


class Arena:
    def __init__(self, nc, base, limit):
        self.nc = nc
        self.base = base
        self.limit = limit
        self.off = base
        self.high = base

    def alloc(self, name, shape, dtype):
        nb = shape[-1] * DT_BYTES[dtype]
        off = (self.off + 31) & ~31
        assert off + nb <= self.limit, f"SBUF arena overflow at {name}: {off + nb} > {self.limit}"
        t = self.nc.alloc_sbuf_tensor_at(name, shape, dtype, offset=off)
        self.off = off + nb
        self.high = max(self.high, self.off)
        return t

    def reset(self):
        self.off = self.base


class _ArenaNc:

    def __init__(self, arena, prefix):
        self._arena = arena
        self._prefix = prefix

    def alloc_sbuf_tensor(self, name, shape, dtype):
        return self._arena.alloc(self._prefix + name, shape, dtype)


def _f32(x):
    return np.ascontiguousarray(x, dtype=np.float32)


def host_constants():
    C = {}
    C["ones_col"] = np.ones((128, 1), np.float32)
    C["ones_row"] = np.ones((1, 128), np.float32)
    C["ident_bf"] = np.eye(128).astype(ml_dtypes.bfloat16)
    C["bitmask"] = np.broadcast_to(
        (np.int64(1) << (np.arange(4096, dtype=np.int64) % 32)).astype(np.int64),
        (128, 4096)).astype(np.uint32).view(np.int32).copy()
    return C


def host_inputs_per_core(inp):
    cores = []
    x = np.asarray(inp["L0_attn_hidden_states"], np.float32)
    qw = np.asarray(inp["L0_ffn_qweight"])
    lut4 = _f32(inp["L0_ffn_lut4"])
    lut8 = _f32(inp["L0_ffn_lut8"])
    for c in range(NCORES):
        b, q4 = c // 4, c % 4
        d = {}
        d["x0"] = _f32(x[b, 1024 * q4:1024 * (q4 + 1), :].T)
        o0 = 512 * c
        d["qwT"] = np.ascontiguousarray(
            qw[:, o0:o0 + 512, :].transpose(0, 2, 1))
        d["lut4"] = _f32(lut4[o0:o0 + 512])
        d["lut8"] = _f32(lut8[o0:o0 + 512])
        cores.append(d)
    return cores


def emit_dequant(P, nc, ap, sb, ps):
    ot_range = range(4)
    for ot in ot_range:
        P.dma("sync", sb["lut4"][:, 16 * ot:16 * (ot + 1)], ap["lut4"][128 * ot:128 * (ot + 1), :],
              writes=[("lut4", ot)], stream="init")
        P.dma("sync", sb["lut8"][:, 256 * ot:256 * (ot + 1)], ap["lut8"][128 * ot:128 * (ot + 1), :],
              writes=[("lut8", ot)], stream="init")
    P.dma("sync", sb["bitmask"][:], ap["bitmask"], writes=["bitmask"], stream="init")
    P.dma("sync", sb["ident"][:], ap["ident_bf"], writes=["ident"], stream="init")

    for ot in ot_range:
        for p in range(8):
            P.dma("gpsimd", sb["qw"][:, 128 * p:128 * (p + 1)],
                  ap["qw"][p, 128 * ot:128 * (ot + 1), :],
                  writes=[("qw", p)], stream="qw")
        for p in range(8):
            qv = sb["qw"][:, 128 * p:128 * (p + 1)].rearrange(
                "o (w u) -> o w u", u=1).broadcast_to((128, 128, 32))
            bm = sb["bitmask"][:].rearrange("o (w v) -> o w v", v=32)
            P.add("vector", lambda e, qv=qv, bm=bm: e.tensor_tensor(
                out=sb["tmp32"][:].rearrange("o (w v) -> o w v", v=32),
                in0=qv, in1=bm, op=ALU.bitwise_and),
                reads=[("qw", p), "bitmask"], writes=["tmp32"])
            P.add("vector", lambda e: e.tensor_scalar(
                out=sb["bit"][:], in0=sb["tmp32"][:], scalar1=0.0, scalar2=None,
                op0=ALU.not_equal),
                reads=["tmp32"], writes=["bit"])
            tgt = "idx4" if p < 4 else "idx8"
            if p == 0:
                P.add("vector", lambda e: e.tensor_scalar(
                    out=sb["idx4"][:], in0=sb["bit"][:], scalar1=1.0, scalar2=None,
                    op0=ALU.mult), reads=["bit"], writes=["idx4"])
            elif p == 4:
                P.add("vector", lambda e: e.tensor_copy(out=sb["idx8"][:], in_=sb["idx4"][:]),
                      reads=["idx4"], writes=["idx8"])
                P.add("vector", lambda e, s=float(1 << p): e.scalar_tensor_tensor(
                    out=sb["idx8"][:], in0=sb["bit"][:], scalar=s, in1=sb["idx8"][:],
                    op0=ALU.mult, op1=ALU.add), reads=["bit", "idx8"], writes=["idx8"])
            else:
                P.add("vector", lambda e, s=float(1 << p), t=tgt: e.scalar_tensor_tensor(
                    out=sb[t][:], in0=sb["bit"][:], scalar=s, in1=sb[t][:],
                    op0=ALU.mult, op1=ALU.add), reads=["bit", tgt], writes=[tgt])

        for (wname, idxname, lutname, ncodes, lutw) in (
                ("wl", "idx4", "lut4", 16, 16), ("wh", "idx8", "lut8", 256, 256)):
            for ci in range(ncodes):
                lut_col = sb[lutname][:, lutw * ot + ci:lutw * ot + ci + 1]
                if ci == 0:
                    P.add("vector", lambda e, ci=ci, lc=lut_col, w=wname, ix=idxname:
                          e.tensor_scalar(out=sb[w][:], in0=sb[ix][:],
                                          scalar1=float(ci), scalar2=lc,
                                          op0=ALU.is_equal, op1=ALU.mult),
                          reads=[idxname, ("lutld", lutname, ot)] if False else [idxname, (lutname, ot)],
                          writes=[wname])
                else:
                    P.add("vector", lambda e, ci=ci, lc=lut_col, ix=idxname:
                          e.tensor_scalar(out=sb["m"][:], in0=sb[ix][:],
                                          scalar1=float(ci), scalar2=lc,
                                          op0=ALU.is_equal, op1=ALU.mult),
                          reads=[idxname, (lutname, ot)], writes=["m"])
                    P.add("vector", lambda e, w=wname: e.tensor_tensor(
                        out=sb[w][:], in0=sb["m"][:], in1=sb[w][:], op=ALU.add),
                        reads=["m", wname], writes=[wname])

        for qi, (wname, outname) in enumerate((("wl", "wl_sh"), ("wh", "wh_sh"))):
            q = "sync" if qi == 0 else "scalar"
            for it in range(32):
                P.add(q, lambda e, it=it, w=wname: e.dma_start_transpose(
                    sb["stag" + w][:, 128 * it:128 * (it + 1)],
                    sb[w][:, 128 * it:128 * (it + 1)]),
                    reads=[wname], writes=[("stag", wname, it)], kind="d",
                    stream="tp" + wname)
            P.dma(q, ap[outname][ot].rearrange("(it p) o -> p it o", p=128),
                  sb["stag" + wname][:].rearrange("p (it o) -> p it o", o=128),
                  reads=[("stag", wname, it2) for it2 in range(32)],
                  writes=[(outname, ot)], stream="st" + wname)
    return


def build_p0():
    nc = bass.Bass("TRN2", target_bir_lowering=False)
    ap = {}
    ap["qw"] = nc.dram_tensor("qw", [8, 512, 128], I32, kind="ExternalInput").ap()
    ap["lut4"] = nc.dram_tensor("lut4", [512, 16], F32, kind="ExternalInput").ap()
    ap["lut8"] = nc.dram_tensor("lut8", [512, 256], F32, kind="ExternalInput").ap()
    ap["bitmask"] = nc.dram_tensor("bitmask", [128, 4096], I32, kind="ExternalInput").ap()
    ap["ident_bf"] = nc.dram_tensor("ident_bf", [128, 128], BF16, kind="ExternalInput").ap()
    ap["wl_sh"] = nc.dram_tensor("wl_sh", [4, 4096, 128], BF16, kind="ExternalOutput").ap()
    ap["wh_sh"] = nc.dram_tensor("wh_sh", [4, 4096, 128], BF16, kind="ExternalOutput").ap()

    sb = {}
    sb["qw"] = nc.alloc_sbuf_tensor("s_qw", [128, 8 * 128], I32)
    sb["lut4"] = nc.alloc_sbuf_tensor("s_lut4", [128, 64], F32)
    sb["lut8"] = nc.alloc_sbuf_tensor("s_lut8", [128, 1024], F32)
    sb["bitmask"] = nc.alloc_sbuf_tensor("s_bitmask", [128, 4096], I32)
    sb["ident"] = nc.alloc_sbuf_tensor("s_ident", [128, 128], BF16)
    sb["tmp32"] = nc.alloc_sbuf_tensor("s_tmp32", [128, 4096], I32)
    sb["bit"] = nc.alloc_sbuf_tensor("s_bit", [128, 4096], BF16)
    sb["idx4"] = nc.alloc_sbuf_tensor("s_idx4", [128, 4096], BF16)
    sb["idx8"] = nc.alloc_sbuf_tensor("s_idx8", [128, 4096], BF16)
    sb["wl"] = nc.alloc_sbuf_tensor("s_wl", [128, 4096], BF16)
    sb["wh"] = nc.alloc_sbuf_tensor("s_wh", [128, 4096], BF16)
    sb["m"] = nc.alloc_sbuf_tensor("s_m", [128, 4096], BF16)
    sb["stagwl"] = nc.alloc_sbuf_tensor("s_stagwl", [128, 4096], BF16)
    sb["stagwh"] = nc.alloc_sbuf_tensor("s_stagwh", [128, 4096], BF16)
    ps = []

    P = Prog(nc)
    with nc.Block() as block:
        emit_dequant(P, nc, ap, sb, ps)
        P.emit(block)
    return nc


def emit_ln(P, nc, ap, sb, ps, compute_e=False, ln_out_hbm=None):
    x_in = ap["x_in"]
    for dt in range(32):
        xb = ("x", dt % 2)
        P.dma("sync", sb["xA"][dt % 2][:], x_in[dt], writes=[xb], stream=f"xA{dt % 2}")
        P.add("scalar", lambda e, dt=dt: e.square(out=sb["xsq"][dt % 2][:], in_=sb["xA"][dt % 2][:]),
              reads=[xb], writes=[("xsq", dt % 2)])
        for h in range(2):
            P.add("tensor", lambda e, dt=dt, h=h: e.matmul(
                ps[h].ap()[0:1, :], sb["ones_col"][:], sb["xA"][dt % 2][:, 512 * h:512 * (h + 1)],
                start=(dt == 0), stop=(dt == 31)),
                reads=[xb, "ones_col"], writes=[("ps", h)])
            P.add("tensor", lambda e, dt=dt, h=h: e.matmul(
                ps[2 + h].ap()[0:1, :], sb["ones_col"][:], sb["xsq"][dt % 2][:, 512 * h:512 * (h + 1)],
                start=(dt == 0), stop=(dt == 31)),
                reads=[("xsq", dt % 2), "ones_col"], writes=[("ps", 2 + h)])
    for h in range(2):
        sl = slice(512 * h, 512 * (h + 1))
        P.add("scalar", lambda e, h=h, sl=sl: e.mul(out=sb["mean"][0:1, sl], in_=ps[h].ap()[0:1, :], mul=1.0 / 4096),
              reads=[("ps", h)], writes=[("mean", h), ("ps", h)])
        P.add("scalar", lambda e, h=h, sl=sl: e.mul(out=sb["msq"][0:1, sl], in_=ps[2 + h].ap()[0:1, :], mul=1.0 / 4096),
              reads=[("ps", 2 + h)], writes=[("msq", h), ("ps", 2 + h)])
    P.add("scalar", lambda e: e.square(out=sb["m2"][0:1, :], in_=sb["mean"][0:1, :]),
          reads=[("mean", 0), ("mean", 1)], writes=["m2"])
    P.add("vector", lambda e: e.scalar_tensor_tensor(out=sb["var"][0:1, :], in0=sb["msq"][0:1, :],
                                                     scalar=EPS, in1=sb["m2"][0:1, :],
                                                     op0=ALU.add, op1=ALU.subtract),
          reads=["m2", ("msq", 0), ("msq", 1)], writes=["var"])
    P.add("scalar", lambda e: e.activation(out=sb["srv"][0:1, :], in_=sb["var"][0:1, :],
                                           func=AF.Sqrt, bias=0.0, scale=1.0),
          reads=["var"], writes=["srv"])
    P.add("vector", lambda e: e.reciprocal(out=sb["rstd"][0:1, :], in_=sb["srv"][0:1, :]),
          reads=["srv"], writes=["rstd"])
    P.add("vector", lambda e: e.tensor_mul(out=sb["nmr"][0:1, :], in0=sb["mean"][0:1, :], in1=sb["rstd"][0:1, :]),
          reads=[("mean", 0), ("mean", 1), "rstd"], writes=["nmr0"])
    P.add("vector", lambda e: e.tensor_scalar_mul(out=sb["nmr"][0:1, :], in0=sb["nmr"][0:1, :], scalar1=-1.0),
          reads=["nmr0"], writes=["nmr"])
    for h in range(2):
        sl = slice(512 * h, 512 * (h + 1))
        P.add("tensor", lambda e, h=h, sl=sl: e.matmul(
            ps[h].ap()[:, :], sb["ones_row"][0:1, :], sb["rstd"][0:1, sl], start=True, stop=True),
            reads=["rstd", "ones_row", ("ps", h)], writes=[("ps", h)])
        P.add("tensor", lambda e, h=h, sl=sl: e.matmul(
            ps[2 + h].ap()[:, :], sb["ones_row"][0:1, :], sb["nmr"][0:1, sl], start=True, stop=True),
            reads=["nmr", "ones_row", ("ps", 2 + h)], writes=[("ps", 2 + h)])
        P.add("scalar", lambda e, h=h, sl=sl: e.copy(out=sb["R"][:, sl], in_=ps[h].ap()[:, :]),
              reads=[("ps", h)], writes=[("R", h), ("ps", h)])
        P.add("scalar", lambda e, h=h, sl=sl: e.copy(out=sb["A"][:, sl], in_=ps[2 + h].ap()[:, :]),
              reads=[("ps", 2 + h)], writes=[("A", h), ("ps", 2 + h)])
    Rk = [("R", 0), ("R", 1)]
    Ak = [("A", 0), ("A", 1)]
    for dt in range(32):
        xb = ("x", dt % 2)
        P.dma("sync", sb["xA"][dt % 2][:], x_in[dt], writes=[xb], stream=f"xA{dt % 2}")
        P.add("vector", lambda e, dt=dt: e.tensor_mul(
            out=sb["t"][dt % 2][:], in0=sb["xA"][dt % 2][:], in1=sb["R"][:]),
            reads=[xb] + Rk, writes=[("t", dt % 2)])
        P.add("vector", lambda e, dt=dt: e.tensor_add(
            out=sb["t"][dt % 2][:], in0=sb["t"][dt % 2][:], in1=sb["A"][:]),
            reads=[("t", dt % 2)] + Ak, writes=[("t", dt % 2)])
        P.add("scalar", lambda e, dt=dt: e.copy(
            out=sb["ln"][:, 1024 * dt:1024 * (dt + 1)], in_=sb["t"][dt % 2][:]),
            reads=[("t", dt % 2)], writes=[("ln", dt)])
        if compute_e:
            P.add("scalar", lambda e, dt=dt: e.square(out=sb["xsq"][dt % 2][:], in_=sb["t"][dt % 2][:]),
                  reads=[("t", dt % 2)], writes=[("xsq", dt % 2)])
            for h in range(2):
                P.add("tensor", lambda e, dt=dt, h=h: e.matmul(
                    ps[4 + h].ap()[0:1, :], sb["ones_col"][:], sb["xsq"][dt % 2][:, 512 * h:512 * (h + 1)],
                    start=(dt == 0), stop=(dt == 31)),
                    reads=[("xsq", dt % 2), "ones_col"], writes=[("ps", 4 + h)])
        if ln_out_hbm is not None:
            P.dma("scalar", ln_out_hbm[dt], sb["ln"][:, 1024 * dt:1024 * (dt + 1)],
                  reads=[("ln", dt)], writes=[("lnout", dt)], stream="lnst")
    if compute_e:
        for h in range(2):
            sl = slice(512 * h, 512 * (h + 1))
            P.add("scalar", lambda e, h=h, sl=sl: e.activation(
                out=sb["srv"][0:1, sl], in_=ps[4 + h].ap()[0:1, :], func=AF.Sqrt, bias=0.0, scale=1.0),
                reads=[("ps", 4 + h)], writes=["srv", ("e", h), ("ps", 4 + h)])
        P.add("vector", lambda e: e.tensor_scalar(
            out=sb["m01"][0:1, :], in0=sb["srv"][0:1, :], scalar1=64.0, scalar2=None, op0=ALU.is_gt),
            reads=[("e", 0), ("e", 1)], writes=["m01"])
        for h in range(2):
            sl = slice(512 * h, 512 * (h + 1))
            P.add("tensor", lambda e, h=h, sl=sl: e.matmul(
                ps[4 + h].ap()[:, :], sb["ones_row"][0:1, :], sb["m01"][0:1, sl], start=True, stop=True),
                reads=["m01", "ones_row", ("ps", 4 + h)], writes=[("ps", 4 + h)])
            P.add("scalar", lambda e, h=h, sl=sl: e.copy(out=sb["Mb"][:, sl], in_=ps[4 + h].ap()[:, :]),
                  reads=[("ps", 4 + h)], writes=[("Mb", h), ("ps", 4 + h)])


def alloc_ln_bufs(nc, sb):
    sb["ones_col"] = nc.alloc_sbuf_tensor("s_ones_col", [128, 1], F32)
    sb["ones_row"] = nc.alloc_sbuf_tensor("s_ones_row", [1, 128], F32)
    sb["xA"] = [nc.alloc_sbuf_tensor(f"s_xA{i}", [128, 1024], F32) for i in range(2)]
    sb["xsq"] = [nc.alloc_sbuf_tensor(f"s_xsq{i}", [128, 1024], F32) for i in range(2)]
    sb["t"] = [nc.alloc_sbuf_tensor(f"s_t{i}", [128, 1024], F32) for i in range(2)]
    sb["ln"] = nc.alloc_sbuf_tensor("s_ln", [128, 32 * 1024], BF16)
    for nm in ("mean", "msq", "m2", "var", "srv", "rstd", "nmr", "m01"):
        sb[nm] = nc.alloc_sbuf_tensor("s_" + nm, [1, 1024], F32)
    sb["R"] = nc.alloc_sbuf_tensor("s_R", [128, 1024], F32)
    sb["A"] = nc.alloc_sbuf_tensor("s_A", [128, 1024], F32)
    sb["Mb"] = nc.alloc_sbuf_tensor("s_Mb", [128, 1024], F32)


def ln_load_consts(P, ap, sb):
    P.dma("sync", sb["ones_col"][:], ap["ones_col"], writes=["ones_col"], stream="init")
    P.dma("sync", sb["ones_row"][:], ap["ones_row"], writes=["ones_row"], stream="init")


def emit_ffn(P, nc, ap, sb, ps, kind):
    dual = (kind == "l0")
    P.dma("sync", sb["bias"][:], ap["bias"], writes=["bias"], stream="init")
    emit_ln(P, nc, ap, sb, ps, compute_e=dual)
    mats = ("wl", "wh") if dual else ("wf",)
    for ot in range(32):
        par = ot % 2
        for mi, m in enumerate(mats):
            P.dma("sync" if mi == 0 else "scalar",
                  sb[m][par][:].rearrange("p (dt o) -> p dt o", o=128),
                  ap[m + "_t"][ot].rearrange("(dt p) o -> p dt o", p=128),
                  reads=tuple(ap.get("wdep", ())),
                  writes=[(m, par)], stream=f"{m}{par}")
        P.dma("gpsimd", sb["xo"][par][:], ap["x_in"][ot], writes=[("xo", par)], stream=f"xo{par}")
        for mi, m in enumerate(mats):
            for h in range(2):
                bi = par * 4 + mi * 2 + h if dual else par * 4 + h
                bank = ps[bi]
                wbuf = sb[m][par]
                for dt in range(32):
                    P.add("tensor", lambda e, wbuf=wbuf, dt=dt, h=h, bank=bank: e.matmul(
                        bank.ap()[:, 0:512],
                        wbuf[:, 128 * dt:128 * (dt + 1)],
                        sb["ln"][:, 1024 * dt + 512 * h:1024 * dt + 512 * h + 512],
                        start=(dt == 0), stop=(dt == 31)),
                        reads=[(m, par), ("ln", dt)], writes=[("ps", bi)])
        for h in range(2):
            sl = slice(512 * h, 512 * (h + 1))
            bias_col = sb["bias"][:, ot:ot + 1]
            if dual:
                bil, bih = par * 4 + h, par * 4 + 2 + h
                bl, bh = ps[bil], ps[bih]
                P.add("scalar", lambda e, bl=bl, sl=sl, bc=bias_col: e.activation(
                    out=sb["a"][:, sl], in_=bl.ap()[:, 0:512], func=AF.Identity, bias=bc, scale=1.0),
                    reads=[("ps", bil), "bias"], writes=[("a", h), ("ps", bil)])
                P.add("scalar", lambda e, bh=bh, sl=sl, bc=bias_col: e.activation(
                    out=sb["hh"][:, sl], in_=bh.ap()[:, 0:512], func=AF.Identity, bias=bc, scale=1.0),
                    reads=[("ps", bih), "bias"], writes=[("hh", h), ("ps", bih)])
                P.add("vector", lambda e, sl=sl: e.tensor_sub(
                    out=sb["hh"][:, sl], in0=sb["hh"][:, sl], in1=sb["a"][:, sl]),
                    reads=[("a", h), ("hh", h)], writes=[("hh", h)])
                P.add("vector", lambda e, sl=sl: e.tensor_mul(
                    out=sb["hh"][:, sl], in0=sb["hh"][:, sl], in1=sb["Mb"][:, sl]),
                    reads=[("hh", h), ("Mb", 0), ("Mb", 1)], writes=[("hh", h)])
                P.add("vector", lambda e, sl=sl: e.tensor_add(
                    out=sb["a"][:, sl], in0=sb["a"][:, sl], in1=sb["hh"][:, sl]),
                    reads=[("a", h), ("hh", h)], writes=[("a", h)])
            else:
                bi = par * 4 + h
                bank = ps[bi]
                P.add("scalar", lambda e, bank=bank, sl=sl, bc=bias_col: e.activation(
                    out=sb["a"][:, sl], in_=bank.ap()[:, 0:512], func=AF.Identity, bias=bc, scale=1.0),
                    reads=[("ps", bi), "bias"], writes=[("a", h), ("ps", bi)])
            P.add("vector", lambda e, sl=sl, par=par: e.scalar_tensor_tensor(
                out=sb["xo"][par][:, sl], in0=sb["a"][:, sl], scalar=0.5, in1=sb["xo"][par][:, sl],
                op0=ALU.mult, op1=ALU.add),
                reads=[("a", h), ("xo", par)], writes=[("xo", par)])
        P.dma("gpsimd", ap["x_out"][ot], sb["xo"][par][:],
              reads=[("xo", par)], writes=[("xout", ot)], stream=f"xst{par}")


def alloc_ffn_bufs(nc, sb, dual):
    mats = ("wl", "wh") if dual else ("wf",)
    for m in mats:
        sb[m] = [nc.alloc_sbuf_tensor(f"s_{m}{i}", [128, 4096], BF16) for i in range(2)]
    sb["bias"] = nc.alloc_sbuf_tensor("s_bias", [128, 32], F32)
    sb["a"] = nc.alloc_sbuf_tensor("s_a", [128, 1024], F32)
    sb["hh"] = nc.alloc_sbuf_tensor("s_hh", [128, 1024], F32)
    sb["xo"] = [nc.alloc_sbuf_tensor(f"s_xo{i}", [128, 1024], F32) for i in range(2)]


def build_ffn(kind):
    nc = bass.Bass("TRN2", target_bir_lowering=False)
    dual = (kind == "l0")
    ap = {}
    ap["x_in"] = nc.dram_tensor("x_in", [32, 128, 1024], F32, kind="ExternalInput").ap()
    ap["ones_col"] = nc.dram_tensor("ones_col", [128, 1], F32, kind="ExternalInput").ap()
    ap["ones_row"] = nc.dram_tensor("ones_row", [1, 128], F32, kind="ExternalInput").ap()
    ap["bias"] = nc.dram_tensor("bias", [128, 32], F32, kind="ExternalInput").ap()
    if dual:
        ap["wl_t"] = nc.dram_tensor("wl_t", [32, 4096, 128], BF16, kind="ExternalInput").ap()
        ap["wh_t"] = nc.dram_tensor("wh_t", [32, 4096, 128], BF16, kind="ExternalInput").ap()
    else:
        ap["wf_t"] = nc.dram_tensor("wf_t", [32, 4096, 128], BF16, kind="ExternalInput").ap()
    ap["x_out"] = nc.dram_tensor("x_out", [32, 128, 1024], F32, kind="ExternalOutput").ap()

    sb = {}
    alloc_ln_bufs(nc, sb)
    alloc_ffn_bufs(nc, sb, dual)
    ps = [nc.alloc_psum_tensor(f"ps{i}", [128, 512], F32) for i in range(8)]
    P = Prog(nc)
    with nc.Block() as block:
        ln_load_consts(P, ap, sb)
        emit_ffn(P, nc, ap, sb, ps, kind)
        P.emit(block)
    return nc


CAST_SPECS = [
    ("woT0_sl", 4096, 512, "wo0_sh"),
    ("woT1_sl", 4096, 512, "wo1_sh"),
    ("kw1T_sl", 2048, 256, "kw1_sh"),
    ("vw1T_sl", 2048, 256, "vw1_sh"),
    ("kw2T", 2048, 128, "kw2_bf"),
    ("vw2T", 2048, 128, "vw2_bf"),
    ("wqkvT_sl", 4096, 1536, "wqkv_bf"),
    ("wqT_sl", 4096, 1024, "wq_bf"),
    ("wkT_sl", 4096, 256, "wk_bf"),
    ("wvT_sl", 4096, 256, "wv_bf"),
]


def _emit_cast(P, sb, in_ap, out_ap, rows, cols, oname, base_otl=0):
    nr = rows // 128
    for r in range(nr):
        par = r % 2
        P.dma("sync", sb["cf"][par][:, 0:cols], in_ap[128 * r:128 * (r + 1), :],
              writes=[("cf", par)], stream=f"cf{par}")
        P.add("vector", lambda e, par=par, cols=cols: e.tensor_copy(
            out=sb["cb"][par][:, 0:cols], in_=sb["cf"][par][:, 0:cols]),
            reads=[("cf", par)], writes=[("cb", par)])
        for otl in range(cols // 128):
            P.dma("gpsimd", out_ap[base_otl + otl, 128 * r:128 * (r + 1), :],
                  sb["cb"][par][:, 128 * otl:128 * (otl + 1)],
                  reads=[("cb", par)], writes=[(oname, r, base_otl + otl)],
                  stream=f"cst{par}")


def emit_weight_prep(P, nc, ap, sb, ps, cast_specs=CAST_SPECS, extra_casts=()):
    for (iname, rows, cols, oname) in cast_specs:
        _emit_cast(P, sb, ap[iname], ap[oname], rows, cols, oname)
    for (in_ap, out_ap, rows, cols, base_otl, oname) in extra_casts:
        _emit_cast(P, sb, in_ap, out_ap, rows, cols, oname, base_otl=base_otl)
    for r in range(32):
        par = r % 2
        P.dma("sync", sb["cf"][par][:, 0:24], ap["wgateT_sl"][128 * r:128 * (r + 1), :],
              writes=[("cf", par)], stream=f"cf{par}")
        P.add("vector", lambda e, par=par: e.tensor_copy(
            out=sb["cb"][par][:, 0:24], in_=sb["cf"][par][:, 0:24]),
            reads=[("cf", par)], writes=[("cb", par)])
        P.dma("gpsimd", ap["wgate_bf"][128 * r:128 * (r + 1), :], sb["cb"][par][:, 0:24],
              reads=[("cb", par)], writes=[("wgate_bf", r)], stream=f"cst{par}")
    P.dma("sync", sb["aer"][:].rearrange("p (t d) -> p t d", d=4096),
          ap["A_er"].rearrange("(t p) d -> p t d", p=128), writes=["aer_f"], stream="init2")
    P.dma("sync", sb["bt"][:].rearrange("p (t o) -> p t o", o=512),
          ap["Bt_sl"].rearrange("(t p) o -> p t o", p=128), writes=["bt_f"], stream="init2")
    for t in range(2):
        P.dma("sync", sb["probs"][t][:], ap["probs_rep"][t],
              writes=[("probs", t)], stream="init2")
    P.add("vector", lambda e: e.tensor_copy(out=sb["aer_bf"][:], in_=sb["aer"][:]),
          reads=["aer_f"], writes=["aer_bf"])
    for t in range(2):
        P.add("vector", lambda e, t=t: e.tensor_scalar(
            out=sb["bt_bf"][:, 512 * t:512 * (t + 1)], in0=sb["bt"][:, 512 * t:512 * (t + 1)],
            scalar1=sb["probs"][t][:, 0:1], scalar2=None, op0=ALU.mult),
            reads=["bt_f", ("probs", t)], writes=["bt_bf"])
    for dt in range(32):
        par = dt % 2
        P.dma("scalar", sb["wt"][par][:], ap["WT_sl"][128 * dt:128 * (dt + 1), :],
              writes=[("wt", par)], stream=f"wt{par}")
        for otl in range(4):
            bi = 4 + 2 * par + (otl % 2)
            P.add("tensor", lambda e, dt=dt, otl=otl, bi=bi: e.matmul(
                ps[bi].ap()[:, 0:128], sb["aer_bf"][:, 4096 * 0 + 128 * dt:4096 * 0 + 128 * dt + 128],
                sb["bt_bf"][:, 512 * 0 + 128 * otl:512 * 0 + 128 * otl + 128],
                start=True, stop=False),
                reads=["aer_bf", "bt_bf"], writes=[("ps", bi)])
            P.add("tensor", lambda e, dt=dt, otl=otl, bi=bi: e.matmul(
                ps[bi].ap()[:, 0:128], sb["aer_bf"][:, 4096 * 1 + 128 * dt:4096 * 1 + 128 * dt + 128],
                sb["bt_bf"][:, 512 * 1 + 128 * otl:512 * 1 + 128 * otl + 128],
                start=False, stop=True),
                reads=["aer_bf", "bt_bf"], writes=[("ps", bi)])
            P.add("vector", lambda e, par=par, otl=otl, bi=bi: e.tensor_add(
                out=sb["weff"][par][:, 128 * otl:128 * (otl + 1)],
                in0=ps[bi].ap()[:, 0:128],
                in1=sb["wt"][par][:, 128 * otl:128 * (otl + 1)]),
                reads=[("ps", bi), ("wt", par)], writes=[("weff", par, otl), ("ps", bi)])
            P.dma("gpsimd", ap["weff_sh"][otl, 128 * dt:128 * (dt + 1), :],
                  sb["weff"][par][:, 128 * otl:128 * (otl + 1)],
                  reads=[("weff", par, otl)], writes=[("weffo", dt, otl)], stream=f"wfs{par}")


def alloc_prep_bufs(nc, sb):
    sb["qw"] = nc.alloc_sbuf_tensor("s_qw", [128, 8 * 128], I32)
    sb["lut4"] = nc.alloc_sbuf_tensor("s_lut4", [128, 64], F32)
    sb["lut8"] = nc.alloc_sbuf_tensor("s_lut8", [128, 1024], F32)
    sb["bitmask"] = nc.alloc_sbuf_tensor("s_bitmask", [128, 4096], I32)
    sb["ident"] = nc.alloc_sbuf_tensor("s_ident", [128, 128], BF16)
    sb["tmp32"] = nc.alloc_sbuf_tensor("s_tmp32", [128, 4096], I32)
    sb["bit"] = nc.alloc_sbuf_tensor("s_bit", [128, 4096], BF16)
    sb["idx4"] = nc.alloc_sbuf_tensor("s_idx4", [128, 4096], BF16)
    sb["idx8"] = nc.alloc_sbuf_tensor("s_idx8", [128, 4096], BF16)
    sb["wl"] = nc.alloc_sbuf_tensor("s_wl", [128, 4096], BF16)
    sb["wh"] = nc.alloc_sbuf_tensor("s_wh", [128, 4096], BF16)
    sb["m"] = nc.alloc_sbuf_tensor("s_m", [128, 4096], BF16)
    sb["stagwl"] = nc.alloc_sbuf_tensor("s_stagwl", [128, 4096], BF16)
    sb["stagwh"] = nc.alloc_sbuf_tensor("s_stagwh", [128, 4096], BF16)
    sb["cf"] = [nc.alloc_sbuf_tensor(f"s_cf{i}", [128, 1536], F32) for i in range(2)]
    sb["cb"] = [nc.alloc_sbuf_tensor(f"s_cb{i}", [128, 1536], BF16) for i in range(2)]
    sb["aer"] = nc.alloc_sbuf_tensor("s_aer", [128, 2 * 4096], F32)
    sb["aer_bf"] = nc.alloc_sbuf_tensor("s_aer_bf", [128, 2 * 4096], BF16)
    sb["bt"] = nc.alloc_sbuf_tensor("s_bt", [128, 2 * 512], F32)
    sb["bt_bf"] = nc.alloc_sbuf_tensor("s_bt_bf", [128, 2 * 512], BF16)
    sb["probs"] = [nc.alloc_sbuf_tensor(f"s_probs{t}", [128, 1], F32) for t in range(2)]
    sb["wt"] = [nc.alloc_sbuf_tensor(f"s_wt{i}", [128, 512], F32) for i in range(2)]
    sb["weff"] = [nc.alloc_sbuf_tensor(f"s_weff{i}", [128, 512], BF16) for i in range(2)]


def build_p0full():
    nc = bass.Bass("TRN2", target_bir_lowering=False)
    ap = {}
    ap["qw"] = nc.dram_tensor("qw", [8, 512, 128], I32, kind="ExternalInput").ap()
    ap["lut4"] = nc.dram_tensor("lut4", [512, 16], F32, kind="ExternalInput").ap()
    ap["lut8"] = nc.dram_tensor("lut8", [512, 256], F32, kind="ExternalInput").ap()
    ap["bitmask"] = nc.dram_tensor("bitmask", [128, 4096], I32, kind="ExternalInput").ap()
    ap["ident_bf"] = nc.dram_tensor("ident_bf", [128, 128], BF16, kind="ExternalInput").ap()
    ap["wl_sh"] = nc.dram_tensor("wl_sh", [4, 4096, 128], BF16, kind="ExternalOutput").ap()
    ap["wh_sh"] = nc.dram_tensor("wh_sh", [4, 4096, 128], BF16, kind="ExternalOutput").ap()
    for (iname, rows, cols, oname) in CAST_SPECS:
        ap[iname] = nc.dram_tensor(iname, [rows, cols], F32, kind="ExternalInput").ap()
        ap[oname] = nc.dram_tensor(oname, [cols // 128, rows, 128], BF16, kind="ExternalOutput").ap()
    ap["wgateT_sl"] = nc.dram_tensor("wgateT_sl", [4096, 24], F32, kind="ExternalInput").ap()
    ap["wgate_bf"] = nc.dram_tensor("wgate_bf", [4096, 24], BF16, kind="ExternalOutput").ap()
    ap["A_er"] = nc.dram_tensor("A_er", [256, 4096], F32, kind="ExternalInput").ap()
    ap["Bt_sl"] = nc.dram_tensor("Bt_sl", [256, 512], F32, kind="ExternalInput").ap()
    ap["probs_rep"] = nc.dram_tensor("probs_rep", [2, 128, 1], F32, kind="ExternalInput").ap()
    ap["WT_sl"] = nc.dram_tensor("WT_sl", [4096, 512], F32, kind="ExternalInput").ap()
    ap["weff_sh"] = nc.dram_tensor("weff_sh", [4, 4096, 128], BF16, kind="ExternalOutput").ap()

    sb = {}
    alloc_prep_bufs(nc, sb)
    ps = [nc.alloc_psum_tensor(f"ps{i}", [128, 512], F32) for i in range(8)]
    P = Prog(nc)
    with nc.Block() as block:
        emit_dequant(P, nc, ap, sb, ps)
        emit_weight_prep(P, nc, ap, sb, ps)
        P.emit(block)
    return nc


def emit_wo(P, nc, ap, sb, ps):
    for dt in range(32):
        P.dma("sync", sb["ln"][:, 1024 * dt:1024 * (dt + 1)], ap["attn_in"][dt],
              writes=[("ln", dt)], stream=f"at{dt % 2}")
    for ot in range(32):
        par = ot % 2
        P.dma("scalar", sb["wf"][par][:].rearrange("p (dt o) -> p dt o", o=128),
              ap["wo_t"][ot].rearrange("(dt p) o -> p dt o", p=128),
              writes=[("wf", par)], stream=f"wf{par}")
        P.dma("gpsimd", sb["xo"][par][:], ap["x_in"][ot], writes=[("xo", par)], stream=f"xo{par}")
        for h in range(2):
            bi = par * 4 + h
            bank = ps[bi]
            wbuf = sb["wf"][par]
            for dt in range(32):
                P.add("tensor", lambda e, wbuf=wbuf, dt=dt, h=h, bank=bank: e.matmul(
                    bank.ap()[:, 0:512],
                    wbuf[:, 128 * dt:128 * (dt + 1)],
                    sb["ln"][:, 1024 * dt + 512 * h:1024 * dt + 512 * h + 512],
                    start=(dt == 0), stop=(dt == 31)),
                    reads=[("wf", par), ("ln", dt)], writes=[("ps", bi)])
            sl = slice(512 * h, 512 * (h + 1))
            P.add("vector", lambda e, sl=sl, par=par, bi=bi: e.tensor_add(
                out=sb["xo"][par][:, sl], in0=ps[bi].ap()[:, 0:512], in1=sb["xo"][par][:, sl]),
                reads=[("ps", bi), ("xo", par)], writes=[("xo", par), ("ps", bi)])
        P.dma("gpsimd", ap["x_out"][ot], sb["xo"][par][:],
              reads=[("xo", par)], writes=[("xout", ot)], stream=f"xst{par}")


def build_wo():
    nc = bass.Bass("TRN2", target_bir_lowering=False)
    ap = {}
    ap["x_in"] = nc.dram_tensor("x_in", [32, 128, 1024], F32, kind="ExternalInput").ap()
    ap["attn_in"] = nc.dram_tensor("attn_in", [32, 128, 1024], BF16, kind="ExternalInput").ap()
    ap["wo_t"] = nc.dram_tensor("wo_t", [32, 4096, 128], BF16, kind="ExternalInput").ap()
    ap["x_out"] = nc.dram_tensor("x_out", [32, 128, 1024], F32, kind="ExternalOutput").ap()
    sb = {}
    sb["ln"] = nc.alloc_sbuf_tensor("s_ln", [128, 32 * 1024], BF16)
    sb["wf"] = [nc.alloc_sbuf_tensor(f"s_wf{i}", [128, 4096], BF16) for i in range(2)]
    sb["xo"] = [nc.alloc_sbuf_tensor(f"s_xo{i}", [128, 1024], F32) for i in range(2)]
    ps = [nc.alloc_psum_tensor(f"ps{i}", [128, 512], F32) for i in range(8)]
    P = Prog(nc)
    with nc.Block() as block:
        emit_wo(P, nc, ap, sb, ps)
        P.emit(block)
    return nc


def build_lnout():
    nc = bass.Bass("TRN2", target_bir_lowering=False)
    ap = {}
    ap["x_in"] = nc.dram_tensor("x_in", [32, 128, 1024], F32, kind="ExternalInput").ap()
    ap["ones_col"] = nc.dram_tensor("ones_col", [128, 1], F32, kind="ExternalInput").ap()
    ap["ones_row"] = nc.dram_tensor("ones_row", [1, 128], F32, kind="ExternalInput").ap()
    ap["ln_out"] = nc.dram_tensor("ln_out", [32, 128, 1024], BF16, kind="ExternalOutput").ap()
    sb = {}
    alloc_ln_bufs(nc, sb)
    ps = [nc.alloc_psum_tensor(f"ps{i}", [128, 512], F32) for i in range(6)]
    P = Prog(nc)
    with nc.Block() as block:
        ln_load_consts(P, ap, sb)
        emit_ln(P, nc, ap, sb, ps, compute_e=False, ln_out_hbm=ap["ln_out"])
        P.emit(block)
    return nc


RSCALE = 1.0 / SCALE
PEN = 904.0
MSKB = PEN * SCALE
CAUS = -30000.0 * RSCALE


def emit_attn1(P, nc, ap, sb, ps):
    for nm in ("C", "S", "protT", "identF", "Ddiag4", "Dprev4"):
        P.dma("sync", sb[nm][:], ap[nm], writes=[nm], stream="init")
    P.dma("sync", sb["DcausQK"][:].rearrange("p (v f) -> p v f", f=512),
          ap["DcausQK"].rearrange("v p f -> p v f"), writes=["DcausQK"], stream="init")
    P.dma("sync", sb["recent4"][:], ap["recent4"], writes=["recent4"], stream="init")
    P.dma("sync", sb["E80"][:].rearrange("p (v f) -> p v f", f=128),
          ap["E80"].rearrange("v p f -> p v f"), writes=["E80"], stream="init")
    P.dma("sync", sb["E01"][:].rearrange("p (v f) -> p v f", f=128),
          ap["E01"].rearrange("v p f -> p v f"), writes=["E01"], stream="init")
    P.dma("sync", sb["init01"][:], ap["init01"], writes=["init01"], stream="init")

    RQK = [("rq", oi, tt) for oi in range(4) for tt in range(8)] + [("rk", tt) for tt in range(8)]
    for pair in range(PAIRS):
        for j in range(4):
            P.dma("sync", sb["wq"][:, 4096 * j:4096 * (j + 1)].rearrange("p (dt o) -> p dt o", o=128),
                  ap["wq_bf"][4 * pair + j].rearrange("(dt p) o -> p dt o", p=128),
                  writes=[("wq", j)], stream="wload")
        P.dma("sync", sb["wk"][:].rearrange("p (dt o) -> p dt o", o=128),
              ap["wk_bf"][pair].rearrange("(dt p) o -> p dt o", p=128),
              writes=["wk"], stream="wload")
        P.dma("sync", sb["wv"][:].rearrange("p (dt o) -> p dt o", o=128),
              ap["wv_bf"][pair].rearrange("(dt p) o -> p dt o", p=128),
              writes=["wv"], stream="wload")
        for tt in range(8):
            tpar = 0
            for dt in range(32):
                P.dma("gpsimd", sb["lnt"][tpar][:, 512 * dt:512 * (dt + 1)],
                      ap["lnf"](dt, 512 * tt, 512 * (tt + 1)),
                      reads=tuple(ap.get("lnf_dep", ())),
                      writes=[("lnt", tpar, dt)], stream=f"lnt{tpar}")
            for oi in range(6):
                bi = (tt % 2) * 4 + (oi % 4) if False else oi
                bank = ps[oi]
                if oi < 4:
                    wsl = sb["wq"][:, 4096 * oi:4096 * oi + 4096]
                    wkey = ("wq", oi)
                elif oi == 4:
                    wsl, wkey = sb["wk"][:], "wk"
                else:
                    wsl, wkey = sb["wv"][:], "wv"
                for dt in range(32):
                    P.add("tensor", lambda e, wsl=wsl, dt=dt, bank=bank, tpar=tpar: e.matmul(
                        bank.ap()[:, 0:512],
                        wsl[:, 128 * dt:128 * (dt + 1)],
                        sb["lnt"][tpar][:, 512 * dt:512 * (dt + 1)],
                        start=(dt == 0), stop=(dt == 31)),
                        reads=[wkey, ("lnt", tpar, dt)], writes=[("ps", oi)])
                if oi == 5:
                    P.add("scalar", lambda e, bank=bank, tt=tt: e.copy(
                        out=sb["v"][:, 512 * tt:512 * (tt + 1)], in_=bank.ap()[:, 0:512]),
                        reads=[("ps", oi)], writes=[("v", tt), ("ps", oi)])
                else:
                    P.add("scalar", lambda e, bank=bank: e.copy(
                        out=sb["qf"][:], in_=bank.ap()[:, 0:512]),
                        reads=[("ps", oi)], writes=["qf", ("ps", oi)])
                    P.add("tensor", lambda e: e.matmul(
                        ps[6].ap()[:, 0:512], sb["protT"][:], sb["qf"][:],
                        start=True, stop=True),
                        reads=["qf", "protT", ("ps", 6)], writes=[("ps", 6)])
                    P.add("vector", lambda e, tt=tt: e.tensor_mul(
                        out=sb["t1"][:], in0=ps[6].ap()[:, 0:512],
                        in1=sb["S"][:, 512 * tt:512 * (tt + 1)]),
                        reads=[("ps", 6), "S"], writes=["t1", ("ps", 6)])
                    P.add("vector", lambda e, tt=tt: e.tensor_mul(
                        out=sb["qf"][:], in0=sb["qf"][:],
                        in1=sb["C"][:, 512 * tt:512 * (tt + 1)]),
                        reads=["qf", "C"], writes=["qf"])
                    dstkey = ("rq", oi, tt) if oi < 4 else ("rk", tt)
                    dst = (sb["q_all"][:, 4096 * oi + 512 * tt:4096 * oi + 512 * (tt + 1)]
                           if oi < 4 else sb["rk"][:, 512 * tt:512 * (tt + 1)])
                    P.add("vector", lambda e, dst=dst: e.tensor_add(
                        out=dst, in0=sb["qf"][:], in1=sb["t1"][:]),
                        reads=["qf", "t1"], writes=[dstkey])
        for kt in range(32):
            P.add("vector", lambda e, kt=kt: e.memset(sb["vt"][:, 256 * kt + 128:256 * kt + 129], 1.0),
                  reads=[], writes=[("vt1", kt)])
            P.add("sync", lambda e, kt=kt: e.dma_start_transpose(
                sb["vt"][:, 256 * kt:256 * kt + 128], sb["v"][:, 128 * kt:128 * (kt + 1)]),
                reads=[("v", kt // 4)], writes=[("vt", kt)], kind="d", stream="vtp")
        for i in range(32):
            ng = i // 4 + 1
            nkt = i + 1
            P.add("vector", lambda e: e.memset(sb["cm"][:], -3.0e38), reads=["cm"], writes=["cm"])
            for hh in range(4):
                for g in range(ng):
                    bi = g % 2
                    bank = ps[bi]
                    P.add("tensor", lambda e, hh=hh, g=g, i=i, bank=bank: e.matmul(
                        bank.ap()[:, 0:512],
                        sb["q_all"][:, 4096 * hh + 128 * i:4096 * hh + 128 * (i + 1)],
                        sb["rk"][:, 512 * g:512 * (g + 1)],
                        start=True, stop=(g != i // 4)),
                        reads=RQK + [("ps", bi)], writes=[("ps", bi)])
                    if g == i // 4:
                        P.add("tensor", lambda e, i=i, bank=bank: e.matmul(
                            bank.ap()[:, 0:512], sb["identF"][:],
                            sb["DcausQK"][:, 512 * (i % 4):512 * (i % 4 + 1)],
                            start=False, stop=True),
                            reads=["identF", "DcausQK", ("ps", bi)], writes=[("ps", bi)])
                    P.add("vector", lambda e, g=g, hh=hh, bank=bank: e.reduce_max(
                        out=sb["cm"][:, 32 * g:32 * (g + 1)],
                        in_=bank.ap()[:, 0:512].rearrange("p (s c) -> p s c", c=16),
                        axis=AX.X),
                        reads=[("ps", bi)], writes=[("cm",), ("ps", bi)])
                P.add("vector", lambda e: e.max(out=sb["m8"][:], in_=sb["cm"][:]),
                      reads=[("cm",)], writes=["m8"])
                P.dma("gpsimd", sb["t8b"][:], sb["m8"][:, 7:8],
                      reads=["m8"], writes=["t8b"], stream="t8cp")
                P.add("vector", lambda e: e.tensor_scalar(
                    out=sb["cok"][:], in0=sb["cm"][:], scalar1=sb["t8b"][:, 0:1], scalar2=None,
                    op0=ALU.is_ge),
                    reads=[("cm",), "t8b"], writes=["cok"])
                P.add("vector", lambda e: e.tensor_max(
                    out=sb["cok"][:], in0=sb["cok"][:], in1=sb["init01"][:]),
                    reads=["cok", "init01"], writes=["cok"])
                for gI in range(2):
                    P.add("scalar", lambda e, gI=gI, hh=hh: e.dma_start_transpose(
                        sb["cokT"][:, 512 * gI + 128 * hh:512 * gI + 128 * (hh + 1)],
                        sb["cok"][:, 128 * gI:128 * (gI + 1)]),
                        reads=["cok"], writes=[("cokT", gI, hh)], kind="d", stream="ctp")
                if hh < 3:
                    P.add("vector", lambda e: e.memset(sb["cm"][:], -3.0e38),
                          reads=[("cm",)], writes=[("cm",)])
            for kt in range(nkt):
                bi = kt % 2
                bank = ps[bi]
                gI = kt // 16
                cokT_keys = [("cokT", gI, hh) for hh in range(4)]
                P.add("tensor", lambda e, kt=kt, i=i, bank=bank: e.matmul(
                    bank.ap()[:, 0:512],
                    sb["rk"][:, 128 * kt:128 * (kt + 1)],
                    sb["q_all"][:].rearrange("p (h n) -> p h n", n=4096)[:, :, 128 * i:128 * (i + 1)],
                    start=True, stop=False),
                    reads=RQK + [("ps", bi)],
                    writes=[("ps", bi)])
                is_diag = (kt == i)
                is_prev = (kt == i - 1)
                if not is_diag:
                    P.add("tensor", lambda e, kt=kt, gI=gI, bank=bank, st=(not is_prev): e.matmul(
                        bank.ap()[:, 0:512],
                        sb["E80"][:, 128 * (kt % 16):128 * (kt % 16 + 1)],
                        sb["cokT"][:, 512 * gI:512 * (gI + 1)],
                        start=False, stop=st),
                        reads=["E80", ("ps", bi)] + cokT_keys, writes=[("ps", bi)])
                if is_prev:
                    P.add("tensor", lambda e, bank=bank: e.matmul(
                        bank.ap()[:, 0:512], sb["identF"][:], sb["Dprev4"][:],
                        start=False, stop=True),
                        reads=["identF", "Dprev4", ("ps", bi)], writes=[("ps", bi)])
                    P.add("tensor", lambda e, kt=kt, gI=gI: e.matmul(
                        ps[2].ap()[:, 0:512],
                        sb["E01"][:, 128 * (kt % 16):128 * (kt % 16 + 1)],
                        sb["cokT"][:, 512 * gI:512 * (gI + 1)],
                        start=True, stop=True),
                        reads=["E01", ("ps", 2)] + cokT_keys, writes=[("ps", 2)])
                    P.add("scalar", lambda e: e.copy(out=sb["a01"][:], in_=ps[2].ap()[:, 0:512]),
                          reads=[("ps", 2)], writes=["a01", ("ps", 2)])
                    P.add("vector", lambda e: e.tensor_mul(
                        out=sb["a01"][:], in0=sb["a01"][:], in1=sb["recent4"][:]),
                        reads=["a01", "recent4"], writes=["a01"])
                if is_diag:
                    P.add("tensor", lambda e, bank=bank: e.matmul(
                        bank.ap()[:, 0:512], sb["identF"][:], sb["Ddiag4"][:],
                        start=False, stop=True),
                        reads=["identF", "Ddiag4", ("ps", bi)], writes=[("ps", bi)])
                ppar = kt % 2
                if is_prev:
                    P.add("vector", lambda e, bank=bank: e.scalar_tensor_tensor(
                        out=sb["sfix"][:], in0=sb["a01"][:], scalar=-PEN, in1=bank.ap()[:, 0:512],
                        op0=ALU.mult, op1=ALU.add),
                        reads=["a01", ("ps", bi)], writes=["sfix"])
                    P.add("scalar", lambda e, ppar=ppar: e.activation(
                        out=sb["pp"][ppar][:], in_=sb["sfix"][:], func=AF.Exp,
                        bias=sb["neg80"][:, 0:1], scale=SCALE),
                        reads=["sfix", "neg80"], writes=[("pp", ppar)])

                else:
                    bcol = sb["zero1"][:, 0:1] if is_diag else sb["neg80"][:, 0:1]
                    bkey = "zero1" if is_diag else "neg80"
                    P.add("scalar", lambda e, bank=bank, ppar=ppar, bcol=bcol: e.activation(
                        out=sb["pp"][ppar][:], in_=bank.ap()[:, 0:512], func=AF.Exp,
                        bias=bcol, scale=SCALE),
                        reads=[("ps", bi), bkey], writes=[("pp", ppar), ("ps", bi)])
                for hh in range(4):
                    pb = ps[3 + hh]
                    P.add("tensor", lambda e, kt=kt, hh=hh, pb=pb, ppar=ppar, i=i: e.matmul(
                        pb.ap()[:, 0:129],
                        sb["pp"][ppar][:, 128 * hh:128 * (hh + 1)],
                        sb["vt"][:, 256 * kt:256 * kt + 129],
                        start=(kt == 0), stop=(kt == i)),
                        reads=[("pp", ppar), ("vt", kt), ("vt1", kt), ("ps", 3 + hh)], writes=[("ps", 3 + hh)])
            for hh in range(4):
                pb = ps[3 + hh]
                P.add("scalar", lambda e, pb=pb: e.copy(
                    out=sb["pv"][:], in_=pb.ap()[:, 0:129]),
                    reads=[("ps", 3 + hh)], writes=["pv", ("ps", 3 + hh)])
                P.add("vector", lambda e: e.reciprocal(
                    out=sb["rs2"][:], in_=sb["pv"][:, 128:129]),
                    reads=["pv"], writes=["rs2"])
                P.add("scalar", lambda e: e.activation(
                    out=sb["att"][:], in_=sb["pv"][:, 0:128], func=AF.Identity,
                    bias=0.0, scale=sb["rs2"][:, 0:1]),
                    reads=["pv", "rs2"], writes=["att"])
                P.add("sync", lambda e: e.dma_start_transpose(sb["attT"][:], sb["att"][:]),
                      reads=["att"], writes=["attT"], kind="d", stream="atp")
                P.dma("gpsimd", ap["attn_out"][4 * pair + hh, :, 128 * i:128 * (i + 1)],
                      sb["attT"][:], reads=["attT"], writes=[("ao", pair, hh, i)],
                      stream="aost")
    return


def alloc_attn1_bufs(nc, sb):
    sb["wq"] = nc.alloc_sbuf_tensor("s_wq", [128, 4 * 4096], BF16)
    sb["wk"] = nc.alloc_sbuf_tensor("s_wk", [128, 4096], BF16)
    sb["wv"] = nc.alloc_sbuf_tensor("s_wv", [128, 4096], BF16)
    sb["lnt"] = [nc.alloc_sbuf_tensor("s_lnt0", [128, 512 * 32], BF16)]
    sb["q_all"] = nc.alloc_sbuf_tensor("s_q_all", [128, 4 * 4096], BF16)
    sb["rk"] = nc.alloc_sbuf_tensor("s_rk", [128, 4096], BF16)
    sb["v"] = nc.alloc_sbuf_tensor("s_v", [128, 4096], BF16)
    sb["vt"] = nc.alloc_sbuf_tensor("s_vt", [128, 32 * 256], BF16)
    sb["qf"] = nc.alloc_sbuf_tensor("s_qf", [128, 512], F32)
    sb["qcb"] = nc.alloc_sbuf_tensor("s_qcb", [128, 512], BF16)
    sb["t1"] = nc.alloc_sbuf_tensor("s_t1", [128, 512], F32)
    sb["C"] = nc.alloc_sbuf_tensor("s_C", [128, 4096], BF16)
    sb["S"] = nc.alloc_sbuf_tensor("s_S", [128, 4096], BF16)
    sb["protT"] = nc.alloc_sbuf_tensor("s_protT", [128, 128], F32)
    sb["identF"] = nc.alloc_sbuf_tensor("s_identF", [128, 128], F32)
    sb["Ddiag4"] = nc.alloc_sbuf_tensor("s_Ddiag4", [128, 512], F32)
    sb["Dprev4"] = nc.alloc_sbuf_tensor("s_Dprev4", [128, 512], F32)
    sb["DcausQK"] = nc.alloc_sbuf_tensor("s_DcausQK", [128, 4 * 512], F32)
    sb["recent4"] = nc.alloc_sbuf_tensor("s_recent4", [128, 512], BF16)
    sb["E80"] = nc.alloc_sbuf_tensor("s_E80", [128, 16 * 128], BF16)
    sb["E01"] = nc.alloc_sbuf_tensor("s_E01", [128, 16 * 128], BF16)
    sb["init01"] = nc.alloc_sbuf_tensor("s_init01", [128, 256], BF16)
    sb["zero512"] = nc.alloc_sbuf_tensor("s_zero512", [1, 512], BF16)
    sb["cm"] = nc.alloc_sbuf_tensor("s_cm", [128, 256], F32)
    sb["m8"] = nc.alloc_sbuf_tensor("s_m8", [128, 8], F32)
    sb["t8b"] = nc.alloc_sbuf_tensor("s_t8b", [128, 1], F32)
    sb["cok"] = nc.alloc_sbuf_tensor("s_cok", [128, 256], BF16)
    sb["cokT"] = nc.alloc_sbuf_tensor("s_cokT", [128, 2 * 512], BF16)
    sb["a01"] = nc.alloc_sbuf_tensor("s_a01", [128, 512], BF16)
    sb["sfix"] = nc.alloc_sbuf_tensor("s_sfix", [128, 512], F32)
    sb["pp"] = [nc.alloc_sbuf_tensor(f"s_pp{i}", [128, 512], BF16) for i in range(2)]
    sb["rs"] = nc.alloc_sbuf_tensor("s_rs", [128, 1], F32)
    sb["rs2"] = nc.alloc_sbuf_tensor("s_rs2", [128, 1], F32)
    sb["pv"] = nc.alloc_sbuf_tensor("s_pv", [128, 129], F32)
    sb["att"] = nc.alloc_sbuf_tensor("s_att", [128, 128], BF16)
    sb["attT"] = nc.alloc_sbuf_tensor("s_attT", [128, 128], BF16)
    sb["neg80"] = nc.alloc_sbuf_tensor("s_neg80", [128, 1], F32)
    sb["zero1"] = nc.alloc_sbuf_tensor("s_zero1", [128, 1], F32)


def build_attn1():
    nc = bass.Bass("TRN2", target_bir_lowering=False)
    ap = {}
    ap["ln_full"] = nc.dram_tensor("ln_full", [32, 128, 4096], BF16, kind="ExternalInput").ap()
    ap["lnf"] = lambda dt, c0, c1: ap["ln_full"][dt, :, c0:c1]
    ap["wq_bf"] = nc.dram_tensor("wq_bf", [8, 4096, 128], BF16, kind="ExternalInput").ap()
    ap["wk_bf"] = nc.dram_tensor("wk_bf", [2, 4096, 128], BF16, kind="ExternalInput").ap()
    ap["wv_bf"] = nc.dram_tensor("wv_bf", [2, 4096, 128], BF16, kind="ExternalInput").ap()
    ap["C"] = nc.dram_tensor("C", [128, 4096], BF16, kind="ExternalInput").ap()
    ap["S"] = nc.dram_tensor("S", [128, 4096], BF16, kind="ExternalInput").ap()
    ap["protT"] = nc.dram_tensor("protT", [128, 128], F32, kind="ExternalInput").ap()
    ap["identF"] = nc.dram_tensor("identF", [128, 128], F32, kind="ExternalInput").ap()
    ap["Ddiag4"] = nc.dram_tensor("Ddiag4", [128, 512], F32, kind="ExternalInput").ap()
    ap["Dprev4"] = nc.dram_tensor("Dprev4", [128, 512], F32, kind="ExternalInput").ap()
    ap["DcausQK"] = nc.dram_tensor("DcausQK", [4, 128, 512], F32, kind="ExternalInput").ap()
    ap["recent4"] = nc.dram_tensor("recent4", [128, 512], BF16, kind="ExternalInput").ap()
    ap["E80"] = nc.dram_tensor("E80", [16, 128, 128], BF16, kind="ExternalInput").ap()
    ap["E01"] = nc.dram_tensor("E01", [16, 128, 128], BF16, kind="ExternalInput").ap()
    ap["init01"] = nc.dram_tensor("init01", [128, 256], BF16, kind="ExternalInput").ap()
    ap["zero512"] = nc.dram_tensor("zero512", [1, 512], BF16, kind="ExternalInput").ap()
    ap["attn_out"] = nc.dram_tensor("attn_out", [8, 128, 4096], BF16, kind="ExternalOutput").ap()

    sb = {}
    alloc_attn1_bufs(nc, sb)
    ps = [nc.alloc_psum_tensor(f"ps{i}", [128, 512], F32) for i in range(7)]
    P = Prog(nc)
    with nc.Block() as block:
        P.dma("sync", sb["zero512"][0:1, :], ap["zero512"], writes=["zero512"], stream="init")
        P.add("vector", lambda e: e.memset(sb["neg80"][:], -MSKB), writes=["neg80"])
        P.add("vector", lambda e: e.memset(sb["zero1"][:], 0.0), writes=["zero1"])
        emit_attn1(P, nc, ap, sb, ps)
        P.emit(block)
    return nc


def host_attn1_consts():
    C = {}
    inv = 1.0 / 10000.0 ** (np.arange(0, 128, 2, dtype=np.float64) / 128)
    pos = np.arange(4096, dtype=np.float64)
    fr = pos[:, None] * inv[None, :]
    emb = np.concatenate([fr, fr], axis=1)
    C["C"] = np.ascontiguousarray(np.cos(emb).T.astype(ml_dtypes.bfloat16))
    C["S"] = np.ascontiguousarray(np.sin(emb).T.astype(ml_dtypes.bfloat16))
    prot = np.zeros((128, 128), np.float32)
    for d in range(64):
        prot[d, d + 64] = -1.0
        prot[d + 64, d] = 1.0
    C["protT"] = np.ascontiguousarray(prot.T)
    C["identF"] = np.eye(128, dtype=np.float32)
    kk = np.arange(128)[:, None]
    rr = np.arange(128)[None, :]
    diag = np.where(kk <= rr, 0.0, CAUS).astype(np.float32)
    C["Ddiag4"] = np.ascontiguousarray(np.tile(diag, (1, 4)))
    prev = np.where(kk > rr, PEN, 0.0).astype(np.float32)
    C["Dprev4"] = np.ascontiguousarray(np.tile(prev, (1, 4)))
    C["recent4"] = np.ascontiguousarray(np.tile((kk > rr), (1, 4)).astype(ml_dtypes.bfloat16))
    dc = np.zeros((4, 128, 512), np.float32)
    for v in range(4):
        for p2 in range(4):
            blk = dc[v][:, 128 * p2:128 * (p2 + 1)]
            if p2 < v:
                blk[:] = 0.0
            elif p2 == v:
                blk[:] = np.where(kk.T >= rr.T, 0.0, CAUS).T
            else:
                blk[:] = CAUS
    for v in range(4):
        dc[v][:, 128 * v:128 * (v + 1)] = np.where(rr <= kk, 0.0, CAUS)
    C["DcausQK"] = np.ascontiguousarray(dc)
    e80 = np.zeros((16, 128, 128), np.float32)
    for v in range(16):
        for j in range(8):
            e80[v, 8 * v + j, 16 * j:16 * (j + 1)] = PEN
    C["E80"] = e80.astype(ml_dtypes.bfloat16)
    C["E01"] = (e80 / PEN).astype(ml_dtypes.bfloat16)
    ini = np.zeros((128, 256), np.float32)
    ini[:, 0:8] = 1.0
    C["init01"] = ini.astype(ml_dtypes.bfloat16)
    C["zero512"] = np.zeros((1, 512), ml_dtypes.bfloat16)
    return C


def host_attn1_weights(inp, c):
    b, q4 = c // 4, c % 4
    kvhs = [2 * q4, 2 * q4 + 1]
    heads = [4 * kvh + g for kvh in kvhs for g in range(4)]
    wq = np.asarray(inp["L1_attn_wq"], np.float32)
    wk = np.asarray(inp["L1_attn_wk"], np.float32)
    wv = np.asarray(inp["L1_attn_wv"], np.float32)
    d = {}
    d["wqT_sl"] = np.ascontiguousarray(
        np.concatenate([wq[128 * h:128 * (h + 1)].T for h in heads], axis=1))
    d["wkT_sl"] = np.ascontiguousarray(
        np.concatenate([wk[128 * kvh:128 * (kvh + 1)].T for kvh in kvhs], axis=1))
    d["wvT_sl"] = np.ascontiguousarray(
        np.concatenate([wv[128 * kvh:128 * (kvh + 1)].T for kvh in kvhs], axis=1))
    return d


def emit_attn0(P, nc, ap, sb, ps):
    for nm in ("C0", "S0", "pswapT", "identF", "Dfdiag4", "Dsdiag4", "Dsprev4"):
        P.dma("sync", sb[nm][:], ap[nm], writes=[nm], stream="init")
    P.dma("sync", sb["E80"][:].rearrange("p (v f) -> p v f", f=128),
          ap["E80"].rearrange("v p f -> p v f"), writes=["E80"], stream="init")
    P.dma("sync", sb["bgate"][0:1, :], ap["bgate_row"], writes=["bgate"], stream="init")
    P.dma("sync", sb["onesr"][0:1, :], ap["ones_row"], writes=["onesr"], stream="init")
    P.dma("sync", sb["kb1"][:], ap["kb1_cols"], writes=["kb1"], stream="init")
    P.dma("sync", sb["vb1"][:], ap["vb1_cols"], writes=["vb1"], stream="init")
    P.dma("sync", sb["kb2"][:], ap["kb2_col"], writes=["kb2"], stream="init")
    P.dma("sync", sb["vb2"][:], ap["vb2_col"], writes=["vb2"], stream="init")
    P.dma("sync", sb["kpos"][:], ap["kpos_cols"], writes=["kpos"], stream="init")
    P.dma("sync", sb["vpos"][:], ap["vpos_cols"], writes=["vpos"], stream="init")
    P.dma("sync", sb["memk"][:], ap["memk_cols"], writes=["memk"], stream="init")
    P.dma("sync", sb["memvr"][0:1, :], ap["memv_rows"].rearrange("a p b -> p (a b)"),
          writes=["memvr"], stream="init")
    P.dma("sync", sb["wgate"][:].rearrange("p (dt o) -> p dt o", o=24),
          ap["wgate_bf"].rearrange("(dt p) o -> p dt o", p=128), writes=["wgate"], stream="init")

    RQK = [("rq", oi, tt) for oi in range(4) for tt in range(8)] + [("rk", tt) for tt in range(8)]
    for pair in range(PAIRS):
        for tt in range(8):
            for dt in range(32):
                P.dma("gpsimd", sb["lnt"][0][:, 512 * dt:512 * (dt + 1)],
                      ap["lnf"](dt, 512 * tt, 512 * (tt + 1)),
                      reads=tuple(ap.get("lnf_dep", ())),
                      writes=[("lnt", 0, dt)], stream="lnt0")
            if pair == 0:
                for sub in range(4):
                    i2 = 4 * tt + sub
                    for dt in range(32):
                        P.add("tensor", lambda e, dt=dt, sub=sub: e.matmul(
                            ps[7].ap()[:, 0:24],
                            sb["lnt"][0][:, 512 * dt + 128 * sub:512 * dt + 128 * (sub + 1)],
                            sb["wgate"][:, 24 * dt:24 * (dt + 1)],
                            start=(dt == 0), stop=False),
                            reads=[("lnt", 0, dt), "wgate", ("ps", 7)], writes=[("ps", 7)])
                    P.add("tensor", lambda e: e.matmul(
                        ps[7].ap()[:, 0:24], sb["onesr"][0:1, :], sb["bgate"][0:1, :],
                        start=False, stop=True),
                        reads=["onesr", "bgate", ("ps", 7)], writes=[("ps", 7)])
                    P.add("scalar", lambda e, i2=i2: e.activation(
                        out=sb["gates"][:, 24 * i2:24 * (i2 + 1)], in_=ps[7].ap()[:, 0:24],
                        func=AF.Sigmoid, bias=0.0, scale=1.0),
                        reads=[("ps", 7)], writes=[("gates", i2), ("ps", 7)])
            for oi in range(6):
                bank = ps[oi]
                wtile = 6 * pair + oi
                P.dma("sync", sb["wje"][0][:].rearrange("p (dt o) -> p dt o", o=128),
                      ap["wqkv_bf"][wtile].rearrange("(dt p) o -> p dt o", p=128),
                      writes=[("wje", 0)], stream="wje0")
                for dt in range(32):
                    P.add("tensor", lambda e, oi=oi, dt=dt, bank=bank: e.matmul(
                        bank.ap()[:, 0:512],
                        sb["wje"][0][:, 128 * dt:128 * (dt + 1)],
                        sb["lnt"][0][:, 512 * dt:512 * (dt + 1)],
                        start=(dt == 0), stop=(dt == 31)),
                        reads=[("wje", 0), ("lnt", 0, dt)], writes=[("ps", oi)])
                if oi == 5:
                    P.add("scalar", lambda e, bank=bank, tt=tt: e.copy(
                        out=sb["v"][:, 512 * tt:512 * (tt + 1)], in_=bank.ap()[:, 0:512]),
                        reads=[("ps", oi)], writes=[("v", tt), ("ps", oi)])
                elif oi == 4:
                    P.add("scalar", lambda e, bank=bank: e.copy(
                        out=sb["kt"][:], in_=bank.ap()[:, 0:512]),
                        reads=[("ps", oi)], writes=[("kt",), ("ps", oi)])
                    for s2 in range(16):
                        kcol = 16 * pair + s2
                        P.add("vector", lambda e, s2=s2, tt=tt, kcol=kcol: e.tensor_scalar(
                            out=sb["cin"][:, 256 * s2 + 32 * tt:256 * s2 + 32 * (tt + 1)],
                            in0=sb["kt"][:].rearrange("p (c s3) -> p c s3", s3=16)[:, :, s2],
                            scalar1=sb["kpos"][:, kcol:kcol + 1], scalar2=None, op0=ALU.add),
                            reads=[("kt",), "kpos"], writes=[("cin", s2)])
                    P.add("tensor", lambda e: e.matmul(
                        ps[6].ap()[:, 0:512], sb["pswapT"][:], sb["kt"][:],
                        start=True, stop=True),
                        reads=[("kt",), "pswapT", ("ps", 6)], writes=[("ps", 6)])
                    P.add("vector", lambda e, tt=tt: e.tensor_mul(
                        out=sb["t1"][:], in0=ps[6].ap()[:, 0:512],
                        in1=sb["S0"][:, 512 * tt:512 * (tt + 1)]),
                        reads=[("ps", 6), "S0"], writes=["t1", ("ps", 6)])
                    P.add("vector", lambda e, tt=tt: e.tensor_mul(
                        out=sb["qf"][:], in0=sb["kt"][:],
                        in1=sb["C0"][:, 512 * tt:512 * (tt + 1)]),
                        reads=[("kt",), "C0"], writes=["qf"])
                    P.add("vector", lambda e, tt=tt: e.tensor_add(
                        out=sb["rk"][:, 512 * tt:512 * (tt + 1)], in0=sb["qf"][:], in1=sb["t1"][:]),
                        reads=["qf", "t1"], writes=[("rk", tt)])
                else:
                    P.add("scalar", lambda e, bank=bank: e.copy(
                        out=sb["qf"][:], in_=bank.ap()[:, 0:512]),
                        reads=[("ps", oi)], writes=["qf", ("ps", oi)])
                    P.add("vector", lambda e: e.tensor_copy(
                        out=sb["qcb"][:], in_=sb["qf"][:]),
                        reads=["qf"], writes=["qcb"])
                    P.dma("gpsimd", ap["qc_hbm"][pair, oi, :, 512 * tt:512 * (tt + 1)],
                          sb["qcb"][:], reads=["qcb"], writes=[("qc", pair, oi, tt)],
                          stream="qcst")
                    P.add("tensor", lambda e: e.matmul(
                        ps[6].ap()[:, 0:512], sb["pswapT"][:], sb["qf"][:],
                        start=True, stop=True),
                        reads=["qf", "pswapT", ("ps", 6)], writes=[("ps", 6)])
                    P.add("vector", lambda e, tt=tt: e.tensor_mul(
                        out=sb["t1"][:], in0=ps[6].ap()[:, 0:512],
                        in1=sb["S0"][:, 512 * tt:512 * (tt + 1)]),
                        reads=[("ps", 6), "S0"], writes=["t1", ("ps", 6)])
                    P.add("vector", lambda e, tt=tt: e.tensor_mul(
                        out=sb["qf"][:], in0=sb["qf"][:],
                        in1=sb["C0"][:, 512 * tt:512 * (tt + 1)]),
                        reads=["qf", "C0"], writes=["qf"])
                    P.add("vector", lambda e, oi=oi, tt=tt: e.tensor_add(
                        out=sb["q_all"][:, 4096 * oi + 512 * tt:4096 * oi + 512 * (tt + 1)],
                        in0=sb["qf"][:], in1=sb["t1"][:]),
                        reads=["qf", "t1"], writes=[("rq", oi, tt)])
        for kt in range(32):
            P.add("vector", lambda e, kt=kt: e.memset(sb["vt"][:, 256 * kt + 128:256 * kt + 129], 1.0),
                  reads=[], writes=[("vt1", kt)])
            P.add("sync", lambda e, kt=kt: e.dma_start_transpose(
                sb["vt"][:, 256 * kt:256 * kt + 128], sb["v"][:, 128 * kt:128 * (kt + 1)]),
                reads=[("v", kt // 4)], writes=[("vt", kt)], kind="d", stream="vtp")
        VALL = [("v", tt) for tt in range(8)]
        _emit_comp_mlp(P, nc, ap, sb, ps, pair, "k")
        for s2 in range(16):
            vcol = 16 * pair + s2
            P.add("vector", lambda e, s2=s2, vcol=vcol: e.tensor_scalar(
                out=sb["cin"][:, 256 * s2:256 * (s2 + 1)],
                in0=sb["v"][:].rearrange("p (c s3) -> p c s3", s3=16)[:, :, s2],
                scalar1=sb["vpos"][:, vcol:vcol + 1],
                scalar2=None, op0=ALU.add),
                reads=VALL + ["vpos", ("ckdone",)], writes=[("cin", s2)])
        _emit_comp_mlp(P, nc, ap, sb, ps, pair, "v")
        for c2 in range(2):
            P.add("vector", lambda e, c2=c2: e.memset(
                sb["cvt"][:, 256 * c2 + 128:256 * c2 + 129], 1.0), reads=[], writes=[("cvt1", c2)])
            P.add("sync", lambda e, c2=c2: e.dma_start_transpose(
                sb["cvt"][:, 256 * c2:256 * c2 + 128], sb["cv"][:, 128 * c2:128 * (c2 + 1)]),
                reads=[("cv",)], writes=[("cvt", c2)], kind="d", stream="vtp")
        P.add("vector", lambda e, pair=pair: e.tensor_scalar_mul(
            out=sb["memk_bf"][:, 0:1], in0=sb["memk"][:, pair:pair + 1], scalar1=SCALE),
            reads=["memk"], writes=["memk_bf"])
        P.add("vector", lambda e, pair=pair: e.tensor_copy(
            out=sb["memv_bf"][0:1, :], in_=sb["memvr"][0:1, 129 * pair:129 * (pair + 1)]),
            reads=["memvr"], writes=["memv_bf"])
        for i in range(32):
            nkt = i + 1
            gslot = ("gates", i)
            for hh in range(4):
                P.dma("sync", sb["qcs"][:, 128 * hh:128 * (hh + 1)],
                      ap["qc_hbm"][pair, hh, :, 128 * i:128 * (i + 1)],
                      reads=[("qc", pair, hh, i // 4)], writes=[("qcs", hh)],
                      stream="qcld")
            for jt in range(2):
                P.dma("gpsimd", sb["Dc"][:], ap["Dc"][2 * i + jt],
                      writes=["Dc"], stream="dcld")
                P.add("tensor", lambda e, jt=jt: e.matmul(
                    ps[jt].ap()[:, 0:512],
                    sb["ck"][:, 128 * jt:128 * (jt + 1)],
                    sb["qcs"][:],
                    start=True, stop=True),
                    reads=[("ck",)] + [("qcs", hh) for hh in range(4)] + [("ps", jt)],
                    writes=[("ps", jt)])
                P.add("vector", lambda e, jt=jt: e.scalar_tensor_tensor(
                    out=sb["csf"][:], in0=sb["Dc"][:], scalar=1.0, in1=ps[jt].ap()[:, 0:512],
                    op0=ALU.mult, op1=ALU.add),
                    reads=["Dc", ("ps", jt)], writes=["csf", ("ps", jt)])
                P.add("scalar", lambda e, jt=jt: e.copy(
                    out=sb["csraw"][:, 512 * jt:512 * (jt + 1)], in_=sb["csf"][:]),
                    reads=["csf"], writes=[("csraw", jt)])
                P.add("scalar", lambda e, jt=jt: e.activation(
                    out=sb["pc"][:, 512 * jt:512 * (jt + 1)], in_=sb["csf"][:],
                    func=AF.Exp, bias=0.0, scale=1.0),
                    reads=["csf"], writes=[("pc", jt)])
            P.add("tensor", lambda e: e.matmul(
                ps[2].ap()[0:1, 0:512], sb["memk_bf"][:, 0:1],
                sb["qcs"][:],
                start=True, stop=True),
                reads=["memk_bf"] + [("qcs", hh) for hh in range(4)] + [("ps", 2)],
                writes=[("ps", 2)])
            P.add("scalar", lambda e: e.activation(
                out=sb["pmem"][0:1, 0:512], in_=ps[2].ap()[0:1, 0:512], func=AF.Exp,
                bias=0.0, scale=1.0),
                reads=[("ps", 2)], writes=["pmem", ("ps", 2)])
            for hh in range(4):
                pb = ps[3 + hh]
                for jt in range(2):
                    P.add("tensor", lambda e, jt=jt, hh=hh, pb=pb: e.matmul(
                        pb.ap()[:, 0:129],
                        sb["pc"][:, 512 * jt + 128 * hh:512 * jt + 128 * (hh + 1)],
                        sb["cvt"][:, 256 * jt:256 * jt + 129],
                        start=(jt == 0), stop=False),
                        reads=[("pc", jt), ("cvt", jt), ("cvt1", jt), ("ps", 3 + hh)],
                        writes=[("ps", 3 + hh)])
                P.add("tensor", lambda e, hh=hh, pb=pb: e.matmul(
                    pb.ap()[:, 0:129], sb["pmem"][0:1, 128 * hh:128 * (hh + 1)],
                    sb["memv_bf"][0:1, 0:129], start=False, stop=True),
                    reads=["pmem", "memv_bf", ("ps", 3 + hh)], writes=[("ps", 3 + hh)])
                P.add("scalar", lambda e, hh=hh, pb=pb: e.copy(
                    out=sb["cpv"][hh][:, 0:129], in_=pb.ap()[:, 0:129]),
                    reads=[("ps", 3 + hh)], writes=[("cpv", hh), ("ps", 3 + hh)])
            for jt in range(2):
                base = 512 * jt
                P.add("vector", lambda e, base=base: e.tensor_add(
                    out=sb["impa"][:], in0=sb["csraw"][:, base:base + 128],
                    in1=sb["csraw"][:, base + 128:base + 256]),
                    reads=[("csraw", jt)], writes=["impa"])
                P.add("vector", lambda e, base=base: e.tensor_add(
                    out=sb["impb"][:], in0=sb["csraw"][:, base + 256:base + 384],
                    in1=sb["csraw"][:, base + 384:base + 512]),
                    reads=[("csraw", jt)], writes=["impb"])
                P.add("vector", lambda e: e.tensor_add(
                    out=sb["impa"][:], in0=sb["impa"][:], in1=sb["impb"][:]),
                    reads=["impa", "impb"], writes=["impa"])
                P.add("vector", lambda e, jt=jt: e.tensor_scalar_mul(
                    out=sb["impj"][:, 128 * jt:128 * (jt + 1)], in0=sb["impa"][:], scalar1=0.25),
                    reads=["impa"], writes=[("impj", jt)])
                P.add("scalar", lambda e, jt=jt: e.dma_start_transpose(
                    sb["impq"][:, 128 * jt:128 * (jt + 1)], sb["impj"][:, 128 * jt:128 * (jt + 1)]),
                    reads=[("impj", jt)], writes=[("impq", jt)], kind="d", stream="itp")
            P.add("vector", lambda e: e.reduce_max(out=sb["im"][:], in_=sb["impq"][:], axis=AX.X),
                  reads=[("impq", 0), ("impq", 1)], writes=["im"])
            P.add("vector", lambda e: e.tensor_scalar_max(out=sb["im"][:], in0=sb["im"][:], scalar1=-1000.0),
                  reads=["im"], writes=["im"])
            P.add("vector", lambda e: e.tensor_scalar_mul(out=sb["imn"][:], in0=sb["im"][:], scalar1=-1.0),
                  reads=["im"], writes=["imn"])
            P.add("scalar", lambda e: e.activation(
                out=sb["uu"][:], in_=sb["impq"][:], func=AF.Exp, bias=sb["imn"][:, 0:1], scale=1.0),
                reads=[("impq", 0), ("impq", 1), "imn"], writes=["uu"])
            P.add("vector", lambda e: e.reduce_sum(out=sb["zz"][:], in_=sb["uu"][:], axis=AX.X),
                  reads=["uu"], writes=["zz"])
            P.add("scalar", lambda e: e.activation(
                out=sb["sen"][:], in_=sb["imn"][:], func=AF.Exp, bias=-1000.0 if False else sb["negk"][:, 0:1],
                scale=1.0), reads=["imn", "negk"], writes=["sen"])
            P.add("vector", lambda e: e.tensor_add(out=sb["zz"][:], in0=sb["zz"][:], in1=sb["sen"][:]),
                  reads=["zz", "sen"], writes=["zz"])
            P.add("vector", lambda e: e.tensor_scalar_mul(out=sb["zz"][:], in0=sb["zz"][:], scalar1=1e-10),
                  reads=["zz"], writes=["zz"])
            P.add("vector", lambda e: e.max(out=sb["m8"][:], in_=sb["uu"][:]), reads=["uu"], writes=["m8"])
            P.dma("gpsimd", sb["t4b"][:], sb["m8"][:, 3:4], reads=["m8"], writes=["t4b"], stream="t8cp")
            P.add("vector", lambda e: e.tensor_scalar(
                out=sb["selq"][:], in0=sb["uu"][:], scalar1=sb["t4b"][:, 0:1], scalar2=None, op0=ALU.is_ge),
                reads=["uu", "t4b"], writes=["selq"])
            P.add("vector", lambda e: e.tensor_scalar(
                out=sb["selq2"][:], in0=sb["uu"][:], scalar1=sb["zz"][:, 0:1], scalar2=None, op0=ALU.is_gt),
                reads=["uu", "zz"], writes=["selq2"])
            P.add("vector", lambda e: e.tensor_mul(
                out=sb["selq"][:], in0=sb["selq"][:], in1=sb["selq2"][:]),
                reads=["selq", "selq2"], writes=["selq"])
            for gI in range(2):
                for rep in range(4):
                    P.add("scalar", lambda e, gI=gI, rep=rep: e.dma_start_transpose(
                        sb["selT"][:, 512 * gI + 128 * rep:512 * gI + 128 * (rep + 1)],
                        sb["selq"][:, 128 * gI:128 * (gI + 1)]),
                        reads=["selq"], writes=[("selT", gI, rep)], kind="d", stream="stp")
            for kt in range(nkt):
                bi = kt % 2
                bank = ps[bi]
                gI = kt // 16
                selk = [("selT", gI, rep) for rep in range(4)]
                P.add("tensor", lambda e, kt=kt, i=i, bank=bank: e.matmul(
                    bank.ap()[:, 0:512],
                    sb["rk"][:, 128 * kt:128 * (kt + 1)],
                    sb["q_all"][:].rearrange("p (h n) -> p h n", n=4096)[:, :, 128 * i:128 * (i + 1)],
                    start=True, stop=False),
                    reads=RQK + [("ps", bi)], writes=[("ps", bi)])
                P.add("tensor", lambda e, kt=kt, gI=gI, bank=bank: e.matmul(
                    bank.ap()[:, 0:512],
                    sb["E80"][:, 128 * (kt % 16):128 * (kt % 16 + 1)],
                    sb["selT"][:, 512 * gI:512 * (gI + 1)],
                    start=False, stop=True),
                    reads=["E80", ("ps", bi)] + selk, writes=[("ps", bi)])
                ppar = kt % 2
                if kt == i:
                    P.add("vector", lambda e, bank=bank: e.scalar_tensor_tensor(
                        out=sb["csf"][:], in0=sb["Dfdiag4"][:], scalar=1.0,
                        in1=bank.ap()[:, 0:512], op0=ALU.mult, op1=ALU.add),
                        reads=["Dfdiag4", ("ps", bi)], writes=["csf", ("ps", bi)])
                    P.add("scalar", lambda e, ppar=ppar: e.activation(
                        out=sb["pp"][ppar][:], in_=sb["csf"][:], func=AF.Exp,
                        bias=sb["neg80"][:, 0:1], scale=SCALE),
                        reads=["csf", "neg80"], writes=[("pp", ppar)])
                else:
                    P.add("scalar", lambda e, bank=bank, ppar=ppar: e.activation(
                        out=sb["pp"][ppar][:], in_=bank.ap()[:, 0:512], func=AF.Exp,
                        bias=sb["neg80"][:, 0:1], scale=SCALE),
                        reads=[("ps", bi), "neg80"], writes=[("pp", ppar), ("ps", bi)])
                for hh in range(4):
                    pb = ps[3 + hh]
                    P.add("tensor", lambda e, kt=kt, hh=hh, pb=pb, ppar=ppar, i=i: e.matmul(
                        pb.ap()[:, 0:129],
                        sb["pp"][ppar][:, 128 * hh:128 * (hh + 1)],
                        sb["vt"][:, 256 * kt:256 * kt + 129],
                        start=(kt == 0), stop=(kt == i)),
                        reads=[("pp", ppar), ("vt", kt), ("vt1", kt), ("ps", 3 + hh)],
                        writes=[("ps", 3 + hh)])
            for hh in range(4):
                P.add("scalar", lambda e, hh=hh: e.copy(
                    out=sb["fpv"][hh][:, 0:129], in_=ps[3 + hh].ap()[:, 0:129]),
                    reads=[("ps", 3 + hh)], writes=[("fpv", hh), ("ps", 3 + hh)])
            skts = [kt for kt in (i - 1, i) if kt >= 0]
            for sj, kt in enumerate(skts):
                bi = sj % 2
                bank = ps[bi]
                dname = "Dsprev4" if kt == i - 1 else "Dsdiag4"
                P.add("tensor", lambda e, kt=kt, i=i, bank=bank: e.matmul(
                    bank.ap()[:, 0:512],
                    sb["rk"][:, 128 * kt:128 * (kt + 1)],
                    sb["q_all"][:].rearrange("p (h n) -> p h n", n=4096)[:, :, 128 * i:128 * (i + 1)],
                    start=True, stop=True),
                    reads=RQK + [("ps", bi)], writes=[("ps", bi)])
                P.add("vector", lambda e, bank=bank, dname=dname: e.scalar_tensor_tensor(
                    out=sb["csf"][:], in0=sb[dname][:], scalar=1.0,
                    in1=bank.ap()[:, 0:512], op0=ALU.mult, op1=ALU.add),
                    reads=[dname, ("ps", bi)], writes=["csf", ("ps", bi)])
                P.add("scalar", lambda e, sj=sj: e.activation(
                    out=sb["pp"][sj][:], in_=sb["csf"][:], func=AF.Exp,
                    bias=sb["neg80"][:, 0:1], scale=SCALE),
                    reads=["csf", "neg80"], writes=[("pp", sj)])
                for hh in range(4):
                    pb = ps[3 + hh]
                    P.add("tensor", lambda e, kt=kt, hh=hh, pb=pb, sj=sj, last=(kt == i): e.matmul(
                        pb.ap()[:, 0:129],
                        sb["pp"][sj][:, 128 * hh:128 * (hh + 1)],
                        sb["vt"][:, 256 * kt:256 * kt + 129],
                        start=(sj == 0), stop=last),
                        reads=[("pp", sj), ("vt", kt), ("vt1", kt), ("ps", 3 + hh)],
                        writes=[("ps", 3 + hh)])
            for hh in range(4):
                pb = ps[3 + hh]
                P.add("scalar", lambda e, hh=hh, pb=pb: e.copy(
                    out=sb["spvh"][hh][:, 0:129], in_=pb.ap()[:, 0:129]),
                    reads=[("ps", 3 + hh)], writes=[("spv", hh), ("ps", 3 + hh)])
            for kk in range(12):
                hh2, br2 = kk // 3, kk % 3
                src = (sb["cpv"], sb["fpv"], sb["spvh"])[br2][hh2]
                skey = (("cpv", hh2), ("fpv", hh2), ("spv", hh2))[br2]
                P.add("vector", lambda e, kk=kk, src=src: e.reciprocal(
                    out=sb["rs2b"][kk][:], in_=src[:, 128:129]),
                    reads=[skey], writes=[("rs2b", kk)])
            for kk in range(12):
                hh2, br2 = kk // 3, kk % 3
                gb = 24 * i + 3 * (4 * pair + hh2) + br2
                P.add("vector", lambda e, kk=kk, gb=gb: e.tensor_mul(
                    out=sb["coefb"][kk][:], in0=sb["rs2b"][kk][:],
                    in1=sb["gates"][:, gb:gb + 1]),
                    reads=[("rs2b", kk), gslot], writes=[("coefb", kk)])
            for hh in range(4):
                for br, buf in ((0, "cpv"), (2, "spv"), (1, "fpv")):
                    kk = 3 * hh + br
                    src2 = {"cpv": sb["cpv"][hh], "fpv": sb["fpv"][hh],
                            "spv": sb["spvh"][hh]}[buf]
                    if br == 0:
                        P.add("vector", lambda e, src2=src2, kk=kk: e.tensor_scalar_mul(
                            out=sb["att"][:], in0=src2[:, 0:128],
                            scalar1=sb["coefb"][kk][:, 0:1]),
                            reads=[(buf, hh), ("coefb", kk)], writes=["att"])
                    else:
                        P.add("vector", lambda e, src2=src2, kk=kk: e.scalar_tensor_tensor(
                            out=sb["att"][:], in0=src2[:, 0:128],
                            scalar=sb["coefb"][kk][:, 0:1], in1=sb["att"][:],
                            op0=ALU.mult, op1=ALU.add),
                            reads=[(buf, hh), ("coefb", kk), "att"], writes=["att"])
                P.add("scalar", lambda e: e.copy(out=sb["attb"][:], in_=sb["att"][:]),
                      reads=["att"], writes=["attb"])
                P.add("sync", lambda e: e.dma_start_transpose(sb["attT"][:], sb["attb"][:]),
                      reads=["attb"], writes=["attT"], kind="d", stream="atp")
                P.dma("gpsimd", ap["attn_out"][4 * pair + hh, :, 128 * i:128 * (i + 1)],
                      sb["attT"][:], reads=["attT"], writes=[("ao", pair, hh, i)], stream="aost")
    return


def _emit_comp_mlp(P, nc, ap, sb, ps, pair, side):
    wsh = ap[("kw1_sh" if side == "k" else "vw1_sh")]
    w2 = ap[("kw2_bf" if side == "k" else "vw2_bf")]
    bcols = "kb1" if side == "k" else "vb1"
    b2col = "kb2" if side == "k" else "vb2"
    outk = "ck" if side == "k" else "cv"
    for o2 in range(16):
        par = 0
        P.dma("scalar", sb["w1b"][par][:].rearrange("p (ct o) -> p ct o", o=128),
              wsh[o2].rearrange("(ct p) o -> p ct o", p=128),
              writes=[("w1b", par)], stream=f"w1b{par}")
        bank = ps[o2 % 2]
        cin_keys = [("cin", s2) for s2 in range(16)]
        for ct in range(16):
            P.add("tensor", lambda e, o2=o2, ct=ct, bank=bank, par=par: e.matmul(
                bank.ap()[:, 0:256],
                sb["w1b"][par][:, 128 * ct:128 * (ct + 1)],
                sb["cin"][:, 256 * ct:256 * (ct + 1)],
                start=(ct == 0), stop=(ct == 15)),
                reads=[("w1b", par)] + cin_keys, writes=[("ps", o2 % 2)])
        P.add("scalar", lambda e, o2=o2, bank=bank, bcols=bcols: e.activation(
            out=sb["h1"][:, 256 * o2:256 * (o2 + 1)], in_=bank.ap()[:, 0:256],
            func=AF.Relu, bias=sb[bcols][:, o2:o2 + 1], scale=1.0),
            reads=[("ps", o2 % 2), bcols], writes=[("h1", o2), ("ps", o2 % 2)])
    P.dma("scalar", sb["w2b"][:].rearrange("p (ct o) -> p ct o", o=128),
          w2[0].rearrange("(ct p) o -> p ct o", p=128), writes=["w2b"], stream="w2b")
    for ct in range(16):
        P.add("tensor", lambda e, ct=ct: e.matmul(
            ps[2].ap()[:, 0:256], sb["w2b"][:, 128 * ct:128 * (ct + 1)],
            sb["h1"][:, 256 * ct:256 * (ct + 1)],
            start=(ct == 0), stop=(ct == 15)),
            reads=["w2b", ("h1", ct)], writes=[("ps", 2)])
    if side == "k":
        P.add("vector", lambda e: e.tensor_scalar(
            out=sb["ck"][:], in0=ps[2].ap()[:, 0:256], scalar1=sb["kb2"][:, 0:1],
            scalar2=SCALE, op0=ALU.add, op1=ALU.mult),
            reads=[("ps", 2), "kb2"], writes=[("ck",), ("ps", 2), ("ckdone",)])
    else:
        P.add("vector", lambda e: e.tensor_scalar(
            out=sb["cv"][:], in0=ps[2].ap()[:, 0:256], scalar1=sb["vb2"][:, 0:1],
            scalar2=None, op0=ALU.add),
            reads=[("ps", 2), "vb2"], writes=[("cv",), ("ps", 2)])


def build_attn0():
    nc = bass.Bass("TRN2", target_bir_lowering=False)
    ap = {}
    ap["ln_full"] = nc.dram_tensor("ln_full", [32, 128, 4096], BF16, kind="ExternalInput").ap()
    ap["lnf"] = lambda dt, c0, c1: ap["ln_full"][dt, :, c0:c1]
    ap["wqkv_bf"] = nc.dram_tensor("wqkv_bf", [12, 4096, 128], BF16, kind="ExternalInput").ap()
    ap["wgate_bf"] = nc.dram_tensor("wgate_bf", [4096, 24], BF16, kind="ExternalInput").ap()
    ap["kw1_sh"] = nc.dram_tensor("kw1_sh", [16, 2048, 128], BF16, kind="ExternalInput").ap()
    ap["vw1_sh"] = nc.dram_tensor("vw1_sh", [16, 2048, 128], BF16, kind="ExternalInput").ap()
    ap["kw2_bf"] = nc.dram_tensor("kw2_bf", [1, 2048, 128], BF16, kind="ExternalInput").ap()
    ap["vw2_bf"] = nc.dram_tensor("vw2_bf", [1, 2048, 128], BF16, kind="ExternalInput").ap()
    ap["C0"] = nc.dram_tensor("C0", [128, 4096], BF16, kind="ExternalInput").ap()
    ap["S0"] = nc.dram_tensor("S0", [128, 4096], BF16, kind="ExternalInput").ap()
    ap["pswapT"] = nc.dram_tensor("pswapT", [128, 128], F32, kind="ExternalInput").ap()
    ap["identF"] = nc.dram_tensor("identF", [128, 128], F32, kind="ExternalInput").ap()
    ap["Dfdiag4"] = nc.dram_tensor("Dfdiag4", [128, 512], BF16, kind="ExternalInput").ap()
    ap["Dsdiag4"] = nc.dram_tensor("Dsdiag4", [128, 512], BF16, kind="ExternalInput").ap()
    ap["Dsprev4"] = nc.dram_tensor("Dsprev4", [128, 512], BF16, kind="ExternalInput").ap()
    ap["Dc"] = nc.dram_tensor("Dc", [64, 128, 512], BF16, kind="ExternalInput").ap()
    ap["E80"] = nc.dram_tensor("E80", [16, 128, 128], BF16, kind="ExternalInput").ap()
    ap["ones_row"] = nc.dram_tensor("ones_row", [1, 128], F32, kind="ExternalInput").ap()
    ap["bgate_row"] = nc.dram_tensor("bgate_row", [1, 24], F32, kind="ExternalInput").ap()
    ap["kb1_cols"] = nc.dram_tensor("kb1_cols", [128, 16], F32, kind="ExternalInput").ap()
    ap["vb1_cols"] = nc.dram_tensor("vb1_cols", [128, 16], F32, kind="ExternalInput").ap()
    ap["kb2_col"] = nc.dram_tensor("kb2_col", [128, 1], F32, kind="ExternalInput").ap()
    ap["vb2_col"] = nc.dram_tensor("vb2_col", [128, 1], F32, kind="ExternalInput").ap()
    ap["kpos_cols"] = nc.dram_tensor("kpos_cols", [128, 32], F32, kind="ExternalInput").ap()
    ap["vpos_cols"] = nc.dram_tensor("vpos_cols", [128, 32], F32, kind="ExternalInput").ap()
    ap["memk_cols"] = nc.dram_tensor("memk_cols", [128, 2], F32, kind="ExternalInput").ap()
    ap["memv_rows"] = nc.dram_tensor("memv_rows", [2, 1, 129], F32, kind="ExternalInput").ap()
    ap["attn_out"] = nc.dram_tensor("attn_out", [8, 128, 4096], BF16, kind="ExternalOutput").ap()
    ap["qc_hbm"] = nc.dram_tensor("qc_hbm", [2, 4, 128, 4096], BF16, kind="Internal").ap()

    sb = {}
    alloc_attn0_bufs(nc, sb)
    ps = [nc.alloc_psum_tensor(f"ps{i}", [128, 512], F32) for i in range(8)]
    P = Prog(nc)
    with nc.Block() as block:
        P.add("vector", lambda e: e.memset(sb["neg80"][:], -MSKB), writes=["neg80"])
        P.add("vector", lambda e: e.memset(sb["negk"][:], -1000.0), writes=["negk"])
        emit_attn0(P, nc, ap, sb, ps)
        P.emit(block)
    return nc


def alloc_attn0_bufs(nc, sb):
    sb["C0"] = nc.alloc_sbuf_tensor("s_C0", [128, 4096], BF16)
    sb["S0"] = nc.alloc_sbuf_tensor("s_S0", [128, 4096], BF16)
    sb["pswapT"] = nc.alloc_sbuf_tensor("s_pswapT", [128, 128], F32)
    sb["identF"] = nc.alloc_sbuf_tensor("s_identF", [128, 128], F32)
    sb["Dfdiag4"] = nc.alloc_sbuf_tensor("s_Dfdiag4", [128, 512], BF16)
    sb["Dsdiag4"] = nc.alloc_sbuf_tensor("s_Dsdiag4", [128, 512], BF16)
    sb["Dsprev4"] = nc.alloc_sbuf_tensor("s_Dsprev4", [128, 512], BF16)
    sb["Dc"] = nc.alloc_sbuf_tensor("s_Dc", [128, 512], BF16)
    sb["E80"] = nc.alloc_sbuf_tensor("s_E80", [128, 16 * 128], BF16)
    sb["onesr"] = nc.alloc_sbuf_tensor("s_onesr", [1, 128], F32)
    sb["bgate"] = nc.alloc_sbuf_tensor("s_bgate", [1, 24], F32)
    sb["wgate"] = nc.alloc_sbuf_tensor("s_wgate", [128, 32 * 24], BF16)
    sb["gates"] = nc.alloc_sbuf_tensor("s_gates", [128, 32 * 24], BF16)
    sb["lng"] = [nc.alloc_sbuf_tensor(f"s_lng{i}", [128, 128], BF16) for i in range(2)]
    sb["lnt"] = [nc.alloc_sbuf_tensor("s_lnt0", [128, 512 * 32], BF16)]
    sb["wje"] = [nc.alloc_sbuf_tensor("s_wje0", [128, 4096], BF16)]
    sb["q_all"] = nc.alloc_sbuf_tensor("s_q_all", [128, 4 * 4096], BF16)
    sb["qcs"] = nc.alloc_sbuf_tensor("s_qcs", [128, 512], BF16)
    sb["rk"] = nc.alloc_sbuf_tensor("s_rk", [128, 4096], BF16)
    sb["kt"] = nc.alloc_sbuf_tensor("s_kt", [128, 512], F32)
    sb["v"] = nc.alloc_sbuf_tensor("s_v", [128, 4096], BF16)
    sb["vt"] = nc.alloc_sbuf_tensor("s_vt", [128, 32 * 256], BF16)
    sb["qf"] = nc.alloc_sbuf_tensor("s_qf", [128, 512], F32)
    sb["qcb"] = nc.alloc_sbuf_tensor("s_qcb", [128, 512], BF16)
    sb["t1"] = nc.alloc_sbuf_tensor("s_t1", [128, 512], F32)
    sb["csf"] = nc.alloc_sbuf_tensor("s_csf", [128, 512], F32)
    sb["kpos"] = nc.alloc_sbuf_tensor("s_kpos", [128, 32], F32)
    sb["vpos"] = nc.alloc_sbuf_tensor("s_vpos", [128, 32], F32)
    sb["posc"] = nc.alloc_sbuf_tensor("s_posc", [128, 1], F32)
    sb["kb1"] = nc.alloc_sbuf_tensor("s_kb1", [128, 16], F32)
    sb["vb1"] = nc.alloc_sbuf_tensor("s_vb1", [128, 16], F32)
    sb["kb2"] = nc.alloc_sbuf_tensor("s_kb2", [128, 1], F32)
    sb["vb2"] = nc.alloc_sbuf_tensor("s_vb2", [128, 1], F32)
    sb["memk"] = nc.alloc_sbuf_tensor("s_memk", [128, 2], F32)
    sb["memvr"] = nc.alloc_sbuf_tensor("s_memvr", [1, 258], F32)
    sb["memk_bf"] = nc.alloc_sbuf_tensor("s_memk_bf", [128, 1], BF16)
    sb["memv_bf"] = nc.alloc_sbuf_tensor("s_memv_bf", [1, 129], BF16)
    sb["cin"] = nc.alloc_sbuf_tensor("s_cin", [128, 16 * 256], BF16)
    sb["w1b"] = [nc.alloc_sbuf_tensor("s_w1b0", [128, 2048], BF16)]
    sb["w2b"] = nc.alloc_sbuf_tensor("s_w2b", [128, 2048], BF16)
    sb["h1"] = nc.alloc_sbuf_tensor("s_h1", [128, 16 * 256], BF16)
    sb["ck"] = nc.alloc_sbuf_tensor("s_ck", [128, 256], BF16)
    sb["cv"] = nc.alloc_sbuf_tensor("s_cv", [128, 256], BF16)
    sb["cvt"] = nc.alloc_sbuf_tensor("s_cvt", [128, 2 * 256], BF16)
    sb["csraw"] = nc.alloc_sbuf_tensor("s_csraw", [128, 1024], BF16)
    sb["pc"] = nc.alloc_sbuf_tensor("s_pc", [128, 1024], BF16)
    sb["pmem"] = nc.alloc_sbuf_tensor("s_pmem", [1, 512], BF16)
    sb["cpv"] = [nc.alloc_sbuf_tensor(f"s_cpv{h}", [128, 132], F32) for h in range(4)]
    sb["fpv"] = [nc.alloc_sbuf_tensor(f"s_fpv{h}", [128, 132], F32) for h in range(4)]
    sb["spvh"] = [nc.alloc_sbuf_tensor(f"s_spvh{h}", [128, 132], F32) for h in range(4)]
    sb["rsb"] = [nc.alloc_sbuf_tensor(f"s_rsb{k}", [128, 1], F32) for k in range(12)]
    sb["rs2b"] = [nc.alloc_sbuf_tensor(f"s_rs2b{k}", [128, 1], F32) for k in range(12)]
    sb["coefb"] = [nc.alloc_sbuf_tensor(f"s_coefb{k}", [128, 1], F32) for k in range(12)]
    sb["g1b"] = [nc.alloc_sbuf_tensor(f"s_g1b{k}", [128, 1], F32) for k in range(12)]
    sb["impa"] = nc.alloc_sbuf_tensor("s_impa", [128, 128], F32)
    sb["impb"] = nc.alloc_sbuf_tensor("s_impb", [128, 128], F32)
    sb["impj"] = nc.alloc_sbuf_tensor("s_impj", [128, 256], BF16)
    sb["impq"] = nc.alloc_sbuf_tensor("s_impq", [128, 256], BF16)
    sb["im"] = nc.alloc_sbuf_tensor("s_im", [128, 1], F32)
    sb["imn"] = nc.alloc_sbuf_tensor("s_imn", [128, 1], F32)
    sb["uu"] = nc.alloc_sbuf_tensor("s_uu", [128, 256], F32)
    sb["zz"] = nc.alloc_sbuf_tensor("s_zz", [128, 1], F32)
    sb["sen"] = nc.alloc_sbuf_tensor("s_sen", [128, 1], F32)
    sb["m8"] = nc.alloc_sbuf_tensor("s_m8", [128, 8], F32)
    sb["t4b"] = nc.alloc_sbuf_tensor("s_t4b", [128, 1], F32)
    sb["selq"] = nc.alloc_sbuf_tensor("s_selq", [128, 256], BF16)
    sb["selq2"] = nc.alloc_sbuf_tensor("s_selq2", [128, 256], BF16)
    sb["selT"] = nc.alloc_sbuf_tensor("s_selT", [128, 2 * 512], BF16)
    sb["pp"] = [nc.alloc_sbuf_tensor(f"s_pp{i}", [128, 512], BF16) for i in range(2)]
    sb["rs"] = nc.alloc_sbuf_tensor("s_rs", [128, 1], F32)
    sb["rs2"] = nc.alloc_sbuf_tensor("s_rs2", [128, 1], F32)
    sb["coef"] = nc.alloc_sbuf_tensor("s_coef", [128, 1], F32)
    sb["att"] = nc.alloc_sbuf_tensor("s_att", [128, 128], F32)
    sb["attb"] = nc.alloc_sbuf_tensor("s_attb", [128, 128], BF16)
    sb["attT"] = nc.alloc_sbuf_tensor("s_attT", [128, 128], BF16)
    sb["neg80"] = nc.alloc_sbuf_tensor("s_neg80", [128, 1], F32)
    sb["negk"] = nc.alloc_sbuf_tensor("s_negk", [128, 1], F32)


def host_attn0_consts():
    C = {}
    inv = 1.0 / 10000.0 ** (np.arange(0, 128, 2, dtype=np.float64) / 128)
    pos = np.arange(4096, dtype=np.float64)
    ang = pos[:, None] * inv[None, :]
    cosv = np.cos(ang); sinv = np.sin(ang)
    C0 = np.zeros((128, 4096), np.float64)
    S0 = np.zeros((128, 4096), np.float64)
    C0[0::2, :] = cosv.T; C0[1::2, :] = cosv.T
    S0[0::2, :] = -sinv.T; S0[1::2, :] = sinv.T
    C["C0"] = C0.astype(ml_dtypes.bfloat16)
    C["S0"] = S0.astype(ml_dtypes.bfloat16)
    pswap = np.zeros((128, 128), np.float32)
    for d in range(0, 128, 2):
        pswap[d, d + 1] = 1.0
        pswap[d + 1, d] = 1.0
    C["pswapT"] = np.ascontiguousarray(pswap.T)
    C["identF"] = np.eye(128, dtype=np.float32)
    kk = np.arange(128)[:, None]
    rr = np.arange(128)[None, :]
    fd = np.where((kk // 16 == rr // 16) & (kk % 16 <= rr % 16), PEN, 0.0).astype(np.float32)
    C["Dfdiag4"] = np.ascontiguousarray(np.tile(fd, (1, 4)).astype(ml_dtypes.bfloat16))
    dq = rr - kk
    sd = np.where((dq >= 0) & (dq <= 64), PEN, 0.0).astype(np.float32)
    C["Dsdiag4"] = np.ascontiguousarray(np.tile(sd, (1, 4)).astype(ml_dtypes.bfloat16))
    dqp = (rr + 128) - kk
    sp = np.where((dqp >= 0) & (dqp <= 64), PEN, 0.0).astype(np.float32)
    C["Dsprev4"] = np.ascontiguousarray(np.tile(sp, (1, 4)).astype(ml_dtypes.bfloat16))
    dc = np.zeros((128, 64 * 512), np.float32)
    for i in range(32):
        for jt in range(2):
            j = 128 * jt + np.arange(128)
            q = 128 * i + np.arange(128)
            allowed = (16 * (j[:, None] + 1) - 1) < q[None, :]
            blk = np.where(allowed, 0.0, -30000.0).astype(np.float32)
            dc[:, 512 * (2 * i + jt):512 * (2 * i + jt) + 512] = np.tile(blk, (1, 4))
    C["Dc"] = np.ascontiguousarray(
        dc.reshape(128, 64, 512).transpose(1, 0, 2).astype(ml_dtypes.bfloat16))
    e80 = np.zeros((16, 128, 128), np.float32)
    for v in range(16):
        for j in range(8):
            e80[v, 8 * v + j, 16 * j:16 * (j + 1)] = PEN
    C["E80"] = e80.astype(ml_dtypes.bfloat16)
    C["ones_row"] = np.ones((1, 128), np.float32)
    return C


def host_attn0_weights(inp, c):
    b, q4 = c // 4, c % 4
    kvhs = [2 * q4, 2 * q4 + 1]
    wqkv = np.asarray(inp["L0_attn_w_qkv"], np.float32)
    cols = []
    for kvh in kvhs:
        for g in range(4):
            h = 4 * kvh + g
            cols.append(wqkv[128 * h:128 * (h + 1)])
    d = {}
    blocks = []
    for pi, kvh in enumerate(kvhs):
        for g in range(4):
            h = 4 * kvh + g
            blocks.append(wqkv[128 * h:128 * (h + 1)])
        blocks.append(wqkv[4096 + 128 * kvh:4096 + 128 * (kvh + 1)])
        blocks.append(wqkv[5120 + 128 * kvh:5120 + 128 * (kvh + 1)])
    d["wqkvT_sl"] = np.ascontiguousarray(np.concatenate([blk.T for blk in blocks], axis=1))
    wg = np.asarray(inp["L0_attn_w_gate"], np.float32)
    gcols = []
    for kvh in kvhs:
        for g in range(4):
            h = 4 * kvh + g
            gcols.append(wg[3 * h:3 * (h + 1)])
    d["wgateT_sl"] = np.ascontiguousarray(np.concatenate([gc.T for gc in gcols], axis=1))
    bg = np.asarray(inp["L0_attn_b_gate"], np.float32)
    d["bgate_row"] = np.ascontiguousarray(
        np.concatenate([bg[3 * (4 * kvh + g):3 * (4 * kvh + g + 1)]
                        for kvh in kvhs for g in range(4)])[None, :])
    kpos = np.asarray(inp["L0_attn_k_pos"], np.float32)
    vpos = np.asarray(inp["L0_attn_v_pos"], np.float32)
    d["kpos_cols"] = np.ascontiguousarray(
        np.concatenate([kpos[kvh].T for kvh in kvhs], axis=1))
    d["vpos_cols"] = np.ascontiguousarray(
        np.concatenate([vpos[kvh].T for kvh in kvhs], axis=1))
    mem = np.asarray(inp["L0_attn_mem_kv"], np.float32)
    d["memk_cols"] = np.ascontiguousarray(
        np.stack([mem[0, kvh, 0] for kvh in kvhs], axis=1))
    rows = np.zeros((2, 1, 129), np.float32)
    for pi, kvh in enumerate(kvhs):
        rows[pi, 0, :128] = mem[1, kvh, 0]
        rows[pi, 0, 128] = 1.0
    d["memv_rows"] = rows
    d["kb1_cols"] = np.ascontiguousarray(np.asarray(inp["L0_attn_k_b1"], np.float32).reshape(16, 128).T)
    d["vb1_cols"] = np.ascontiguousarray(np.asarray(inp["L0_attn_v_b1"], np.float32).reshape(16, 128).T)
    d["kb2_col"] = np.asarray(inp["L0_attn_k_b2"], np.float32).reshape(128, 1).copy()
    d["vb2_col"] = np.asarray(inp["L0_attn_v_b2"], np.float32).reshape(128, 1).copy()
    return d


MEGA_CAST_SPECS = [s for s in CAST_SPECS if s[0] not in ("woT0_sl", "woT1_sl")]


def emit_wo_partial(P, nc, ap, sb, ps):
    for k in range(8):
        P.dma("sync", sb["ash"][:, 4096 * k:4096 * (k + 1)], ap["attn_sh"][k],
              writes=[("ash", k)], stream=f"at{k % 2}")
    for ot in range(32):
        par = ot % 2
        P.dma("scalar", sb["wop"][par][:].rearrange("p (k o) -> p k o", o=128),
              ap["wo_t"][ot].rearrange("(k p) o -> p k o", p=128),
              writes=[("wop", par)], stream=f"wf{par}")
        for qt in range(4):
            for h in range(2):
                bi = par * 4 + 2 * (qt % 2) + h
                bank = ps[bi]
                for k in range(8):
                    P.add("tensor", lambda e, k=k, qt=qt, h=h, bank=bank, par=par: e.matmul(
                        bank.ap()[:, 0:512],
                        sb["wop"][par][:, 128 * k:128 * (k + 1)],
                        sb["ash"][:, 4096 * k + 1024 * qt + 512 * h:
                                  4096 * k + 1024 * qt + 512 * h + 512],
                        start=(k == 0), stop=(k == 7)),
                        reads=[("wop", par), ("ash", k), ("ps", bi)], writes=[("ps", bi)])
                pwi = 2 * (qt % 2) + h
                P.add("scalar", lambda e, bank=bank, pwi=pwi: e.copy(
                    out=sb["pw"][pwi][:], in_=bank.ap()[:, 0:512]),
                    reads=[("ps", bi)], writes=[("pw", pwi), ("ps", bi)])
                P.dma("gpsimd", ap["prs"][qt, ot, :, 512 * h:512 * (h + 1)], sb["pw"][pwi][:],
                      reads=[("pw", pwi)], writes=[("prs", qt, ot, h)], stream=f"pst{par}")


def alloc_wo_bufs(nc, sb):
    sb["ash"] = nc.alloc_sbuf_tensor("s_ash", [128, 8 * 4096], BF16)
    sb["wop"] = [nc.alloc_sbuf_tensor(f"s_wop{i}", [128, 1024], BF16) for i in range(2)]
    sb["pw"] = [nc.alloc_sbuf_tensor(f"s_pw{i}", [128, 512], F32) for i in range(4)]


def emit_addres(P, nc, ap, sb, ps):
    for ot in range(32):
        par = ot % 2
        P.dma("sync", sb["xo"][par][:], ap["x_in"][ot], writes=[("xo", par)], stream=f"xo{par}")
        P.dma("scalar", sb["dl"][par][:], ap["delta"][ot],
              reads=tuple(ap.get("dl_dep", ())),
              writes=[("dl", par)], stream=f"dl{par}")
        P.add("vector", lambda e, par=par: e.tensor_add(
            out=sb["xo"][par][:], in0=sb["xo"][par][:], in1=sb["dl"][par][:]),
            reads=[("xo", par), ("dl", par)], writes=[("xo", par)])
        P.dma("gpsimd", ap["x_out"][ot], sb["xo"][par][:],
              reads=[("xo", par)], writes=[("xout", ot)], stream=f"xst{par}")


def alloc_addres_bufs(nc, sb):
    sb["xo"] = [nc.alloc_sbuf_tensor(f"s_xo{i}", [128, 1024], F32) for i in range(2)]
    sb["dl"] = [nc.alloc_sbuf_tensor(f"s_dl{i}", [128, 1024], F32) for i in range(2)]


def _mk_lnf(v4):
    def f(dt, c0, c1):
        j = c0 // 1024
        return v4[j, dt, :, c0 - 1024 * j:c1 - 1024 * j]
    return f


def build_mega():
    nc = bass.Bass("TRN2", target_bir_lowering=False)

    def ein(n, s, d):
        return nc.dram_tensor(n, s, d, kind="ExternalInput").ap()

    def eout(n, s, d):
        return nc.dram_tensor(n, s, d, kind="ExternalOutput").ap()

    def idram(n, s, d):
        return nc.dram_tensor(n, s, d, kind="Internal")

    ap = {}
    ap["qw"] = ein("qw", [8, 512, 128], I32)
    ap["lut4"] = ein("lut4", [512, 16], F32)
    ap["lut8"] = ein("lut8", [512, 256], F32)
    ap["bitmask"] = ein("bitmask", [128, 4096], I32)
    ap["ident_bf"] = ein("ident_bf", [128, 128], BF16)
    for (iname, rows, cols, oname) in MEGA_CAST_SPECS:
        ap[iname] = ein(iname, [rows, cols], F32)
    ap["wgateT_sl"] = ein("wgateT_sl", [4096, 24], F32)
    ap["A_er"] = ein("A_er", [256, 4096], F32)
    ap["Bt_sl"] = ein("Bt_sl", [256, 512], F32)
    ap["probs_rep"] = ein("probs_rep", [2, 128, 1], F32)
    ap["WT_sl"] = ein("WT_sl", [4096, 512], F32)
    ap["woP0"] = ein("woP0", [4, 1024, 1024], F32)
    ap["woP1"] = ein("woP1", [4, 1024, 1024], F32)
    ap["x_in0"] = ein("x_in0", [32, 128, 1024], F32)
    ap["bias0"] = ein("bias0", [128, 32], F32)
    ap["bias1"] = ein("bias1", [128, 32], F32)
    ap["ones_col"] = ein("ones_col", [128, 1], F32)
    ap["ones_row"] = ein("ones_row", [1, 128], F32)
    ap["C0"] = ein("C0", [128, 4096], BF16)
    ap["S0"] = ein("S0", [128, 4096], BF16)
    ap["pswapT"] = ein("pswapT", [128, 128], F32)
    ap["identF"] = ein("identF", [128, 128], F32)
    ap["Dfdiag4"] = ein("Dfdiag4", [128, 512], BF16)
    ap["Dsdiag4"] = ein("Dsdiag4", [128, 512], BF16)
    ap["Dsprev4"] = ein("Dsprev4", [128, 512], BF16)
    ap["Dc"] = ein("Dc", [64, 128, 512], BF16)
    ap["E80"] = ein("E80", [16, 128, 128], BF16)
    ap["bgate_row"] = ein("bgate_row", [1, 24], F32)
    ap["kb1_cols"] = ein("kb1_cols", [128, 16], F32)
    ap["vb1_cols"] = ein("vb1_cols", [128, 16], F32)
    ap["kb2_col"] = ein("kb2_col", [128, 1], F32)
    ap["vb2_col"] = ein("vb2_col", [128, 1], F32)
    ap["kpos_cols"] = ein("kpos_cols", [128, 32], F32)
    ap["vpos_cols"] = ein("vpos_cols", [128, 32], F32)
    ap["memk_cols"] = ein("memk_cols", [128, 2], F32)
    ap["memv_rows"] = ein("memv_rows", [2, 1, 129], F32)
    ap["C"] = ein("C", [128, 4096], BF16)
    ap["S"] = ein("S", [128, 4096], BF16)
    ap["protT"] = ein("protT", [128, 128], F32)
    ap["Ddiag4"] = ein("Ddiag4", [128, 512], F32)
    ap["Dprev4"] = ein("Dprev4", [128, 512], F32)
    ap["DcausQK"] = ein("DcausQK", [4, 128, 512], F32)
    ap["recent4"] = ein("recent4", [128, 512], BF16)
    ap["E01"] = ein("E01", [16, 128, 128], BF16)
    ap["init01"] = ein("init01", [128, 256], BF16)
    ap["zero512"] = ein("zero512", [1, 512], BF16)
    ap["x_out"] = eout("x_out", [32, 128, 1024], F32)

    cc_pairs = {}
    shard3 = {}
    full3 = {}
    for (nm, a, b) in (("wl", 4, 4096), ("wh", 4, 4096), ("weff", 4, 4096),
                       ("kw1", 2, 2048), ("vw1", 2, 2048)):
        sh = idram(nm + "_shd", [a * b, 128], BF16)
        fu = idram(nm + "_ful", [8 * a * b, 128], BF16)
        cc_pairs[nm] = (sh, fu)
        shard3[nm] = sh.ap().rearrange("(a b) c -> a b c", b=b)
        full3[nm] = fu.ap().rearrange("(a b) c -> a b c", b=b)
    kw2_bf = idram("kw2_bfT", [1, 2048, 128], BF16).ap()
    vw2_bf = idram("vw2_bfT", [1, 2048, 128], BF16).ap()
    wqkv_bf = idram("wqkv_bfT", [12, 4096, 128], BF16).ap()
    wgate_bf = idram("wgate_bfT", [4096, 24], BF16).ap()
    wq_bf = idram("wq_bfT", [8, 4096, 128], BF16).ap()
    wk_bf = idram("wk_bfT", [2, 4096, 128], BF16).ap()
    wv_bf = idram("wv_bfT", [2, 4096, 128], BF16).ap()
    wo_t = [idram(f"wo_t{l}", [32, 1024, 128], BF16).ap() for l in range(2)]
    qc_hbm = idram("qc_hbm", [2, 4, 128, 4096], BF16).ap()
    xa = idram("xa", [32, 128, 1024], F32).ap()
    xb = idram("xb", [32, 128, 1024], F32).ap()
    lnsh = [idram(f"lnsh{l}", [4096, 1024], BF16) for l in range(2)]
    lng = [idram(f"lng{l}", [16384, 1024], BF16) for l in range(2)]
    lnsh3 = [t.ap().rearrange("(a p) t -> a p t", p=128) for t in lnsh]
    lng4 = [t.ap().rearrange("(r a p) t -> r a p t", a=32, p=128) for t in lng]
    atsh = [idram(f"atsh{l}", [8, 128, 4096], BF16).ap() for l in range(2)]
    prs = [idram(f"prs{l}", [16384, 1024], F32) for l in range(2)]
    prs4 = [t.ap().rearrange("(q a p) t -> q a p t", a=32, p=128) for t in prs]
    dl = [idram(f"dl{l}", [4096, 1024], F32) for l in range(2)]
    dl3 = [t.ap().rearrange("(a p) t -> a p t", p=128) for t in dl]

    ps = [nc.alloc_psum_tensor(f"ps{i}", [128, 512], F32) for i in range(8)]
    arena = Arena(nc, nc.sbuf_base, nc.sbuf_top)
    P = Prog(nc)
    groups8 = [list(range(8))]
    groups4 = [[0, 1, 2, 3], [4, 5, 6, 7]]

    def cc_op(kind, alu, groups, in_t, out_t, wkey):
        P.add("gpsimd", lambda e, in_t=in_t, out_t=out_t: e.collective_compute(
            kind, alu, replica_groups=groups,
            ins=[in_t.ap().opt()], outs=[out_t.ap().opt()]),
            reads=(), writes=[wkey], kind="cc")

    def base_ap(**kw):
        d = dict(ap)
        d.update(kw)
        return d

    def ffn_phase(prefix, kind, x_in, x_out, wdep=()):
        arena.reset()
        an = _ArenaNc(arena, prefix)
        sb = {}
        alloc_ln_bufs(an, sb)
        alloc_ffn_bufs(an, sb, kind == "l0")
        if kind == "l0":
            apd = base_ap(x_in=x_in, x_out=x_out, bias=ap["bias0"],
                          wl_t=full3["wl"], wh_t=full3["wh"], wdep=wdep)
        else:
            apd = base_ap(x_in=x_in, x_out=x_out, bias=ap["bias1"],
                          wf_t=full3["weff"], wdep=wdep)
        ln_load_consts(P, apd, sb)
        emit_ffn(P, nc, apd, sb, ps, kind)
        P.barrier()

    with nc.Block() as block:
        arena.reset()
        an = _ArenaNc(arena, "p0_")
        sb = {}
        alloc_prep_bufs(an, sb)
        ap_prep = base_ap(wl_sh=shard3["wl"], wh_sh=shard3["wh"],
                          weff_sh=shard3["weff"], kw1_sh=shard3["kw1"],
                          vw1_sh=shard3["vw1"], kw2_bf=kw2_bf, vw2_bf=vw2_bf,
                          wqkv_bf=wqkv_bf, wgate_bf=wgate_bf, wq_bf=wq_bf,
                          wk_bf=wk_bf, wv_bf=wv_bf)
        wo_casts = []
        for l in range(2):
            for cb in range(4):
                wo_casts.append((ap[f"woP{l}"][cb], wo_t[l], 1024, 1024,
                                 8 * cb, f"wo_t{l}"))
        emit_dequant(P, nc, ap_prep, sb, ps)
        emit_weight_prep(P, nc, ap_prep, sb, ps, cast_specs=MEGA_CAST_SPECS,
                         extra_casts=wo_casts)
        P.barrier()
        for nm in ("wl", "wh", "weff", "kw1", "vw1"):
            cc_op("AllGather", ALU.bypass, groups8, cc_pairs[nm][0],
                  cc_pairs[nm][1], nm + "_ful")

        for l, (x_pre, x_mid) in enumerate(((ap["x_in0"], xa), (xa, xb))):
            kind = "l0" if l == 0 else "l1"
            ffn_phase(f"f{l}a_", kind, x_pre, x_mid,
                      wdep=("wl_ful", "wh_ful") if l == 0 else ("weff_ful",))
            arena.reset()
            an = _ArenaNc(arena, f"ln{l}_")
            sb = {}
            alloc_ln_bufs(an, sb)
            apd = base_ap(x_in=x_mid)
            ln_load_consts(P, apd, sb)
            emit_ln(P, nc, apd, sb, ps, compute_e=False, ln_out_hbm=lnsh3[l])
            P.barrier()
            cc_op("AllGather", ALU.bypass, groups4, lnsh[l], lng[l], ("lng", l))
            arena.reset()
            an = _ArenaNc(arena, f"a{l}_")
            sb = {}
            if l == 0:
                alloc_attn0_bufs(an, sb)
                apd = base_ap(lnf=_mk_lnf(lng4[l]), attn_out=atsh[l],
                              qc_hbm=qc_hbm, kw1_sh=full3["kw1"],
                              vw1_sh=full3["vw1"], kw2_bf=kw2_bf,
                              vw2_bf=vw2_bf, wqkv_bf=wqkv_bf,
                              wgate_bf=wgate_bf, lnf_dep=(("lng", l),))
                P.add("vector", lambda e, sb=sb: e.memset(sb["neg80"][:], -MSKB),
                      writes=["neg80"])
                P.add("vector", lambda e, sb=sb: e.memset(sb["negk"][:], -1000.0),
                      writes=["negk"])
                emit_attn0(P, nc, apd, sb, ps)
            else:
                alloc_attn1_bufs(an, sb)
                apd = base_ap(lnf=_mk_lnf(lng4[l]), attn_out=atsh[l],
                              wq_bf=wq_bf, wk_bf=wk_bf, wv_bf=wv_bf,
                              lnf_dep=(("lng", l),))
                P.dma("sync", sb["zero512"][0:1, :], apd["zero512"],
                      writes=["zero512"], stream="init")
                P.add("vector", lambda e, sb=sb: e.memset(sb["neg80"][:], -MSKB),
                      writes=["neg80"])
                P.add("vector", lambda e, sb=sb: e.memset(sb["zero1"][:], 0.0),
                      writes=["zero1"])
                emit_attn1(P, nc, apd, sb, ps)
            P.barrier()
            arena.reset()
            an = _ArenaNc(arena, f"w{l}_")
            sb = {}
            alloc_wo_bufs(an, sb)
            apd = base_ap(attn_sh=atsh[l], wo_t=wo_t[l], prs=prs4[l])
            emit_wo_partial(P, nc, apd, sb, ps)
            P.barrier()
            cc_op("ReduceScatter", ALU.add, groups4, prs[l], dl[l], ("dl", l))
            arena.reset()
            an = _ArenaNc(arena, f"r{l}_")
            sb = {}
            alloc_addres_bufs(an, sb)
            x_post = xb if l == 0 else xa
            apd = base_ap(x_in=x_mid, delta=dl3[l], x_out=x_post,
                          dl_dep=(("dl", l),))
            emit_addres(P, nc, apd, sb, ps)
            P.barrier()
            x_fin = xa if l == 0 else ap["x_out"]
            ffn_phase(f"f{l}b_", kind, x_post, x_fin)
        P.emit(block)
    return nc


def host_mega_inputs(inputs, consts, ac0, ac1, c):
    b, q4 = c // 4, c % 4
    m = {}
    qw = np.asarray(inputs["L0_ffn_qweight"])[:, 512 * c:512 * (c + 1), :]
    m["qw"] = np.ascontiguousarray(qw)
    m["lut4"] = _f32(np.asarray(inputs["L0_ffn_lut4"], np.float32)[512 * c:512 * (c + 1)])
    m["lut8"] = _f32(np.asarray(inputs["L0_ffn_lut8"], np.float32)[512 * c:512 * (c + 1)])
    m["bitmask"] = consts["bitmask"]
    m["ident_bf"] = consts["ident_bf"]
    w0 = host_attn0_weights(inputs, c)
    w1 = host_attn1_weights(inputs, c)
    m["wqkvT_sl"] = w0["wqkvT_sl"]
    m["wgateT_sl"] = w0["wgateT_sl"]
    m["wqT_sl"] = w1["wqT_sl"]
    m["wkT_sl"] = w1["wkT_sl"]
    m["wvT_sl"] = w1["wvT_sl"]
    m["kw1T_sl"] = _f32(np.asarray(inputs["L0_attn_k_w1"], np.float32).T[:, 256 * c:256 * (c + 1)])
    m["vw1T_sl"] = _f32(np.asarray(inputs["L0_attn_v_w1"], np.float32).T[:, 256 * c:256 * (c + 1)])
    m["kw2T"] = _f32(np.asarray(inputs["L0_attn_k_w2"], np.float32).T)
    m["vw2T"] = _f32(np.asarray(inputs["L0_attn_v_w2"], np.float32).T)
    m["A_er"] = _f32(np.asarray(inputs["L1_ffn_A"], np.float32).reshape(256, 4096))
    m["Bt_sl"] = _f32(np.asarray(inputs["L1_ffn_B"], np.float32)
                      .transpose(0, 2, 1).reshape(256, 4096)[:, 512 * c:512 * (c + 1)])
    pr = np.asarray(inputs["L1_ffn_probs"], np.float32)
    prep = np.zeros((2, 128, 1), np.float32)
    for t in range(2):
        prep[t, :, 0] = pr[(t * 128 + np.arange(128)) // 16]
    m["probs_rep"] = prep
    m["WT_sl"] = _f32(np.asarray(inputs["L1_ffn_W"], np.float32).T[:, 512 * c:512 * (c + 1)])
    for l, wname in ((0, "L0_attn_w_o"), (1, "L1_attn_wo")):
        woT = np.asarray(inputs[wname], np.float32).T[1024 * q4:1024 * (q4 + 1), :]
        m[f"woP{l}"] = np.ascontiguousarray(
            woT.reshape(1024, 4, 1024).transpose(1, 0, 2))
    x0 = np.asarray(inputs["L0_attn_hidden_states"], np.float32)
    m["x_in0"] = _fm(x0[b], q4)
    m["bias0"] = np.ascontiguousarray(np.asarray(inputs["L0_ffn_bias"], np.float32).reshape(32, 128).T)
    m["bias1"] = np.ascontiguousarray(np.asarray(inputs["L1_ffn_b"], np.float32).reshape(32, 128).T)
    m["ones_col"] = consts["ones_col"]
    m["ones_row"] = consts["ones_row"]
    for k in ("C0", "S0", "pswapT", "identF", "Dfdiag4", "Dsdiag4", "Dsprev4",
              "Dc", "E80"):
        m[k] = ac0[k]
    for k in ("bgate_row", "kb1_cols", "vb1_cols", "kb2_col", "vb2_col",
              "kpos_cols", "vpos_cols", "memk_cols", "memv_rows"):
        m[k] = w0[k]
    for k in ("C", "S", "protT", "Ddiag4", "Dprev4", "DcausQK", "recent4",
              "E01", "init01", "zero512"):
        m[k] = ac1[k]
    return m


def _fm(xb, q4):
    return np.ascontiguousarray(
        xb[1024 * q4:1024 * (q4 + 1), :].T.reshape(32, 128, 1024))


def host_p0full_inputs(inp, consts, c):
    m = {}
    qw = np.asarray(inp["L0_ffn_qweight"])[:, 512 * c:512 * (c + 1), :]
    m["qw"] = np.ascontiguousarray(qw)
    m["lut4"] = _f32(np.asarray(inp["L0_ffn_lut4"], np.float32)[512 * c:512 * (c + 1)])
    m["lut8"] = _f32(np.asarray(inp["L0_ffn_lut8"], np.float32)[512 * c:512 * (c + 1)])
    m["bitmask"] = consts["bitmask"]
    m["ident_bf"] = consts["ident_bf"]
    w0 = host_attn0_weights(inp, c)
    w1 = host_attn1_weights(inp, c)
    m["wqkvT_sl"] = w0["wqkvT_sl"]
    m["wgateT_sl"] = w0["wgateT_sl"]
    m["wqT_sl"] = w1["wqT_sl"]
    m["wkT_sl"] = w1["wkT_sl"]
    m["wvT_sl"] = w1["wvT_sl"]
    m["woT0_sl"] = _f32(np.asarray(inp["L0_attn_w_o"], np.float32).T[:, 512 * c:512 * (c + 1)])
    m["woT1_sl"] = _f32(np.asarray(inp["L1_attn_wo"], np.float32).T[:, 512 * c:512 * (c + 1)])
    m["kw1T_sl"] = _f32(np.asarray(inp["L0_attn_k_w1"], np.float32).T[:, 256 * c:256 * (c + 1)])
    m["vw1T_sl"] = _f32(np.asarray(inp["L0_attn_v_w1"], np.float32).T[:, 256 * c:256 * (c + 1)])
    m["kw2T"] = _f32(np.asarray(inp["L0_attn_k_w2"], np.float32).T)
    m["vw2T"] = _f32(np.asarray(inp["L0_attn_v_w2"], np.float32).T)
    m["A_er"] = _f32(np.asarray(inp["L1_ffn_A"], np.float32).reshape(256, 4096))
    m["Bt_sl"] = _f32(np.asarray(inp["L1_ffn_B"], np.float32)
                      .transpose(0, 2, 1).reshape(256, 4096)[:, 512 * c:512 * (c + 1)])
    pr = np.asarray(inp["L1_ffn_probs"], np.float32)
    prep = np.zeros((2, 128, 1), np.float32)
    for t in range(2):
        prep[t, :, 0] = pr[(t * 128 + np.arange(128)) // 16]
    m["probs_rep"] = prep
    m["WT_sl"] = _f32(np.asarray(inp["L1_ffn_W"], np.float32).T[:, 512 * c:512 * (c + 1)])
    return m


def kernel(**inputs):
    consts = host_constants()
    ac0 = host_attn0_consts()
    ac1 = host_attn1_consts()
    nc = build_mega()
    maps = [host_mega_inputs(inputs, consts, ac0, ac1, c) for c in range(8)]
    res = run_bass_kernel_spmd(nc, maps, core_ids=list(range(8)))
    out = np.empty((2, 4096, 4096), np.float32)
    for c in range(8):
        b, q4 = c // 4, c % 4
        xs = np.asarray(res.results[c]["x_out"])
        out[b, 1024 * q4:1024 * (q4 + 1), :] = xs.reshape(4096, 1024).T
    return out


def kernel_unfused(**inputs):
    consts = host_constants()
    nc = build_p0full()
    res = run_bass_kernel_spmd(
        nc, [host_p0full_inputs(inputs, consts, c) for c in range(8)],
        core_ids=list(range(8)))
    W = {}
    for nm in ("wl_sh", "wh_sh", "wo0_sh", "wo1_sh", "weff_sh", "kw1_sh", "vw1_sh"):
        W[nm] = np.ascontiguousarray(
            np.concatenate([np.asarray(res.results[c][nm]) for c in range(8)], axis=0))
    for nm in ("kw2_bf", "vw2_bf"):
        W[nm] = np.ascontiguousarray(np.asarray(res.results[0][nm]))
    for nm in ("wqkv_bf", "wgate_bf", "wq_bf", "wk_bf", "wv_bf"):
        W[nm] = [np.ascontiguousarray(np.asarray(res.results[c][nm])) for c in range(8)]

    bias0 = np.ascontiguousarray(np.asarray(inputs["L0_ffn_bias"], np.float32).reshape(32, 128).T)
    bias1 = np.ascontiguousarray(np.asarray(inputs["L1_ffn_b"], np.float32).reshape(32, 128).T)

    def run_ffn(kind, xs):
        nc = build_ffn(kind)
        maps = []
        for c in range(8):
            m = {"x_in": xs[c], "ones_col": consts["ones_col"],
                 "ones_row": consts["ones_row"]}
            if kind == "l0":
                m["bias"] = bias0
                m["wl_t"] = W["wl_sh"]
                m["wh_t"] = W["wh_sh"]
            else:
                m["bias"] = bias1
                m["wf_t"] = W["weff_sh"]
            maps.append(m)
        r = run_bass_kernel_spmd(nc, maps, core_ids=list(range(8)))
        return [np.ascontiguousarray(np.asarray(r.results[c]["x_out"])) for c in range(8)]

    def run_lnout(xs):
        nc = build_lnout()
        maps = [{"x_in": xs[c], "ones_col": consts["ones_col"],
                 "ones_row": consts["ones_row"]} for c in range(8)]
        r = run_bass_kernel_spmd(nc, maps, core_ids=list(range(8)))
        ln = [np.asarray(r.results[c]["ln_out"]) for c in range(8)]
        return [np.ascontiguousarray(np.concatenate([ln[4 * b + j] for j in range(4)], axis=2))
                for b in range(2)]

    def run_wo(xs, att_by_batch, wo_t):
        nc = build_wo()
        maps = []
        for c in range(8):
            b, q4 = c // 4, c % 4
            sl = slice(1024 * q4, 1024 * (q4 + 1))
            maps.append({"x_in": xs[c], "wo_t": wo_t,
                         "attn_in": np.ascontiguousarray(att_by_batch[b][:, :, sl])})
        r = run_bass_kernel_spmd(nc, maps, core_ids=list(range(8)))
        return [np.ascontiguousarray(np.asarray(r.results[c]["x_out"])) for c in range(8)]

    x0 = np.asarray(inputs["L0_attn_hidden_states"], np.float32)
    xs = [_fm(x0[c // 4], c % 4) for c in range(8)]

    xs = run_ffn("l0", xs)
    ln_b = run_lnout(xs)
    ac0 = host_attn0_consts()
    nc = build_attn0()
    maps = []
    for c in range(8):
        b = c // 4
        w0 = host_attn0_weights(inputs, c)
        m = {"ln_full": ln_b[b], "wqkv_bf": W["wqkv_bf"][c], "wgate_bf": W["wgate_bf"][c],
             "kw1_sh": W["kw1_sh"], "vw1_sh": W["vw1_sh"],
             "kw2_bf": W["kw2_bf"], "vw2_bf": W["vw2_bf"]}
        for kk in ("bgate_row", "kb1_cols", "vb1_cols", "kb2_col", "vb2_col",
                   "kpos_cols", "vpos_cols", "memk_cols", "memv_rows"):
            m[kk] = w0[kk]
        m.update(ac0)
        maps.append(m)
    r = run_bass_kernel_spmd(nc, maps, core_ids=list(range(8)))
    att_b = [np.concatenate([np.asarray(r.results[4 * b + j]["attn_out"]) for j in range(4)],
                            axis=0) for b in range(2)]
    xs = run_wo(xs, att_b, W["wo0_sh"])
    xs = run_ffn("l0", xs)

    xs = run_ffn("l1", xs)
    ln_b = run_lnout(xs)
    ac1 = host_attn1_consts()
    nc = build_attn1()
    maps = []
    for c in range(8):
        b = c // 4
        m = {"ln_full": ln_b[b], "wq_bf": W["wq_bf"][c], "wk_bf": W["wk_bf"][c],
             "wv_bf": W["wv_bf"][c]}
        m.update(ac1)
        maps.append(m)
    r = run_bass_kernel_spmd(nc, maps, core_ids=list(range(8)))
    att_b = [np.concatenate([np.asarray(r.results[4 * b + j]["attn_out"]) for j in range(4)],
                            axis=0) for b in range(2)]
    xs = run_wo(xs, att_b, W["wo1_sh"])
    xs = run_ffn("l1", xs)

    out = np.empty((2, 4096, 4096), np.float32)
    for c in range(8):
        b, q4 = c // 4, c % 4
        out[b, 1024 * q4:1024 * (q4 + 1), :] = xs[c].reshape(4096, 1024).T
    return out
